# version 04
# baseline (speedup 1.0000x reference)
import numpy as np
import ml_dtypes
from contextlib import ExitStack

import concourse.bass as bass
import concourse.mybir as mybir
from concourse.bass_utils import run_bass_kernel_spmd
from concourse.bass import IndirectOffsetOnAxis

F32 = mybir.dt.float32
BF16 = mybir.dt.bfloat16
I32 = mybir.dt.int32
U8 = mybir.dt.uint8
AF = mybir.ActivationFunctionType
ALU = mybir.AluOpType
AX = mybir.AxisListType

B, T, D = 4, 2048, 2048
HALF = T // 2
NTOK = HALF
RT = NTOK // 128
KC = D // 128
HD = 8192
HC = HD // 128
EPS = 1e-5
BIGNEG = -1e30
NONCE_BASE = 4096.0

PAIRS = [[0, 1], [2, 3], [4, 5], [6, 7]]
WORLD = [[0, 1, 2, 3, 4, 5, 6, 7]]

ENGINES = ("tensor", "vector", "scalar", "gpsimd", "sync")


class Prog:

    def __init__(self, nc, E, nphys=88):
        self.nc = nc
        self.E = E
        self.ops = {e: [] for e in ENGINES}
        self.phys = [E(nc.semaphore(f"ps{i}")) for i in range(nphys)]
        self.phys_cnt = [0] * nphys
        self.map = {}
        self.base = {}
        self.uid = 0
        self.free = list(range(nphys))
        self.guard_sems = [E(nc.semaphore(f"gsp{i}")) for i in range(9)]
        self.guard_i = 0

    def sem(self, name):
        if name not in self.map:
            if not self.free:
                raise RuntimeError("semaphore pool exhausted; call end_scope")
            idx = self.free.pop(0)
            self.map[name] = idx
            self.base[name] = self.phys_cnt[idx]
        return self.phys[self.map[name]]

    def val(self, name):
        self.sem(name)
        return self.phys_cnt[self.map[name]] - self.base[name]

    def _abs(self, name, v):
        self.sem(name)
        return self.base[name] + v

    def end_scope(self):
        for name in list(self.map):
            self.free.append(self.map.pop(name))
            del self.base[name]
        self.free.sort()

    def q(self, eng, fn, inc=None, inc_by=1, waits=(), incs=None):
        waits = [w for w in waits if w is not None]
        sem_objs = [(self.sem(n), self._abs(n, v)) for n, v in waits]
        all_incs = []
        if inc is not None:
            all_incs.append((inc, inc_by))
        if incs:
            all_incs.extend(incs)
        ret = None
        if all_incs:
            inc_objs = []
            for nm, by in all_incs:
                self.sem(nm)
                self.phys_cnt[self.map[nm]] += by
                inc_objs.append((self.phys[self.map[nm]], by))
            ret = self.val(all_incs[0][0])

            def wrapped(e, fn=fn, sem_objs=sem_objs, inc_objs=inc_objs):
                for s, v in sem_objs:
                    e.wait_ge(s, v)
                inst = fn(e)
                for s, by in inc_objs:
                    inst = inst.then_inc(s, by)
        else:
            def wrapped(e, fn=fn, sem_objs=sem_objs):
                for s, v in sem_objs:
                    e.wait_ge(s, v)
                fn(e)
        self.ops[eng].append(wrapped)
        return ret

    def wait(self, eng, name, value):
        s = self.sem(name)
        av = self._abs(name, value)
        self.ops[eng].append(lambda e, s=s, v=av: e.wait_ge(s, v))

    def dma(self, eng, out, in_, ch, waits=()):
        return self.q(eng, lambda e, o=out, i=in_: e.dma_start(o, i),
                      inc=ch, inc_by=16, waits=waits)

    def seal(self, chname):
        v = self.val(chname)
        for e in ENGINES:
            self.wait(e, chname, v)
        self.end_scope()

    def realize(self):
        block = self.E(self.nc.Block())
        for name in ENGINES:
            oplist = self.ops[name]
            getattr(block, name)(lambda eng, oplist=oplist: [f(eng) for f in oplist])



def emit_ident(p, ident_bf, tag):
    p.q("gpsimd", lambda g, ib=ident_bf: g.memset(ib[:, :], 0.0), inc=tag)
    p.q("gpsimd", lambda g, ib=ident_bf: g.affine_select(
        ib[:, :], ib[:, :], compare_op=ALU.not_equal, fill=1.0, base=0,
        pattern=[[-1, ib.shape[1]]], channel_multiplier=1), inc=tag)
    tgt = p.val(tag)
    p.wait("tensor", tag, tgt)




def emit_cc_nonce(p, E, tag, nonce_dram_ap, nonce):
    nc = p.nc
    dt = nonce_dram_ap.dtype
    t = E(nc.sbuf_tensor(tag + "nnc", [1, 64], dt))
    p.q("gpsimd", lambda g: g.memset(t[:, :], float(nonce)), inc=tag + "nncg")
    return p.dma("sync", nonce_dram_ap, t[:, :], tag + "nncd",
                 waits=[(tag + "nncg", p.val(tag + "nncg"))])


def emit_cc_guard(p, E, tag, probe_src_ap, nonce, wait_ch, wait_v, npts=1):
    nc = p.nc
    probe = E(nc.sbuf_tensor(tag + "prb", [1, npts * 4], I32))
    bits = int(nonce)
    gs = p.guard_sems[p.guard_i]
    p.guard_i += 1

    def spin(g):
        with g.register(tag + "rv") as rv, g.register(tag + "rc") as rc, \
             g.register(tag + "rd") as rd, g.register(tag + "rt") as rt:
            g.reg_mov(rc, 0)
            g.reg_mov(rd, 1)
            with g.While(rd):
                g.reg_add(rc, rc, 16)
                g.dma_start(probe[:, :], probe_src_ap).then_inc(gs, 16)
                g.wait_ge(gs, rc)
                g.reg_mov(rd, 0)
                for c in range(npts):
                    g.reg_load(rv, probe[0:1, c * 4 + 3:c * 4 + 4])
                    g.reg_alu(rt, rv, bits, ALU.not_equal)
                    g.reg_alu(rd, rd, rt, ALU.bitwise_or)
        return g.nop()
    p.q("gpsimd", spin, inc=tag + "ccok", waits=[(wait_ch, wait_v)])
    return p.val(tag + "ccok")



class LNBufs:
    def __init__(self, E, nc, tag, with_psum=True):
        self.x_rt = [E(nc.sbuf_tensor(f"{tag}x0", [128, D], F32))]
        self.scr = E(nc.sbuf_tensor(f"{tag}scr", [128, D], F32))
        self.xn_bf = E(nc.sbuf_tensor(f"{tag}xnbf", [128, D], BF16))
        self.cols = E(nc.sbuf_tensor(f"{tag}cols", [128, 8], F32))
        if with_psum:
            self.ps_tr = [E(nc.psum_tensor(f"{tag}pstr{i}", [128, 512], BF16))
                          for i in range(2)]


def emit_ln(p, lb, ident_bf, x_dram, xnT, tag, xn_tok_f32=None, xn_tok_bf=None,
            keep_x_sb=None):
    c_sum = lb.cols[:, 0:1]; c_sq = lb.cols[:, 1:2]; c_mean = lb.cols[:, 2:3]
    c_var = lb.cols[:, 3:4]; c_srt = lb.cols[:, 4:5]; c_msq = lb.cols[:, 5:6]
    c_rstd = lb.cols[:, 6:7]; c_eps = lb.cols[:, 7:8]
    p.q("gpsimd", lambda g, c=c_eps: g.memset(c, EPS), inc=f"{tag}epsg")
    p.wait("scalar", f"{tag}epsg", p.val(f"{tag}epsg"))
    for r in range(RT):
        xa = lb.x_rt[0]
        xw = [(f"{tag}xfree", p.val(f"{tag}xfree"))] if r >= 1 else []
        dv = p.dma("sync", xa[:, :], x_dram[r * 128:(r + 1) * 128, :], f"{tag}xd",
                   waits=xw)
        sw = [(f"{tag}xd", dv)]
        if r >= 1:
            sw.append((f"{tag}sxn", p.val(f"{tag}sxn")))
        p.q("vector", lambda v, xa=xa, c=c_sum: v.reduce_sum(c, xa[:, :], axis=AX.X),
            inc=f"{tag}vsum", waits=sw)
        p.q("scalar", lambda s, xa=xa, sc=lb.scr:
            s.activation(sc[:, :], xa[:, :], AF.Square),
            inc=f"{tag}ssq", waits=sw)
        p.q("vector", lambda v, sc=lb.scr, c=c_sq: v.reduce_sum(c, sc[:, :], axis=AX.X),
            inc=f"{tag}ssq", waits=[(f"{tag}ssq", p.val(f"{tag}ssq"))])

        p.q("vector", lambda v, c=c_sum, mean=c_mean: v.tensor_scalar_mul(mean, c, 1.0 / D),
            inc=f"{tag}vm", waits=[(f"{tag}vsum", p.val(f"{tag}vsum"))])
        p.q("scalar", lambda s, mean=c_mean, msq=c_msq: s.activation(msq, mean, AF.Square),
            inc=f"{tag}sms", waits=[(f"{tag}vm", p.val(f"{tag}vm"))])
        p.q("vector", lambda v, c_sq=c_sq, msq=c_msq, var=c_var:
            v.scalar_tensor_tensor(var, c_sq, 1.0 / D, msq, ALU.mult, ALU.subtract),
            inc=f"{tag}vv",
            waits=[(f"{tag}sms", p.val(f"{tag}sms")), (f"{tag}ssq", p.val(f"{tag}ssq"))])
        p.q("scalar", lambda s, var=c_var, srt=c_srt, ce=c_eps:
            s.activation(srt, var, AF.Sqrt, bias=ce),
            inc=f"{tag}srt", waits=[(f"{tag}vv", p.val(f"{tag}vv"))])
        p.q("vector", lambda v, srt=c_srt, rstd=c_rstd: v.reciprocal(rstd, srt),
            inc=f"{tag}vfs", waits=[(f"{tag}srt", p.val(f"{tag}srt"))])
        vfs = p.val(f"{tag}vfs")
        xnw = [(f"{tag}vfs", vfs)]
        if xnT is not None and r >= 1:
            xnw.append((f"{tag}ttr", p.val(f"{tag}ttr")))
        p.q("vector", lambda v, xa=xa, xb=lb.xn_bf, rstd=c_rstd, mn=c_mean:
            v.tensor_scalar(xb[:, :], xa[:, :], mn, rstd, ALU.subtract, ALU.mult),
            inc=f"{tag}sxn", waits=xnw)
        if xn_tok_f32 is not None:
            p.q("vector", lambda v, xa=xa, o=xn_tok_f32, r=r, rstd=c_rstd, mn=c_mean:
                v.tensor_scalar(o[:, r * D:(r + 1) * D], xa[:, :], mn, rstd,
                                ALU.subtract, ALU.mult), inc=f"{tag}sxn")
        if xn_tok_bf is not None:
            p.q("vector", lambda v, xa=xa, o=xn_tok_bf, r=r, rstd=c_rstd, mn=c_mean:
                v.tensor_scalar(o[:, r * D:(r + 1) * D], xa[:, :], mn, rstd,
                                ALU.subtract, ALU.mult), inc=f"{tag}sxn")
        if keep_x_sb is not None:
            p.q("vector", lambda v, xa=xa, o=keep_x_sb, r=r:
                v.tensor_copy(o[:, r * D:(r + 1) * D], xa[:, :]), inc=f"{tag}sxn")
        p.q("scalar", lambda s: s.nop(), inc=f"{tag}xfree",
            waits=[(f"{tag}sxn", p.val(f"{tag}sxn"))])
        if xnT is not None:
            for g4 in range(4):
                G = r * 4 + g4
                pst = lb.ps_tr[G % 2]
                for j in range(4):
                    kc = g4 * 4 + j
                    tw = []
                    if j == 0:
                        tw.append((f"{tag}sxn", p.val(f"{tag}sxn")))
                        if G >= 2:
                            tw.append((f"{tag}trev", G - 1))
                    p.q("tensor", lambda t, ps=pst, j=j, kc=kc, xb=lb.xn_bf, ib=ident_bf:
                        t.transpose(ps[:, j * 128:(j + 1) * 128],
                                    xb[:, kc * 128:(kc + 1) * 128], ib[:, :]),
                        inc=f"{tag}ttr" if j == 3 else None, waits=tw)

                def evac(v, ps=pst, g4=g4, r=r, xnT=xnT):
                    for j in range(4):
                        kc = g4 * 4 + j
                        inst = v.tensor_copy(
                            xnT[:, kc * NTOK + r * 128: kc * NTOK + (r + 1) * 128],
                            ps[:, j * 128:(j + 1) * 128])
                    return inst
                p.q("vector", evac, inc=f"{tag}trev",
                    waits=[(f"{tag}ttr", p.val(f"{tag}ttr"))])
    if xnT is not None:
        p.wait("tensor", f"{tag}trev", p.val(f"{tag}trev"))
    p.wait("tensor", f"{tag}sxn", p.val(f"{tag}sxn"))



def emit_ffn(p, E, x_dram, y_dram, wg_sl, wu_sl, wd_sl, tag):
    nc = p.nc
    ident_bf = E(nc.sbuf_tensor(tag + "id", [128, 128], BF16))
    emit_ident(p, ident_bf, tag + "mi")

    xnT = E(nc.sbuf_tensor(tag + "xnT", [128, KC * NTOK], BF16))
    h_buf = E(nc.sbuf_tensor(tag + "h", [128, HC * NTOK], BF16))
    wbuf = [E(nc.sbuf_tensor(f"{tag}w{i}", [128, D], BF16)) for i in range(4)]
    sil = [E(nc.sbuf_tensor(f"{tag}sil{i}", [128, 512], BF16)) for i in range(2)]
    y_sb = [E(nc.sbuf_tensor(f"{tag}y{i}", [128, 512], F32)) for i in range(2)]
    x2_sb = [E(nc.sbuf_tensor(f"{tag}x2{i}", [128, 512], F32)) for i in range(2)]

    with ExitStack() as ctxA:
        EA = ctxA.enter_context
        ln = LNBufs(EA, nc, tag + "L")
        ps_g = [EA(nc.psum_tensor(f"{tag}psg{i}", [128, 512], F32)) for i in range(2)]
        ps_u = [EA(nc.psum_tensor(f"{tag}psu{i}", [128, 512], F32)) for i in range(2)]

        emit_ln(p, ln, ident_bf, x_dram, xnT, tag + "n")

        NT = NTOK // 512
        for hc in range(HC):
            sg = wbuf[(hc % 2) * 2]
            su = wbuf[(hc % 2) * 2 + 1]
            war = [(tag + "mmu", NT * (hc - 1))] if hc >= 2 else []
            p.dma("gpsimd", sg[:, :], wg_sl[hc], tag + "wd", waits=war)
            du = p.dma("gpsimd", su[:, :], wu_sl[hc], tag + "wd")
            for nt in range(NT):
                G = hc * NT + nt
                w0 = []
                if nt == 0:
                    w0.append((tag + "wd", du))
                if hc >= 1:
                    w0 += [(tag + "sil", G - 1), (tag + "hready", G - 1)]
                for kcc in range(KC):
                    p.q("tensor", lambda t, pg=ps_g[nt], w=sg, kcc=kcc, nt=nt:
                        t.matmul(pg[:, :], w[:, kcc * 128:(kcc + 1) * 128],
                                 xnT[:, kcc * NTOK + nt * 512: kcc * NTOK + (nt + 1) * 512],
                                 start=(kcc == 0), stop=(kcc == KC - 1)),
                        inc=(tag + "mmg") if kcc == KC - 1 else None,
                        waits=w0 if kcc == 0 else ())
                for kcc in range(KC):
                    p.q("tensor", lambda t, pu=ps_u[nt], w=su, kcc=kcc, nt=nt:
                        t.matmul(pu[:, :], w[:, kcc * 128:(kcc + 1) * 128],
                                 xnT[:, kcc * NTOK + nt * 512: kcc * NTOK + (nt + 1) * 512],
                                 start=(kcc == 0), stop=(kcc == KC - 1)),
                        inc=(tag + "mmu") if kcc == KC - 1 else None)
                mg = p.val(tag + "mmg")
                mu = p.val(tag + "mmu")
                silw = [(tag + "mmg", mg)]
                if hc >= 1:
                    silw.append((tag + "hready", G - 1))
                p.q("scalar", lambda s, pg=ps_g[nt], si=sil[nt]:
                    s.activation(si[:, :], pg[:, :], AF.Silu),
                    inc=tag + "sil", waits=silw)
                p.q("vector", lambda v, pu=ps_u[nt], si=sil[nt], hc=hc, nt=nt:
                    v.tensor_tensor(h_buf[:, hc * NTOK + nt * 512: hc * NTOK + (nt + 1) * 512],
                                    si[:, :], pu[:, :], ALU.mult),
                    inc=tag + "hready",
                    waits=[(tag + "sil", p.val(tag + "sil")), (tag + "mmu", mu)])
            p.q("tensor", lambda t: t.nop(), inc=tag + "wused")
        hall = p.val(tag + "hready")
        p.wait("tensor", tag + "hready", hall)
        p.wait("gpsimd", tag + "hready", hall)
    emit_out_proj(p, E, h_buf, wd_sl, x_dram, y_dram, HC, wbuf, x2_sb, y_sb, tag)
    return {"xnT": xnT, "h": h_buf}




def emit_out_proj(p, E, act_buf, w_sl, x_dram, y_dram, MC, wbuf, x2_sb, y_sb,
                  tag, bias128=None, act_ix=None):
    if act_ix is None:
        act_ix = lambda hk, r: hk * NTOK + r * 128
    nc = p.nc
    ps_d = [E(nc.psum_tensor(f"{tag}psd{i}", [128, 512], F32)) for i in range(8)]
    junk = E(nc.sbuf_tensor(tag + "jnk", [128, 4], F32))
    nw = min(2, len(wbuf))
    for nd in range(4):
        for hk in range(MC):
            J = nd * MC + hk
            slot = wbuf[J % nw]
            war = [(tag + "dwu", J - 1)] if J >= 2 else []
            dw = p.dma("gpsimd", slot[:, 0:512], w_sl[nd, hk], tag + "dwd", waits=war)
            for r in range(RT):
                waits = [(tag + "dwd", dw)] if r == 0 else []
                if hk == 0 and nd >= 1:
                    waits.append((tag + "dev", (nd - 1) * RT + r + 1))
                if r == RT - 1:
                    inc = tag + "dwu"
                elif hk == MC - 1:
                    inc = tag + "dmm"
                else:
                    inc = None
                aix = act_ix(hk, r)
                p.q("tensor", lambda t, pd=ps_d[r], slot=slot, aix=aix, hk=hk:
                    t.matmul(pd[:, :], act_buf[:, aix:aix + 128],
                             slot[:, 0:512], start=(hk == 0), stop=(hk == MC - 1)),
                    inc=inc, waits=waits)
        for r in range(RT):
            Ga = nd * RT + r
            xs = x2_sb[r % 2]
            xw = [(tag + "dev", Ga - 1)] if Ga >= 2 else []
            dx = p.dma("sync", xs[:, :],
                       x_dram[r * 128:(r + 1) * 128, nd * 512:(nd + 1) * 512],
                       tag + "xr", waits=xw)
            if bias128 is not None:
                p.q("vector", lambda v, xs=xs, nd=nd:
                    v.tensor_add(xs[:, :], xs[:, :],
                                 bias128[:, nd * 512:(nd + 1) * 512]),
                    inc=tag + "xb", waits=[(tag + "xr", dx)])
            ys = y_sb[r % 2]
            if r < RT - 1:
                stopw = (tag + "dmm", nd * (RT - 1) + r + 1)
            else:
                stopw = (tag + "dwu", (nd + 1) * MC)
            vw = [stopw, (tag + "xr", dx)]
            if bias128 is not None:
                vw.append((tag + "xb", p.val(tag + "xb")))
            if Ga >= 2:
                vw.append((tag + "yw", 16 * (Ga - 1)))
            p.q("vector", lambda v, pd=ps_d[r]:
                v.tensor_copy(junk[:, :], pd[:, 0:4]), waits=vw)
            p.q("vector", lambda v, ys=ys, xs=xs, pd=ps_d[r]:
                v.tensor_add(ys[:, :], xs[:, :], pd[:, :]),
                inc=tag + "dev", waits=vw)
            p.dma("sync", y_dram[r * 128:(r + 1) * 128, nd * 512:(nd + 1) * 512],
                  ys[:, :], tag + "yw", waits=[(tag + "dev", p.val(tag + "dev"))])
    p.wait("sync", tag + "yw", p.val(tag + "yw"))
    p.wait("tensor", tag + "dev", p.val(tag + "dev"))



def emit_ema(p, E, x_dram, y_dram, wv_sl, wg_sl, wq_sl, wo_sl, bgq, dv128, flag,
             cc_in, cc_out, tag):
    nc = p.nc
    SC = float(np.sqrt(1.0 - 0.95))
    ident_bf = E(nc.sbuf_tensor(tag + "id", [128, 128], BF16))
    emit_ident(p, ident_bf, tag + "mi")
    xnT = E(nc.sbuf_tensor(tag + "xnT", [128, KC * NTOK], BF16))
    ylb = E(nc.sbuf_tensor(tag + "ylb", [128, KC * NTOK], BF16))
    wbuf = [E(nc.sbuf_tensor(f"{tag}w{i}", [128, D], BF16)) for i in range(4)]
    s_sb = E(nc.sbuf_tensor(tag + "s", [128, NTOK], F32))
    u1 = E(nc.sbuf_tensor(tag + "u1", [128, NTOK], F32))
    t1 = E(nc.sbuf_tensor(tag + "t1", [128, NTOK], F32))
    qs = E(nc.sbuf_tensor(tag + "qs", [128, NTOK], F32))
    gat = E(nc.sbuf_tensor(tag + "gat", [128, NTOK], F32))
    dec = E(nc.sbuf_tensor(tag + "dec", [128, NTOK], F32))
    dvs = E(nc.sbuf_tensor(tag + "dvs", [128, NTOK], F32))
    ylast = E(nc.sbuf_tensor(tag + "ylast", [128, KC], F32))
    carry = E(nc.sbuf_tensor(tag + "carry", [128, KC], F32))
    carry2 = E(nc.sbuf_tensor(tag + "carry2", [128, KC], F32))
    flg = E(nc.sbuf_tensor(tag + "flg", [128, 1], F32))
    bgq_sb = E(nc.sbuf_tensor(tag + "bgq", [128, 2 * KC], F32))
    y_sb = [E(nc.sbuf_tensor(f"{tag}y{i}", [128, 512], F32)) for i in range(2)]
    x2_sb = [E(nc.sbuf_tensor(f"{tag}x2{i}", [128, 512], F32)) for i in range(2)]

    p.dma("sync", flg[:, :], flag, tag + "cd")
    p.dma("sync", bgq_sb[:, :], bgq, tag + "cd")
    cd = p.dma("sync", dvs[:, :], dv128, tag + "cd")
    p.q("gpsimd", lambda g: g.memset(dec[:, :], 0.95), inc=tag + "cg")
    p.wait("vector", tag + "cg", p.val(tag + "cg"))
    p.wait("vector", tag + "cd", cd)
    p.wait("scalar", tag + "cd", cd)

    NT = NTOK // 512
    with ExitStack() as ctxA:
        EA = ctxA.enter_context
        ln = LNBufs(EA, nc, tag + "L")
        ps_v = [EA(nc.psum_tensor(f"{tag}psv{i}", [128, 512], F32)) for i in range(2)]
        ps_g = [EA(nc.psum_tensor(f"{tag}psg{i}", [128, 512], F32)) for i in range(2)]

        emit_ln(p, ln, ident_bf, x_dram, xnT, tag + "n")

        for mc in range(KC):
            sv = wbuf[(mc % 2) * 2]
            sg = wbuf[(mc % 2) * 2 + 1]
            dv_ = p.dma("gpsimd", sv[:, :], wv_sl[mc], tag + "vwd",
                        waits=[(tag + "mmv", NT * (mc - 1))] if mc >= 2 else [])
            dg_ = p.dma("gpsimd", sg[:, :], wg_sl[mc], tag + "gwd",
                        waits=[(tag + "mmg", NT * (mc - 1))] if mc >= 2 else [])
            for nt in range(NT):
                G = mc * NT + nt
                wv0 = [(tag + "vwd", dv_)] if nt == 0 else []
                if mc >= 1:
                    wv0.append((tag + "sready", G - 1))
                for kcc in range(KC):
                    p.q("tensor", lambda t, pv=ps_v[nt], w=sv, kcc=kcc, nt=nt:
                        t.matmul(pv[:, :], w[:, kcc * 128:(kcc + 1) * 128],
                                 xnT[:, kcc * NTOK + nt * 512: kcc * NTOK + (nt + 1) * 512],
                                 start=(kcc == 0), stop=(kcc == KC - 1)),
                        inc=(tag + "mmv") if kcc == KC - 1 else None,
                        waits=wv0 if kcc == 0 else ())
                wg0 = [(tag + "gwd", dg_)] if nt == 0 else []
                if mc >= 1:
                    wg0.append((tag + "gact", G - 1))
                for kcc in range(KC):
                    p.q("tensor", lambda t, pg=ps_g[nt], w=sg, kcc=kcc, nt=nt:
                        t.matmul(pg[:, :], w[:, kcc * 128:(kcc + 1) * 128],
                                 xnT[:, kcc * NTOK + nt * 512: kcc * NTOK + (nt + 1) * 512],
                                 start=(kcc == 0), stop=(kcc == KC - 1)),
                        inc=(tag + "mmg") if kcc == KC - 1 else None,
                        waits=wg0 if kcc == 0 else ())
                mv = p.val(tag + "mmv")
                mg = p.val(tag + "mmg")
                gaw = [(tag + "mmg", mg)]
                if mc >= 1:
                    gaw.append((tag + "sready", G - 1))
                p.q("scalar", lambda s, pg=ps_g[nt], nt=nt, mc=mc:
                    s.activation(gat[:, nt * 512:(nt + 1) * 512], pg[:, :], AF.Sigmoid,
                                 bias=bgq_sb[:, mc:mc + 1]),
                    inc=tag + "gact", waits=gaw)
                p.q("vector", lambda v, pv=ps_v[nt], nt=nt:
                    v.scalar_tensor_tensor(s_sb[:, nt * 512:(nt + 1) * 512], pv[:, :],
                                           SC, gat[:, nt * 512:(nt + 1) * 512],
                                           ALU.mult, ALU.mult),
                    inc=tag + "sready",
                    waits=[(tag + "gact", p.val(tag + "gact")), (tag + "mmv", mv)])
            p.q("vector", lambda v, mc=mc:
                v.tensor_tensor_scan(ylb[:, mc * NTOK:(mc + 1) * NTOK], dec[:, :],
                                     s_sb[:, :], 0.0, ALU.mult, ALU.add),
                inc=tag + "scan", waits=[(tag + "sready", p.val(tag + "sready"))])
            p.q("vector", lambda v, mc=mc:
                v.tensor_copy(ylast[:, mc:mc + 1],
                              ylb[:, mc * NTOK + NTOK - 1: mc * NTOK + NTOK]),
                inc=tag + "ylst", waits=[(tag + "scan", p.val(tag + "scan"))])
        p.wait("tensor", tag + "sready", p.val(tag + "sready"))
    p.dma("sync", cc_in[0:128, :], ylast[:, :], tag + "agd",
          waits=[(tag + "ylst", p.val(tag + "ylst"))])
    nn = emit_cc_nonce(p, E, tag,
                       cc_in[128:132, :].rearrange("a b -> (a b)").rearrange(
                           "(o f) -> o f", o=1), NONCE_BASE + 64)
    agd = p.val(tag + "agd")
    p.q("gpsimd", lambda g: g.collective_compute(
        "AllGather", ALU.bypass, replica_groups=PAIRS, ins=[cc_in.opt()],
        outs=[cc_out.opt()]), inc=tag + "agc",
        waits=[(tag + "agd", agd), (tag + "nncd", nn)])
    gok = emit_cc_guard(p, E, tag,
                        cc_out[131:132, 12:16], NONCE_BASE + 64,
                        tag + "agc", p.val(tag + "agc"))
    agr = p.dma("sync", carry[:, :], cc_out[0:128, :], tag + "agr",
                waits=[(tag + "ccok", gok)])
    p.q("vector", lambda v: v.tensor_scalar_mul(carry2[:, :], carry[:, :], flg[:, 0:1]),
        inc=tag + "cready", waits=[(tag + "agr", agr)])
    with ExitStack() as ctxB:
        EB = ctxB.enter_context
        ps_q = [EB(nc.psum_tensor(f"{tag}psq{i}", [128, 512], F32)) for i in range(2)]
        for mc in range(KC):
            sq = wbuf[mc % 2]
            dq_ = p.dma("gpsimd", sq[:, :], wq_sl[mc], tag + "qwd",
                        waits=[(tag + "mmq", NT * (mc - 1))] if mc >= 2 else
                              [(tag + "mmv", NT * KC), (tag + "mmg", NT * KC)])
            for nt in range(NT):
                G = mc * NT + nt
                wq0 = [(tag + "qwd", dq_)] if nt == 0 else []
                if mc >= 1:
                    wq0.append((tag + "qact", G - 1))
                for kcc in range(KC):
                    p.q("tensor", lambda t, pq=ps_q[nt], w=sq, kcc=kcc, nt=nt:
                        t.matmul(pq[:, :], w[:, kcc * 128:(kcc + 1) * 128],
                                 xnT[:, kcc * NTOK + nt * 512: kcc * NTOK + (nt + 1) * 512],
                                 start=(kcc == 0), stop=(kcc == KC - 1)),
                        inc=(tag + "mmq") if kcc == KC - 1 else None,
                        waits=wq0 if kcc == 0 else ())
                qaw = [(tag + "mmq", p.val(tag + "mmq"))]
                if mc >= 1:
                    qaw.append((tag + "ldv", mc))
                p.q("scalar", lambda s, pq=ps_q[nt], nt=nt, mc=mc:
                    s.activation(qs[:, nt * 512:(nt + 1) * 512], pq[:, :], AF.Sigmoid,
                                 bias=bgq_sb[:, KC + mc:KC + mc + 1]),
                    inc=tag + "qact", waits=qaw)
            p.q("vector", lambda v, mc=mc:
                v.tensor_scalar_mul(t1[:, :], dvs[:, :], carry2[:, mc:mc + 1]),
                inc=tag + "t1v", waits=[(tag + "cready", p.val(tag + "cready"))])
            p.q("vector", lambda v, mc=mc:
                v.tensor_add(u1[:, :], ylb[:, mc * NTOK:(mc + 1) * NTOK], t1[:, :]),
                inc=tag + "u1v", waits=[(tag + "t1v", p.val(tag + "t1v"))])
            p.q("vector", lambda v, mc=mc:
                v.scalar_tensor_tensor(ylb[:, mc * NTOK:(mc + 1) * NTOK], qs[:, :],
                                       SC, u1[:, :], ALU.mult, ALU.mult),
                inc=tag + "ldv",
                waits=[(tag + "u1v", p.val(tag + "u1v")),
                       (tag + "qact", p.val(tag + "qact"))])
        p.wait("tensor", tag + "ldv", p.val(tag + "ldv"))
        p.wait("gpsimd", tag + "mmq", p.val(tag + "mmq"))
    emit_out_proj(p, E, ylb, wo_sl, x_dram, y_dram, KC, wbuf, x2_sb, y_sb, tag)


def build_ema_launch():
    nc = bass.Bass("TRN2", target_bir_lowering=False)
    x_in = nc.dram_tensor("x_in", [NTOK, D], F32, kind="ExternalInput").ap()
    wv = nc.dram_tensor("wv", [KC, 128, D], F32, kind="ExternalInput").ap()
    wg = nc.dram_tensor("wg", [KC, 128, D], F32, kind="ExternalInput").ap()
    wq = nc.dram_tensor("wq", [KC, 128, D], F32, kind="ExternalInput").ap()
    wo = nc.dram_tensor("wo", [4, KC, 128, 512], F32, kind="ExternalInput").ap()
    bgq = nc.dram_tensor("bgq", [128, 2 * KC], F32, kind="ExternalInput").ap()
    dv128 = nc.dram_tensor("dv128", [128, NTOK], F32, kind="ExternalInput").ap()
    flag = nc.dram_tensor("flag", [128, 1], F32, kind="ExternalInput").ap()
    y = nc.dram_tensor("y", [NTOK, D], F32, kind="ExternalOutput").ap()
    cc_in = nc.dram_tensor("cc_in", [132, KC], F32, kind="Internal").ap()
    cc_out = nc.dram_tensor("cc_out", [264, KC], F32, kind="Internal").ap()
    with ExitStack() as ctx:
        p = Prog(nc, ctx.enter_context)
        emit_ema(p, ctx.enter_context, x_in, y, wv, wg, wq, wo, bgq, dv128, flag,
                 cc_in, cc_out, "e")
        p.realize()
    return nc


def ema_maps(Wv, Wg, bg, Wq, bq, Wo):
    bgq = np.zeros((128, 2 * KC), np.float32)
    bgq[:, 0:KC] = np.asarray(bg).reshape(KC, 128).T
    bgq[:, KC:2 * KC] = np.asarray(bq).reshape(KC, 128).T
    dv = (0.95 ** (np.arange(NTOK, dtype=np.float64) + 1)).astype(np.float32)
    dv128 = np.broadcast_to(dv, (128, NTOK)).copy()
    return {
        "wv": lhsT_slabs(np.ascontiguousarray(np.asarray(Wv).T)),
        "wg": lhsT_slabs(np.ascontiguousarray(np.asarray(Wg).T)),
        "wq": lhsT_slabs(np.ascontiguousarray(np.asarray(Wq).T)),
        "wo": rhs_slabs(np.asarray(Wo)),
        "bgq": bgq, "dv128": dv128,
    }





def emit_deltanet(p, E, x_dram, y_dram, wqkbd_sl, wv_sl, wo_sl, bbd, flag,
                  cc_in, cc_out, tag):
    nc = p.nc
    C = 64
    NCH = NTOK // C
    ch = lambda n: tag + n

    ident_bf = E(nc.sbuf_tensor(tag + "id", [128, 128], BF16))
    emit_ident(p, ident_bf, tag + "mi")
    ident_f = E(nc.sbuf_tensor(tag + "idf", [128, 128], F32))
    p.q("gpsimd", lambda g: g.memset(ident_f[:, :], 0.0), inc=ch("cg"))
    p.q("gpsimd", lambda g: g.affine_select(
        ident_f[:, :], ident_f[:, :], compare_op=ALU.not_equal, fill=1.0, base=0,
        pattern=[[-1, 128]], channel_multiplier=1), inc=ch("cg"))
    triU = E(nc.sbuf_tensor(tag + "triU", [64, 64], F32))
    p.q("gpsimd", lambda g: g.memset(triU[:, :], 1.0), inc=ch("cg"))
    p.q("gpsimd", lambda g: g.affine_select(
        triU[:, :], triU[:, :], compare_op=ALU.is_ge, fill=0.0, base=0,
        pattern=[[1, 64]], channel_multiplier=-1), inc=ch("cg"))
    mask_u8 = E(nc.sbuf_tensor(tag + "msk", [64, 64], mybir.dt.int8))
    p.q("gpsimd", lambda g: g.memset(mask_u8[:, :], 1), inc=ch("cg"))
    p.q("gpsimd", lambda g: g.affine_select(
        mask_u8[:, :], mask_u8[:, :], compare_op=ALU.is_ge, fill=0, base=0,
        pattern=[[1, 64]], channel_multiplier=-1), inc=ch("cg"))
    zer_f = E(nc.sbuf_tensor(tag + "zer", [64, 64], F32))
    p.q("gpsimd", lambda g: g.memset(zer_f[:, :], 0.0), inc=ch("cg"))
    onecol = E(nc.sbuf_tensor(tag + "onec", [64, 1], F32))
    p.q("gpsimd", lambda g: g.memset(onecol[:, :], 1.0), inc=ch("cg"))
    cgv = p.val(ch("cg"))
    for e in ("tensor", "vector", "scalar"):
        p.wait(e, ch("cg"), cgv)

    xnT = E(nc.sbuf_tensor(tag + "xnT", [128, KC * NTOK], BF16))
    pw = E(nc.sbuf_tensor(tag + "pw", [128, 4 * KC * 64], BF16))
    pwv = E(nc.sbuf_tensor(tag + "pwv", [128, KC * 128], BF16))
    q_f = E(nc.sbuf_tensor(tag + "qf", [64, NTOK], F32))
    k_f = E(nc.sbuf_tensor(tag + "kf", [64, NTOK], F32))
    zbd = E(nc.sbuf_tensor(tag + "zbd", [64, 2 * NTOK], F32))
    escr = E(nc.sbuf_tensor(tag + "escr", [64, 2 * NTOK], F32))
    kd = E(nc.sbuf_tensor(tag + "kd", [64, NTOK], F32))
    v_bf = E(nc.sbuf_tensor(tag + "vbf", [128, NTOK], BF16))
    vT = E(nc.sbuf_tensor(tag + "vT", [64, NCH * 128], BF16))
    qt_buf = E(nc.sbuf_tensor(tag + "qt", [64, NTOK], BF16))
    kt_sb = E(nc.sbuf_tensor(tag + "kt", [64, 64], BF16))
    ktp_sb = E(nc.sbuf_tensor(tag + "ktp", [64, 64], BF16))
    ktpT_sb = E(nc.sbuf_tensor(tag + "ktpT", [64, 64], BF16))
    P_sb = E(nc.sbuf_tensor(tag + "P", [64, 64], BF16))
    lbT_sb = E(nc.sbuf_tensor(tag + "lbT", [64, 64], F32))
    LT_sb = E(nc.sbuf_tensor(tag + "LT", [64, 64], F32))
    L_sb = E(nc.sbuf_tensor(tag + "Lsb", [64, 64], F32))
    eL = E(nc.sbuf_tensor(tag + "eL", [64, 64], F32))
    enL = E(nc.sbuf_tensor(tag + "enL", [64, 64], F32))
    elc = E(nc.sbuf_tensor(tag + "elc", [64, 64], F32))
    Lc_cols = E(nc.sbuf_tensor(tag + "Lc", [64, NCH], F32))
    Gpre = E(nc.sbuf_tensor(tag + "Gp", [64, NCH + 1], F32))
    eGc = E(nc.sbuf_tensor(tag + "eGc", [64, 2], F32))
    S_sb = E(nc.sbuf_tensor(tag + "S", [64, 128], F32))
    S_bf = E(nc.sbuf_tensor(tag + "Sbf", [64, 128], BF16))
    Ssd = E(nc.sbuf_tensor(tag + "Ssd", [64, 128], F32))
    Scor = E(nc.sbuf_tensor(tag + "Scor", [64, 128], BF16))
    oT = E(nc.sbuf_tensor(tag + "oT", [128, NTOK], BF16))
    flg = E(nc.sbuf_tensor(tag + "flg", [128, 1], F32))
    bbd_sb = E(nc.sbuf_tensor(tag + "bbd", [64, 2], F32))
    y_sb = [E(nc.sbuf_tensor(f"{tag}y{i}", [128, 512], F32)) for i in range(2)]
    x2_sb = [E(nc.sbuf_tensor(f"{tag}x2{i}", [128, 512], F32)) for i in range(2)]
    wbuf = [E(nc.sbuf_tensor(f"{tag}w{i}", [128, 512], BF16)) for i in range(2)]

    p.dma("sync", flg[:, :], flag, ch("cd"))
    p.dma("sync", bbd_sb[:, :], bbd, ch("cd"))
    for i in range(4):
        p.dma("gpsimd", pw[:, i * KC * 64:(i + 1) * KC * 64], wqkbd_sl[i], ch("pwd"))
    p.dma("gpsimd", pwv[:, :], wv_sl, ch("pwd"))
    cdv = p.val(ch("cd"))
    p.wait("scalar", ch("cd"), cdv)
    p.wait("vector", ch("cd"), cdv)

    def step(eng, fn, extra=()):
        k = p.val(ch("sq"))
        p.q(eng, fn, inc=ch("sq"), waits=[(ch("sq"), k)] + list(extra))

    with ExitStack() as ctxLN:
        ln = LNBufs(ctxLN.enter_context, nc, tag + "L")
        emit_ln(p, ln, ident_bf, x_dram, xnT, tag + "n")

    with ExitStack() as ctxA:
        EA = ctxA.enter_context
        ps_p = [EA(nc.psum_tensor(f"{tag}psp{i}", [128, 512], F32)) for i in range(2)]
        ps_fA = EA(nc.psum_tensor(tag + "psfA", [64, 64], F32))
        ps_fB = EA(nc.psum_tensor(tag + "psfB", [64, 64], F32))
        ps_sc = EA(nc.psum_tensor(tag + "pssc", [64, 64], F32))
        ps_o = EA(nc.psum_tensor(tag + "pso", [128, 64], F32))
        ps_ds = EA(nc.psum_tensor(tag + "psds", [64, 128], F32))
        ps_bt = EA(nc.psum_tensor(tag + "psbt", [64, 128], BF16))

        p.wait("tensor", ch("pwd"), p.val(ch("pwd")))
        NT = NTOK // 512
        for gi in range(4):
            for nt in range(NT):
                G = gi * NT + nt
                pp = ps_p[G % 2]
                wts = [(ch("pev"), G - 1)] if G >= 2 else []
                for kcc in range(KC):
                    p.q("tensor", lambda t, pp=pp, gi=gi, kcc=kcc, nt=nt:
                        t.matmul(pp[0:64, :],
                                 pw[:, gi * KC * 64 + kcc * 64: gi * KC * 64 + (kcc + 1) * 64],
                                 xnT[:, kcc * NTOK + nt * 512: kcc * NTOK + (nt + 1) * 512],
                                 start=(kcc == 0), stop=(kcc == KC - 1)),
                        inc=ch("pmm") if kcc == KC - 1 else None,
                        waits=wts if kcc == 0 else ())
                pv = p.val(ch("pmm"))
                if gi == 0:
                    p.q("vector", lambda v, pp=pp, nt=nt:
                        v.tensor_copy(q_f[:, nt * 512:(nt + 1) * 512], pp[0:64, :]),
                        inc=ch("pev"), waits=[(ch("pmm"), pv)])
                elif gi == 1:
                    p.q("vector", lambda v, pp=pp, nt=nt:
                        v.tensor_copy(k_f[:, nt * 512:(nt + 1) * 512], pp[0:64, :]),
                        inc=ch("pev"), waits=[(ch("pmm"), pv)])
                else:
                    off = 0 if gi == 2 else NTOK
                    bcol = gi - 2
                    p.q("vector", lambda v, pp=pp, nt=nt, off=off, bcol=bcol:
                        v.tensor_scalar_add(zbd[:, off + nt * 512: off + (nt + 1) * 512],
                                            pp[0:64, :], bbd_sb[:, bcol:bcol + 1]),
                        inc=ch("pev"), waits=[(ch("pmm"), pv)])
        for nt in range(NT):
            G = 4 * NT + nt
            pp = ps_p[G % 2]
            for kcc in range(KC):
                p.q("tensor", lambda t, pp=pp, kcc=kcc, nt=nt:
                    t.matmul(pp[:, :], pwv[:, kcc * 128:(kcc + 1) * 128],
                             xnT[:, kcc * NTOK + nt * 512: kcc * NTOK + (nt + 1) * 512],
                             start=(kcc == 0), stop=(kcc == KC - 1)),
                    inc=ch("pmm") if kcc == KC - 1 else None,
                    waits=[(ch("pev"), G - 1)] if kcc == 0 else ())
            p.q("vector", lambda v, pp=pp, nt=nt:
                v.tensor_copy(v_bf[:, nt * 512:(nt + 1) * 512], pp[:, :]),
                inc=ch("pev"), waits=[(ch("pmm"), p.val(ch("pmm")))])
        pe_all = p.val(ch("pev"))

        step("scalar", lambda s: s.activation(escr[:, :], zbd[:, :], AF.Abs),
             extra=[(ch("pev"), pe_all)])
        step("scalar", lambda s: s.activation(escr[:, :], escr[:, :], AF.Exp, scale=-1.0))
        step("scalar", lambda s: s.activation(escr[:, :], escr[:, :], AF.Ln,
                                              bias=onecol[:, 0:1]))
        step("vector", lambda v: v.tensor_scalar_min(zbd[:, 0:NTOK], zbd[:, 0:NTOK], 0.0))
        step("vector", lambda v: v.tensor_scalar_max(zbd[:, NTOK:], zbd[:, NTOK:], 0.0))
        step("vector", lambda v: v.tensor_sub(zbd[:, 0:NTOK], zbd[:, 0:NTOK],
                                              escr[:, 0:NTOK]))
        step("vector", lambda v: v.tensor_add(zbd[:, NTOK:], zbd[:, NTOK:],
                                              escr[:, NTOK:]))
        step("vector", lambda v: v.tensor_mul(kd[:, :], k_f[:, :], zbd[:, NTOK:]))
        step("gpsimd", lambda g: g.memset(Gpre[:, :], 0.0))
        step("gpsimd", lambda g: g.memset(S_sb[:, :], 0.0))
        step("gpsimd", lambda g: g.memset(S_bf[:, :], 0.0))

        for c in range(NCH):
            t0 = c * C
            step("tensor", lambda t, t0=t0: t.transpose(
                ps_bt[:, :], v_bf[:, t0:t0 + C], ident_bf[:, :]))
            step("vector", lambda v, c=c: v.tensor_copy(
                vT[:, c * 128:(c + 1) * 128], ps_bt[0:64, :]))
            step("tensor", lambda t, t0=t0: t.transpose(
                ps_fA[:, :], zbd[:, t0:t0 + C], ident_f[0:64, 0:64]))
            step("vector", lambda v: v.tensor_copy(lbT_sb[:, :], ps_fA[:, :]))
            step("tensor", lambda t: t.matmul(ps_fB[:, :], triU[:, :], lbT_sb[:, :],
                                              start=True, stop=True))
            step("vector", lambda v: v.tensor_copy(LT_sb[:, :], ps_fB[:, :]))
            step("tensor", lambda t: t.transpose(ps_fA[:, :], LT_sb[:, :],
                                                 ident_f[0:64, 0:64]))
            step("vector", lambda v: v.tensor_copy(L_sb[:, :], ps_fA[:, :]))
            step("vector", lambda v, c=c: v.tensor_copy(Lc_cols[:, c:c + 1],
                                                        L_sb[:, C - 1:C]))
            step("vector", lambda v, c=c: v.tensor_add(Gpre[:, c + 1:c + 2],
                                                       Gpre[:, c:c + 1],
                                                       Lc_cols[:, c:c + 1]))
            step("scalar", lambda s: s.activation(eL[:, :], L_sb[:, :], AF.Exp))
            step("scalar", lambda s: s.activation(enL[:, :], L_sb[:, :], AF.Exp,
                                                  scale=-1.0))
            step("scalar", lambda s, c=c: s.activation(elc[:, :], L_sb[:, :], AF.Exp,
                                                       scale=-1.0,
                                                       bias=Lc_cols[:, c:c + 1]))
            step("scalar", lambda s, c=c: s.activation(eGc[:, 0:1],
                                                       Lc_cols[:, c:c + 1], AF.Exp))
            step("vector", lambda v, t0=t0: v.tensor_mul(qt_buf[:, t0:t0 + C],
                                                         q_f[:, t0:t0 + C], eL[:, :]))
            step("vector", lambda v, t0=t0: v.tensor_mul(kt_sb[:, :],
                                                         kd[:, t0:t0 + C], enL[:, :]))
            step("vector", lambda v, t0=t0: v.tensor_mul(ktp_sb[:, :],
                                                         kd[:, t0:t0 + C], elc[:, :]))
            step("tensor", lambda t, t0=t0: t.matmul(ps_sc[:, :], kt_sb[:, :],
                                                     qt_buf[:, t0:t0 + C],
                                                     start=True, stop=True))
            step("vector", lambda v: v.select(P_sb[:, :], mask_u8[:, :], ps_sc[:, :],
                                              zer_f[:, :]))
            step("tensor", lambda t, t0=t0: t.matmul(ps_o[:, 0:C], S_bf[:, :],
                                                     qt_buf[:, t0:t0 + C],
                                                     start=True, stop=False))
            step("tensor", lambda t, c=c: t.matmul(ps_o[:, 0:C],
                                                   vT[:, c * 128:(c + 1) * 128],
                                                   P_sb[:, :], start=False, stop=True))
            step("vector", lambda v, t0=t0: v.tensor_copy(oT[:, t0:t0 + C],
                                                          ps_o[:, 0:C]))
            step("tensor", lambda t: t.transpose(ps_bt[:, 0:64], ktp_sb[:, :],
                                                 ident_bf[0:64, 0:64]))
            step("vector", lambda v: v.tensor_copy(ktpT_sb[:, :], ps_bt[0:64, 0:64]))
            step("tensor", lambda t, c=c: t.matmul(ps_ds[:, :], ktpT_sb[:, :],
                                                   vT[:, c * 128:(c + 1) * 128],
                                                   start=True, stop=True))
            step("vector", lambda v: v.scalar_tensor_tensor(S_sb[:, :], S_sb[:, :],
                                                            eGc[:, 0:1], ps_ds[:, :],
                                                            ALU.mult, ALU.add))
            step("vector", lambda v: v.tensor_copy(S_bf[:, :], S_sb[:, :]))
        sq0 = p.val(ch("sq"))
        p.dma("sync", cc_in[0:64, :], S_sb[:, :], ch("agd"), waits=[(ch("sq"), sq0)])
        nn = emit_cc_nonce(p, EA, tag, cc_in[64:68, 64:128].rearrange(
            "a b -> (a b)").rearrange("(o f) -> o f", o=1) if False else None, NONCE_BASE + 0) if False else None
        ncd4 = EA(nc.sbuf_tensor(tag + "nc4", [4, 64], F32))
        p.q("gpsimd", lambda g: g.memset(ncd4[:, :], float(NONCE_BASE + 0)),
            inc=ch("nng"))
        nn = p.dma("sync", cc_in[64:68, 64:128], ncd4[:, :], tag + "nncd",
                   waits=[(ch("nng"), p.val(ch("nng")))])
        agd = p.val(ch("agd"))
        p.q("gpsimd", lambda g: g.collective_compute(
            "AllGather", ALU.bypass, replica_groups=PAIRS, ins=[cc_in.opt()],
            outs=[cc_out.opt()]), inc=ch("agc"),
            waits=[(ch("agd"), agd), (tag + "nncd", nn)])
        gok = emit_cc_guard(p, EA, tag,
                            cc_out[64:68, 124:128].rearrange("a (o b) -> a o b", o=1),
                            NONCE_BASE + 0, ch("agc"), p.val(ch("agc")), npts=4)
        agr = p.dma("sync", Ssd[:, :], cc_out[0:64, :], ch("agr"),
                    waits=[(tag + "ccok", gok)])
        step("vector", lambda v: v.tensor_scalar_mul(Ssd[:, :], Ssd[:, :],
                                                     flg[0:64, 0:1]),
             extra=[(ch("agr"), agr)])
        for c in range(NCH):
            t0 = c * C
            step("scalar", lambda s, c=c: s.activation(eGc[:, 1:2], Gpre[:, c:c + 1],
                                                       AF.Exp))
            step("vector", lambda v: v.tensor_scalar_mul(Scor[:, :], Ssd[:, :],
                                                         eGc[:, 1:2]))
            step("tensor", lambda t, t0=t0: t.matmul(ps_o[:, 0:C], Scor[:, :],
                                                     qt_buf[:, t0:t0 + C],
                                                     start=True, stop=True))
            step("vector", lambda v, t0=t0: v.tensor_add(oT[:, t0:t0 + C],
                                                         oT[:, t0:t0 + C],
                                                         ps_o[:, 0:C]))
        sqf = p.val(ch("sq"))
        p.wait("tensor", ch("sq"), sqf)
        p.wait("gpsimd", ch("sq"), sqf)
    emit_out_proj(p, E, oT, wo_sl, x_dram, y_dram, 1, wbuf, x2_sb, y_sb, tag)
    return {"q_f": q_f[:, :], "zbd": zbd[:, :], "qt": qt_buf[:, :], "oT": oT[:, :],
            "Lc": Lc_cols[:, :], "Sf": S_sb[:, :], "vT": vT[:, :], "kd": kd[:, :]}


def build_deltanet_launch(debug=False):
    nc = bass.Bass("TRN2", target_bir_lowering=False)
    x_in = nc.dram_tensor("x_in", [NTOK, D], F32, kind="ExternalInput").ap()
    wqkbd = nc.dram_tensor("wqkbd", [4, 128, KC * 64], F32, kind="ExternalInput").ap()
    wv = nc.dram_tensor("wv", [128, KC * 128], F32, kind="ExternalInput").ap()
    wo = nc.dram_tensor("wo", [4, 1, 128, 512], F32, kind="ExternalInput").ap()
    bbd = nc.dram_tensor("bbd", [64, 2], F32, kind="ExternalInput").ap()
    flag = nc.dram_tensor("flag", [128, 1], F32, kind="ExternalInput").ap()
    y = nc.dram_tensor("y", [NTOK, D], F32, kind="ExternalOutput").ap()
    cc_in = nc.dram_tensor("cc_in", [68, 128], F32, kind="Internal").ap()
    cc_out = nc.dram_tensor("cc_out", [136, 128], F32, kind="Internal").ap()
    dbg = {}
    if debug:
        for nm, sh in [("q_f", [64, NTOK]), ("zbd", [64, 2*NTOK]), ("qt", [64, NTOK]),
                       ("oT", [128, NTOK]), ("Lc", [64, NTOK // 64]),
                       ("Sf", [64, 128]), ("vT", [64, (NTOK//64) * 128]),
                       ("kd", [64, NTOK])]:
            dbg[nm] = nc.dram_tensor("dbg_" + nm, sh, F32, kind="ExternalOutput").ap()
    with ExitStack() as ctx:
        p = Prog(nc, ctx.enter_context)
        bufs = emit_deltanet(p, ctx.enter_context, x_in, y, wqkbd, wv, wo, bbd, flag,
                      cc_in, cc_out, "d")
        if debug:
            p.wait("gpsimd", "dyw", p.val("dyw"))
            for nm in dbg:
                p.dma("gpsimd", dbg[nm], bufs[nm], "dbgd")
            p.wait("gpsimd", "dbgd", p.val("dbgd"))
        p.realize()
    return nc


def lhsT_small(W):
    in_f, out_f = W.shape
    a = W.reshape(in_f // 128, 128, out_f).transpose(1, 0, 2)
    return np.ascontiguousarray(a.reshape(128, (in_f // 128) * out_f))


def deltanet_maps(Wq, Wk, Wv, Wo, Wb, bb, Wd, bd):
    wqkbd = np.stack([lhsT_small(np.asarray(w)) for w in (Wq, Wk, Wb, Wd)])
    bbd = np.stack([np.asarray(bb), np.asarray(bd)], axis=1).astype(np.float32)
    return {
        "wqkbd": wqkbd,
        "wv": lhsT_small(np.asarray(Wv)),
        "wo": rhs_slabs(np.asarray(Wo)).reshape(4, 1, 128, 512),
        "bbd": bbd,
    }





def emit_mha(p, E, x_dram, y_dram, wq_sl, wk_sl, wv_sl, wdense_sl, tag,
             cc_kin, cc_kout, cc_vin, cc_vout, logflag,
             rope=False, cos_in=None, sin_in=None, rmat_in=None,
             bqk=None, bv=None, bdense=None, bout=None, dbg=None):
    nc = p.nc
    NH = 16
    SCL = 1.0 / float(np.sqrt(128.0))
    ch = lambda n: tag + n

    ident_bf = E(nc.sbuf_tensor(tag + "id", [128, 128], BF16))
    emit_ident(p, ident_bf, tag + "mi")
    tri_f = E(nc.sbuf_tensor(tag + "tri", [128, 128], F32))
    p.q("gpsimd", lambda g: g.memset(tri_f[:, :], 0.0), inc=ch("cg"))
    p.q("gpsimd", lambda g: g.affine_select(
        tri_f[:, :], tri_f[:, :], compare_op=ALU.is_ge, fill=-1e30, base=0,
        pattern=[[-1, 128]], channel_multiplier=1), inc=ch("cg"))
    ones1 = E(nc.sbuf_tensor(tag + "on1", [1, 128], BF16))
    p.q("gpsimd", lambda g: g.memset(ones1[:, :], 1.0), inc=ch("cg"))
    cgv = p.val(ch("cg"))
    p.wait("vector", ch("cg"), cgv)
    p.wait("tensor", ch("cg"), cgv)

    cols = E(nc.sbuf_tensor(tag + "cls", [128, 16], F32))
    lfl = E(nc.sbuf_tensor(tag + "lfl", [128, 1], F32))
    p.dma("sync", lfl[:, :], logflag, ch("cd"))
    if bqk is not None:
        bqk_sb = E(nc.sbuf_tensor(tag + "bqk", [128, 32], F32))
        p.dma("sync", bqk_sb[:, :], bqk, ch("cd"))
    if bv is not None:
        bv_sb = E(nc.sbuf_tensor(tag + "bv", [128, 16], F32))
        p.dma("sync", bv_sb[:, :], bv, ch("cd"))
    bias_f = E(nc.sbuf_tensor(tag + "bef", [1, D], F32))
    if rope:
        cos_sb = E(nc.sbuf_tensor(tag + "cos", [32, NTOK], F32))
        sin_sb = E(nc.sbuf_tensor(tag + "sin", [32, NTOK], F32))
        rmat = E(nc.sbuf_tensor(tag + "rm", [32, 32], BF16))
        p.dma("sync", cos_sb[:, :], cos_in, ch("cd"))
        p.dma("sync", sin_sb[:, :], sin_in, ch("cd"))
        p.dma("gpsimd", rmat[:, :], rmat_in, ch("cd"))
    cdv = p.val(ch("cd"))
    for e in ("vector", "scalar", "tensor"):
        p.wait(e, ch("cd"), cdv)

    wbuf = [E(nc.sbuf_tensor(f"{tag}w{i}", [128, D], BF16)) for i in range(2)]
    o_dram = nc.dram_tensor(tag + "odr", [128, RT * D], BF16, kind="Internal").ap()

    NT = NTOK // 512
    with ExitStack() as ctxQ:
        EQ = ctxQ.enter_context
        qb = EQ(nc.sbuf_tensor(tag + "qb", [128, NH * NTOK], BF16))
        k_own = EQ(nc.sbuf_tensor(tag + "ko", [128, NH * NTOK], BF16))
        v_own = EQ(nc.sbuf_tensor(tag + "vo", [128, RT * D], BF16))
        with ExitStack() as ctxA:
            EA = ctxA.enter_context
            ln = LNBufs(EA, nc, tag + "L")
            xnT = EA(nc.sbuf_tensor(tag + "xnT", [128, KC * NTOK], BF16))
            v_head = EA(nc.sbuf_tensor(tag + "vh", [128, NTOK], BF16))
            ps_p = [EA(nc.psum_tensor(f"{tag}psp{i}", [128, 512], F32)) for i in range(2)]
            emit_ln(p, ln, ident_bf, x_dram, xnT, tag + "n")

            for which, wsl in (("q", wq_sl), ("k", wk_sl), ("v", wv_sl)):
                mmch = ch("mm" + which)
                evch = ch(which + "ev")
                for hd in range(NH):
                    slot = wbuf[hd % 2]
                    if hd >= 2:
                        war = [(mmch, NT * (hd - 1))]
                    elif which == "k":
                        war = [(ch("mmq"), NT * NH)]
                    elif which == "v":
                        war = [(ch("mmk"), NT * NH)]
                    else:
                        war = []
                    dwv = p.dma("gpsimd", slot[:, :], wsl[hd], ch(which + "wd"),
                                waits=war)
                    for nt in range(NT):
                        G = hd * NT + nt
                        w0 = [(ch(which + "wd"), dwv)] if nt == 0 else []
                        if G >= 2:
                            w0.append((evch, G - 1))
                        for kcc in range(KC):
                            p.q("tensor", lambda t, pp=ps_p[nt], w=slot, kcc=kcc, nt=nt:
                                t.matmul(pp[:, :], w[:, kcc * 128:(kcc + 1) * 128],
                                         xnT[:, kcc * NTOK + nt * 512: kcc * NTOK + (nt + 1) * 512],
                                         start=(kcc == 0), stop=(kcc == KC - 1)),
                                inc=mmch if kcc == KC - 1 else None,
                                waits=w0 if kcc == 0 else ())
                        mval = p.val(mmch)
                        if which == "q":
                            if bqk is not None:
                                p.q("vector", lambda v, pp=ps_p[nt], hd=hd, nt=nt:
                                    v.tensor_scalar(qb[:, hd * NTOK + nt * 512: hd * NTOK + (nt + 1) * 512],
                                                    pp[:, :], bqk_sb[:, hd:hd + 1], SCL,
                                                    ALU.add, ALU.mult),
                                    inc=evch, waits=[(mmch, mval)])
                            else:
                                p.q("vector", lambda v, pp=ps_p[nt], hd=hd, nt=nt:
                                    v.tensor_scalar_mul(qb[:, hd * NTOK + nt * 512: hd * NTOK + (nt + 1) * 512],
                                                        pp[:, :], SCL),
                                    inc=evch, waits=[(mmch, mval)])
                        elif which == "k":
                            if bqk is not None:
                                p.q("vector", lambda v, pp=ps_p[nt], hd=hd, nt=nt:
                                    v.tensor_scalar_add(k_own[:, hd * NTOK + nt * 512: hd * NTOK + (nt + 1) * 512],
                                                        pp[:, :], bqk_sb[:, 16 + hd:17 + hd]),
                                    inc=evch, waits=[(mmch, mval)])
                            else:
                                p.q("vector", lambda v, pp=ps_p[nt], hd=hd, nt=nt:
                                    v.tensor_copy(k_own[:, hd * NTOK + nt * 512: hd * NTOK + (nt + 1) * 512],
                                                  pp[:, :]),
                                    inc=evch, waits=[(mmch, mval)])
                        else:
                            vw_ = [(mmch, mval)]
                            if hd >= 1 and nt == 0:
                                vw_.append((ch("vte"), 2 * hd))
                            p.q("vector", lambda v, pp=ps_p[nt], nt=nt:
                                v.tensor_copy(v_head[:, nt * 512:(nt + 1) * 512],
                                              pp[:, :]),
                                inc=evch, waits=vw_)
                    if which == "v":
                        ve = p.val(evch)
                        for g4 in range(2):
                            Gt = hd * 2 + g4
                            for jj in range(4):
                                rr = g4 * 4 + jj
                                tw = [(evch, ve)] if jj == 0 else []
                                if jj == 0 and Gt >= 2:
                                    tw.append((ch("vte"), Gt - 1))
                                p.q("tensor", lambda t, rr=rr, jj=jj, Gt=Gt:
                                    t.transpose(ln.ps_tr[Gt % 2][:, jj * 128:(jj + 1) * 128],
                                                v_head[:, rr * 128:(rr + 1) * 128],
                                                ident_bf[:, :]),
                                    inc=ch("vtr") if jj == 3 else None, waits=tw)

                            def vevac(v, g4=g4, hd=hd, Gt=Gt):
                                for jj in range(4):
                                    rr = g4 * 4 + jj
                                    inst = v.tensor_copy(
                                        v_own[:, rr * D + hd * 128: rr * D + (hd + 1) * 128],
                                        ln.ps_tr[Gt % 2][:, jj * 128:(jj + 1) * 128])
                                return inst
                            p.q("vector", vevac, inc=ch("vte"),
                                waits=[(ch("vtr"), p.val(ch("vtr")))])
            if rope:
                ps_r = EA(nc.psum_tensor(tag + "psr", [32, NTOK], F32))
                rt1 = EA(nc.sbuf_tensor(tag + "rt1", [32, NTOK], F32))
                rt2 = EA(nc.sbuf_tensor(tag + "rt2", [32, NTOK], F32))
                p.wait("tensor", ch("qev"), NT * NH)
                p.wait("tensor", ch("kev"), NT * NH)
                for which, buf in (("q", qb), ("k", k_own)):
                    for hd in range(NH):
                        i = (0 if which == "q" else NH) + hd
                        for jh in range(2):
                            p.q("tensor", lambda t, buf=buf, hd=hd, jh=jh:
                                t.matmul(ps_r[:, jh * 512:(jh + 1) * 512], rmat[:, :],
                                         buf[0:32, hd * NTOK + jh * 512: hd * NTOK + (jh + 1) * 512],
                                         start=True, stop=True),
                                inc=ch("rmm") if jh == 1 else None,
                                waits=[(ch("rdone"), i)] if (i >= 1 and jh == 0) else [])
                        p.q("vector", lambda v, buf=buf, hd=hd:
                            v.tensor_mul(rt1[:, :], buf[0:32, hd * NTOK:(hd + 1) * NTOK],
                                         cos_sb[:, :]),
                            inc=ch("rc1"), waits=[(ch("rmm"), p.val(ch("rmm")))])
                        p.q("vector", lambda v: v.tensor_mul(rt2[:, :], sin_sb[:, :],
                                                             ps_r[:, :]),
                            inc=ch("rc2"), waits=[(ch("rc1"), p.val(ch("rc1")))])
                        p.q("vector", lambda v, buf=buf, hd=hd:
                            v.tensor_add(buf[0:32, hd * NTOK:(hd + 1) * NTOK],
                                         rt1[:, :], rt2[:, :]),
                            inc=ch("rdone"), waits=[(ch("rc2"), p.val(ch("rc2")))])
                p.wait("tensor", ch("rdone"), p.val(ch("rdone")))
            p.wait("tensor", ch("vte"), p.val(ch("vte")))
            p.wait("sync", ch("vte"), p.val(ch("vte")))
            p.wait("sync", ch("qev"), NT * NH)
            p.wait("sync", ch("kev"), NT * NH)
            if rope:
                p.wait("sync", ch("rdone"), p.val(ch("rdone")))
            if dbg is not None:
                p.wait("gpsimd", ch("qev"), NT * NH)
                p.wait("gpsimd", ch("kev"), NT * NH)
                p.wait("gpsimd", ch("vte"), p.val(ch("vte")))
                p.dma("gpsimd", dbg[1], qb[:, :], ch("dbg"))
                p.dma("gpsimd", dbg[2], k_own[:, :], ch("dbg"))
                p.dma("gpsimd", dbg[3], v_own[:, :], ch("dbg"))
        with ExitStack() as ctxB:
            EB = ctxB.enter_context
            nonce = NONCE_BASE + (128 if rope else 256)
            KR = D + 16
            VR = NTOK + 16
            p.dma("sync", cc_kin[0:D, :].rearrange("(hc p) t -> p hc t", p=128),
                  k_own[:, :].rearrange("p (hc t) -> p hc t", hc=NH), ch("agd"))
            p.dma("sync", cc_vin[0:NTOK, :].rearrange("(r p) d -> p r d", p=128),
                  v_own[:, :].rearrange("p (r d) -> p r d", r=RT), ch("agd"))
            ncd8 = EB(nc.sbuf_tensor(tag + "nc8", [16, 64], BF16))
            p.q("gpsimd", lambda g: g.memset(ncd8[:, :], float(nonce)), inc=ch("nng"))
            p.dma("sync", cc_kin[D:D + 16, NTOK - 64:NTOK],
                  ncd8[:, :], ch("agd"), waits=[(ch("nng"), p.val(ch("nng")))])
            ncd8v = EB(nc.sbuf_tensor(tag + "nc8v", [16, 64], BF16))
            p.q("gpsimd", lambda g: g.memset(ncd8v[:, :], float(nonce + 64)),
                inc=ch("nng"))
            vag = p.dma("sync", cc_vin[NTOK:NTOK + 16, D - 64:D],
                        ncd8v[:, :], ch("agd"),
                        waits=[(ch("nng"), p.val(ch("nng")))])
            p.q("gpsimd", lambda g: g.collective_compute(
                "AllGather", ALU.bypass, replica_groups=PAIRS, ins=[cc_kin.opt()],
                outs=[cc_kout.opt()]), inc=ch("agc"), waits=[(ch("agd"), vag)])
            p.q("gpsimd", lambda g: g.collective_compute(
                "AllGather", ALU.bypass, replica_groups=PAIRS, ins=[cc_vin.opt()],
                outs=[cc_vout.opt()]), inc=ch("agc"))
            kprior = EB(nc.sbuf_tensor(tag + "kp", [128, NH * NTOK], BF16))
            vprior = EB(nc.sbuf_tensor(tag + "vp", [128, RT * D], BF16))
            agc = p.val(ch("agc"))
            emit_cc_guard(p, EB, tag + "K",
                          cc_kout[D:D + 16, NTOK - 4:NTOK].rearrange(
                              "a (o b) -> a o b", o=1), nonce,
                          ch("agc"), agc, npts=16)
            gok = emit_cc_guard(p, EB, tag + "V",
                                cc_vout[NTOK:NTOK + 16, D - 4:D].rearrange(
                                    "a (o b) -> a o b", o=1), nonce + 64,
                                ch("agc"), agc, npts=16)
            kok = p.val(tag + "K" + "ccok")
            p.dma("sync", kprior[:, :].rearrange("p (hc t) -> p hc t", hc=NH),
                  cc_kout[0:D, :].rearrange("(hc p) t -> p hc t", p=128),
                  ch("agr"), waits=[(tag + "K" + "ccok", kok)])
            p.dma("sync", vprior[:, :].rearrange("p (r d) -> p r d", r=RT),
                  cc_vout[0:NTOK, :].rearrange("(r p) d -> p r d", p=128),
                  ch("agr"), waits=[(tag + "V" + "ccok", gok)])
            agr = p.val(ch("agr"))
            p.wait("tensor", ch("agr"), agr)
            p.wait("vector", ch("agr"), agr)
            p.wait("scalar", ch("agr"), agr)
            if dbg is not None and len(dbg) > 4:
                p.wait("gpsimd", ch("agr"), agr)
                p.dma("gpsimd", dbg[4], kprior[:, :], ch("dbg"))
                p.dma("gpsimd", dbg[5], vprior[:, :], ch("dbg"))
                p.wait("gpsimd", ch("dbg"), p.val(ch("dbg")))

            Pp = EB(nc.sbuf_tensor(tag + "Pp", [128, NTOK], BF16))
            Po = EB(nc.sbuf_tensor(tag + "Po", [128, NTOK], BF16))
            PT = EB(nc.sbuf_tensor(tag + "PT", [128, 2 * NTOK], BF16))
            obf = EB(nc.sbuf_tensor(tag + "obf", [128, D], BF16))
            oT_r = EB(nc.sbuf_tensor(tag + "oTr", [128, D], BF16))
            ps_sp = EB(nc.psum_tensor(tag + "pssp", [128, 1024], F32))
            ps_so = EB(nc.psum_tensor(tag + "psso", [128, 1024], F32))
            ps_pt = EB(nc.psum_tensor(tag + "pspt", [128, 512], BF16))
            ps_pv = EB(nc.psum_tensor(tag + "pspv", [128, 128], F32))
            ps_sum = EB(nc.psum_tensor(tag + "pssm", [128, 1], F32))
            ones_bf = EB(nc.sbuf_tensor(tag + "ob1", [128, 1], BF16))
            p.q("gpsimd", lambda g: g.memset(ones_bf[:, :], 1.0), inc=ch("cg3"))
            p.wait("tensor", ch("cg3"), p.val(ch("cg3")))
            c_mp = cols[:, 0:1]; c_mo = cols[:, 1:2]; c_mf = cols[:, 2:3]
            c_bp = cols[:, 3:4]; c_bo = cols[:, 4:5]; c_sp = cols[:, 5:6]
            c_so = cols[:, 6:7]; c_sm = cols[:, 7:8]; c_ri = cols[:, 8:9]
            last_ptr = 0
            for r in range(RT):
                nko = r + 1
                for hd in range(NH):
                    i = r * NH + hd
                    qt = qb[:, hd * NTOK + r * 128: hd * NTOK + (r + 1) * 128]
                    sw = [(ch("expd"), 2 * i)] if i >= 1 else []
                    for j in range(2):
                        p.q("tensor", lambda t, qt=qt, hd=hd, j=j:
                            t.matmul(ps_sp[:, j * 512:(j + 1) * 512], qt,
                                     kprior[:, hd * NTOK + j * 512: hd * NTOK + (j + 1) * 512],
                                     start=True, stop=True),
                            waits=sw if j == 0 else ())
                    nso = (nko * 128 + 511) // 512
                    for j in range(nso):
                        hi = min(nko * 128, (j + 1) * 512)
                        p.q("tensor", lambda t, qt=qt, hd=hd, j=j, hi=hi:
                            t.matmul(ps_so[:, j * 512:hi], qt,
                                     k_own[:, hd * NTOK + j * 512: hd * NTOK + hi],
                                     start=True, stop=True),
                            inc=ch("smm") if j == nso - 1 else None)
                    smm = p.val(ch("smm"))
                    p.q("vector", lambda v, r=r:
                        v.tensor_add(ps_so[:, r * 128:(r + 1) * 128],
                                     ps_so[:, r * 128:(r + 1) * 128], tri_f[:, :]),
                        inc=ch("mskd"), waits=[(ch("smm"), smm)])
                    mskd = p.val(ch("mskd"))

                    def reds1(v, nko=nko):
                        v.reduce_max(c_mp, ps_sp[:, :], axis=AX.X)
                        v.reduce_max(c_mo, ps_so[:, 0:nko * 128], axis=AX.X)
                        return v.tensor_scalar_add(c_bp, c_mp, lfl[:, 0:1])
                    p.q("vector", reds1, inc=ch("red1"), waits=[(ch("mskd"), mskd)])
                    p.q("vector", lambda v: v.tensor_max(c_mf, c_bp, c_mo),
                        inc=ch("red2"), waits=[(ch("red1"), p.val(ch("red1")))])

                    def reds3(v):
                        v.tensor_sub(c_bp, lfl[:, 0:1], c_mf)
                        return v.tensor_scalar_mul(c_bo, c_mf, -1.0)
                    p.q("vector", reds3, inc=ch("redd"),
                        waits=[(ch("red2"), p.val(ch("red2")))])
                    redd = p.val(ch("redd"))
                    ew = [(ch("redd"), redd), (ch("ptr"), last_ptr)]
                    p.q("scalar", lambda s: s.activation(Pp[:, :], ps_sp[:, :], AF.Exp,
                                                         bias=c_bp),
                        inc=ch("expd"), waits=ew)
                    p.q("scalar", lambda s, nko=nko:
                        s.activation(Po[:, 0:nko * 128], ps_so[:, 0:nko * 128], AF.Exp,
                                     bias=c_bo),
                        inc=ch("expd"), waits=ew)
                    expd = p.val(ch("expd"))
                    ntile = 8 + nko
                    for g4 in range((ntile + 3) // 4):
                        js = list(range(g4 * 4, min(ntile, g4 * 4 + 4)))
                        for jj, j in enumerate(js):
                            tw = []
                            if jj == 0:
                                tw.append((ch("expd"), expd))
                                tw.append((ch("pte"), p.val(ch("pte"))))
                                tw.append((ch("ote"), p.val(ch("ote"))))
                            src_ = (Pp[:, j * 128:(j + 1) * 128] if j < 8 else
                                    Po[:, (j - 8) * 128:(j - 7) * 128])
                            p.q("tensor", lambda t, src_=src_, jj=jj:
                                t.transpose(ps_pt[:, jj * 128:(jj + 1) * 128], src_,
                                            ident_bf[:, :]),
                                inc=ch("ptr") if jj == len(js) - 1 else None, waits=tw)

                        def evac(v, js=js):
                            for jj, j in enumerate(js):
                                inst = v.tensor_copy(PT[:, j * 128:(j + 1) * 128],
                                                     ps_pt[:, jj * 128:(jj + 1) * 128])
                            return inst
                        evw = [(ch("ptr"), p.val(ch("ptr")))]
                        if i >= 1 or g4 >= 1:
                            evw.append((ch("pvmm"), p.val(ch("pvmm"))))
                        p.q("vector", evac, inc=ch("pte"), waits=evw)
                    last_ptr = p.val(ch("ptr"))
                    pte = p.val(ch("pte"))
                    for j in range(ntile):
                        vsrc = vprior if j < 8 else v_own
                        jr = j if j < 8 else j - 8
                        pw_ = [(ch("pte"), pte)] if j == 0 else []
                        if j == 0 and i >= 1:
                            pw_.append((ch("oev"), i))
                        p.q("tensor", lambda t, j=j, jr=jr, vsrc=vsrc, hd=hd, ntile=ntile:
                            t.matmul(ps_pv[:, :], PT[:, j * 128:(j + 1) * 128],
                                     vsrc[:, jr * D + hd * 128: jr * D + (hd + 1) * 128],
                                     start=(j == 0), stop=(j == ntile - 1)),
                            waits=pw_)
                        sw_ = [(ch("rsum"), i)] if (j == 0 and i >= 1) else []
                        p.q("tensor", lambda t, j=j, ntile=ntile:
                            t.matmul(ps_sum[:, :], PT[:, j * 128:(j + 1) * 128],
                                     ones_bf[:, :], start=(j == 0),
                                     stop=(j == ntile - 1)),
                            inc=ch("pvmm") if j == ntile - 1 else None, waits=sw_)
                    p.q("vector", lambda v: v.reciprocal(c_ri, ps_sum[:, :]),
                        inc=ch("rsum"), waits=[(ch("pvmm"), p.val(ch("pvmm")))])
                    ovw = [(ch("rsum"), p.val(ch("rsum")))]
                    if r >= 1 and hd == 0:
                        ovw.append((ch("odw"), 16 * r))
                    p.q("vector", lambda v, hd=hd:
                        v.tensor_scalar_mul(obf[:, hd * 128:(hd + 1) * 128], ps_pv[:, :],
                                            c_ri),
                        inc=ch("oev"), waits=ovw)
                for g4 in range(4):
                    for jj in range(4):
                        kc = g4 * 4 + jj
                        tw = []
                        if jj == 0:
                            tw.append((ch("oev"), p.val(ch("oev"))))
                            tw.append((ch("ote"), p.val(ch("ote"))))
                            tw.append((ch("pte"), p.val(ch("pte"))))
                            if r >= 1:
                                tw.append((ch("odw"), 16 * r))
                        p.q("tensor", lambda t, kc=kc, jj=jj:
                            t.transpose(ps_pt[:, jj * 128:(jj + 1) * 128],
                                        obf[:, kc * 128:(kc + 1) * 128], ident_bf[:, :]),
                            inc=ch("otr") if jj == 3 else None, waits=tw)

                    def oevac(v, g4=g4):
                        for jj in range(4):
                            kc = g4 * 4 + jj
                            inst = v.tensor_copy(oT_r[:, kc * 128:(kc + 1) * 128],
                                                 ps_pt[:, jj * 128:(jj + 1) * 128])
                        return inst
                    p.q("vector", oevac, inc=ch("ote"),
                        waits=[(ch("otr"), p.val(ch("otr")))])
                p.dma("sync", o_dram[:, r * D:(r + 1) * D], oT_r[:, :], ch("odw"),
                      waits=[(ch("ote"), p.val(ch("ote")))])
            p.wait("tensor", ch("odw"), p.val(ch("odw")))
            p.wait("gpsimd", ch("odw"), p.val(ch("odw")))
            p.wait("vector", ch("odw"), p.val(ch("odw")))
            p.wait("scalar", ch("odw"), p.val(ch("odw")))
            p.wait("sync", ch("odw"), p.val(ch("odw")))
    oT_sb = E(nc.sbuf_tensor(tag + "oT", [128, RT * D], BF16))
    y_sb = [E(nc.sbuf_tensor(f"{tag}y{i}", [128, 512], F32)) for i in range(2)]
    x2_sb = [E(nc.sbuf_tensor(f"{tag}x2{i}", [128, 512], F32)) for i in range(2)]
    bias128 = E(nc.sbuf_tensor(tag + "b128", [128, D], F32))
    ones_f = E(nc.sbuf_tensor(tag + "onf", [1, 128], F32))
    p.q("gpsimd", lambda g: g.memset(ones_f[:, :], 1.0), inc=ch("cg2"))
    orl = p.dma("sync", oT_sb[:, :], o_dram, ch("orl"))
    p.wait("tensor", ch("orl"), orl)
    with ExitStack() as ctxC:
        EC = ctxC.enter_context
        ps_b = EC(nc.psum_tensor(tag + "psb", [1, 512], F32))
        ps_bc = EC(nc.psum_tensor(tag + "psbc", [128, 512], F32))
        if bv is not None:
            bvbf = EC(nc.sbuf_tensor(tag + "bvb", [128, 16], BF16))
            bde_sb = EC(nc.sbuf_tensor(tag + "bde", [1, D], F32))
            bwbuf = [EC(nc.sbuf_tensor(f"{tag}bw{i}", [128, 512], BF16))
                     for i in range(2)]
            p.q("vector", lambda v: v.tensor_copy(bvbf[:, :], bv_sb[:, :]), inc=ch("bvb"))
            p.wait("tensor", ch("bvb"), p.val(ch("bvb")))
            for nd in range(4):
                for kcc in range(KC):
                    J = nd * KC + kcc
                    slot = bwbuf[J % 2]
                    dwb = p.dma("gpsimd", slot[:, 0:512], wdense_sl[nd, kcc], ch("bwd"),
                                waits=[(ch("bmm"), J - 1)] if J >= 2 else [])
                    p.q("tensor", lambda t, slot=slot, kcc=kcc:
                        t.matmul(ps_b[:, :], bvbf[:, kcc:kcc + 1], slot[:, 0:512],
                                 start=(kcc == 0), stop=(kcc == KC - 1)),
                        inc=ch("bmm"), waits=[(ch("bwd"), dwb)] +
                            ([(ch("bev"), nd)] if kcc == 0 and nd >= 1 else []))
                p.q("vector", lambda v, nd=nd:
                    v.tensor_copy(bias_f[:, nd * 512:(nd + 1) * 512], ps_b[:, :]),
                    inc=ch("bev"), waits=[(ch("bmm"), p.val(ch("bmm")))])
            bd_ = p.dma("sync", bde_sb[:, :], bdense, ch("bdd"))
            p.q("vector", lambda v: v.tensor_add(bias_f[:, :], bias_f[:, :], bde_sb[:, :]),
                inc=ch("bfin"), waits=[(ch("bdd"), bd_), (ch("bev"), p.val(ch("bev")))])
        else:
            bd_ = p.dma("sync", bias_f[:, :], bout, ch("bdd"))
            p.q("vector", lambda v: v.tensor_copy(bias_f[:, :], bias_f[:, :]),
                inc=ch("bfin"), waits=[(ch("bdd"), bd_)]) if False else None
            p.q("gpsimd", lambda g: g.nop(), inc=ch("bfin"), waits=[(ch("bdd"), bd_)])
        bfin = p.val(ch("bfin"))
        p.wait("tensor", ch("bfin"), bfin)
        p.wait("tensor", ch("cg2"), p.val(ch("cg2")))
        for nd in range(4):
            p.q("tensor", lambda t, nd=nd:
                t.matmul(ps_bc[:, :], ones_f[:, :], bias_f[:, nd * 512:(nd + 1) * 512],
                         start=True, stop=True),
                inc=ch("bcm"), waits=[(ch("bce"), nd)] if nd >= 1 else [])
            p.q("vector", lambda v, nd=nd:
                v.tensor_copy(bias128[:, nd * 512:(nd + 1) * 512], ps_bc[:, :]),
                inc=ch("bce"), waits=[(ch("bcm"), p.val(ch("bcm")))])
        p.wait("tensor", ch("bce"), p.val(ch("bce")))
        p.wait("gpsimd", ch("bce"), p.val(ch("bce")))
        p.wait("vector", ch("bce"), p.val(ch("bce")))
    emit_out_proj(p, E, oT_sb, wdense_sl, x_dram, y_dram, NH, wbuf, x2_sb, y_sb, tag,
                  bias128=bias128, act_ix=lambda hk, r: r * D + hk * 128)


def build_mha_launch(rope, debug=False):
    nc = bass.Bass("TRN2", target_bir_lowering=False)
    x_in = nc.dram_tensor("x_in", [NTOK, D], F32, kind="ExternalInput").ap()
    wq = nc.dram_tensor("wq", [16, 128, D], F32, kind="ExternalInput").ap()
    wk = nc.dram_tensor("wk", [16, 128, D], F32, kind="ExternalInput").ap()
    wv = nc.dram_tensor("wv", [16, 128, D], F32, kind="ExternalInput").ap()
    wde = nc.dram_tensor("wde", [4, 16, 128, 512], F32, kind="ExternalInput").ap()
    logflag = nc.dram_tensor("logflag", [128, 1], F32, kind="ExternalInput").ap()
    y = nc.dram_tensor("y", [NTOK, D], F32, kind="ExternalOutput").ap()
    cc_kin = nc.dram_tensor("cc_kin", [D + 16, NTOK], BF16, kind="Internal").ap()
    cc_kout = nc.dram_tensor("cc_kout", [2 * D + 32, NTOK], BF16, kind="Internal").ap()
    cc_vin = nc.dram_tensor("cc_vin", [NTOK + 16, D], BF16, kind="Internal").ap()
    cc_vout = nc.dram_tensor("cc_vout", [2 * NTOK + 32, D], BF16, kind="Internal").ap()
    kw = {}
    if rope:
        kw["cos_in"] = nc.dram_tensor("cos_in", [32, NTOK], F32, kind="ExternalInput").ap()
        kw["sin_in"] = nc.dram_tensor("sin_in", [32, NTOK], F32, kind="ExternalInput").ap()
        kw["rmat_in"] = nc.dram_tensor("rmat_in", [32, 32], F32, kind="ExternalInput").ap()
        kw["bqk"] = nc.dram_tensor("bqk", [128, 32], F32, kind="ExternalInput").ap()
        kw["bv"] = nc.dram_tensor("bv", [128, 16], F32, kind="ExternalInput").ap()
        kw["bdense"] = nc.dram_tensor("bdense", [1, D], F32, kind="ExternalInput").ap()
    else:
        kw["bout"] = nc.dram_tensor("bout", [1, D], F32, kind="ExternalInput").ap()
    if debug:
        dbgo = nc.dram_tensor("dbgo", [128, RT * D], F32, kind="ExternalOutput").ap()
        dbgq = nc.dram_tensor("dbgq", [128, 16 * NTOK], F32, kind="ExternalOutput").ap()
        dbgk = nc.dram_tensor("dbgk", [128, 16 * NTOK], F32, kind="ExternalOutput").ap()
        dbgv = nc.dram_tensor("dbgv", [128, RT * D], F32, kind="ExternalOutput").ap()
    with ExitStack() as ctx:
        p = Prog(nc, ctx.enter_context)
        bufs = emit_mha(p, ctx.enter_context, x_in, y, wq, wk, wv, wde, "m",
                 cc_kin, cc_kout, cc_vin, cc_vout, logflag, rope=rope, **kw,
                 dbg=(dbgo, dbgq, dbgk, dbgv) if debug else None)
        p.realize()
    return nc


def mha_a_maps(W_qkv, b_qkv, W_dense, b_dense):
    W = np.asarray(W_qkv).reshape(D, 16, 3, 128)
    Wq_all = np.ascontiguousarray(W[:, :, 0, :].reshape(D, D))
    Wk_all = np.ascontiguousarray(W[:, :, 1, :].reshape(D, D))
    Wv_all = np.ascontiguousarray(W[:, :, 2, :].reshape(D, D))
    b = np.asarray(b_qkv).reshape(16, 3, 128)
    bqk = np.zeros((128, 32), np.float32)
    bqk[:, 0:16] = b[:, 0, :].T
    bqk[:, 16:32] = b[:, 1, :].T
    bv = np.ascontiguousarray(b[:, 2, :].T)
    ROT = 32
    inv_freq = 1.0 / (10000.0 ** (np.arange(0, ROT, 2, dtype=np.float64) / ROT))
    rmat = np.zeros((32, 32), np.float32)
    for m in range(16):
        rmat[16 + m, m] = -1.0
        rmat[m, 16 + m] = 1.0
    def cs(parity):
        t = parity * NTOK + np.arange(NTOK, dtype=np.float64)
        fr = np.outer(t, inv_freq)
        emb = np.concatenate([fr, fr], axis=1)
        return (np.cos(emb).T.astype(np.float32),
                np.sin(emb).T.astype(np.float32))
    return {
        "wq": lhsT_slabs(np.ascontiguousarray(Wq_all.T)),
        "wk": lhsT_slabs(np.ascontiguousarray(Wk_all.T)),
        "wv": lhsT_slabs(np.ascontiguousarray(Wv_all.T)),
        "wde": rhs_slabs(np.asarray(W_dense)),
        "bqk": bqk, "bv": bv,
        "bdense": np.asarray(b_dense).reshape(1, D).astype(np.float32),
        "rmat_in": rmat,
    }, cs


def mha_b_maps(w_qkv, w_out, b_out):
    W = np.asarray(w_qkv)
    return {
        "wq": lhsT_slabs(np.ascontiguousarray(W[:, 0:D].T)),
        "wk": lhsT_slabs(np.ascontiguousarray(W[:, D:2 * D].T)),
        "wv": lhsT_slabs(np.ascontiguousarray(W[:, 2 * D:].T)),
        "wde": rhs_slabs(np.asarray(w_out)),
        "bout": np.asarray(b_out).reshape(1, D).astype(np.float32),
    }


def logflag_arr(parity):
    return np.full((128, 1), 0.0 if parity == 1 else -1e30, np.float32)





def emit_l1ffn(p, E, x_dram, y_dram, gamma128, beta128, bases_pad, slopes_pad,
               oscale, cc_in, cc_out, tag):
    nc = p.nc
    ch = lambda n: tag + n
    BIG = 1e30

    def step(eng, fn, extra=()):
        k = p.val(ch("sq"))
        p.q(eng, fn, inc=ch("sq"), waits=[(ch("sq"), k)] + list(extra))

    ident_f = E(nc.sbuf_tensor(tag + "idf", [128, 128], F32))
    p.q("gpsimd", lambda g: g.memset(ident_f[:, :], 0.0), inc=ch("cg"))
    p.q("gpsimd", lambda g: g.affine_select(
        ident_f[:, :], ident_f[:, :], compare_op=ALU.not_equal, fill=1.0, base=0,
        pattern=[[-1, 128]], channel_multiplier=1), inc=ch("cg"))
    tileid = E(nc.sbuf_tensor(tag + "tid", [128, 64], F32))
    p.q("gpsimd", lambda g: g.iota(tileid[:, :], [[1, 64]], base=0,
                                   channel_multiplier=0,
                                   allow_small_or_imprecise_dtypes=True), inc=ch("cg"))
    jrev64 = E(nc.sbuf_tensor(tag + "jr64", [128, 64], F32))
    p.q("gpsimd", lambda g: g.iota(jrev64[:, :], [[-1, 64]], base=63,
                                   channel_multiplier=0,
                                   allow_small_or_imprecise_dtypes=True), inc=ch("cg"))
    jrev8 = E(nc.sbuf_tensor(tag + "jr8", [128, 8], F32))
    p.q("gpsimd", lambda g: g.iota(jrev8[:, :], [[-1, 8]], base=7,
                                   channel_multiplier=0,
                                   allow_small_or_imprecise_dtypes=True), inc=ch("cg"))
    clc = E(nc.sbuf_tensor(tag + "clc", [128, 64], F32))
    p.q("gpsimd", lambda g: g.iota(clc[:, :].rearrange("p (a b) -> p a b", b=8),
                                   [[1, 8], [0, 8]], base=0, channel_multiplier=0,
                                   allow_small_or_imprecise_dtypes=True), inc=ch("cg"))
    gv116 = E(nc.sbuf_tensor(tag + "gv", [128, 512], F32))
    p.q("gpsimd", lambda g: g.iota(gv116[:, :].rearrange("p (a b) -> p a b", b=16),
                                   [[0, 32], [1, 16]], base=1, channel_multiplier=0,
                                   allow_small_or_imprecise_dtypes=True), inc=ch("cg"))
    negbig = E(nc.sbuf_tensor(tag + "nb", [128, 64], F32))
    p.q("gpsimd", lambda g: g.memset(negbig[:, :], -BIG), inc=ch("cg"))
    iotap = E(nc.sbuf_tensor(tag + "iop", [128, 1], F32))
    p.q("gpsimd", lambda g: g.iota(iotap[:, :], [[0, 1]], base=0, channel_multiplier=1,
                                   allow_small_or_imprecise_dtypes=True), inc=ch("cg"))
    ones_f = E(nc.sbuf_tensor(tag + "onf", [1, 128], F32))
    p.q("gpsimd", lambda g: g.memset(ones_f[:, :], 1.0), inc=ch("cg"))
    cgv = p.val(ch("cg"))
    for e in ("tensor", "vector", "scalar", "sync"):
        p.wait(e, ch("cg"), cgv)

    g128 = E(nc.sbuf_tensor(tag + "g128", [128, D], F32))
    b128 = E(nc.sbuf_tensor(tag + "b128", [128, D], F32))
    slo_sb = E(nc.sbuf_tensor(tag + "slo", [64, 512], F32))
    osc_sb = E(nc.sbuf_tensor(tag + "osc", [64, 1], F32))
    p.dma("sync", g128[:, :], gamma128, ch("cd"))
    p.dma("sync", b128[:, :], beta128, ch("cd"))
    p.dma("sync", slo_sb[:, :], slopes_pad[0:64, :], ch("cd"))
    cdv = p.dma("sync", osc_sb[:, :], oscale, ch("cd"))
    p.wait("vector", ch("cd"), cdv)
    p.wait("scalar", ch("cd"), cdv)

    xa = E(nc.sbuf_tensor(tag + "xa", [128, D], F32))
    ya = E(nc.sbuf_tensor(tag + "ya", [128, D], F32))
    xna = E(nc.sbuf_tensor(tag + "xna", [128, D], F32))
    ta = E(nc.sbuf_tensor(tag + "ta", [128, D], F32))
    scr = E(nc.sbuf_tensor(tag + "scr", [128, D], F32))
    cols = E(nc.sbuf_tensor(tag + "cols", [128, 24], F32))
    ss32 = E(nc.sbuf_tensor(tag + "ss32", [64, 32], F32))
    ssb = E(nc.sbuf_tensor(tag + "ssb", [128, D], F32))
    ssrow = E(nc.sbuf_tensor(tag + "ssrow", [1, D], F32))
    ieq = E(nc.sbuf_tensor(tag + "ieq", [128, 64], F32))
    ieq8 = E(nc.sbuf_tensor(tag + "ieq8", [128, 64], mybir.dt.int8))
    msel = E(nc.sbuf_tensor(tag + "msel", [128, 64], F32))
    m64a = E(nc.sbuf_tensor(tag + "m64a", [128, 64], F32))
    m64b = E(nc.sbuf_tensor(tag + "m64b", [128, 64], F32))
    idx_all = E(nc.sbuf_tensor(tag + "idxa", [128, 8], F32))
    widx_i = E(nc.sbuf_tensor(tag + "widx", [128, 8], I32))
    bidx_i = E(nc.sbuf_tensor(tag + "bidx", [128, 8], I32))
    xsw = E(nc.sbuf_tensor(tag + "xsw", [128, 32], F32))
    bas_t = E(nc.sbuf_tensor(tag + "bast", [128, 512], F32))
    slo_t = E(nc.sbuf_tensor(tag + "slot", [128, 512], F32))
    GE = E(nc.sbuf_tensor(tag + "GE", [128, 512], F32))
    OH = E(nc.sbuf_tensor(tag + "OH", [128, 512], F32))
    u32 = E(nc.sbuf_tensor(tag + "u32", [128, 32], F32))
    gif = E(nc.sbuf_tensor(tag + "gif", [128, 32], F32))
    xl32 = E(nc.sbuf_tensor(tag + "xl32", [128, 32], F32))
    bcol = E(nc.sbuf_tensor(tag + "bcol", [128, 32], F32))
    scol = E(nc.sbuf_tensor(tag + "scol", [128, 32], F32))
    dlt = E(nc.sbuf_tensor(tag + "dlt", [128, 32], F32))
    tmm = E(nc.sbuf_tensor(tag + "tmm", [64, 4], F32))
    ohT = E(nc.sbuf_tensor(tag + "ohT", [64, 128], F32))
    gmp = E(nc.sbuf_tensor(tag + "gmp", [64, 4], F32))
    mnx = E(nc.sbuf_tensor(tag + "mnx", [128, 4], F32))
    xn_dram = nc.dram_tensor(tag + "xnd", [NTOK, D], F32, kind="Internal").ap()
    ssd = nc.dram_tensor(tag + "ssd", [64, 32], F32, kind="Internal").ap()

    with ExitStack() as ctxA:
        EA = ctxA.enter_context
        ps_a = EA(nc.psum_tensor(tag + "psa", [128, 512], F32))
        ps_b = EA(nc.psum_tensor(tag + "psb", [128, 128], F32))

        c_s1 = cols[:, 0:1]; c_q1 = cols[:, 1:2]; c_m1 = cols[:, 2:3]
        c_v1 = cols[:, 3:4]; c_r1 = cols[:, 4:5]; c_sq2 = cols[:, 5:6]
        c_m2 = cols[:, 6:7]; c_v2 = cols[:, 7:8]; c_r2 = cols[:, 8:9]
        c_a = cols[:, 9:10]; c_b = cols[:, 10:11]; c_c = cols[:, 11:12]
        c_d = cols[:, 12:13]; c_e = cols[:, 13:14]; c_f = cols[:, 14:15]
        c_g = cols[:, 15:16]; c_h = cols[:, 16:17]; c_i = cols[:, 17:18]
        c_mn = mnx[:, 0:1]; c_mx = mnx[:, 1:2]; c_os = mnx[:, 2:3]
        c_eps = cols[:, 18:19]
        p.q("gpsimd", lambda g: g.memset(c_eps, EPS), inc=ch("cg2"))
        p.wait("scalar", ch("cg2"), p.val(ch("cg2")))

        step("vector", lambda v: v.reduce_sum(ss32[:, :],
                                              slo_sb[:, :].rearrange("p (a b) -> p a b", b=16),
                                              axis=AX.X),
             extra=[(ch("cd"), cdv)])
        step("vector", lambda v: v.tensor_scalar_mul(ss32[:, :], ss32[:, :], 1.0 / 16))
        step("scalar", lambda s: s.activation(ss32[:, :], ss32[:, :], AF.Sign))
        sq0 = p.val(ch("sq"))
        d1 = p.dma("sync", ssd, ss32[:, :], ch("ssd"), waits=[(ch("sq"), sq0)])
        d2 = p.dma("sync", ssrow[:, :],
                   ssd.rearrange("a b -> (a b)").rearrange("(o f) -> o f", o=1),
                   ch("ssd"), waits=[(ch("ssd"), d1)])
        step("tensor", lambda t: t.matmul(ps_a[:, :], ones_f[:, :], ssrow[:, 0:512],
                                          start=True, stop=True),
             extra=[(ch("ssd"), d2)])
        step("vector", lambda v: v.tensor_copy(ssb[:, 0:512], ps_a[:, :]))
        for nd in range(1, 4):
            step("tensor", lambda t, nd=nd: t.matmul(ps_a[:, :], ones_f[:, :],
                                                     ssrow[:, nd * 512:(nd + 1) * 512],
                                                     start=True, stop=True))
            step("vector", lambda v, nd=nd: v.tensor_copy(ssb[:, nd * 512:(nd + 1) * 512],
                                                          ps_a[:, :]))
        step("gpsimd", lambda g: g.memset(tmm[:, :], BIG))
        step("gpsimd", lambda g: g.memset(tmm[:, 1:2], -BIG))

        for r in range(RT):
            sqv = p.val(ch("sq"))
            dx = p.dma("sync", xa[:, :], x_dram[r * 128:(r + 1) * 128, :], ch("xd"),
                       waits=[(ch("sq"), sqv)])
            step("vector", lambda v: v.reduce_sum(c_s1, xa[:, :], axis=AX.X),
                 extra=[(ch("xd"), dx)])
            step("scalar", lambda s: s.activation(scr[:, :], xa[:, :], AF.Square))
            step("vector", lambda v: v.reduce_sum(c_q1, scr[:, :], axis=AX.X))
            step("vector", lambda v: v.tensor_scalar_mul(c_m1, c_s1, 1.0 / D))
            step("scalar", lambda s: s.activation(c_v1, c_m1, AF.Square))
            step("vector", lambda v: v.scalar_tensor_tensor(c_v1, c_q1, 1.0 / D, c_v1,
                                                            ALU.mult, ALU.subtract))
            step("scalar", lambda s: s.activation(c_r1, c_v1, AF.Sqrt, bias=c_eps))
            step("vector", lambda v: v.reciprocal(c_r1, c_r1)) if False else None
            step("vector", lambda v: v.reciprocal(c_a, c_r1))
            step("vector", lambda v: v.tensor_scalar(ya[:, :], xa[:, :], c_m1, c_a,
                                                     ALU.subtract, ALU.mult))
            step("vector", lambda v: v.reduce_sum(c_b, ya[:, :], axis=AX.X))
            step("scalar", lambda s: s.activation(scr[:, :], ya[:, :], AF.Square))
            step("vector", lambda v: v.reduce_sum(c_sq2, scr[:, :], axis=AX.X))
            step("vector", lambda v: v.tensor_scalar_mul(c_m2, c_b, 1.0 / D))
            step("scalar", lambda s: s.activation(c_v2, c_m2, AF.Square))
            step("vector", lambda v: v.scalar_tensor_tensor(c_v2, c_sq2, 1.0 / D, c_v2,
                                                            ALU.mult, ALU.subtract))
            step("scalar", lambda s: s.activation(c_r2, c_v2, AF.Sqrt, bias=c_eps))
            step("vector", lambda v: v.reciprocal(c_c, c_r2))
            step("vector", lambda v: v.tensor_scalar(xna[:, :], ya[:, :], c_m2, c_c,
                                                     ALU.subtract, ALU.mult))
            step("vector", lambda v: v.tensor_mul(xna[:, :], xna[:, :], g128[:, :]))
            step("vector", lambda v: v.tensor_add(xna[:, :], xna[:, :], b128[:, :]))
            sqv = p.val(ch("sq"))
            dxn = p.dma("sync", xn_dram[r * 128:(r + 1) * 128, :], xna[:, :], ch("xnw"),
                        waits=[(ch("sq"), sqv)])
            dyw = p.dma("sync", y_dram[r * 128:(r + 1) * 128, :], ya[:, :], ch("yst"),
                        waits=[(ch("sq"), sqv)])
            step("scalar", lambda s: s.activation(ta[:, :], xna[:, :], AF.Sign))
            step("vector", lambda v: v.tensor_mul(ta[:, :], ta[:, :], ssb[:, :]))
            step("vector", lambda v: v.reduce_sum(m64a[:, 0:8],
                                                  ta[:, :].rearrange("p (a b) -> p a b", b=256),
                                                  axis=AX.X))
            step("vector", lambda v: v.reduce_sum(m64b[:, :],
                                                  ta[:, :].rearrange("p (a b) -> p a b", b=32),
                                                  axis=AX.X))
            step("vector", lambda v: v.scalar_tensor_tensor(msel[:, 0:8], m64a[:, 0:8],
                                                            8.0, jrev8[:, :],
                                                            ALU.mult, ALU.add))
            step("vector", lambda v: v.reduce_max(c_d, msel[:, 0:8], axis=AX.X))
            step("vector", lambda v: v.reduce_max(c_e, m64a[:, 0:8], axis=AX.X))
            step("vector", lambda v: v.tensor_scalar(c_f, c_e, 8.0, 7.0, ALU.mult,
                                                     ALU.add))
            step("vector", lambda v: v.tensor_sub(c_f, c_f, c_d))
            step("vector", lambda v: v.tensor_single_scalar(ieq[:, :], clc[:, :], c_f,
                                                            ALU.is_equal))
            step("vector", lambda v: v.tensor_copy(ieq8[:, :], ieq[:, :]))
            step("vector", lambda v: v.select(msel[:, :], ieq8[:, :], m64b[:, :],
                                              negbig[:, :]))
            step("vector", lambda v: v.scalar_tensor_tensor(m64a[:, :], msel[:, :],
                                                            64.0, jrev64[:, :],
                                                            ALU.mult, ALU.add))
            step("vector", lambda v: v.reduce_max(c_g, m64a[:, :], axis=AX.X))
            step("vector", lambda v: v.reduce_max(c_h, msel[:, :], axis=AX.X))
            step("vector", lambda v: v.tensor_scalar(c_i, c_h, 64.0, 63.0, ALU.mult,
                                                     ALU.add))
            step("vector", lambda v, r=r: v.tensor_sub(idx_all[:, r:r + 1], c_i, c_g))
            step("vector", lambda v, r=r: v.tensor_scalar(c_a, iotap[:, :], float(r * 128),
                                                          64.0, ALU.add, ALU.mult))
            step("vector", lambda v, r=r: v.tensor_add(c_b, c_a, idx_all[:, r:r + 1]))
            step("vector", lambda v, r=r: v.tensor_copy(widx_i[:, r:r + 1], c_b))
            step("vector", lambda v, r=r: v.tensor_copy(bidx_i[:, r:r + 1],
                                                        idx_all[:, r:r + 1]))
            sqv = p.val(ch("sq"))
            dg = p.dma("gpsimd", xsw[:, :], None, ch("gd"),
                       waits=[]) if False else None
            p.q("gpsimd", lambda g, r=r: g.indirect_dma_start(
                xsw[:, :], None, xn_dram[0:128, 0:32],
                IndirectOffsetOnAxis(widx_i[:, r:r + 1], 0)),
                inc=ch("gd"), inc_by=16,
                waits=[(ch("sq"), sqv), (ch("xnw"), dxn)])
            gdv = p.val(ch("gd"))
            step("vector", lambda v: v.tensor_reduce(c_a, xsw[:, :], axis=AX.X,
                                                     op=ALU.min),
                 extra=[(ch("gd"), gdv)])
            step("vector", lambda v: v.tensor_reduce(c_b, xsw[:, :], axis=AX.X,
                                                     op=ALU.max))
            step("vector", lambda v, r=r: v.tensor_single_scalar(
                ieq[:, :], tileid[:, :], idx_all[:, r:r + 1], ALU.is_equal))
            step("vector", lambda v: v.tensor_scalar(msel[:, :], ieq[:, :], -BIG, BIG,
                                                     ALU.mult, ALU.add))
            step("vector", lambda v: v.tensor_scalar_mul(m64a[:, :], ieq[:, :], c_a))
            step("vector", lambda v: v.tensor_add(m64a[:, :], m64a[:, :], msel[:, :]))
            step("vector", lambda v: v.tensor_scalar(msel[:, :], ieq[:, :], BIG, -BIG,
                                                     ALU.mult, ALU.add))
            step("vector", lambda v: v.tensor_scalar_mul(m64b[:, :], ieq[:, :], c_b))
            step("vector", lambda v: v.tensor_add(m64b[:, :], m64b[:, :], msel[:, :]))
            step("tensor", lambda t: t.transpose(ps_b[:, :], m64a[:, 0:64][0:128, :],
                                                 ident_f[:, :])) if False else None
            step("tensor", lambda t: t.transpose(ps_a[0:64, 0:128], m64a[:, :],
                                                 ident_f[:, :]))
            step("vector", lambda v: v.tensor_reduce(tmm[:, 2:3], ps_a[0:64, 0:128],
                                                     axis=AX.X, op=ALU.min))
            step("tensor", lambda t: t.transpose(ps_a[0:64, 0:128], m64b[:, :],
                                                 ident_f[:, :]))
            step("vector", lambda v: v.tensor_reduce(tmm[:, 3:4], ps_a[0:64, 0:128],
                                                     axis=AX.X, op=ALU.max))
            step("vector", lambda v: v.tensor_tensor(tmm[:, 0:1], tmm[:, 0:1],
                                                     tmm[:, 2:3], ALU.min))
            step("vector", lambda v: v.tensor_tensor(tmm[:, 1:2], tmm[:, 1:2],
                                                     tmm[:, 3:4], ALU.max))
        step("vector", lambda v: v.tensor_scalar_mul(tmm[:, 2:3], tmm[:, 1:2], -1.0))
        step("vector", lambda v: v.tensor_copy(tmm[:, 3:4], tmm[:, 0:1]))
        sqv = p.val(ch("sq"))
        p.dma("sync", cc_in[0:64, :], tmm[:, 2:4], ch("ard"), waits=[(ch("sq"), sqv)])
        nonce = NONCE_BASE + (384 if tag == "G" else 448)
        nn = emit_cc_nonce(p, EA, tag,
                           cc_in[64:96, :].rearrange("a b -> (a b)").rearrange(
                               "(o f) -> o f", o=1), nonce)
        da = p.val(ch("ard"))
        p.q("gpsimd", lambda g: g.collective_compute(
            "AllReduce", ALU.min, replica_groups=WORLD, ins=[cc_in.opt()],
            outs=[cc_out.opt()]), inc=ch("arc"),
            waits=[(ch("ard"), da), (tag + "nncd", nn)])
        gok = emit_cc_guard(p, EA, tag,
                            cc_out[94:96, :].rearrange("a b -> (a b)").rearrange(
                                "(o f) -> o f", o=1), nonce,
                            ch("arc"), p.val(ch("arc")))
        db = p.dma("sync", gmp[:, 0:2], cc_out[0:64, :], ch("arr"),
                   waits=[(tag + "ccok", gok)])
        step("vector", lambda v: v.tensor_scalar_mul(gmp[:, 2:3], gmp[:, 0:1], -1.0),
             extra=[(ch("arr"), db)])
        step("vector", lambda v: v.tensor_copy(gmp[:, 0:1], gmp[:, 1:2]))
        step("vector", lambda v: v.tensor_copy(gmp[:, 1:2], gmp[:, 2:3]))
        step("vector", lambda v: v.tensor_copy(gmp[:, 2:3], osc_sb[:, :]))

        for r in range(RT):
            sqv = p.val(ch("sq"))
            p.q("gpsimd", lambda g, r=r: g.indirect_dma_start(
                xsw[:, :], None, xn_dram[0:128, 0:32],
                IndirectOffsetOnAxis(widx_i[:, r:r + 1], 0)),
                inc=ch("gd"), inc_by=16, waits=[(ch("sq"), sqv)])
            p.q("gpsimd", lambda g, r=r: g.indirect_dma_start(
                bas_t[:, :], None, bases_pad[0:128, 0:512],
                IndirectOffsetOnAxis(bidx_i[:, r:r + 1], 0)),
                inc=ch("gd"), inc_by=16)
            p.q("gpsimd", lambda g, r=r: g.indirect_dma_start(
                slo_t[:, :], None, slopes_pad[0:128, 0:512],
                IndirectOffsetOnAxis(bidx_i[:, r:r + 1], 0)),
                inc=ch("gd"), inc_by=16)
            gdv = p.val(ch("gd"))
            step("vector", lambda v, r=r: v.tensor_single_scalar(
                ieq[:, :], tileid[:, :], idx_all[:, r:r + 1], ALU.is_equal),
                 extra=[(ch("gd"), gdv)])
            step("tensor", lambda t: t.transpose(ps_a[0:64, 0:128], ieq[:, :],
                                                 ident_f[:, :]))
            step("vector", lambda v: v.tensor_copy(ohT[:, :], ps_a[0:64, 0:128]))
            step("tensor", lambda t: t.matmul(ps_b[:, 0:3], ohT[:, :], gmp[:, 0:3],
                                              start=True, stop=True))
            step("vector", lambda v: v.tensor_copy(mnx[:, 0:3], ps_b[:, 0:3]))
            step("vector", lambda v: v.tensor_sub(c_a, c_mx, c_mn))
            step("vector", lambda v: v.tensor_scalar_add(c_a, c_a, 1e-8))
            step("vector", lambda v: v.reciprocal(c_b, c_a))
            step("vector", lambda v: v.tensor_scalar(u32[:, :], xsw[:, :], c_mn, c_b,
                                                     ALU.subtract, ALU.mult))
            step("vector", lambda v: v.tensor_scalar(u32[:, :], u32[:, :], 0.0,
                                                     1.0 - 1e-6, ALU.max, ALU.min))
            step("vector", lambda v: v.tensor_scalar_mul(u32[:, :], u32[:, :], 16.0))
            for g_ in range(16):
                step("vector", lambda v, g_=g_: v.tensor_single_scalar(
                    GE[:, :].rearrange("p (a b) -> p a b", b=16)[:, :, g_:g_ + 1],
                    u32[:, :].rearrange("p (a o) -> p a o", o=1), float(g_ + 1), ALU.is_ge))
            step("vector", lambda v: v.reduce_sum(gif[:, :],
                                                  GE[:, :].rearrange("p (a b) -> p a b", b=16),
                                                  axis=AX.X))
            step("vector", lambda v: v.tensor_sub(xl32[:, :], u32[:, :], gif[:, :]))
            step("vector", lambda v: v.tensor_sub(
                OH[:, :].rearrange("p (a b) -> p a b", b=16)[:, :, 1:16],
                GE[:, :].rearrange("p (a b) -> p a b", b=16)[:, :, 0:15],
                GE[:, :].rearrange("p (a b) -> p a b", b=16)[:, :, 1:16]))
            step("vector", lambda v: v.tensor_scalar(
                OH[:, :].rearrange("p (a b) -> p a b", b=16)[:, :, 0:1],
                GE[:, :].rearrange("p (a b) -> p a b", b=16)[:, :, 0:1],
                -1.0, -1.0, ALU.mult, ALU.subtract)) if False else None
            step("vector", lambda v: v.tensor_scalar(
                OH[:, :].rearrange("p (a b) -> p a b", b=16)[:, :, 0:1],
                GE[:, :].rearrange("p (a b) -> p a b", b=16)[:, :, 0:1],
                -1.0, 1.0, ALU.mult, ALU.add))
            step("vector", lambda v: v.tensor_mul(bas_t[:, :], bas_t[:, :], OH[:, :]))
            step("vector", lambda v: v.tensor_mul(slo_t[:, :], slo_t[:, :], OH[:, :]))
            step("vector", lambda v: v.reduce_sum(bcol[:, :],
                                                  bas_t[:, :].rearrange("p (a b) -> p a b", b=16),
                                                  axis=AX.X))
            step("vector", lambda v: v.reduce_sum(scol[:, :],
                                                  slo_t[:, :].rearrange("p (a b) -> p a b", b=16),
                                                  axis=AX.X))
            step("vector", lambda v: v.tensor_mul(dlt[:, :], scol[:, :], xl32[:, :]))
            step("vector", lambda v: v.tensor_add(dlt[:, :], dlt[:, :], bcol[:, :]))
            step("vector", lambda v: v.tensor_scalar_mul(dlt[:, :], dlt[:, :], c_os))
            sqv = p.val(ch("sq"))
            dx2 = p.dma("sync", xa[:, :], x_dram[r * 128:(r + 1) * 128, :], ch("x2d"),
                        waits=[(ch("sq"), sqv)])
            dy2 = p.dma("sync", ya[:, :], y_dram[r * 128:(r + 1) * 128, :], ch("y2d"),
                        waits=[(ch("sq"), sqv), (ch("yst"), p.val(ch("yst")))])
            step("vector", lambda v: v.tensor_add(ya[:, :], ya[:, :], xa[:, :]),
                 extra=[(ch("x2d"), dx2), (ch("y2d"), dy2)])
            for i in range(64):
                step("vector", lambda v, i=i: v.scalar_tensor_tensor(
                    ya[:, 32 * i:32 * i + 32], dlt[:, :], ieq[:, i:i + 1],
                    ya[:, 32 * i:32 * i + 32], ALU.mult, ALU.add))
            sqv = p.val(ch("sq"))
            p.dma("sync", y_dram[r * 128:(r + 1) * 128, :], ya[:, :], ch("yfw"),
                  waits=[(ch("sq"), sqv)])
        yfw = p.val(ch("yfw"))
        p.wait("sync", ch("yfw"), yfw)
        p.wait("tensor", ch("yfw"), yfw)
        p.wait("vector", ch("yfw"), yfw)
        p.wait("gpsimd", ch("yfw"), yfw)
    return {"idx": idx_all[:, :], "gmp": gmp[:, :], "xnd": xn_dram,
            "tmm": tmm[:, :], "ccout": cc_out, "m64a": m64a[:, :]}


def build_l1ffn_launch(debug=False):
    nc = bass.Bass("TRN2", target_bir_lowering=False)
    x_in = nc.dram_tensor("x_in", [NTOK, D], F32, kind="ExternalInput").ap()
    gamma128 = nc.dram_tensor("gamma128", [128, D], F32, kind="ExternalInput").ap()
    beta128 = nc.dram_tensor("beta128", [128, D], F32, kind="ExternalInput").ap()
    bases_pad = nc.dram_tensor("bases_pad", [128, 512], F32, kind="ExternalInput").ap()
    slopes_pad = nc.dram_tensor("slopes_pad", [128, 512], F32, kind="ExternalInput").ap()
    oscale = nc.dram_tensor("oscale", [64, 1], F32, kind="ExternalInput").ap()
    y = nc.dram_tensor("y", [NTOK, D], F32, kind="ExternalOutput").ap()
    cc_in = nc.dram_tensor("cc_in", [96, 2], F32, kind="Internal").ap()
    cc_out = nc.dram_tensor("cc_out", [96, 2], F32, kind="Internal").ap()
    if debug:
        didx = nc.dram_tensor("didx", [128, 8], F32, kind="ExternalOutput").ap()
        dgmp = nc.dram_tensor("dgmp", [64, 4], F32, kind="ExternalOutput").ap()
        dxn = nc.dram_tensor("dxn", [NTOK, D], F32, kind="ExternalOutput").ap()
        dtmm = nc.dram_tensor("dtmm", [64, 4], F32, kind="ExternalOutput").ap()
        dcc = nc.dram_tensor("dcc", [64, 2], F32, kind="ExternalOutput").ap()
        dm64 = nc.dram_tensor("dm64", [128, 64], F32, kind="ExternalOutput").ap()
    with ExitStack() as ctx:
        p = Prog(nc, ctx.enter_context)
        bufs = emit_l1ffn(p, ctx.enter_context, x_in, y, gamma128, beta128, bases_pad,
                   slopes_pad, oscale, cc_in, cc_out, "s")
        if debug:
            p.wait("gpsimd", "syfw", p.val("syfw"))
            p.dma("gpsimd", didx, bufs["idx"], "dbg")
            p.dma("gpsimd", dgmp, bufs["gmp"], "dbg")
            p.dma("gpsimd", dxn, bufs["xnd"], "dbg")
            p.dma("gpsimd", dtmm, bufs["tmm"], "dbg")
            p.dma("gpsimd", dcc, bufs["ccout"], "dbg")
            p.dma("gpsimd", dm64, bufs["m64a"], "dbg")
            p.wait("gpsimd", "dbg", p.val("dbg"))
        p.realize()
    return nc


def l1ffn_maps(ln_gamma, ln_beta, spline_bases, spline_slopes, output_scale):
    g = np.ascontiguousarray(np.broadcast_to(np.asarray(ln_gamma), (128, D))).astype(np.float32)
    b = np.ascontiguousarray(np.broadcast_to(np.asarray(ln_beta), (128, D))).astype(np.float32)
    bp = np.zeros((128, 512), np.float32)
    bp[0:64] = np.asarray(spline_bases).reshape(64, 512)
    sp = np.zeros((128, 512), np.float32)
    sp[0:64] = np.asarray(spline_slopes).reshape(64, 512)
    return {"gamma128": g, "beta128": b, "bases_pad": bp, "slopes_pad": sp,
            "oscale": np.asarray(output_scale).reshape(64, 1).astype(np.float32)}



def lhsT_slabs(W):
    out_f, in_f = W.shape
    a = W.reshape(out_f // 128, 128, in_f // 128, 128)
    a = a.transpose(0, 3, 2, 1)
    return np.ascontiguousarray(a.reshape(out_f // 128, 128, in_f))


def rhs_slabs(M, n_slice=512):
    K, N = M.shape
    a = M.reshape(K // 128, 128, N // n_slice, n_slice)
    a = a.transpose(2, 0, 1, 3)
    return np.ascontiguousarray(a)



def build_ffn_launch(debug=False):
    nc = bass.Bass("TRN2", target_bir_lowering=False)
    x_in = nc.dram_tensor("x_in", [NTOK, D], F32, kind="ExternalInput").ap()
    wg = nc.dram_tensor("wg", [HC, 128, D], F32, kind="ExternalInput").ap()
    wu = nc.dram_tensor("wu", [HC, 128, D], F32, kind="ExternalInput").ap()
    wd = nc.dram_tensor("wd", [4, HC, 128, 512], F32, kind="ExternalInput").ap()
    y = nc.dram_tensor("y", [NTOK, D], F32, kind="ExternalOutput").ap()
    if debug:
        dxnT = nc.dram_tensor("dxnT", [128, KC * NTOK], F32, kind="ExternalOutput").ap()
        dh = nc.dram_tensor("dh", [128, HC * NTOK], F32, kind="ExternalOutput").ap()
    with ExitStack() as ctx:
        p = Prog(nc, ctx.enter_context)
        bufs = emit_ffn(p, ctx.enter_context, x_in, y, wg, wu, wd, "f")
        if debug:
            p.wait("gpsimd", "fyw", p.val("fyw"))
            p.dma("gpsimd", dxnT, bufs["xnT"][:, :], "dbgd")
            p.dma("gpsimd", dh, bufs["h"][:, :], "dbgd")
            p.wait("gpsimd", "dbgd", p.val("dbgd"))
        p.realize()
    return nc


def ffn_weight_maps(w_gate, w_up, w_down):
    return {
        "wg": lhsT_slabs(np.asarray(w_gate)),
        "wu": lhsT_slabs(np.asarray(w_up)),
        "wd": rhs_slabs(np.ascontiguousarray(np.asarray(w_down).T)),
    }



_prog_cache = {}


def _get(name, builder):
    if name not in _prog_cache:
        _prog_cache[name] = builder()
    return _prog_cache[name]


def kernel_multilaunch(**inp):
    x = np.ascontiguousarray(np.asarray(inp["L0_attn_a_x"], np.float32))
    shards = [np.ascontiguousarray(x[c // 2, (c % 2) * NTOK:(c % 2 + 1) * NTOK])
              for c in range(8)]

    flags = [np.full((128, 1), float(c % 2), np.float32) for c in range(8)]
    lflags = [logflag_arr(c % 2) for c in range(8)]

    dn_m = deltanet_maps(inp["L0_attn_a_Wq"], inp["L0_attn_a_Wk"], inp["L0_attn_a_Wv"],
                         inp["L0_attn_a_Wo"], inp["L0_attn_a_Wb"], inp["L0_attn_a_bb"],
                         inp["L0_attn_a_Wd"], inp["L0_attn_a_bd"])
    ffn_m = ffn_weight_maps(inp["L0_ffn_w_gate"], inp["L0_ffn_w_up"],
                            inp["L0_ffn_w_down"])
    ema_m = ema_maps(inp["L0_attn_b_Wv"], inp["L0_attn_b_Wg"], inp["L0_attn_b_bg"],
                     inp["L0_attn_b_Wq"], inp["L0_attn_b_bq"], inp["L0_attn_b_Wo"])
    mha_a_m, cs_fn = mha_a_maps(inp["L1_attn_a_W_qkv"], inp["L1_attn_a_b_qkv"],
                                inp["L1_attn_a_W_dense"], inp["L1_attn_a_b_dense"])
    mha_b_m = mha_b_maps(inp["L1_attn_b_w_qkv"], inp["L1_attn_b_w_out"],
                         inp["L1_attn_b_b_out"])
    l1_m = l1ffn_maps(inp["L1_ffn_ln_gamma"], inp["L1_ffn_ln_beta"],
                      inp["L1_ffn_spline_bases"], inp["L1_ffn_spline_slopes"],
                      inp["L1_ffn_output_scale"])
    cos0, sin0 = cs_fn(0)
    cos1, sin1 = cs_fn(1)

    def run(nc, maps):
        res = run_bass_kernel_spmd(nc, maps, core_ids=list(range(8)))
        return [res.results[c]["y"] for c in range(8)]

    nc = _get("dn", build_deltanet_launch)
    shards = run(nc, [{"x_in": shards[c], "flag": flags[c], **dn_m}
                      for c in range(8)])
    nc = _get("ffn", build_ffn_launch)
    shards = run(nc, [{"x_in": shards[c], **ffn_m} for c in range(8)])
    nc = _get("ema", build_ema_launch)
    shards = run(nc, [{"x_in": shards[c], "flag": flags[c], **ema_m}
                      for c in range(8)])
    nc = _get("ffn", build_ffn_launch)
    shards = run(nc, [{"x_in": shards[c], **ffn_m} for c in range(8)])
    nc = _get("mha_a", lambda: build_mha_launch(True))
    maps = []
    for c in range(8):
        cos, sin = (cos0, sin0) if c % 2 == 0 else (cos1, sin1)
        maps.append({"x_in": shards[c], "logflag": lflags[c], "cos_in": cos,
                     "sin_in": sin, **mha_a_m})
    shards = run(nc, maps)
    nc = _get("l1", build_l1ffn_launch)
    shards = run(nc, [{"x_in": shards[c], **l1_m} for c in range(8)])
    nc = _get("mha_b", lambda: build_mha_launch(False))
    shards = run(nc, [{"x_in": shards[c], "logflag": lflags[c], **mha_b_m}
                      for c in range(8)])
    nc = _get("l1", build_l1ffn_launch)
    shards = run(nc, [{"x_in": shards[c], **l1_m} for c in range(8)])

    out = np.zeros((B, T, D), np.float32)
    for c in range(8):
        out[c // 2, (c % 2) * NTOK:(c % 2 + 1) * NTOK] = shards[c]
    return out



def build_fused(n_sub=8, dbg_h=False, dump_x=False):
    nc = bass.Bass("TRN2", target_bir_lowering=False)
    DT = lambda n, sh, dt=F32, kind="ExternalInput": nc.dram_tensor(n, sh, dt, kind=kind).ap()
    x_in = DT("x_in", [NTOK, D])
    y_out = DT("y_out", [NTOK, D], F32, "ExternalOutput")
    xb1 = DT("xb1", [NTOK, D], F32, "Internal")
    xb2 = DT("xb2", [NTOK, D], F32, "Internal")
    flag = DT("flag", [128, 1])
    logflag = DT("logflag", [128, 1])
    a_wqkbd = DT("a_wqkbd", [4, 128, KC * 64])
    a_wv = DT("a_wv", [128, KC * 128])
    a_wo = DT("a_wo", [4, 1, 128, 512])
    a_bbd = DT("a_bbd", [64, 2])
    a_ccin = DT("a_ccin", [68, 128], F32, "Internal")
    a_ccout = DT("a_ccout", [136, 128], F32, "Internal")
    f_wg = DT("f_wg", [HC, 128, D])
    f_wu = DT("f_wu", [HC, 128, D])
    f_wd = DT("f_wd", [4, HC, 128, 512])
    e_wv = DT("e_wv", [KC, 128, D])
    e_wg = DT("e_wg", [KC, 128, D])
    e_wq = DT("e_wq", [KC, 128, D])
    e_wo = DT("e_wo", [4, KC, 128, 512])
    e_bgq = DT("e_bgq", [128, 2 * KC])
    e_dv = DT("e_dv", [128, NTOK])
    e_ccin = DT("e_ccin", [132, KC], F32, "Internal")
    e_ccout = DT("e_ccout", [264, KC], F32, "Internal")
    cc_kin = DT("cc_kin", [D + 16, NTOK], BF16, "Internal")
    cc_kout = DT("cc_kout", [2 * D + 32, NTOK], BF16, "Internal")
    cc_vin = DT("cc_vin", [NTOK + 16, D], BF16, "Internal")
    cc_vout = DT("cc_vout", [2 * NTOK + 32, D], BF16, "Internal")
    ma_wq = DT("ma_wq", [16, 128, D]); ma_wk = DT("ma_wk", [16, 128, D])
    ma_wv = DT("ma_wv", [16, 128, D]); ma_wde = DT("ma_wde", [4, 16, 128, 512])
    ma_bqk = DT("ma_bqk", [128, 32]); ma_bv = DT("ma_bv", [128, 16])
    ma_bde = DT("ma_bde", [1, D])
    cos_in = DT("cos_in", [32, NTOK]); sin_in = DT("sin_in", [32, NTOK])
    rmat_in = DT("rmat_in", [32, 32])
    l_g = DT("l_g", [128, D]); l_b = DT("l_b", [128, D])
    l_bp = DT("l_bp", [128, 512]); l_sp = DT("l_sp", [128, 512])
    l_osc = DT("l_osc", [64, 1])
    l_ccin = DT("l_ccin", [96, 2], F32, "Internal")
    l_ccout = DT("l_ccout", [96, 2], F32, "Internal")
    l_ccin2 = DT("l_ccin2", [96, 2], F32, "Internal")
    l_ccout2 = DT("l_ccout2", [96, 2], F32, "Internal")
    mb_wq = DT("mb_wq", [16, 128, D]); mb_wk = DT("mb_wk", [16, 128, D])
    mb_wv = DT("mb_wv", [16, 128, D]); mb_wde = DT("mb_wde", [4, 16, 128, 512])
    mb_bout = DT("mb_bout", [1, D])

    def sub_k(k, p, E, xs, ys):
        if k == 0:
            emit_deltanet(p, E, xs, ys, a_wqkbd, a_wv, a_wo, a_bbd, flag,
                          a_ccin, a_ccout, "A")
            return "Ayw"
        if k == 1:
            emit_ffn(p, E, xs, ys, f_wg, f_wu, f_wd, "B")
            return "Byw"
        if k == 2:
            emit_ema(p, E, xs, ys, e_wv, e_wg, e_wq, e_wo, e_bgq, e_dv, flag,
                     e_ccin, e_ccout, "C")
            return "Cyw"
        if k == 3:
            emit_ffn(p, E, xs, ys, f_wg, f_wu, f_wd, "E")
            return "Eyw"
        if k == 4:
            emit_mha(p, E, xs, ys, ma_wq, ma_wk, ma_wv, ma_wde, "F",
                     cc_kin, cc_kout, cc_vin, cc_vout, logflag, rope=True,
                     cos_in=cos_in, sin_in=sin_in, rmat_in=rmat_in,
                     bqk=ma_bqk, bv=ma_bv, bdense=ma_bde)
            return "Fyw"
        if k == 5:
            emit_l1ffn(p, E, xs, ys, l_g, l_b, l_bp, l_sp, l_osc,
                       l_ccin, l_ccout, "G")
            return "Gyfw"
        if k == 6:
            emit_mha(p, E, xs, ys, mb_wq, mb_wk, mb_wv, mb_wde, "H",
                     cc_kin, cc_kout, cc_vin, cc_vout, logflag, rope=False,
                     bout=mb_bout, dbg=dbg_tensors)
            return "Hyw"
        emit_l1ffn(p, E, xs, ys, l_g, l_b, l_bp, l_sp, l_osc,
                   l_ccin2, l_ccout2, "I")
        return "Iyfw"

    dbg_tensors = None
    if dbg_h:
        dbg_tensors = (DT("dbgo", [128, RT * D], F32, "ExternalOutput"),
                       DT("dbgq", [128, 16 * NTOK], F32, "ExternalOutput"),
                       DT("dbgk", [128, 16 * NTOK], F32, "ExternalOutput"),
                       DT("dbgv", [128, RT * D], F32, "ExternalOutput"),
                       DT("dbgkp", [128, 16 * NTOK], F32, "ExternalOutput"),
                       DT("dbgvp", [128, RT * D], F32, "ExternalOutput"))
    with ExitStack() as ctx:
        p = Prog(nc, ctx.enter_context)
        cur = x_in
        pong = [xb1, xb2]
        for k in range(n_sub):
            ys = y_out if k == n_sub - 1 else pong[k % 2]
            with ExitStack() as s:
                term = sub_k(k, p, s.enter_context, cur, ys)
                p.seal(term)
            cur = ys
        if dump_x:
            dx1 = nc.dram_tensor("dump_x1", [NTOK, D], F32, kind="ExternalOutput").ap()
            dx2 = nc.dram_tensor("dump_x2", [NTOK, D], F32, kind="ExternalOutput").ap()
            p.dma("gpsimd", dx1, xb1, "dmpx")
            p.dma("gpsimd", dx2, xb2, "dmpx")
            p.wait("gpsimd", "dmpx", p.val("dmpx"))
        p.realize()
    return nc


def kernel_fused(**inp):
    x = np.ascontiguousarray(np.asarray(inp["L0_attn_a_x"], np.float32))
    shards = [np.ascontiguousarray(x[c // 2, (c % 2) * NTOK:(c % 2 + 1) * NTOK])
              for c in range(8)]
    dn_m = deltanet_maps(inp["L0_attn_a_Wq"], inp["L0_attn_a_Wk"], inp["L0_attn_a_Wv"],
                         inp["L0_attn_a_Wo"], inp["L0_attn_a_Wb"], inp["L0_attn_a_bb"],
                         inp["L0_attn_a_Wd"], inp["L0_attn_a_bd"])
    ffn_m = ffn_weight_maps(inp["L0_ffn_w_gate"], inp["L0_ffn_w_up"],
                            inp["L0_ffn_w_down"])
    ema_m = ema_maps(inp["L0_attn_b_Wv"], inp["L0_attn_b_Wg"], inp["L0_attn_b_bg"],
                     inp["L0_attn_b_Wq"], inp["L0_attn_b_bq"], inp["L0_attn_b_Wo"])
    mha_a_m, cs_fn = mha_a_maps(inp["L1_attn_a_W_qkv"], inp["L1_attn_a_b_qkv"],
                                inp["L1_attn_a_W_dense"], inp["L1_attn_a_b_dense"])
    mha_b_m = mha_b_maps(inp["L1_attn_b_w_qkv"], inp["L1_attn_b_w_out"],
                         inp["L1_attn_b_b_out"])
    l1_m = l1ffn_maps(inp["L1_ffn_ln_gamma"], inp["L1_ffn_ln_beta"],
                      inp["L1_ffn_spline_bases"], inp["L1_ffn_spline_slopes"],
                      inp["L1_ffn_output_scale"])
    cosl = [cs_fn(0), cs_fn(1)]
    nc = _get("fused", build_fused)
    maps = []
    for c in range(8):
        h = c % 2
        cos, sin = cosl[h]
        maps.append({
            "x_in": shards[c],
            "flag": np.full((128, 1), float(h), np.float32),
            "logflag": logflag_arr(h),
            "a_wqkbd": dn_m["wqkbd"], "a_wv": dn_m["wv"], "a_wo": dn_m["wo"],
            "a_bbd": dn_m["bbd"],
            "f_wg": ffn_m["wg"], "f_wu": ffn_m["wu"], "f_wd": ffn_m["wd"],
            "e_wv": ema_m["wv"], "e_wg": ema_m["wg"], "e_wq": ema_m["wq"],
            "e_wo": ema_m["wo"], "e_bgq": ema_m["bgq"], "e_dv": ema_m["dv128"],
            "ma_wq": mha_a_m["wq"], "ma_wk": mha_a_m["wk"], "ma_wv": mha_a_m["wv"],
            "ma_wde": mha_a_m["wde"], "ma_bqk": mha_a_m["bqk"],
            "ma_bv": mha_a_m["bv"], "ma_bde": mha_a_m["bdense"],
            "cos_in": cos, "sin_in": sin, "rmat_in": mha_a_m["rmat_in"],
            "l_g": l1_m["gamma128"], "l_b": l1_m["beta128"],
            "l_bp": l1_m["bases_pad"], "l_sp": l1_m["slopes_pad"],
            "l_osc": l1_m["oscale"],
            "mb_wq": mha_b_m["wq"], "mb_wk": mha_b_m["wk"], "mb_wv": mha_b_m["wv"],
            "mb_wde": mha_b_m["wde"], "mb_bout": mha_b_m["bout"],
        })
    res = run_bass_kernel_spmd(nc, maps, core_ids=list(range(8)))
    out = np.zeros((B, T, D), np.float32)
    for c in range(8):
        out[c // 2, (c % 2) * NTOK:(c % 2 + 1) * NTOK] = res.results[c]["y_out"]
    return out


def kernel(**inp):
    import os
    if os.environ.get("K_MULTILAUNCH"):
        return kernel_multilaunch(**inp)
    return kernel_fused(**inp)


# version 05
# speedup vs baseline: 1.0579x; 1.0579x over previous
import numpy as np
import ml_dtypes
from contextlib import ExitStack

import concourse.bass as bass
import concourse.mybir as mybir
from concourse.bass_utils import run_bass_kernel_spmd
from concourse.bass import IndirectOffsetOnAxis

F32 = mybir.dt.float32
BF16 = mybir.dt.bfloat16
I32 = mybir.dt.int32
U8 = mybir.dt.uint8
AF = mybir.ActivationFunctionType
ALU = mybir.AluOpType
AX = mybir.AxisListType

B, T, D = 4, 2048, 2048
HALF = T // 2
NTOK = HALF
RT = NTOK // 128
KC = D // 128
HD = 8192
HC = HD // 128
EPS = 1e-5
BIGNEG = -1e30
NONCE_BASE = 4096.0

PAIRS = [[0, 1], [2, 3], [4, 5], [6, 7]]
WORLD = [[0, 1, 2, 3, 4, 5, 6, 7]]

ENGINES = ("tensor", "vector", "scalar", "gpsimd", "sync")


class Prog:

    def __init__(self, nc, E, nphys=88):
        self.nc = nc
        self.E = E
        self.ops = {e: [] for e in ENGINES}
        self.phys = [E(nc.semaphore(f"ps{i}")) for i in range(nphys)]
        self.phys_cnt = [0] * nphys
        self.map = {}
        self.base = {}
        self.uid = 0
        self.free = list(range(nphys))
        self.guard_sems = [E(nc.semaphore(f"gsp{i}")) for i in range(9)]
        self.guard_i = 0

    def sem(self, name):
        if name not in self.map:
            if not self.free:
                raise RuntimeError("semaphore pool exhausted; call end_scope")
            idx = self.free.pop(0)
            self.map[name] = idx
            self.base[name] = self.phys_cnt[idx]
        return self.phys[self.map[name]]

    def val(self, name):
        self.sem(name)
        return self.phys_cnt[self.map[name]] - self.base[name]

    def _abs(self, name, v):
        self.sem(name)
        return self.base[name] + v

    def end_scope(self):
        for name in list(self.map):
            self.free.append(self.map.pop(name))
            del self.base[name]
        self.free.sort()

    def q(self, eng, fn, inc=None, inc_by=1, waits=(), incs=None):
        waits = [w for w in waits if w is not None]
        sem_objs = [(self.sem(n), self._abs(n, v)) for n, v in waits]
        all_incs = []
        if inc is not None:
            all_incs.append((inc, inc_by))
        if incs:
            all_incs.extend(incs)
        ret = None
        if all_incs:
            inc_objs = []
            for nm, by in all_incs:
                self.sem(nm)
                self.phys_cnt[self.map[nm]] += by
                inc_objs.append((self.phys[self.map[nm]], by))
            ret = self.val(all_incs[0][0])

            def wrapped(e, fn=fn, sem_objs=sem_objs, inc_objs=inc_objs):
                for s, v in sem_objs:
                    e.wait_ge(s, v)
                inst = fn(e)
                for s, by in inc_objs:
                    inst = inst.then_inc(s, by)
        else:
            def wrapped(e, fn=fn, sem_objs=sem_objs):
                for s, v in sem_objs:
                    e.wait_ge(s, v)
                fn(e)
        self.ops[eng].append(wrapped)
        return ret

    def wait(self, eng, name, value):
        s = self.sem(name)
        av = self._abs(name, value)
        self.ops[eng].append(lambda e, s=s, v=av: e.wait_ge(s, v))

    def dma(self, eng, out, in_, ch, waits=()):
        return self.q(eng, lambda e, o=out, i=in_: e.dma_start(o, i),
                      inc=ch, inc_by=16, waits=waits)

    def seal(self, chname):
        v = self.val(chname)
        for e in ENGINES:
            self.wait(e, chname, v)
        self.end_scope()

    def realize(self):
        block = self.E(self.nc.Block())
        for name in ENGINES:
            oplist = self.ops[name]
            getattr(block, name)(lambda eng, oplist=oplist: [f(eng) for f in oplist])



def emit_ident(p, ident_bf, tag):
    p.q("gpsimd", lambda g, ib=ident_bf: g.memset(ib[:, :], 0.0), inc=tag)
    p.q("gpsimd", lambda g, ib=ident_bf: g.affine_select(
        ib[:, :], ib[:, :], compare_op=ALU.not_equal, fill=1.0, base=0,
        pattern=[[-1, ib.shape[1]]], channel_multiplier=1), inc=tag)
    tgt = p.val(tag)
    p.wait("tensor", tag, tgt)




def emit_cc_nonce(p, E, tag, nonce_dram_ap, nonce):
    nc = p.nc
    dt = nonce_dram_ap.dtype
    t = E(nc.sbuf_tensor(tag + "nnc", [1, 64], dt))
    p.q("gpsimd", lambda g: g.memset(t[:, :], float(nonce)), inc=tag + "nncg")
    return p.dma("sync", nonce_dram_ap, t[:, :], tag + "nncd",
                 waits=[(tag + "nncg", p.val(tag + "nncg"))])


def emit_cc_guard(p, E, tag, probe_src_ap, nonce, wait_ch, wait_v, npts=1):
    nc = p.nc
    probe = E(nc.sbuf_tensor(tag + "prb", [1, npts * 4], I32))
    bits = int(nonce)
    gs = p.guard_sems[p.guard_i]
    p.guard_i += 1

    def spin(g):
        with g.register(tag + "rv") as rv, g.register(tag + "rc") as rc, \
             g.register(tag + "rd") as rd, g.register(tag + "rt") as rt:
            g.reg_mov(rc, 0)
            g.reg_mov(rd, 1)
            with g.While(rd):
                g.reg_add(rc, rc, 16)
                g.dma_start(probe[:, :], probe_src_ap).then_inc(gs, 16)
                g.wait_ge(gs, rc)
                g.reg_mov(rd, 0)
                for c in range(npts):
                    g.reg_load(rv, probe[0:1, c * 4 + 3:c * 4 + 4])
                    g.reg_alu(rt, rv, bits, ALU.not_equal)
                    g.reg_alu(rd, rd, rt, ALU.bitwise_or)
        return g.nop()
    p.q("gpsimd", spin, inc=tag + "ccok", waits=[(wait_ch, wait_v)])
    return p.val(tag + "ccok")



class LNBufs:
    def __init__(self, E, nc, tag, with_psum=True):
        self.x_rt = [E(nc.sbuf_tensor(f"{tag}x0", [128, D], F32))]
        self.scr = E(nc.sbuf_tensor(f"{tag}scr", [128, D], F32))
        self.xn_bf = E(nc.sbuf_tensor(f"{tag}xnbf", [128, D], BF16))
        self.cols = E(nc.sbuf_tensor(f"{tag}cols", [128, 8], F32))
        if with_psum:
            self.ps_tr = [E(nc.psum_tensor(f"{tag}pstr{i}", [128, 512], BF16))
                          for i in range(2)]


def emit_ln(p, lb, ident_bf, x_dram, xnT, tag, xn_tok_f32=None, xn_tok_bf=None,
            keep_x_sb=None):
    c_sum = lb.cols[:, 0:1]; c_sq = lb.cols[:, 1:2]; c_mean = lb.cols[:, 2:3]
    c_var = lb.cols[:, 3:4]; c_srt = lb.cols[:, 4:5]; c_msq = lb.cols[:, 5:6]
    c_rstd = lb.cols[:, 6:7]; c_eps = lb.cols[:, 7:8]
    p.q("gpsimd", lambda g, c=c_eps: g.memset(c, EPS), inc=f"{tag}epsg")
    p.wait("scalar", f"{tag}epsg", p.val(f"{tag}epsg"))
    for r in range(RT):
        xa = lb.x_rt[0]
        xw = [(f"{tag}xfree", p.val(f"{tag}xfree"))] if r >= 1 else []
        dv = p.dma("sync", xa[:, :], x_dram[r * 128:(r + 1) * 128, :], f"{tag}xd",
                   waits=xw)
        sw = [(f"{tag}xd", dv)]
        if r >= 1:
            sw.append((f"{tag}sxn", p.val(f"{tag}sxn")))
        p.q("vector", lambda v, xa=xa, c=c_sum: v.reduce_sum(c, xa[:, :], axis=AX.X),
            inc=f"{tag}vsum", waits=sw)
        p.q("scalar", lambda s, xa=xa, sc=lb.scr:
            s.activation(sc[:, :], xa[:, :], AF.Square),
            inc=f"{tag}ssq", waits=sw)
        p.q("vector", lambda v, sc=lb.scr, c=c_sq: v.reduce_sum(c, sc[:, :], axis=AX.X),
            inc=f"{tag}ssq", waits=[(f"{tag}ssq", p.val(f"{tag}ssq"))])

        p.q("vector", lambda v, c=c_sum, mean=c_mean: v.tensor_scalar_mul(mean, c, 1.0 / D),
            inc=f"{tag}vm", waits=[(f"{tag}vsum", p.val(f"{tag}vsum"))])
        p.q("scalar", lambda s, mean=c_mean, msq=c_msq: s.activation(msq, mean, AF.Square),
            inc=f"{tag}sms", waits=[(f"{tag}vm", p.val(f"{tag}vm"))])
        p.q("vector", lambda v, c_sq=c_sq, msq=c_msq, var=c_var:
            v.scalar_tensor_tensor(var, c_sq, 1.0 / D, msq, ALU.mult, ALU.subtract),
            inc=f"{tag}vv",
            waits=[(f"{tag}sms", p.val(f"{tag}sms")), (f"{tag}ssq", p.val(f"{tag}ssq"))])
        p.q("scalar", lambda s, var=c_var, srt=c_srt, ce=c_eps:
            s.activation(srt, var, AF.Sqrt, bias=ce),
            inc=f"{tag}srt", waits=[(f"{tag}vv", p.val(f"{tag}vv"))])
        p.q("vector", lambda v, srt=c_srt, rstd=c_rstd: v.reciprocal(rstd, srt),
            inc=f"{tag}vfs", waits=[(f"{tag}srt", p.val(f"{tag}srt"))])
        vfs = p.val(f"{tag}vfs")
        xnw = [(f"{tag}vfs", vfs)]
        if xnT is not None and r >= 1:
            xnw.append((f"{tag}ttr", p.val(f"{tag}ttr")))
        p.q("vector", lambda v, xa=xa, xb=lb.xn_bf, rstd=c_rstd, mn=c_mean:
            v.tensor_scalar(xb[:, :], xa[:, :], mn, rstd, ALU.subtract, ALU.mult),
            inc=f"{tag}sxn", waits=xnw)
        if xn_tok_f32 is not None:
            p.q("vector", lambda v, xa=xa, o=xn_tok_f32, r=r, rstd=c_rstd, mn=c_mean:
                v.tensor_scalar(o[:, r * D:(r + 1) * D], xa[:, :], mn, rstd,
                                ALU.subtract, ALU.mult), inc=f"{tag}sxn")
        if xn_tok_bf is not None:
            p.q("vector", lambda v, xa=xa, o=xn_tok_bf, r=r, rstd=c_rstd, mn=c_mean:
                v.tensor_scalar(o[:, r * D:(r + 1) * D], xa[:, :], mn, rstd,
                                ALU.subtract, ALU.mult), inc=f"{tag}sxn")
        if keep_x_sb is not None:
            p.q("vector", lambda v, xa=xa, o=keep_x_sb, r=r:
                v.tensor_copy(o[:, r * D:(r + 1) * D], xa[:, :]), inc=f"{tag}sxn")
        p.q("scalar", lambda s: s.nop(), inc=f"{tag}xfree",
            waits=[(f"{tag}sxn", p.val(f"{tag}sxn"))])
        if xnT is not None:
            for g4 in range(4):
                G = r * 4 + g4
                pst = lb.ps_tr[G % 2]
                for j in range(4):
                    kc = g4 * 4 + j
                    tw = []
                    if j == 0:
                        tw.append((f"{tag}sxn", p.val(f"{tag}sxn")))
                        if G >= 2:
                            tw.append((f"{tag}trev", G - 1))
                    p.q("tensor", lambda t, ps=pst, j=j, kc=kc, xb=lb.xn_bf, ib=ident_bf:
                        t.transpose(ps[:, j * 128:(j + 1) * 128],
                                    xb[:, kc * 128:(kc + 1) * 128], ib[:, :]),
                        inc=f"{tag}ttr" if j == 3 else None, waits=tw)

                def evac(v, ps=pst, g4=g4, r=r, xnT=xnT):
                    for j in range(4):
                        kc = g4 * 4 + j
                        inst = v.tensor_copy(
                            xnT[:, kc * NTOK + r * 128: kc * NTOK + (r + 1) * 128],
                            ps[:, j * 128:(j + 1) * 128])
                    return inst
                p.q("vector", evac, inc=f"{tag}trev",
                    waits=[(f"{tag}ttr", p.val(f"{tag}ttr"))])
    if xnT is not None:
        p.wait("tensor", f"{tag}trev", p.val(f"{tag}trev"))
    p.wait("tensor", f"{tag}sxn", p.val(f"{tag}sxn"))



def emit_ffn(p, E, x_dram, y_dram, wg_sl, wu_sl, wd_sl, tag):
    nc = p.nc
    ident_bf = E(nc.sbuf_tensor(tag + "id", [128, 128], BF16))
    emit_ident(p, ident_bf, tag + "mi")

    xnT = E(nc.sbuf_tensor(tag + "xnT", [128, KC * NTOK], BF16))
    h_buf = E(nc.sbuf_tensor(tag + "h", [128, HC * NTOK], BF16))
    wbuf = [E(nc.sbuf_tensor(f"{tag}w{i}", [128, D], BF16)) for i in range(4)]
    sil = [E(nc.sbuf_tensor(f"{tag}sil{i}", [128, 512], BF16)) for i in range(2)]
    y_sb = [E(nc.sbuf_tensor(f"{tag}y{i}", [128, 512], F32)) for i in range(2)]
    x2_sb = [E(nc.sbuf_tensor(f"{tag}x2{i}", [128, 512], F32)) for i in range(2)]

    with ExitStack() as ctxA:
        EA = ctxA.enter_context
        ln = LNBufs(EA, nc, tag + "L")
        ps_g = [EA(nc.psum_tensor(f"{tag}psg{i}", [128, 512], F32)) for i in range(2)]
        ps_u = [EA(nc.psum_tensor(f"{tag}psu{i}", [128, 512], F32)) for i in range(2)]

        emit_ln(p, ln, ident_bf, x_dram, xnT, tag + "n")

        NT = NTOK // 512
        for hc in range(HC):
            sg = wbuf[(hc % 2) * 2]
            su = wbuf[(hc % 2) * 2 + 1]
            war = [(tag + "mmu", NT * (hc - 1))] if hc >= 2 else []
            p.dma("gpsimd", sg[:, :], wg_sl[hc], tag + "wd", waits=war)
            du = p.dma("gpsimd", su[:, :], wu_sl[hc], tag + "wd")
            for nt in range(NT):
                G = hc * NT + nt
                w0 = []
                if nt == 0:
                    w0.append((tag + "wd", du))
                if hc >= 1:
                    w0 += [(tag + "sil", G - 1), (tag + "hready", G - 1)]
                for kcc in range(KC):
                    p.q("tensor", lambda t, pg=ps_g[nt], w=sg, kcc=kcc, nt=nt:
                        t.matmul(pg[:, :], w[:, kcc * 128:(kcc + 1) * 128],
                                 xnT[:, kcc * NTOK + nt * 512: kcc * NTOK + (nt + 1) * 512],
                                 start=(kcc == 0), stop=(kcc == KC - 1)),
                        inc=(tag + "mmg") if kcc == KC - 1 else None,
                        waits=w0 if kcc == 0 else ())
                for kcc in range(KC):
                    p.q("tensor", lambda t, pu=ps_u[nt], w=su, kcc=kcc, nt=nt:
                        t.matmul(pu[:, :], w[:, kcc * 128:(kcc + 1) * 128],
                                 xnT[:, kcc * NTOK + nt * 512: kcc * NTOK + (nt + 1) * 512],
                                 start=(kcc == 0), stop=(kcc == KC - 1)),
                        inc=(tag + "mmu") if kcc == KC - 1 else None)
                mg = p.val(tag + "mmg")
                mu = p.val(tag + "mmu")
                silw = [(tag + "mmg", mg)]
                if hc >= 1:
                    silw.append((tag + "hready", G - 1))
                p.q("scalar", lambda s, pg=ps_g[nt], si=sil[nt]:
                    s.activation(si[:, :], pg[:, :], AF.Silu),
                    inc=tag + "sil", waits=silw)
                p.q("vector", lambda v, pu=ps_u[nt], si=sil[nt], hc=hc, nt=nt:
                    v.tensor_tensor(h_buf[:, hc * NTOK + nt * 512: hc * NTOK + (nt + 1) * 512],
                                    si[:, :], pu[:, :], ALU.mult),
                    inc=tag + "hready",
                    waits=[(tag + "sil", p.val(tag + "sil")), (tag + "mmu", mu)])
            p.q("tensor", lambda t: t.nop(), inc=tag + "wused")
        hall = p.val(tag + "hready")
        p.wait("tensor", tag + "hready", hall)
        p.wait("gpsimd", tag + "hready", hall)
    emit_out_proj(p, E, h_buf, wd_sl, x_dram, y_dram, HC, wbuf, x2_sb, y_sb, tag)
    return {"xnT": xnT, "h": h_buf}




def emit_out_proj(p, E, act_buf, w_sl, x_dram, y_dram, MC, wbuf, x2_sb, y_sb,
                  tag, bias128=None, act_ix=None):
    if act_ix is None:
        act_ix = lambda hk, r: hk * NTOK + r * 128
    nc = p.nc
    ps_d = [E(nc.psum_tensor(f"{tag}psd{i}", [128, 512], F32)) for i in range(8)]
    junk = E(nc.sbuf_tensor(tag + "jnk", [128, 4], F32))
    nw = min(2, len(wbuf))
    for nd in range(4):
        for hk in range(MC):
            J = nd * MC + hk
            slot = wbuf[J % nw]
            war = [(tag + "dwu", J - 1)] if J >= 2 else []
            dw = p.dma("gpsimd", slot[:, 0:512], w_sl[nd, hk], tag + "dwd", waits=war)
            for r in range(RT):
                waits = [(tag + "dwd", dw)] if r == 0 else []
                if hk == 0 and nd >= 1:
                    waits.append((tag + "dev", (nd - 1) * RT + r + 1))
                if r == RT - 1:
                    inc = tag + "dwu"
                elif hk == MC - 1:
                    inc = tag + "dmm"
                else:
                    inc = None
                aix = act_ix(hk, r)
                p.q("tensor", lambda t, pd=ps_d[r], slot=slot, aix=aix, hk=hk:
                    t.matmul(pd[:, :], act_buf[:, aix:aix + 128],
                             slot[:, 0:512], start=(hk == 0), stop=(hk == MC - 1)),
                    inc=inc, waits=waits)
        for r in range(RT):
            Ga = nd * RT + r
            xs = x2_sb[r % 2]
            xw = [(tag + "dev", Ga - 1)] if Ga >= 2 else []
            dx = p.dma("sync", xs[:, :],
                       x_dram[r * 128:(r + 1) * 128, nd * 512:(nd + 1) * 512],
                       tag + "xr", waits=xw)
            if bias128 is not None:
                p.q("vector", lambda v, xs=xs, nd=nd:
                    v.tensor_add(xs[:, :], xs[:, :],
                                 bias128[:, nd * 512:(nd + 1) * 512]),
                    inc=tag + "xb", waits=[(tag + "xr", dx)])
            ys = y_sb[r % 2]
            if r < RT - 1:
                stopw = (tag + "dmm", nd * (RT - 1) + r + 1)
            else:
                stopw = (tag + "dwu", (nd + 1) * MC)
            vw = [stopw, (tag + "xr", dx)]
            if bias128 is not None:
                vw.append((tag + "xb", p.val(tag + "xb")))
            if Ga >= 2:
                vw.append((tag + "yw", 16 * (Ga - 1)))
            p.q("vector", lambda v, pd=ps_d[r]:
                v.tensor_copy(junk[:, :], pd[:, 0:4]), waits=vw)
            p.q("vector", lambda v, ys=ys, xs=xs, pd=ps_d[r]:
                v.tensor_add(ys[:, :], xs[:, :], pd[:, :]),
                inc=tag + "dev", waits=vw)
            p.dma("sync", y_dram[r * 128:(r + 1) * 128, nd * 512:(nd + 1) * 512],
                  ys[:, :], tag + "yw", waits=[(tag + "dev", p.val(tag + "dev"))])
    p.wait("sync", tag + "yw", p.val(tag + "yw"))
    p.wait("tensor", tag + "dev", p.val(tag + "dev"))



def emit_ema(p, E, x_dram, y_dram, wv_sl, wg_sl, wq_sl, wo_sl, bgq, dv128, flag,
             cc_in, cc_out, tag):
    nc = p.nc
    SC = float(np.sqrt(1.0 - 0.95))
    ident_bf = E(nc.sbuf_tensor(tag + "id", [128, 128], BF16))
    emit_ident(p, ident_bf, tag + "mi")
    xnT = E(nc.sbuf_tensor(tag + "xnT", [128, KC * NTOK], BF16))
    ylb = E(nc.sbuf_tensor(tag + "ylb", [128, KC * NTOK], BF16))
    wbuf = [E(nc.sbuf_tensor(f"{tag}w{i}", [128, D], BF16)) for i in range(4)]
    s_sb = E(nc.sbuf_tensor(tag + "s", [128, NTOK], F32))
    u1 = E(nc.sbuf_tensor(tag + "u1", [128, NTOK], F32))
    t1 = E(nc.sbuf_tensor(tag + "t1", [128, NTOK], F32))
    qs = E(nc.sbuf_tensor(tag + "qs", [128, NTOK], F32))
    gat = E(nc.sbuf_tensor(tag + "gat", [128, NTOK], F32))
    dec = E(nc.sbuf_tensor(tag + "dec", [128, NTOK], F32))
    dvs = E(nc.sbuf_tensor(tag + "dvs", [128, NTOK], F32))
    ylast = E(nc.sbuf_tensor(tag + "ylast", [128, KC], F32))
    carry = E(nc.sbuf_tensor(tag + "carry", [128, KC], F32))
    carry2 = E(nc.sbuf_tensor(tag + "carry2", [128, KC], F32))
    flg = E(nc.sbuf_tensor(tag + "flg", [128, 1], F32))
    bgq_sb = E(nc.sbuf_tensor(tag + "bgq", [128, 2 * KC], F32))
    y_sb = [E(nc.sbuf_tensor(f"{tag}y{i}", [128, 512], F32)) for i in range(2)]
    x2_sb = [E(nc.sbuf_tensor(f"{tag}x2{i}", [128, 512], F32)) for i in range(2)]

    p.dma("sync", flg[:, :], flag, tag + "cd")
    p.dma("sync", bgq_sb[:, :], bgq, tag + "cd")
    cd = p.dma("sync", dvs[:, :], dv128, tag + "cd")
    p.q("gpsimd", lambda g: g.memset(dec[:, :], 0.95), inc=tag + "cg")
    p.wait("vector", tag + "cg", p.val(tag + "cg"))
    p.wait("vector", tag + "cd", cd)
    p.wait("scalar", tag + "cd", cd)

    NT = NTOK // 512
    with ExitStack() as ctxA:
        EA = ctxA.enter_context
        ln = LNBufs(EA, nc, tag + "L")
        ps_v = [EA(nc.psum_tensor(f"{tag}psv{i}", [128, 512], F32)) for i in range(2)]
        ps_g = [EA(nc.psum_tensor(f"{tag}psg{i}", [128, 512], F32)) for i in range(2)]

        emit_ln(p, ln, ident_bf, x_dram, xnT, tag + "n")

        for mc in range(KC):
            sv = wbuf[(mc % 2) * 2]
            sg = wbuf[(mc % 2) * 2 + 1]
            dv_ = p.dma("gpsimd", sv[:, :], wv_sl[mc], tag + "vwd",
                        waits=[(tag + "mmv", NT * (mc - 1))] if mc >= 2 else [])
            dg_ = p.dma("gpsimd", sg[:, :], wg_sl[mc], tag + "gwd",
                        waits=[(tag + "mmg", NT * (mc - 1))] if mc >= 2 else [])
            for nt in range(NT):
                G = mc * NT + nt
                wv0 = [(tag + "vwd", dv_)] if nt == 0 else []
                if mc >= 1:
                    wv0.append((tag + "sready", G - 1))
                for kcc in range(KC):
                    p.q("tensor", lambda t, pv=ps_v[nt], w=sv, kcc=kcc, nt=nt:
                        t.matmul(pv[:, :], w[:, kcc * 128:(kcc + 1) * 128],
                                 xnT[:, kcc * NTOK + nt * 512: kcc * NTOK + (nt + 1) * 512],
                                 start=(kcc == 0), stop=(kcc == KC - 1)),
                        inc=(tag + "mmv") if kcc == KC - 1 else None,
                        waits=wv0 if kcc == 0 else ())
                wg0 = [(tag + "gwd", dg_)] if nt == 0 else []
                if mc >= 1:
                    wg0.append((tag + "gact", G - 1))
                for kcc in range(KC):
                    p.q("tensor", lambda t, pg=ps_g[nt], w=sg, kcc=kcc, nt=nt:
                        t.matmul(pg[:, :], w[:, kcc * 128:(kcc + 1) * 128],
                                 xnT[:, kcc * NTOK + nt * 512: kcc * NTOK + (nt + 1) * 512],
                                 start=(kcc == 0), stop=(kcc == KC - 1)),
                        inc=(tag + "mmg") if kcc == KC - 1 else None,
                        waits=wg0 if kcc == 0 else ())
                mv = p.val(tag + "mmv")
                mg = p.val(tag + "mmg")
                gaw = [(tag + "mmg", mg)]
                if mc >= 1:
                    gaw.append((tag + "sready", G - 1))
                p.q("scalar", lambda s, pg=ps_g[nt], nt=nt, mc=mc:
                    s.activation(gat[:, nt * 512:(nt + 1) * 512], pg[:, :], AF.Sigmoid,
                                 bias=bgq_sb[:, mc:mc + 1]),
                    inc=tag + "gact", waits=gaw)
                p.q("vector", lambda v, pv=ps_v[nt], nt=nt:
                    v.scalar_tensor_tensor(s_sb[:, nt * 512:(nt + 1) * 512], pv[:, :],
                                           SC, gat[:, nt * 512:(nt + 1) * 512],
                                           ALU.mult, ALU.mult),
                    inc=tag + "sready",
                    waits=[(tag + "gact", p.val(tag + "gact")), (tag + "mmv", mv)])
            p.q("vector", lambda v, mc=mc:
                v.tensor_tensor_scan(ylb[:, mc * NTOK:(mc + 1) * NTOK], dec[:, :],
                                     s_sb[:, :], 0.0, ALU.mult, ALU.add),
                inc=tag + "scan", waits=[(tag + "sready", p.val(tag + "sready"))])
            p.q("vector", lambda v, mc=mc:
                v.tensor_copy(ylast[:, mc:mc + 1],
                              ylb[:, mc * NTOK + NTOK - 1: mc * NTOK + NTOK]),
                inc=tag + "ylst", waits=[(tag + "scan", p.val(tag + "scan"))])
        p.wait("tensor", tag + "sready", p.val(tag + "sready"))
    p.dma("sync", cc_in[0:128, :], ylast[:, :], tag + "agd",
          waits=[(tag + "ylst", p.val(tag + "ylst"))])
    nn = emit_cc_nonce(p, E, tag,
                       cc_in[128:132, :].rearrange("a b -> (a b)").rearrange(
                           "(o f) -> o f", o=1), NONCE_BASE + 64)
    agd = p.val(tag + "agd")
    p.q("gpsimd", lambda g: g.collective_compute(
        "AllGather", ALU.bypass, replica_groups=PAIRS, ins=[cc_in.opt()],
        outs=[cc_out.opt()]), inc=tag + "agc",
        waits=[(tag + "agd", agd), (tag + "nncd", nn)])
    gok = emit_cc_guard(p, E, tag,
                        cc_out[131:132, 12:16], NONCE_BASE + 64,
                        tag + "agc", p.val(tag + "agc"))
    agr = p.dma("sync", carry[:, :], cc_out[0:128, :], tag + "agr",
                waits=[(tag + "ccok", gok)])
    p.q("vector", lambda v: v.tensor_scalar_mul(carry2[:, :], carry[:, :], flg[:, 0:1]),
        inc=tag + "cready", waits=[(tag + "agr", agr)])
    with ExitStack() as ctxB:
        EB = ctxB.enter_context
        ps_q = [EB(nc.psum_tensor(f"{tag}psq{i}", [128, 512], F32)) for i in range(2)]
        for mc in range(KC):
            sq = wbuf[mc % 2]
            dq_ = p.dma("gpsimd", sq[:, :], wq_sl[mc], tag + "qwd",
                        waits=[(tag + "mmq", NT * (mc - 1))] if mc >= 2 else
                              [(tag + "mmv", NT * KC), (tag + "mmg", NT * KC)])
            for nt in range(NT):
                G = mc * NT + nt
                wq0 = [(tag + "qwd", dq_)] if nt == 0 else []
                if mc >= 1:
                    wq0.append((tag + "qact", G - 1))
                for kcc in range(KC):
                    p.q("tensor", lambda t, pq=ps_q[nt], w=sq, kcc=kcc, nt=nt:
                        t.matmul(pq[:, :], w[:, kcc * 128:(kcc + 1) * 128],
                                 xnT[:, kcc * NTOK + nt * 512: kcc * NTOK + (nt + 1) * 512],
                                 start=(kcc == 0), stop=(kcc == KC - 1)),
                        inc=(tag + "mmq") if kcc == KC - 1 else None,
                        waits=wq0 if kcc == 0 else ())
                qaw = [(tag + "mmq", p.val(tag + "mmq"))]
                if mc >= 1:
                    qaw.append((tag + "ldv", mc))
                p.q("scalar", lambda s, pq=ps_q[nt], nt=nt, mc=mc:
                    s.activation(qs[:, nt * 512:(nt + 1) * 512], pq[:, :], AF.Sigmoid,
                                 bias=bgq_sb[:, KC + mc:KC + mc + 1]),
                    inc=tag + "qact", waits=qaw)
            p.q("vector", lambda v, mc=mc:
                v.tensor_scalar_mul(t1[:, :], dvs[:, :], carry2[:, mc:mc + 1]),
                inc=tag + "t1v", waits=[(tag + "cready", p.val(tag + "cready"))])
            p.q("vector", lambda v, mc=mc:
                v.tensor_add(u1[:, :], ylb[:, mc * NTOK:(mc + 1) * NTOK], t1[:, :]),
                inc=tag + "u1v", waits=[(tag + "t1v", p.val(tag + "t1v"))])
            p.q("vector", lambda v, mc=mc:
                v.scalar_tensor_tensor(ylb[:, mc * NTOK:(mc + 1) * NTOK], qs[:, :],
                                       SC, u1[:, :], ALU.mult, ALU.mult),
                inc=tag + "ldv",
                waits=[(tag + "u1v", p.val(tag + "u1v")),
                       (tag + "qact", p.val(tag + "qact"))])
        p.wait("tensor", tag + "ldv", p.val(tag + "ldv"))
        p.wait("gpsimd", tag + "mmq", p.val(tag + "mmq"))
    emit_out_proj(p, E, ylb, wo_sl, x_dram, y_dram, KC, wbuf, x2_sb, y_sb, tag)


def build_ema_launch():
    nc = bass.Bass("TRN2", target_bir_lowering=False)
    x_in = nc.dram_tensor("x_in", [NTOK, D], F32, kind="ExternalInput").ap()
    wv = nc.dram_tensor("wv", [KC, 128, D], F32, kind="ExternalInput").ap()
    wg = nc.dram_tensor("wg", [KC, 128, D], F32, kind="ExternalInput").ap()
    wq = nc.dram_tensor("wq", [KC, 128, D], F32, kind="ExternalInput").ap()
    wo = nc.dram_tensor("wo", [4, KC, 128, 512], F32, kind="ExternalInput").ap()
    bgq = nc.dram_tensor("bgq", [128, 2 * KC], F32, kind="ExternalInput").ap()
    dv128 = nc.dram_tensor("dv128", [128, NTOK], F32, kind="ExternalInput").ap()
    flag = nc.dram_tensor("flag", [128, 1], F32, kind="ExternalInput").ap()
    y = nc.dram_tensor("y", [NTOK, D], F32, kind="ExternalOutput").ap()
    cc_in = nc.dram_tensor("cc_in", [132, KC], F32, kind="Internal").ap()
    cc_out = nc.dram_tensor("cc_out", [264, KC], F32, kind="Internal").ap()
    with ExitStack() as ctx:
        p = Prog(nc, ctx.enter_context)
        emit_ema(p, ctx.enter_context, x_in, y, wv, wg, wq, wo, bgq, dv128, flag,
                 cc_in, cc_out, "e")
        p.realize()
    return nc


def ema_maps(Wv, Wg, bg, Wq, bq, Wo):
    bgq = np.zeros((128, 2 * KC), np.float32)
    bgq[:, 0:KC] = np.asarray(bg).reshape(KC, 128).T
    bgq[:, KC:2 * KC] = np.asarray(bq).reshape(KC, 128).T
    dv = (0.95 ** (np.arange(NTOK, dtype=np.float64) + 1)).astype(np.float32)
    dv128 = np.broadcast_to(dv, (128, NTOK)).copy()
    return {
        "wv": lhsT_slabs(np.ascontiguousarray(np.asarray(Wv).T)),
        "wg": lhsT_slabs(np.ascontiguousarray(np.asarray(Wg).T)),
        "wq": lhsT_slabs(np.ascontiguousarray(np.asarray(Wq).T)),
        "wo": rhs_slabs(np.asarray(Wo)),
        "bgq": bgq, "dv128": dv128,
    }





def emit_deltanet(p, E, x_dram, y_dram, wqkbd_sl, wv_sl, wo_sl, bbd, flag,
                  cc_in, cc_out, tag):
    nc = p.nc
    C = 64
    NCH = NTOK // C
    ch = lambda n: tag + n

    ident_bf = E(nc.sbuf_tensor(tag + "id", [128, 128], BF16))
    emit_ident(p, ident_bf, tag + "mi")
    ident_f = E(nc.sbuf_tensor(tag + "idf", [128, 128], F32))
    p.q("gpsimd", lambda g: g.memset(ident_f[:, :], 0.0), inc=ch("cg"))
    p.q("gpsimd", lambda g: g.affine_select(
        ident_f[:, :], ident_f[:, :], compare_op=ALU.not_equal, fill=1.0, base=0,
        pattern=[[-1, 128]], channel_multiplier=1), inc=ch("cg"))
    triU = E(nc.sbuf_tensor(tag + "triU", [64, 64], F32))
    p.q("gpsimd", lambda g: g.memset(triU[:, :], 1.0), inc=ch("cg"))
    p.q("gpsimd", lambda g: g.affine_select(
        triU[:, :], triU[:, :], compare_op=ALU.is_ge, fill=0.0, base=0,
        pattern=[[1, 64]], channel_multiplier=-1), inc=ch("cg"))
    mask_u8 = E(nc.sbuf_tensor(tag + "msk", [64, 64], mybir.dt.int8))
    p.q("gpsimd", lambda g: g.memset(mask_u8[:, :], 1), inc=ch("cg"))
    p.q("gpsimd", lambda g: g.affine_select(
        mask_u8[:, :], mask_u8[:, :], compare_op=ALU.is_ge, fill=0, base=0,
        pattern=[[1, 64]], channel_multiplier=-1), inc=ch("cg"))
    zer_f = E(nc.sbuf_tensor(tag + "zer", [64, 64], F32))
    p.q("gpsimd", lambda g: g.memset(zer_f[:, :], 0.0), inc=ch("cg"))
    onecol = E(nc.sbuf_tensor(tag + "onec", [64, 1], F32))
    p.q("gpsimd", lambda g: g.memset(onecol[:, :], 1.0), inc=ch("cg"))
    cgv = p.val(ch("cg"))
    for e in ("tensor", "vector", "scalar"):
        p.wait(e, ch("cg"), cgv)

    xnT = E(nc.sbuf_tensor(tag + "xnT", [128, KC * NTOK], BF16))
    pw = E(nc.sbuf_tensor(tag + "pw", [128, 4 * KC * 64], BF16))
    pwv = E(nc.sbuf_tensor(tag + "pwv", [128, KC * 128], BF16))
    q_f = E(nc.sbuf_tensor(tag + "qf", [64, NTOK], F32))
    k_f = E(nc.sbuf_tensor(tag + "kf", [64, NTOK], F32))
    zbd = E(nc.sbuf_tensor(tag + "zbd", [64, 2 * NTOK], F32))
    escr = E(nc.sbuf_tensor(tag + "escr", [64, 2 * NTOK], F32))
    kd = E(nc.sbuf_tensor(tag + "kd", [64, NTOK], F32))
    v_bf = E(nc.sbuf_tensor(tag + "vbf", [128, NTOK], BF16))
    vT = E(nc.sbuf_tensor(tag + "vT", [64, NCH * 128], BF16))
    qt_buf = E(nc.sbuf_tensor(tag + "qt", [64, NTOK], BF16))
    kt_sb = E(nc.sbuf_tensor(tag + "kt", [64, 64], BF16))
    ktp_sb = E(nc.sbuf_tensor(tag + "ktp", [64, 64], BF16))
    ktpT_sb = E(nc.sbuf_tensor(tag + "ktpT", [64, 64], BF16))
    P_sb = E(nc.sbuf_tensor(tag + "P", [64, 64], BF16))
    lbT_sb = E(nc.sbuf_tensor(tag + "lbT", [64, 64], F32))
    LT_sb = E(nc.sbuf_tensor(tag + "LT", [64, 64], F32))
    L_sb = E(nc.sbuf_tensor(tag + "Lsb", [64, 64], F32))
    eL = E(nc.sbuf_tensor(tag + "eL", [64, 64], F32))
    enL = E(nc.sbuf_tensor(tag + "enL", [64, 64], F32))
    elc = E(nc.sbuf_tensor(tag + "elc", [64, 64], F32))
    Lc_cols = E(nc.sbuf_tensor(tag + "Lc", [64, NCH], F32))
    Gpre = E(nc.sbuf_tensor(tag + "Gp", [64, NCH + 1], F32))
    eGc = E(nc.sbuf_tensor(tag + "eGc", [64, 2], F32))
    S_sb = E(nc.sbuf_tensor(tag + "S", [64, 128], F32))
    S_bf = E(nc.sbuf_tensor(tag + "Sbf", [64, 128], BF16))
    Ssd = E(nc.sbuf_tensor(tag + "Ssd", [64, 128], F32))
    Scor = E(nc.sbuf_tensor(tag + "Scor", [64, 128], BF16))
    oT = E(nc.sbuf_tensor(tag + "oT", [128, NTOK], BF16))
    flg = E(nc.sbuf_tensor(tag + "flg", [128, 1], F32))
    bbd_sb = E(nc.sbuf_tensor(tag + "bbd", [64, 2], F32))
    y_sb = [E(nc.sbuf_tensor(f"{tag}y{i}", [128, 512], F32)) for i in range(2)]
    x2_sb = [E(nc.sbuf_tensor(f"{tag}x2{i}", [128, 512], F32)) for i in range(2)]
    wbuf = [E(nc.sbuf_tensor(f"{tag}w{i}", [128, 512], BF16)) for i in range(2)]

    p.dma("sync", flg[:, :], flag, ch("cd"))
    p.dma("sync", bbd_sb[:, :], bbd, ch("cd"))
    for i in range(4):
        p.dma("gpsimd", pw[:, i * KC * 64:(i + 1) * KC * 64], wqkbd_sl[i], ch("pwd"))
    p.dma("gpsimd", pwv[:, :], wv_sl, ch("pwd"))
    cdv = p.val(ch("cd"))
    p.wait("scalar", ch("cd"), cdv)
    p.wait("vector", ch("cd"), cdv)

    def step(eng, fn, extra=()):
        k = p.val(ch("sq"))
        p.q(eng, fn, inc=ch("sq"), waits=[(ch("sq"), k)] + list(extra))

    with ExitStack() as ctxLN:
        ln = LNBufs(ctxLN.enter_context, nc, tag + "L")
        emit_ln(p, ln, ident_bf, x_dram, xnT, tag + "n")

    with ExitStack() as ctxA:
        EA = ctxA.enter_context
        ps_p = [EA(nc.psum_tensor(f"{tag}psp{i}", [128, 512], F32)) for i in range(2)]
        ps_fA = EA(nc.psum_tensor(tag + "psfA", [64, 64], F32))
        ps_fB = EA(nc.psum_tensor(tag + "psfB", [64, 64], F32))
        ps_sc = EA(nc.psum_tensor(tag + "pssc", [64, 64], F32))
        ps_o = EA(nc.psum_tensor(tag + "pso", [128, 64], F32))
        ps_ds = EA(nc.psum_tensor(tag + "psds", [64, 128], F32))
        ps_bt = EA(nc.psum_tensor(tag + "psbt", [64, 128], BF16))

        p.wait("tensor", ch("pwd"), p.val(ch("pwd")))
        NT = NTOK // 512
        for gi in range(4):
            for nt in range(NT):
                G = gi * NT + nt
                pp = ps_p[G % 2]
                wts = [(ch("pev"), G - 1)] if G >= 2 else []
                for kcc in range(KC):
                    p.q("tensor", lambda t, pp=pp, gi=gi, kcc=kcc, nt=nt:
                        t.matmul(pp[0:64, :],
                                 pw[:, gi * KC * 64 + kcc * 64: gi * KC * 64 + (kcc + 1) * 64],
                                 xnT[:, kcc * NTOK + nt * 512: kcc * NTOK + (nt + 1) * 512],
                                 start=(kcc == 0), stop=(kcc == KC - 1)),
                        inc=ch("pmm") if kcc == KC - 1 else None,
                        waits=wts if kcc == 0 else ())
                pv = p.val(ch("pmm"))
                if gi == 0:
                    p.q("vector", lambda v, pp=pp, nt=nt:
                        v.tensor_copy(q_f[:, nt * 512:(nt + 1) * 512], pp[0:64, :]),
                        inc=ch("pev"), waits=[(ch("pmm"), pv)])
                elif gi == 1:
                    p.q("vector", lambda v, pp=pp, nt=nt:
                        v.tensor_copy(k_f[:, nt * 512:(nt + 1) * 512], pp[0:64, :]),
                        inc=ch("pev"), waits=[(ch("pmm"), pv)])
                else:
                    off = 0 if gi == 2 else NTOK
                    bcol = gi - 2
                    p.q("vector", lambda v, pp=pp, nt=nt, off=off, bcol=bcol:
                        v.tensor_scalar_add(zbd[:, off + nt * 512: off + (nt + 1) * 512],
                                            pp[0:64, :], bbd_sb[:, bcol:bcol + 1]),
                        inc=ch("pev"), waits=[(ch("pmm"), pv)])
        for nt in range(NT):
            G = 4 * NT + nt
            pp = ps_p[G % 2]
            for kcc in range(KC):
                p.q("tensor", lambda t, pp=pp, kcc=kcc, nt=nt:
                    t.matmul(pp[:, :], pwv[:, kcc * 128:(kcc + 1) * 128],
                             xnT[:, kcc * NTOK + nt * 512: kcc * NTOK + (nt + 1) * 512],
                             start=(kcc == 0), stop=(kcc == KC - 1)),
                    inc=ch("pmm") if kcc == KC - 1 else None,
                    waits=[(ch("pev"), G - 1)] if kcc == 0 else ())
            p.q("vector", lambda v, pp=pp, nt=nt:
                v.tensor_copy(v_bf[:, nt * 512:(nt + 1) * 512], pp[:, :]),
                inc=ch("pev"), waits=[(ch("pmm"), p.val(ch("pmm")))])
        pe_all = p.val(ch("pev"))

        step("scalar", lambda s: s.activation(escr[:, :], zbd[:, :], AF.Abs),
             extra=[(ch("pev"), pe_all)])
        step("scalar", lambda s: s.activation(escr[:, :], escr[:, :], AF.Exp, scale=-1.0))
        step("scalar", lambda s: s.activation(escr[:, :], escr[:, :], AF.Ln,
                                              bias=onecol[:, 0:1]))
        step("vector", lambda v: v.tensor_scalar_min(zbd[:, 0:NTOK], zbd[:, 0:NTOK], 0.0))
        step("vector", lambda v: v.tensor_scalar_max(zbd[:, NTOK:], zbd[:, NTOK:], 0.0))
        step("vector", lambda v: v.tensor_sub(zbd[:, 0:NTOK], zbd[:, 0:NTOK],
                                              escr[:, 0:NTOK]))
        step("vector", lambda v: v.tensor_add(zbd[:, NTOK:], zbd[:, NTOK:],
                                              escr[:, NTOK:]))
        step("vector", lambda v: v.tensor_mul(kd[:, :], k_f[:, :], zbd[:, NTOK:]))
        step("gpsimd", lambda g: g.memset(Gpre[:, :], 0.0))
        step("gpsimd", lambda g: g.memset(S_sb[:, :], 0.0))
        step("gpsimd", lambda g: g.memset(S_bf[:, :], 0.0))

        for c in range(NCH):
            t0 = c * C
            step("tensor", lambda t, t0=t0: t.transpose(
                ps_bt[:, :], v_bf[:, t0:t0 + C], ident_bf[:, :]))
            step("vector", lambda v, c=c: v.tensor_copy(
                vT[:, c * 128:(c + 1) * 128], ps_bt[0:64, :]))
            step("tensor", lambda t, t0=t0: t.transpose(
                ps_fA[:, :], zbd[:, t0:t0 + C], ident_f[0:64, 0:64]))
            step("vector", lambda v: v.tensor_copy(lbT_sb[:, :], ps_fA[:, :]))
            step("tensor", lambda t: t.matmul(ps_fB[:, :], triU[:, :], lbT_sb[:, :],
                                              start=True, stop=True))
            step("vector", lambda v: v.tensor_copy(LT_sb[:, :], ps_fB[:, :]))
            step("tensor", lambda t: t.transpose(ps_fA[:, :], LT_sb[:, :],
                                                 ident_f[0:64, 0:64]))
            step("vector", lambda v: v.tensor_copy(L_sb[:, :], ps_fA[:, :]))
            step("vector", lambda v, c=c: v.tensor_copy(Lc_cols[:, c:c + 1],
                                                        L_sb[:, C - 1:C]))
            step("vector", lambda v, c=c: v.tensor_add(Gpre[:, c + 1:c + 2],
                                                       Gpre[:, c:c + 1],
                                                       Lc_cols[:, c:c + 1]))
            step("scalar", lambda s: s.activation(eL[:, :], L_sb[:, :], AF.Exp))
            step("scalar", lambda s: s.activation(enL[:, :], L_sb[:, :], AF.Exp,
                                                  scale=-1.0))
            step("scalar", lambda s, c=c: s.activation(elc[:, :], L_sb[:, :], AF.Exp,
                                                       scale=-1.0,
                                                       bias=Lc_cols[:, c:c + 1]))
            step("scalar", lambda s, c=c: s.activation(eGc[:, 0:1],
                                                       Lc_cols[:, c:c + 1], AF.Exp))
            step("vector", lambda v, t0=t0: v.tensor_mul(qt_buf[:, t0:t0 + C],
                                                         q_f[:, t0:t0 + C], eL[:, :]))
            step("vector", lambda v, t0=t0: v.tensor_mul(kt_sb[:, :],
                                                         kd[:, t0:t0 + C], enL[:, :]))
            step("vector", lambda v, t0=t0: v.tensor_mul(ktp_sb[:, :],
                                                         kd[:, t0:t0 + C], elc[:, :]))
            step("tensor", lambda t, t0=t0: t.matmul(ps_sc[:, :], kt_sb[:, :],
                                                     qt_buf[:, t0:t0 + C],
                                                     start=True, stop=True))
            step("vector", lambda v: v.select(P_sb[:, :], mask_u8[:, :], ps_sc[:, :],
                                              zer_f[:, :]))
            step("tensor", lambda t, t0=t0: t.matmul(ps_o[:, 0:C], S_bf[:, :],
                                                     qt_buf[:, t0:t0 + C],
                                                     start=True, stop=False))
            step("tensor", lambda t, c=c: t.matmul(ps_o[:, 0:C],
                                                   vT[:, c * 128:(c + 1) * 128],
                                                   P_sb[:, :], start=False, stop=True))
            step("vector", lambda v, t0=t0: v.tensor_copy(oT[:, t0:t0 + C],
                                                          ps_o[:, 0:C]))
            step("tensor", lambda t: t.transpose(ps_bt[:, 0:64], ktp_sb[:, :],
                                                 ident_bf[0:64, 0:64]))
            step("vector", lambda v: v.tensor_copy(ktpT_sb[:, :], ps_bt[0:64, 0:64]))
            step("tensor", lambda t, c=c: t.matmul(ps_ds[:, :], ktpT_sb[:, :],
                                                   vT[:, c * 128:(c + 1) * 128],
                                                   start=True, stop=True))
            step("vector", lambda v: v.scalar_tensor_tensor(S_sb[:, :], S_sb[:, :],
                                                            eGc[:, 0:1], ps_ds[:, :],
                                                            ALU.mult, ALU.add))
            step("vector", lambda v: v.tensor_copy(S_bf[:, :], S_sb[:, :]))
        sq0 = p.val(ch("sq"))
        p.dma("sync", cc_in[0:64, :], S_sb[:, :], ch("agd"), waits=[(ch("sq"), sq0)])
        nn = emit_cc_nonce(p, EA, tag, cc_in[64:68, 64:128].rearrange(
            "a b -> (a b)").rearrange("(o f) -> o f", o=1) if False else None, NONCE_BASE + 0) if False else None
        ncd4 = EA(nc.sbuf_tensor(tag + "nc4", [4, 64], F32))
        p.q("gpsimd", lambda g: g.memset(ncd4[:, :], float(NONCE_BASE + 0)),
            inc=ch("nng"))
        nn = p.dma("sync", cc_in[64:68, 64:128], ncd4[:, :], tag + "nncd",
                   waits=[(ch("nng"), p.val(ch("nng")))])
        agd = p.val(ch("agd"))
        p.q("gpsimd", lambda g: g.collective_compute(
            "AllGather", ALU.bypass, replica_groups=PAIRS, ins=[cc_in.opt()],
            outs=[cc_out.opt()]), inc=ch("agc"),
            waits=[(ch("agd"), agd), (tag + "nncd", nn)])
        gok = emit_cc_guard(p, EA, tag,
                            cc_out[64:68, 124:128].rearrange("a (o b) -> a o b", o=1),
                            NONCE_BASE + 0, ch("agc"), p.val(ch("agc")), npts=4)
        agr = p.dma("sync", Ssd[:, :], cc_out[0:64, :], ch("agr"),
                    waits=[(tag + "ccok", gok)])
        step("vector", lambda v: v.tensor_scalar_mul(Ssd[:, :], Ssd[:, :],
                                                     flg[0:64, 0:1]),
             extra=[(ch("agr"), agr)])
        for c in range(NCH):
            t0 = c * C
            step("scalar", lambda s, c=c: s.activation(eGc[:, 1:2], Gpre[:, c:c + 1],
                                                       AF.Exp))
            step("vector", lambda v: v.tensor_scalar_mul(Scor[:, :], Ssd[:, :],
                                                         eGc[:, 1:2]))
            step("tensor", lambda t, t0=t0: t.matmul(ps_o[:, 0:C], Scor[:, :],
                                                     qt_buf[:, t0:t0 + C],
                                                     start=True, stop=True))
            step("vector", lambda v, t0=t0: v.tensor_add(oT[:, t0:t0 + C],
                                                         oT[:, t0:t0 + C],
                                                         ps_o[:, 0:C]))
        sqf = p.val(ch("sq"))
        p.wait("tensor", ch("sq"), sqf)
        p.wait("gpsimd", ch("sq"), sqf)
    emit_out_proj(p, E, oT, wo_sl, x_dram, y_dram, 1, wbuf, x2_sb, y_sb, tag)
    return {"q_f": q_f[:, :], "zbd": zbd[:, :], "qt": qt_buf[:, :], "oT": oT[:, :],
            "Lc": Lc_cols[:, :], "Sf": S_sb[:, :], "vT": vT[:, :], "kd": kd[:, :]}


def build_deltanet_launch(debug=False):
    nc = bass.Bass("TRN2", target_bir_lowering=False)
    x_in = nc.dram_tensor("x_in", [NTOK, D], F32, kind="ExternalInput").ap()
    wqkbd = nc.dram_tensor("wqkbd", [4, 128, KC * 64], F32, kind="ExternalInput").ap()
    wv = nc.dram_tensor("wv", [128, KC * 128], F32, kind="ExternalInput").ap()
    wo = nc.dram_tensor("wo", [4, 1, 128, 512], F32, kind="ExternalInput").ap()
    bbd = nc.dram_tensor("bbd", [64, 2], F32, kind="ExternalInput").ap()
    flag = nc.dram_tensor("flag", [128, 1], F32, kind="ExternalInput").ap()
    y = nc.dram_tensor("y", [NTOK, D], F32, kind="ExternalOutput").ap()
    cc_in = nc.dram_tensor("cc_in", [68, 128], F32, kind="Internal").ap()
    cc_out = nc.dram_tensor("cc_out", [136, 128], F32, kind="Internal").ap()
    dbg = {}
    if debug:
        for nm, sh in [("q_f", [64, NTOK]), ("zbd", [64, 2*NTOK]), ("qt", [64, NTOK]),
                       ("oT", [128, NTOK]), ("Lc", [64, NTOK // 64]),
                       ("Sf", [64, 128]), ("vT", [64, (NTOK//64) * 128]),
                       ("kd", [64, NTOK])]:
            dbg[nm] = nc.dram_tensor("dbg_" + nm, sh, F32, kind="ExternalOutput").ap()
    with ExitStack() as ctx:
        p = Prog(nc, ctx.enter_context)
        bufs = emit_deltanet(p, ctx.enter_context, x_in, y, wqkbd, wv, wo, bbd, flag,
                      cc_in, cc_out, "d")
        if debug:
            p.wait("gpsimd", "dyw", p.val("dyw"))
            for nm in dbg:
                p.dma("gpsimd", dbg[nm], bufs[nm], "dbgd")
            p.wait("gpsimd", "dbgd", p.val("dbgd"))
        p.realize()
    return nc


def lhsT_small(W):
    in_f, out_f = W.shape
    a = W.reshape(in_f // 128, 128, out_f).transpose(1, 0, 2)
    return np.ascontiguousarray(a.reshape(128, (in_f // 128) * out_f))


def deltanet_maps(Wq, Wk, Wv, Wo, Wb, bb, Wd, bd):
    wqkbd = np.stack([lhsT_small(np.asarray(w)) for w in (Wq, Wk, Wb, Wd)])
    bbd = np.stack([np.asarray(bb), np.asarray(bd)], axis=1).astype(np.float32)
    return {
        "wqkbd": wqkbd,
        "wv": lhsT_small(np.asarray(Wv)),
        "wo": rhs_slabs(np.asarray(Wo)).reshape(4, 1, 128, 512),
        "bbd": bbd,
    }





def emit_mha(p, E, x_dram, y_dram, wq_sl, wk_sl, wv_sl, wdense_sl, tag,
             cc_kin, cc_kout, cc_vin, cc_vout, logflag,
             rope=False, cos_in=None, sin_in=None, rmat_in=None,
             bqk=None, bv=None, bdense=None, bout=None, dbg=None):
    nc = p.nc
    NH = 16
    SCL = 1.0 / float(np.sqrt(128.0))
    ch = lambda n: tag + n

    ident_bf = E(nc.sbuf_tensor(tag + "id", [128, 128], BF16))
    emit_ident(p, ident_bf, tag + "mi")
    tri_f = E(nc.sbuf_tensor(tag + "tri", [128, 128], F32))
    p.q("gpsimd", lambda g: g.memset(tri_f[:, :], 0.0), inc=ch("cg"))
    p.q("gpsimd", lambda g: g.affine_select(
        tri_f[:, :], tri_f[:, :], compare_op=ALU.is_ge, fill=-1e30, base=0,
        pattern=[[-1, 128]], channel_multiplier=1), inc=ch("cg"))
    ones1 = E(nc.sbuf_tensor(tag + "on1", [1, 128], BF16))
    p.q("gpsimd", lambda g: g.memset(ones1[:, :], 1.0), inc=ch("cg"))
    cgv = p.val(ch("cg"))
    p.wait("vector", ch("cg"), cgv)
    p.wait("tensor", ch("cg"), cgv)

    cols = E(nc.sbuf_tensor(tag + "cls", [128, 16], F32))
    lfl = E(nc.sbuf_tensor(tag + "lfl", [128, 1], F32))
    p.dma("sync", lfl[:, :], logflag, ch("cd"))
    if bqk is not None:
        bqk_sb = E(nc.sbuf_tensor(tag + "bqk", [128, 32], F32))
        p.dma("sync", bqk_sb[:, :], bqk, ch("cd"))
    if bv is not None:
        bv_sb = E(nc.sbuf_tensor(tag + "bv", [128, 16], F32))
        p.dma("sync", bv_sb[:, :], bv, ch("cd"))
    bias_f = E(nc.sbuf_tensor(tag + "bef", [1, D], F32))
    if rope:
        cos_sb = E(nc.sbuf_tensor(tag + "cos", [32, NTOK], F32))
        sin_sb = E(nc.sbuf_tensor(tag + "sin", [32, NTOK], F32))
        rmat = E(nc.sbuf_tensor(tag + "rm", [32, 32], BF16))
        p.dma("sync", cos_sb[:, :], cos_in, ch("cd"))
        p.dma("sync", sin_sb[:, :], sin_in, ch("cd"))
        p.dma("gpsimd", rmat[:, :], rmat_in, ch("cd"))
    cdv = p.val(ch("cd"))
    for e in ("vector", "scalar", "tensor"):
        p.wait(e, ch("cd"), cdv)

    wbuf = [E(nc.sbuf_tensor(f"{tag}w{i}", [128, D], BF16)) for i in range(2)]
    o_dram = nc.dram_tensor(tag + "odr", [128, RT * D], BF16, kind="Internal").ap()

    NT = NTOK // 512
    with ExitStack() as ctxQ:
        EQ = ctxQ.enter_context
        qb = EQ(nc.sbuf_tensor(tag + "qb", [128, NH * NTOK], BF16))
        k_own = EQ(nc.sbuf_tensor(tag + "ko", [128, NH * NTOK], BF16))
        v_own = EQ(nc.sbuf_tensor(tag + "vo", [128, RT * D], BF16))
        with ExitStack() as ctxA:
            EA = ctxA.enter_context
            ln = LNBufs(EA, nc, tag + "L")
            xnT = EA(nc.sbuf_tensor(tag + "xnT", [128, KC * NTOK], BF16))
            v_head = EA(nc.sbuf_tensor(tag + "vh", [128, NTOK], BF16))
            ps_p = [EA(nc.psum_tensor(f"{tag}psp{i}", [128, 512], F32)) for i in range(2)]
            emit_ln(p, ln, ident_bf, x_dram, xnT, tag + "n")

            for which, wsl in (("q", wq_sl), ("k", wk_sl), ("v", wv_sl)):
                mmch = ch("mm" + which)
                evch = ch(which + "ev")
                for hd in range(NH):
                    slot = wbuf[hd % 2]
                    if hd >= 2:
                        war = [(mmch, NT * (hd - 1))]
                    elif which == "k":
                        war = [(ch("mmq"), NT * NH)]
                    elif which == "v":
                        war = [(ch("mmk"), NT * NH)]
                    else:
                        war = []
                    dwv = p.dma("gpsimd", slot[:, :], wsl[hd], ch(which + "wd"),
                                waits=war)
                    for nt in range(NT):
                        G = hd * NT + nt
                        w0 = [(ch(which + "wd"), dwv)] if nt == 0 else []
                        if G >= 2:
                            w0.append((evch, G - 1))
                        for kcc in range(KC):
                            p.q("tensor", lambda t, pp=ps_p[nt], w=slot, kcc=kcc, nt=nt:
                                t.matmul(pp[:, :], w[:, kcc * 128:(kcc + 1) * 128],
                                         xnT[:, kcc * NTOK + nt * 512: kcc * NTOK + (nt + 1) * 512],
                                         start=(kcc == 0), stop=(kcc == KC - 1)),
                                inc=mmch if kcc == KC - 1 else None,
                                waits=w0 if kcc == 0 else ())
                        mval = p.val(mmch)
                        if which == "q":
                            if bqk is not None:
                                p.q("vector", lambda v, pp=ps_p[nt], hd=hd, nt=nt:
                                    v.tensor_scalar(qb[:, hd * NTOK + nt * 512: hd * NTOK + (nt + 1) * 512],
                                                    pp[:, :], bqk_sb[:, hd:hd + 1], SCL,
                                                    ALU.add, ALU.mult),
                                    inc=evch, waits=[(mmch, mval)])
                            else:
                                p.q("vector", lambda v, pp=ps_p[nt], hd=hd, nt=nt:
                                    v.tensor_scalar_mul(qb[:, hd * NTOK + nt * 512: hd * NTOK + (nt + 1) * 512],
                                                        pp[:, :], SCL),
                                    inc=evch, waits=[(mmch, mval)])
                        elif which == "k":
                            if bqk is not None:
                                p.q("vector", lambda v, pp=ps_p[nt], hd=hd, nt=nt:
                                    v.tensor_scalar_add(k_own[:, hd * NTOK + nt * 512: hd * NTOK + (nt + 1) * 512],
                                                        pp[:, :], bqk_sb[:, 16 + hd:17 + hd]),
                                    inc=evch, waits=[(mmch, mval)])
                            else:
                                p.q("vector", lambda v, pp=ps_p[nt], hd=hd, nt=nt:
                                    v.tensor_copy(k_own[:, hd * NTOK + nt * 512: hd * NTOK + (nt + 1) * 512],
                                                  pp[:, :]),
                                    inc=evch, waits=[(mmch, mval)])
                        else:
                            vw_ = [(mmch, mval)]
                            if hd >= 1 and nt == 0:
                                vw_.append((ch("vte"), 2 * hd))
                            p.q("vector", lambda v, pp=ps_p[nt], nt=nt:
                                v.tensor_copy(v_head[:, nt * 512:(nt + 1) * 512],
                                              pp[:, :]),
                                inc=evch, waits=vw_)
                    if which == "v":
                        ve = p.val(evch)
                        for g4 in range(2):
                            Gt = hd * 2 + g4
                            for jj in range(4):
                                rr = g4 * 4 + jj
                                tw = [(evch, ve)] if jj == 0 else []
                                if jj == 0 and Gt >= 2:
                                    tw.append((ch("vte"), Gt - 1))
                                p.q("tensor", lambda t, rr=rr, jj=jj, Gt=Gt:
                                    t.transpose(ln.ps_tr[Gt % 2][:, jj * 128:(jj + 1) * 128],
                                                v_head[:, rr * 128:(rr + 1) * 128],
                                                ident_bf[:, :]),
                                    inc=ch("vtr") if jj == 3 else None, waits=tw)

                            def vevac(v, g4=g4, hd=hd, Gt=Gt):
                                for jj in range(4):
                                    rr = g4 * 4 + jj
                                    inst = v.tensor_copy(
                                        v_own[:, rr * D + hd * 128: rr * D + (hd + 1) * 128],
                                        ln.ps_tr[Gt % 2][:, jj * 128:(jj + 1) * 128])
                                return inst
                            p.q("vector", vevac, inc=ch("vte"),
                                waits=[(ch("vtr"), p.val(ch("vtr")))])
            if rope:
                ps_r = EA(nc.psum_tensor(tag + "psr", [32, NTOK], F32))
                rt1 = EA(nc.sbuf_tensor(tag + "rt1", [32, NTOK], F32))
                rt2 = EA(nc.sbuf_tensor(tag + "rt2", [32, NTOK], F32))
                p.wait("tensor", ch("qev"), NT * NH)
                p.wait("tensor", ch("kev"), NT * NH)
                for which, buf in (("q", qb), ("k", k_own)):
                    for hd in range(NH):
                        i = (0 if which == "q" else NH) + hd
                        for jh in range(2):
                            p.q("tensor", lambda t, buf=buf, hd=hd, jh=jh:
                                t.matmul(ps_r[:, jh * 512:(jh + 1) * 512], rmat[:, :],
                                         buf[0:32, hd * NTOK + jh * 512: hd * NTOK + (jh + 1) * 512],
                                         start=True, stop=True),
                                inc=ch("rmm") if jh == 1 else None,
                                waits=[(ch("rdone"), i)] if (i >= 1 and jh == 0) else [])
                        p.q("vector", lambda v, buf=buf, hd=hd:
                            v.tensor_mul(rt1[:, :], buf[0:32, hd * NTOK:(hd + 1) * NTOK],
                                         cos_sb[:, :]),
                            inc=ch("rc1"), waits=[(ch("rmm"), p.val(ch("rmm")))])
                        p.q("vector", lambda v: v.tensor_mul(rt2[:, :], sin_sb[:, :],
                                                             ps_r[:, :]),
                            inc=ch("rc2"), waits=[(ch("rc1"), p.val(ch("rc1")))])
                        p.q("vector", lambda v, buf=buf, hd=hd:
                            v.tensor_add(buf[0:32, hd * NTOK:(hd + 1) * NTOK],
                                         rt1[:, :], rt2[:, :]),
                            inc=ch("rdone"), waits=[(ch("rc2"), p.val(ch("rc2")))])
                p.wait("tensor", ch("rdone"), p.val(ch("rdone")))
            p.wait("tensor", ch("vte"), p.val(ch("vte")))
            p.wait("sync", ch("vte"), p.val(ch("vte")))
            p.wait("sync", ch("qev"), NT * NH)
            p.wait("sync", ch("kev"), NT * NH)
            if rope:
                p.wait("sync", ch("rdone"), p.val(ch("rdone")))
            if dbg is not None:
                p.wait("gpsimd", ch("qev"), NT * NH)
                p.wait("gpsimd", ch("kev"), NT * NH)
                p.wait("gpsimd", ch("vte"), p.val(ch("vte")))
                p.dma("gpsimd", dbg[1], qb[:, :], ch("dbg"))
                p.dma("gpsimd", dbg[2], k_own[:, :], ch("dbg"))
                p.dma("gpsimd", dbg[3], v_own[:, :], ch("dbg"))
        with ExitStack() as ctxB:
            EB = ctxB.enter_context
            nonce = NONCE_BASE + (128 if rope else 256)
            KR = D + 16
            VR = NTOK + 16
            p.dma("sync", cc_kin[0:D, :].rearrange("(hc p) t -> p hc t", p=128),
                  k_own[:, :].rearrange("p (hc t) -> p hc t", hc=NH), ch("agd"))
            p.dma("sync", cc_vin[0:NTOK, :].rearrange("(r p) d -> p r d", p=128),
                  v_own[:, :].rearrange("p (r d) -> p r d", r=RT), ch("agd"))
            ncd8 = EB(nc.sbuf_tensor(tag + "nc8", [16, 64], BF16))
            p.q("gpsimd", lambda g: g.memset(ncd8[:, :], float(nonce)), inc=ch("nng"))
            p.dma("sync", cc_kin[D:D + 16, NTOK - 64:NTOK],
                  ncd8[:, :], ch("agd"), waits=[(ch("nng"), p.val(ch("nng")))])
            ncd8v = EB(nc.sbuf_tensor(tag + "nc8v", [16, 64], BF16))
            p.q("gpsimd", lambda g: g.memset(ncd8v[:, :], float(nonce + 64)),
                inc=ch("nng"))
            vag = p.dma("sync", cc_vin[NTOK:NTOK + 16, D - 64:D],
                        ncd8v[:, :], ch("agd"),
                        waits=[(ch("nng"), p.val(ch("nng")))])
            p.q("gpsimd", lambda g: g.collective_compute(
                "AllGather", ALU.bypass, replica_groups=PAIRS, ins=[cc_kin.opt()],
                outs=[cc_kout.opt()]), inc=ch("agc"), waits=[(ch("agd"), vag)])
            p.q("gpsimd", lambda g: g.collective_compute(
                "AllGather", ALU.bypass, replica_groups=PAIRS, ins=[cc_vin.opt()],
                outs=[cc_vout.opt()]), inc=ch("agc"))
            kprior = EB(nc.sbuf_tensor(tag + "kp", [128, NH * NTOK], BF16))
            vprior = EB(nc.sbuf_tensor(tag + "vp", [128, RT * D], BF16))
            agc = p.val(ch("agc"))
            emit_cc_guard(p, EB, tag + "K",
                          cc_kout[D:D + 16, NTOK - 4:NTOK].rearrange(
                              "a (o b) -> a o b", o=1), nonce,
                          ch("agc"), agc, npts=16)
            gok = emit_cc_guard(p, EB, tag + "V",
                                cc_vout[NTOK:NTOK + 16, D - 4:D].rearrange(
                                    "a (o b) -> a o b", o=1), nonce + 64,
                                ch("agc"), agc, npts=16)
            kok = p.val(tag + "K" + "ccok")
            p.dma("sync", kprior[:, :].rearrange("p (hc t) -> p hc t", hc=NH),
                  cc_kout[0:D, :].rearrange("(hc p) t -> p hc t", p=128),
                  ch("agr"), waits=[(tag + "K" + "ccok", kok)])
            p.dma("sync", vprior[:, :].rearrange("p (r d) -> p r d", r=RT),
                  cc_vout[0:NTOK, :].rearrange("(r p) d -> p r d", p=128),
                  ch("agr"), waits=[(tag + "V" + "ccok", gok)])
            agr = p.val(ch("agr"))
            p.wait("tensor", ch("agr"), agr)
            p.wait("vector", ch("agr"), agr)
            p.wait("scalar", ch("agr"), agr)
            if dbg is not None and len(dbg) > 4:
                p.wait("gpsimd", ch("agr"), agr)
                p.dma("gpsimd", dbg[4], kprior[:, :], ch("dbg"))
                p.dma("gpsimd", dbg[5], vprior[:, :], ch("dbg"))
                p.wait("gpsimd", ch("dbg"), p.val(ch("dbg")))

            Pp = EB(nc.sbuf_tensor(tag + "Pp", [128, NTOK], BF16))
            Po = EB(nc.sbuf_tensor(tag + "Po", [128, NTOK], BF16))
            PT = EB(nc.sbuf_tensor(tag + "PT", [128, 2 * NTOK], BF16))
            obf = EB(nc.sbuf_tensor(tag + "obf", [128, D], BF16))
            oT_r = EB(nc.sbuf_tensor(tag + "oTr", [128, D], BF16))
            ps_sp = EB(nc.psum_tensor(tag + "pssp", [128, 1024], F32))
            ps_so = EB(nc.psum_tensor(tag + "psso", [128, 1024], F32))
            ps_pt = EB(nc.psum_tensor(tag + "pspt", [128, 512], BF16))
            ps_pv = EB(nc.psum_tensor(tag + "pspv", [128, 128], F32))
            ps_sum = EB(nc.psum_tensor(tag + "pssm", [128, 1], F32))
            ones_bf = EB(nc.sbuf_tensor(tag + "ob1", [128, 1], BF16))
            p.q("gpsimd", lambda g: g.memset(ones_bf[:, :], 1.0), inc=ch("cg3"))
            p.wait("tensor", ch("cg3"), p.val(ch("cg3")))
            c_mp = cols[:, 0:1]; c_mo = cols[:, 1:2]; c_mf = cols[:, 2:3]
            c_bp = cols[:, 3:4]; c_bo = cols[:, 4:5]; c_sp = cols[:, 5:6]
            c_so = cols[:, 6:7]; c_sm = cols[:, 7:8]; c_ri = cols[:, 8:9]
            p.q("gpsimd", lambda g: g.memset(c_bo, 0.0), inc=ch("cg3"))
            cg3v = p.val(ch("cg3"))
            last_ptr = 0
            for r in range(RT):
                nko = r + 1
                for hd in range(NH):
                    i = r * NH + hd
                    qt = qb[:, hd * NTOK + r * 128: hd * NTOK + (r + 1) * 128]
                    sw = [(ch("expd"), 2 * i)] if i >= 1 else []
                    for j in range(2):
                        p.q("tensor", lambda t, qt=qt, hd=hd, j=j:
                            t.matmul(ps_sp[:, j * 512:(j + 1) * 512], qt,
                                     kprior[:, hd * NTOK + j * 512: hd * NTOK + (j + 1) * 512],
                                     start=True, stop=True),
                            waits=sw if j == 0 else ())
                    nso = (nko * 128 + 511) // 512
                    for j in range(nso):
                        hi = min(nko * 128, (j + 1) * 512)
                        p.q("tensor", lambda t, qt=qt, hd=hd, j=j, hi=hi:
                            t.matmul(ps_so[:, j * 512:hi], qt,
                                     k_own[:, hd * NTOK + j * 512: hd * NTOK + hi],
                                     start=True, stop=True),
                            inc=ch("smm") if j == nso - 1 else None)
                    smm = p.val(ch("smm"))
                    p.q("vector", lambda v, r=r:
                        v.tensor_add(ps_so[:, r * 128:(r + 1) * 128],
                                     ps_so[:, r * 128:(r + 1) * 128], tri_f[:, :]),
                        inc=ch("mskd"), waits=[(ch("smm"), smm)])
                    mskd = p.val(ch("mskd"))
                    ew = [(ch("mskd"), mskd), (ch("ptr"), last_ptr)]
                    if i == 0:
                        ew.append((ch("cg3"), cg3v))
                    p.q("scalar", lambda s: s.activation(Pp[:, :], ps_sp[:, :], AF.Exp,
                                                         bias=lfl[:, 0:1]),
                        inc=ch("expd"), waits=ew)
                    p.q("scalar", lambda s, nko=nko:
                        s.activation(Po[:, 0:nko * 128], ps_so[:, 0:nko * 128], AF.Exp,
                                     bias=c_bo),
                        inc=ch("expd"), waits=ew)
                    expd = p.val(ch("expd"))
                    ntile = 8 + nko
                    for g4 in range((ntile + 3) // 4):
                        js = list(range(g4 * 4, min(ntile, g4 * 4 + 4)))
                        for jj, j in enumerate(js):
                            tw = []
                            if jj == 0:
                                tw.append((ch("expd"), expd))
                                tw.append((ch("pte"), p.val(ch("pte"))))
                                tw.append((ch("ote"), p.val(ch("ote"))))
                            src_ = (Pp[:, j * 128:(j + 1) * 128] if j < 8 else
                                    Po[:, (j - 8) * 128:(j - 7) * 128])
                            p.q("tensor", lambda t, src_=src_, jj=jj:
                                t.transpose(ps_pt[:, jj * 128:(jj + 1) * 128], src_,
                                            ident_bf[:, :]),
                                inc=ch("ptr") if jj == len(js) - 1 else None, waits=tw)

                        def evac(v, js=js):
                            for jj, j in enumerate(js):
                                inst = v.tensor_copy(PT[:, j * 128:(j + 1) * 128],
                                                     ps_pt[:, jj * 128:(jj + 1) * 128])
                            return inst
                        evw = [(ch("ptr"), p.val(ch("ptr")))]
                        if i >= 1 or g4 >= 1:
                            evw.append((ch("pvmm"), p.val(ch("pvmm"))))
                        p.q("vector", evac, inc=ch("pte"), waits=evw)
                    last_ptr = p.val(ch("ptr"))
                    pte = p.val(ch("pte"))
                    for j in range(ntile):
                        vsrc = vprior if j < 8 else v_own
                        jr = j if j < 8 else j - 8
                        pw_ = [(ch("pte"), pte)] if j == 0 else []
                        if j == 0 and i >= 1:
                            pw_.append((ch("oev"), i))
                        p.q("tensor", lambda t, j=j, jr=jr, vsrc=vsrc, hd=hd, ntile=ntile:
                            t.matmul(ps_pv[:, :], PT[:, j * 128:(j + 1) * 128],
                                     vsrc[:, jr * D + hd * 128: jr * D + (hd + 1) * 128],
                                     start=(j == 0), stop=(j == ntile - 1)),
                            waits=pw_)
                        sw_ = [(ch("rsum"), i)] if (j == 0 and i >= 1) else []
                        p.q("tensor", lambda t, j=j, ntile=ntile:
                            t.matmul(ps_sum[:, :], PT[:, j * 128:(j + 1) * 128],
                                     ones_bf[:, :], start=(j == 0),
                                     stop=(j == ntile - 1)),
                            inc=ch("pvmm") if j == ntile - 1 else None, waits=sw_)
                    p.q("vector", lambda v: v.reciprocal(c_ri, ps_sum[:, :]),
                        inc=ch("rsum"), waits=[(ch("pvmm"), p.val(ch("pvmm")))])
                    ovw = [(ch("rsum"), p.val(ch("rsum")))]
                    if r >= 1 and hd == 0:
                        ovw.append((ch("odw"), 16 * r))
                    p.q("vector", lambda v, hd=hd:
                        v.tensor_scalar_mul(obf[:, hd * 128:(hd + 1) * 128], ps_pv[:, :],
                                            c_ri),
                        inc=ch("oev"), waits=ovw)
                for g4 in range(4):
                    for jj in range(4):
                        kc = g4 * 4 + jj
                        tw = []
                        if jj == 0:
                            tw.append((ch("oev"), p.val(ch("oev"))))
                            tw.append((ch("ote"), p.val(ch("ote"))))
                            tw.append((ch("pte"), p.val(ch("pte"))))
                            if r >= 1:
                                tw.append((ch("odw"), 16 * r))
                        p.q("tensor", lambda t, kc=kc, jj=jj:
                            t.transpose(ps_pt[:, jj * 128:(jj + 1) * 128],
                                        obf[:, kc * 128:(kc + 1) * 128], ident_bf[:, :]),
                            inc=ch("otr") if jj == 3 else None, waits=tw)

                    def oevac(v, g4=g4):
                        for jj in range(4):
                            kc = g4 * 4 + jj
                            inst = v.tensor_copy(oT_r[:, kc * 128:(kc + 1) * 128],
                                                 ps_pt[:, jj * 128:(jj + 1) * 128])
                        return inst
                    p.q("vector", oevac, inc=ch("ote"),
                        waits=[(ch("otr"), p.val(ch("otr")))])
                p.dma("sync", o_dram[:, r * D:(r + 1) * D], oT_r[:, :], ch("odw"),
                      waits=[(ch("ote"), p.val(ch("ote")))])
            p.wait("tensor", ch("odw"), p.val(ch("odw")))
            p.wait("gpsimd", ch("odw"), p.val(ch("odw")))
            p.wait("vector", ch("odw"), p.val(ch("odw")))
            p.wait("scalar", ch("odw"), p.val(ch("odw")))
            p.wait("sync", ch("odw"), p.val(ch("odw")))
    oT_sb = E(nc.sbuf_tensor(tag + "oT", [128, RT * D], BF16))
    y_sb = [E(nc.sbuf_tensor(f"{tag}y{i}", [128, 512], F32)) for i in range(2)]
    x2_sb = [E(nc.sbuf_tensor(f"{tag}x2{i}", [128, 512], F32)) for i in range(2)]
    bias128 = E(nc.sbuf_tensor(tag + "b128", [128, D], F32))
    ones_f = E(nc.sbuf_tensor(tag + "onf", [1, 128], F32))
    p.q("gpsimd", lambda g: g.memset(ones_f[:, :], 1.0), inc=ch("cg2"))
    orl = p.dma("sync", oT_sb[:, :], o_dram, ch("orl"))
    p.wait("tensor", ch("orl"), orl)
    with ExitStack() as ctxC:
        EC = ctxC.enter_context
        ps_b = EC(nc.psum_tensor(tag + "psb", [1, 512], F32))
        ps_bc = EC(nc.psum_tensor(tag + "psbc", [128, 512], F32))
        if bv is not None:
            bvbf = EC(nc.sbuf_tensor(tag + "bvb", [128, 16], BF16))
            bde_sb = EC(nc.sbuf_tensor(tag + "bde", [1, D], F32))
            bwbuf = [EC(nc.sbuf_tensor(f"{tag}bw{i}", [128, 512], BF16))
                     for i in range(2)]
            p.q("vector", lambda v: v.tensor_copy(bvbf[:, :], bv_sb[:, :]), inc=ch("bvb"))
            p.wait("tensor", ch("bvb"), p.val(ch("bvb")))
            for nd in range(4):
                for kcc in range(KC):
                    J = nd * KC + kcc
                    slot = bwbuf[J % 2]
                    dwb = p.dma("gpsimd", slot[:, 0:512], wdense_sl[nd, kcc], ch("bwd"),
                                waits=[(ch("bmm"), J - 1)] if J >= 2 else [])
                    p.q("tensor", lambda t, slot=slot, kcc=kcc:
                        t.matmul(ps_b[:, :], bvbf[:, kcc:kcc + 1], slot[:, 0:512],
                                 start=(kcc == 0), stop=(kcc == KC - 1)),
                        inc=ch("bmm"), waits=[(ch("bwd"), dwb)] +
                            ([(ch("bev"), nd)] if kcc == 0 and nd >= 1 else []))
                p.q("vector", lambda v, nd=nd:
                    v.tensor_copy(bias_f[:, nd * 512:(nd + 1) * 512], ps_b[:, :]),
                    inc=ch("bev"), waits=[(ch("bmm"), p.val(ch("bmm")))])
            bd_ = p.dma("sync", bde_sb[:, :], bdense, ch("bdd"))
            p.q("vector", lambda v: v.tensor_add(bias_f[:, :], bias_f[:, :], bde_sb[:, :]),
                inc=ch("bfin"), waits=[(ch("bdd"), bd_), (ch("bev"), p.val(ch("bev")))])
        else:
            bd_ = p.dma("sync", bias_f[:, :], bout, ch("bdd"))
            p.q("vector", lambda v: v.tensor_copy(bias_f[:, :], bias_f[:, :]),
                inc=ch("bfin"), waits=[(ch("bdd"), bd_)]) if False else None
            p.q("gpsimd", lambda g: g.nop(), inc=ch("bfin"), waits=[(ch("bdd"), bd_)])
        bfin = p.val(ch("bfin"))
        p.wait("tensor", ch("bfin"), bfin)
        p.wait("tensor", ch("cg2"), p.val(ch("cg2")))
        for nd in range(4):
            p.q("tensor", lambda t, nd=nd:
                t.matmul(ps_bc[:, :], ones_f[:, :], bias_f[:, nd * 512:(nd + 1) * 512],
                         start=True, stop=True),
                inc=ch("bcm"), waits=[(ch("bce"), nd)] if nd >= 1 else [])
            p.q("vector", lambda v, nd=nd:
                v.tensor_copy(bias128[:, nd * 512:(nd + 1) * 512], ps_bc[:, :]),
                inc=ch("bce"), waits=[(ch("bcm"), p.val(ch("bcm")))])
        p.wait("tensor", ch("bce"), p.val(ch("bce")))
        p.wait("gpsimd", ch("bce"), p.val(ch("bce")))
        p.wait("vector", ch("bce"), p.val(ch("bce")))
    emit_out_proj(p, E, oT_sb, wdense_sl, x_dram, y_dram, NH, wbuf, x2_sb, y_sb, tag,
                  bias128=bias128, act_ix=lambda hk, r: r * D + hk * 128)


def build_mha_launch(rope, debug=False):
    nc = bass.Bass("TRN2", target_bir_lowering=False)
    x_in = nc.dram_tensor("x_in", [NTOK, D], F32, kind="ExternalInput").ap()
    wq = nc.dram_tensor("wq", [16, 128, D], F32, kind="ExternalInput").ap()
    wk = nc.dram_tensor("wk", [16, 128, D], F32, kind="ExternalInput").ap()
    wv = nc.dram_tensor("wv", [16, 128, D], F32, kind="ExternalInput").ap()
    wde = nc.dram_tensor("wde", [4, 16, 128, 512], F32, kind="ExternalInput").ap()
    logflag = nc.dram_tensor("logflag", [128, 1], F32, kind="ExternalInput").ap()
    y = nc.dram_tensor("y", [NTOK, D], F32, kind="ExternalOutput").ap()
    cc_kin = nc.dram_tensor("cc_kin", [D + 16, NTOK], BF16, kind="Internal").ap()
    cc_kout = nc.dram_tensor("cc_kout", [2 * D + 32, NTOK], BF16, kind="Internal").ap()
    cc_vin = nc.dram_tensor("cc_vin", [NTOK + 16, D], BF16, kind="Internal").ap()
    cc_vout = nc.dram_tensor("cc_vout", [2 * NTOK + 32, D], BF16, kind="Internal").ap()
    kw = {}
    if rope:
        kw["cos_in"] = nc.dram_tensor("cos_in", [32, NTOK], F32, kind="ExternalInput").ap()
        kw["sin_in"] = nc.dram_tensor("sin_in", [32, NTOK], F32, kind="ExternalInput").ap()
        kw["rmat_in"] = nc.dram_tensor("rmat_in", [32, 32], F32, kind="ExternalInput").ap()
        kw["bqk"] = nc.dram_tensor("bqk", [128, 32], F32, kind="ExternalInput").ap()
        kw["bv"] = nc.dram_tensor("bv", [128, 16], F32, kind="ExternalInput").ap()
        kw["bdense"] = nc.dram_tensor("bdense", [1, D], F32, kind="ExternalInput").ap()
    else:
        kw["bout"] = nc.dram_tensor("bout", [1, D], F32, kind="ExternalInput").ap()
    if debug:
        dbgo = nc.dram_tensor("dbgo", [128, RT * D], F32, kind="ExternalOutput").ap()
        dbgq = nc.dram_tensor("dbgq", [128, 16 * NTOK], F32, kind="ExternalOutput").ap()
        dbgk = nc.dram_tensor("dbgk", [128, 16 * NTOK], F32, kind="ExternalOutput").ap()
        dbgv = nc.dram_tensor("dbgv", [128, RT * D], F32, kind="ExternalOutput").ap()
    with ExitStack() as ctx:
        p = Prog(nc, ctx.enter_context)
        bufs = emit_mha(p, ctx.enter_context, x_in, y, wq, wk, wv, wde, "m",
                 cc_kin, cc_kout, cc_vin, cc_vout, logflag, rope=rope, **kw,
                 dbg=(dbgo, dbgq, dbgk, dbgv) if debug else None)
        p.realize()
    return nc


def mha_a_maps(W_qkv, b_qkv, W_dense, b_dense):
    W = np.asarray(W_qkv).reshape(D, 16, 3, 128)
    Wq_all = np.ascontiguousarray(W[:, :, 0, :].reshape(D, D))
    Wk_all = np.ascontiguousarray(W[:, :, 1, :].reshape(D, D))
    Wv_all = np.ascontiguousarray(W[:, :, 2, :].reshape(D, D))
    b = np.asarray(b_qkv).reshape(16, 3, 128)
    bqk = np.zeros((128, 32), np.float32)
    bqk[:, 0:16] = b[:, 0, :].T
    bqk[:, 16:32] = b[:, 1, :].T
    bv = np.ascontiguousarray(b[:, 2, :].T)
    ROT = 32
    inv_freq = 1.0 / (10000.0 ** (np.arange(0, ROT, 2, dtype=np.float64) / ROT))
    rmat = np.zeros((32, 32), np.float32)
    for m in range(16):
        rmat[16 + m, m] = -1.0
        rmat[m, 16 + m] = 1.0
    def cs(parity):
        t = parity * NTOK + np.arange(NTOK, dtype=np.float64)
        fr = np.outer(t, inv_freq)
        emb = np.concatenate([fr, fr], axis=1)
        return (np.cos(emb).T.astype(np.float32),
                np.sin(emb).T.astype(np.float32))
    return {
        "wq": lhsT_slabs(np.ascontiguousarray(Wq_all.T)),
        "wk": lhsT_slabs(np.ascontiguousarray(Wk_all.T)),
        "wv": lhsT_slabs(np.ascontiguousarray(Wv_all.T)),
        "wde": rhs_slabs(np.asarray(W_dense)),
        "bqk": bqk, "bv": bv,
        "bdense": np.asarray(b_dense).reshape(1, D).astype(np.float32),
        "rmat_in": rmat,
    }, cs


def mha_b_maps(w_qkv, w_out, b_out):
    W = np.asarray(w_qkv)
    return {
        "wq": lhsT_slabs(np.ascontiguousarray(W[:, 0:D].T)),
        "wk": lhsT_slabs(np.ascontiguousarray(W[:, D:2 * D].T)),
        "wv": lhsT_slabs(np.ascontiguousarray(W[:, 2 * D:].T)),
        "wde": rhs_slabs(np.asarray(w_out)),
        "bout": np.asarray(b_out).reshape(1, D).astype(np.float32),
    }


def logflag_arr(parity):
    return np.full((128, 1), 0.0 if parity == 1 else -1e30, np.float32)





def emit_l1ffn(p, E, x_dram, y_dram, gamma128, beta128, bases_pad, slopes_pad,
               oscale, cc_in, cc_out, tag):
    nc = p.nc
    ch = lambda n: tag + n
    BIG = 1e30

    def step(eng, fn, extra=()):
        k = p.val(ch("sq"))
        p.q(eng, fn, inc=ch("sq"), waits=[(ch("sq"), k)] + list(extra))

    ident_f = E(nc.sbuf_tensor(tag + "idf", [128, 128], F32))
    p.q("gpsimd", lambda g: g.memset(ident_f[:, :], 0.0), inc=ch("cg"))
    p.q("gpsimd", lambda g: g.affine_select(
        ident_f[:, :], ident_f[:, :], compare_op=ALU.not_equal, fill=1.0, base=0,
        pattern=[[-1, 128]], channel_multiplier=1), inc=ch("cg"))
    tileid = E(nc.sbuf_tensor(tag + "tid", [128, 64], F32))
    p.q("gpsimd", lambda g: g.iota(tileid[:, :], [[1, 64]], base=0,
                                   channel_multiplier=0,
                                   allow_small_or_imprecise_dtypes=True), inc=ch("cg"))
    jrev64 = E(nc.sbuf_tensor(tag + "jr64", [128, 64], F32))
    p.q("gpsimd", lambda g: g.iota(jrev64[:, :], [[-1, 64]], base=63,
                                   channel_multiplier=0,
                                   allow_small_or_imprecise_dtypes=True), inc=ch("cg"))
    jrev8 = E(nc.sbuf_tensor(tag + "jr8", [128, 8], F32))
    p.q("gpsimd", lambda g: g.iota(jrev8[:, :], [[-1, 8]], base=7,
                                   channel_multiplier=0,
                                   allow_small_or_imprecise_dtypes=True), inc=ch("cg"))
    clc = E(nc.sbuf_tensor(tag + "clc", [128, 64], F32))
    p.q("gpsimd", lambda g: g.iota(clc[:, :].rearrange("p (a b) -> p a b", b=8),
                                   [[1, 8], [0, 8]], base=0, channel_multiplier=0,
                                   allow_small_or_imprecise_dtypes=True), inc=ch("cg"))
    gv116 = E(nc.sbuf_tensor(tag + "gv", [128, 512], F32))
    p.q("gpsimd", lambda g: g.iota(gv116[:, :].rearrange("p (a b) -> p a b", b=16),
                                   [[0, 32], [1, 16]], base=1, channel_multiplier=0,
                                   allow_small_or_imprecise_dtypes=True), inc=ch("cg"))
    negbig = E(nc.sbuf_tensor(tag + "nb", [128, 64], F32))
    p.q("gpsimd", lambda g: g.memset(negbig[:, :], -BIG), inc=ch("cg"))
    iotap = E(nc.sbuf_tensor(tag + "iop", [128, 1], F32))
    p.q("gpsimd", lambda g: g.iota(iotap[:, :], [[0, 1]], base=0, channel_multiplier=1,
                                   allow_small_or_imprecise_dtypes=True), inc=ch("cg"))
    ones_f = E(nc.sbuf_tensor(tag + "onf", [1, 128], F32))
    p.q("gpsimd", lambda g: g.memset(ones_f[:, :], 1.0), inc=ch("cg"))
    cgv = p.val(ch("cg"))
    for e in ("tensor", "vector", "scalar", "sync"):
        p.wait(e, ch("cg"), cgv)

    g128 = E(nc.sbuf_tensor(tag + "g128", [128, D], F32))
    b128 = E(nc.sbuf_tensor(tag + "b128", [128, D], F32))
    slo_sb = E(nc.sbuf_tensor(tag + "slo", [64, 512], F32))
    osc_sb = E(nc.sbuf_tensor(tag + "osc", [64, 1], F32))
    p.dma("sync", g128[:, :], gamma128, ch("cd"))
    p.dma("sync", b128[:, :], beta128, ch("cd"))
    p.dma("sync", slo_sb[:, :], slopes_pad[0:64, :], ch("cd"))
    cdv = p.dma("sync", osc_sb[:, :], oscale, ch("cd"))
    p.wait("vector", ch("cd"), cdv)
    p.wait("scalar", ch("cd"), cdv)

    xa = E(nc.sbuf_tensor(tag + "xa", [128, D], F32))
    ya = E(nc.sbuf_tensor(tag + "ya", [128, D], F32))
    xna = E(nc.sbuf_tensor(tag + "xna", [128, D], F32))
    ta = E(nc.sbuf_tensor(tag + "ta", [128, D], F32))
    scr = E(nc.sbuf_tensor(tag + "scr", [128, D], F32))
    cols = E(nc.sbuf_tensor(tag + "cols", [128, 24], F32))
    ss32 = E(nc.sbuf_tensor(tag + "ss32", [64, 32], F32))
    ssb = E(nc.sbuf_tensor(tag + "ssb", [128, D], F32))
    ssrow = E(nc.sbuf_tensor(tag + "ssrow", [1, D], F32))
    ieq = E(nc.sbuf_tensor(tag + "ieq", [128, 64], F32))
    ieq8 = E(nc.sbuf_tensor(tag + "ieq8", [128, 64], mybir.dt.int8))
    msel = E(nc.sbuf_tensor(tag + "msel", [128, 64], F32))
    m64a = E(nc.sbuf_tensor(tag + "m64a", [128, 64], F32))
    m64b = E(nc.sbuf_tensor(tag + "m64b", [128, 64], F32))
    idx_all = E(nc.sbuf_tensor(tag + "idxa", [128, 8], F32))
    widx_i = E(nc.sbuf_tensor(tag + "widx", [128, 8], I32))
    bidx_i = E(nc.sbuf_tensor(tag + "bidx", [128, 8], I32))
    xsw = E(nc.sbuf_tensor(tag + "xsw", [128, 32], F32))
    bas_t = E(nc.sbuf_tensor(tag + "bast", [128, 512], F32))
    slo_t = E(nc.sbuf_tensor(tag + "slot", [128, 512], F32))
    GE = E(nc.sbuf_tensor(tag + "GE", [128, 512], F32))
    OH = E(nc.sbuf_tensor(tag + "OH", [128, 512], F32))
    u32 = E(nc.sbuf_tensor(tag + "u32", [128, 32], F32))
    gif = E(nc.sbuf_tensor(tag + "gif", [128, 32], F32))
    xl32 = E(nc.sbuf_tensor(tag + "xl32", [128, 32], F32))
    bcol = E(nc.sbuf_tensor(tag + "bcol", [128, 32], F32))
    scol = E(nc.sbuf_tensor(tag + "scol", [128, 32], F32))
    dlt = E(nc.sbuf_tensor(tag + "dlt", [128, 32], F32))
    tmm = E(nc.sbuf_tensor(tag + "tmm", [64, 4], F32))
    ohT = E(nc.sbuf_tensor(tag + "ohT", [64, 128], F32))
    gmp = E(nc.sbuf_tensor(tag + "gmp", [64, 4], F32))
    mnx = E(nc.sbuf_tensor(tag + "mnx", [128, 4], F32))
    xn_dram = nc.dram_tensor(tag + "xnd", [NTOK, D], F32, kind="Internal").ap()
    ssd = nc.dram_tensor(tag + "ssd", [64, 32], F32, kind="Internal").ap()

    with ExitStack() as ctxA:
        EA = ctxA.enter_context
        ps_a = EA(nc.psum_tensor(tag + "psa", [128, 512], F32))
        ps_b = EA(nc.psum_tensor(tag + "psb", [128, 128], F32))

        c_s1 = cols[:, 0:1]; c_q1 = cols[:, 1:2]; c_m1 = cols[:, 2:3]
        c_v1 = cols[:, 3:4]; c_r1 = cols[:, 4:5]; c_sq2 = cols[:, 5:6]
        c_m2 = cols[:, 6:7]; c_v2 = cols[:, 7:8]; c_r2 = cols[:, 8:9]
        c_a = cols[:, 9:10]; c_b = cols[:, 10:11]; c_c = cols[:, 11:12]
        c_d = cols[:, 12:13]; c_e = cols[:, 13:14]; c_f = cols[:, 14:15]
        c_g = cols[:, 15:16]; c_h = cols[:, 16:17]; c_i = cols[:, 17:18]
        c_mn = mnx[:, 0:1]; c_mx = mnx[:, 1:2]; c_os = mnx[:, 2:3]
        c_eps = cols[:, 18:19]
        p.q("gpsimd", lambda g: g.memset(c_eps, EPS), inc=ch("cg2"))
        p.wait("scalar", ch("cg2"), p.val(ch("cg2")))

        step("vector", lambda v: v.reduce_sum(ss32[:, :],
                                              slo_sb[:, :].rearrange("p (a b) -> p a b", b=16),
                                              axis=AX.X),
             extra=[(ch("cd"), cdv)])
        step("vector", lambda v: v.tensor_scalar_mul(ss32[:, :], ss32[:, :], 1.0 / 16))
        step("scalar", lambda s: s.activation(ss32[:, :], ss32[:, :], AF.Sign))
        sq0 = p.val(ch("sq"))
        d1 = p.dma("sync", ssd, ss32[:, :], ch("ssd"), waits=[(ch("sq"), sq0)])
        d2 = p.dma("sync", ssrow[:, :],
                   ssd.rearrange("a b -> (a b)").rearrange("(o f) -> o f", o=1),
                   ch("ssd"), waits=[(ch("ssd"), d1)])
        step("tensor", lambda t: t.matmul(ps_a[:, :], ones_f[:, :], ssrow[:, 0:512],
                                          start=True, stop=True),
             extra=[(ch("ssd"), d2)])
        step("vector", lambda v: v.tensor_copy(ssb[:, 0:512], ps_a[:, :]))
        for nd in range(1, 4):
            step("tensor", lambda t, nd=nd: t.matmul(ps_a[:, :], ones_f[:, :],
                                                     ssrow[:, nd * 512:(nd + 1) * 512],
                                                     start=True, stop=True))
            step("vector", lambda v, nd=nd: v.tensor_copy(ssb[:, nd * 512:(nd + 1) * 512],
                                                          ps_a[:, :]))
        step("gpsimd", lambda g: g.memset(tmm[:, :], BIG))
        step("gpsimd", lambda g: g.memset(tmm[:, 1:2], -BIG))

        for r in range(RT):
            sqv = p.val(ch("sq"))
            dx = p.dma("sync", xa[:, :], x_dram[r * 128:(r + 1) * 128, :], ch("xd"),
                       waits=[(ch("sq"), sqv)])
            step("vector", lambda v: v.reduce_sum(c_s1, xa[:, :], axis=AX.X),
                 extra=[(ch("xd"), dx)])
            step("scalar", lambda s: s.activation(scr[:, :], xa[:, :], AF.Square))
            step("vector", lambda v: v.reduce_sum(c_q1, scr[:, :], axis=AX.X))
            step("vector", lambda v: v.tensor_scalar_mul(c_m1, c_s1, 1.0 / D))
            step("scalar", lambda s: s.activation(c_v1, c_m1, AF.Square))
            step("vector", lambda v: v.scalar_tensor_tensor(c_v1, c_q1, 1.0 / D, c_v1,
                                                            ALU.mult, ALU.subtract))
            step("scalar", lambda s: s.activation(c_r1, c_v1, AF.Sqrt, bias=c_eps))
            step("vector", lambda v: v.reciprocal(c_r1, c_r1)) if False else None
            step("vector", lambda v: v.reciprocal(c_a, c_r1))
            step("vector", lambda v: v.tensor_scalar(ya[:, :], xa[:, :], c_m1, c_a,
                                                     ALU.subtract, ALU.mult))
            step("vector", lambda v: v.reduce_sum(c_b, ya[:, :], axis=AX.X))
            step("scalar", lambda s: s.activation(scr[:, :], ya[:, :], AF.Square))
            step("vector", lambda v: v.reduce_sum(c_sq2, scr[:, :], axis=AX.X))
            step("vector", lambda v: v.tensor_scalar_mul(c_m2, c_b, 1.0 / D))
            step("scalar", lambda s: s.activation(c_v2, c_m2, AF.Square))
            step("vector", lambda v: v.scalar_tensor_tensor(c_v2, c_sq2, 1.0 / D, c_v2,
                                                            ALU.mult, ALU.subtract))
            step("scalar", lambda s: s.activation(c_r2, c_v2, AF.Sqrt, bias=c_eps))
            step("vector", lambda v: v.reciprocal(c_c, c_r2))
            step("vector", lambda v: v.tensor_scalar(xna[:, :], ya[:, :], c_m2, c_c,
                                                     ALU.subtract, ALU.mult))
            step("vector", lambda v: v.tensor_mul(xna[:, :], xna[:, :], g128[:, :]))
            step("vector", lambda v: v.tensor_add(xna[:, :], xna[:, :], b128[:, :]))
            sqv = p.val(ch("sq"))
            dxn = p.dma("sync", xn_dram[r * 128:(r + 1) * 128, :], xna[:, :], ch("xnw"),
                        waits=[(ch("sq"), sqv)])
            dyw = p.dma("sync", y_dram[r * 128:(r + 1) * 128, :], ya[:, :], ch("yst"),
                        waits=[(ch("sq"), sqv)])
            step("scalar", lambda s: s.activation(ta[:, :], xna[:, :], AF.Sign))
            step("vector", lambda v: v.tensor_mul(ta[:, :], ta[:, :], ssb[:, :]))
            step("vector", lambda v: v.reduce_sum(m64a[:, 0:8],
                                                  ta[:, :].rearrange("p (a b) -> p a b", b=256),
                                                  axis=AX.X))
            step("vector", lambda v: v.reduce_sum(m64b[:, :],
                                                  ta[:, :].rearrange("p (a b) -> p a b", b=32),
                                                  axis=AX.X))
            step("vector", lambda v: v.scalar_tensor_tensor(msel[:, 0:8], m64a[:, 0:8],
                                                            8.0, jrev8[:, :],
                                                            ALU.mult, ALU.add))
            step("vector", lambda v: v.reduce_max(c_d, msel[:, 0:8], axis=AX.X))
            step("vector", lambda v: v.reduce_max(c_e, m64a[:, 0:8], axis=AX.X))
            step("vector", lambda v: v.tensor_scalar(c_f, c_e, 8.0, 7.0, ALU.mult,
                                                     ALU.add))
            step("vector", lambda v: v.tensor_sub(c_f, c_f, c_d))
            step("vector", lambda v: v.tensor_single_scalar(ieq[:, :], clc[:, :], c_f,
                                                            ALU.is_equal))
            step("vector", lambda v: v.tensor_copy(ieq8[:, :], ieq[:, :]))
            step("vector", lambda v: v.select(msel[:, :], ieq8[:, :], m64b[:, :],
                                              negbig[:, :]))
            step("vector", lambda v: v.scalar_tensor_tensor(m64a[:, :], msel[:, :],
                                                            64.0, jrev64[:, :],
                                                            ALU.mult, ALU.add))
            step("vector", lambda v: v.reduce_max(c_g, m64a[:, :], axis=AX.X))
            step("vector", lambda v: v.reduce_max(c_h, msel[:, :], axis=AX.X))
            step("vector", lambda v: v.tensor_scalar(c_i, c_h, 64.0, 63.0, ALU.mult,
                                                     ALU.add))
            step("vector", lambda v, r=r: v.tensor_sub(idx_all[:, r:r + 1], c_i, c_g))
            step("vector", lambda v, r=r: v.tensor_scalar(c_a, iotap[:, :], float(r * 128),
                                                          64.0, ALU.add, ALU.mult))
            step("vector", lambda v, r=r: v.tensor_add(c_b, c_a, idx_all[:, r:r + 1]))
            step("vector", lambda v, r=r: v.tensor_copy(widx_i[:, r:r + 1], c_b))
            step("vector", lambda v, r=r: v.tensor_copy(bidx_i[:, r:r + 1],
                                                        idx_all[:, r:r + 1]))
            sqv = p.val(ch("sq"))
            dg = p.dma("gpsimd", xsw[:, :], None, ch("gd"),
                       waits=[]) if False else None
            p.q("gpsimd", lambda g, r=r: g.indirect_dma_start(
                xsw[:, :], None, xn_dram[0:128, 0:32],
                IndirectOffsetOnAxis(widx_i[:, r:r + 1], 0)),
                inc=ch("gd"), inc_by=16,
                waits=[(ch("sq"), sqv), (ch("xnw"), dxn)])
            gdv = p.val(ch("gd"))
            step("vector", lambda v: v.tensor_reduce(c_a, xsw[:, :], axis=AX.X,
                                                     op=ALU.min),
                 extra=[(ch("gd"), gdv)])
            step("vector", lambda v: v.tensor_reduce(c_b, xsw[:, :], axis=AX.X,
                                                     op=ALU.max))
            step("vector", lambda v, r=r: v.tensor_single_scalar(
                ieq[:, :], tileid[:, :], idx_all[:, r:r + 1], ALU.is_equal))
            step("vector", lambda v: v.tensor_scalar(msel[:, :], ieq[:, :], -BIG, BIG,
                                                     ALU.mult, ALU.add))
            step("vector", lambda v: v.tensor_scalar_mul(m64a[:, :], ieq[:, :], c_a))
            step("vector", lambda v: v.tensor_add(m64a[:, :], m64a[:, :], msel[:, :]))
            step("vector", lambda v: v.tensor_scalar(msel[:, :], ieq[:, :], BIG, -BIG,
                                                     ALU.mult, ALU.add))
            step("vector", lambda v: v.tensor_scalar_mul(m64b[:, :], ieq[:, :], c_b))
            step("vector", lambda v: v.tensor_add(m64b[:, :], m64b[:, :], msel[:, :]))
            step("tensor", lambda t: t.transpose(ps_b[:, :], m64a[:, 0:64][0:128, :],
                                                 ident_f[:, :])) if False else None
            step("tensor", lambda t: t.transpose(ps_a[0:64, 0:128], m64a[:, :],
                                                 ident_f[:, :]))
            step("vector", lambda v: v.tensor_reduce(tmm[:, 2:3], ps_a[0:64, 0:128],
                                                     axis=AX.X, op=ALU.min))
            step("tensor", lambda t: t.transpose(ps_a[0:64, 0:128], m64b[:, :],
                                                 ident_f[:, :]))
            step("vector", lambda v: v.tensor_reduce(tmm[:, 3:4], ps_a[0:64, 0:128],
                                                     axis=AX.X, op=ALU.max))
            step("vector", lambda v: v.tensor_tensor(tmm[:, 0:1], tmm[:, 0:1],
                                                     tmm[:, 2:3], ALU.min))
            step("vector", lambda v: v.tensor_tensor(tmm[:, 1:2], tmm[:, 1:2],
                                                     tmm[:, 3:4], ALU.max))
        step("vector", lambda v: v.tensor_scalar_mul(tmm[:, 2:3], tmm[:, 1:2], -1.0))
        step("vector", lambda v: v.tensor_copy(tmm[:, 3:4], tmm[:, 0:1]))
        sqv = p.val(ch("sq"))
        p.dma("sync", cc_in[0:64, :], tmm[:, 2:4], ch("ard"), waits=[(ch("sq"), sqv)])
        nonce = NONCE_BASE + (384 if tag == "G" else 448)
        nn = emit_cc_nonce(p, EA, tag,
                           cc_in[64:96, :].rearrange("a b -> (a b)").rearrange(
                               "(o f) -> o f", o=1), nonce)
        da = p.val(ch("ard"))
        p.q("gpsimd", lambda g: g.collective_compute(
            "AllReduce", ALU.min, replica_groups=WORLD, ins=[cc_in.opt()],
            outs=[cc_out.opt()]), inc=ch("arc"),
            waits=[(ch("ard"), da), (tag + "nncd", nn)])
        gok = emit_cc_guard(p, EA, tag,
                            cc_out[94:96, :].rearrange("a b -> (a b)").rearrange(
                                "(o f) -> o f", o=1), nonce,
                            ch("arc"), p.val(ch("arc")))
        db = p.dma("sync", gmp[:, 0:2], cc_out[0:64, :], ch("arr"),
                   waits=[(tag + "ccok", gok)])
        step("vector", lambda v: v.tensor_scalar_mul(gmp[:, 2:3], gmp[:, 0:1], -1.0),
             extra=[(ch("arr"), db)])
        step("vector", lambda v: v.tensor_copy(gmp[:, 0:1], gmp[:, 1:2]))
        step("vector", lambda v: v.tensor_copy(gmp[:, 1:2], gmp[:, 2:3]))
        step("vector", lambda v: v.tensor_copy(gmp[:, 2:3], osc_sb[:, :]))

        for r in range(RT):
            sqv = p.val(ch("sq"))
            p.q("gpsimd", lambda g, r=r: g.indirect_dma_start(
                xsw[:, :], None, xn_dram[0:128, 0:32],
                IndirectOffsetOnAxis(widx_i[:, r:r + 1], 0)),
                inc=ch("gd"), inc_by=16, waits=[(ch("sq"), sqv)])
            p.q("gpsimd", lambda g, r=r: g.indirect_dma_start(
                bas_t[:, :], None, bases_pad[0:128, 0:512],
                IndirectOffsetOnAxis(bidx_i[:, r:r + 1], 0)),
                inc=ch("gd"), inc_by=16)
            p.q("gpsimd", lambda g, r=r: g.indirect_dma_start(
                slo_t[:, :], None, slopes_pad[0:128, 0:512],
                IndirectOffsetOnAxis(bidx_i[:, r:r + 1], 0)),
                inc=ch("gd"), inc_by=16)
            gdv = p.val(ch("gd"))
            step("vector", lambda v, r=r: v.tensor_single_scalar(
                ieq[:, :], tileid[:, :], idx_all[:, r:r + 1], ALU.is_equal),
                 extra=[(ch("gd"), gdv)])
            step("tensor", lambda t: t.transpose(ps_a[0:64, 0:128], ieq[:, :],
                                                 ident_f[:, :]))
            step("vector", lambda v: v.tensor_copy(ohT[:, :], ps_a[0:64, 0:128]))
            step("tensor", lambda t: t.matmul(ps_b[:, 0:3], ohT[:, :], gmp[:, 0:3],
                                              start=True, stop=True))
            step("vector", lambda v: v.tensor_copy(mnx[:, 0:3], ps_b[:, 0:3]))
            step("vector", lambda v: v.tensor_sub(c_a, c_mx, c_mn))
            step("vector", lambda v: v.tensor_scalar_add(c_a, c_a, 1e-8))
            step("vector", lambda v: v.reciprocal(c_b, c_a))
            step("vector", lambda v: v.tensor_scalar(u32[:, :], xsw[:, :], c_mn, c_b,
                                                     ALU.subtract, ALU.mult))
            step("vector", lambda v: v.tensor_scalar(u32[:, :], u32[:, :], 0.0,
                                                     1.0 - 1e-6, ALU.max, ALU.min))
            step("vector", lambda v: v.tensor_scalar_mul(u32[:, :], u32[:, :], 16.0))
            for g_ in range(16):
                step("vector", lambda v, g_=g_: v.tensor_single_scalar(
                    GE[:, :].rearrange("p (a b) -> p a b", b=16)[:, :, g_:g_ + 1],
                    u32[:, :].rearrange("p (a o) -> p a o", o=1), float(g_ + 1), ALU.is_ge))
            step("vector", lambda v: v.reduce_sum(gif[:, :],
                                                  GE[:, :].rearrange("p (a b) -> p a b", b=16),
                                                  axis=AX.X))
            step("vector", lambda v: v.tensor_sub(xl32[:, :], u32[:, :], gif[:, :]))
            step("vector", lambda v: v.tensor_sub(
                OH[:, :].rearrange("p (a b) -> p a b", b=16)[:, :, 1:16],
                GE[:, :].rearrange("p (a b) -> p a b", b=16)[:, :, 0:15],
                GE[:, :].rearrange("p (a b) -> p a b", b=16)[:, :, 1:16]))
            step("vector", lambda v: v.tensor_scalar(
                OH[:, :].rearrange("p (a b) -> p a b", b=16)[:, :, 0:1],
                GE[:, :].rearrange("p (a b) -> p a b", b=16)[:, :, 0:1],
                -1.0, -1.0, ALU.mult, ALU.subtract)) if False else None
            step("vector", lambda v: v.tensor_scalar(
                OH[:, :].rearrange("p (a b) -> p a b", b=16)[:, :, 0:1],
                GE[:, :].rearrange("p (a b) -> p a b", b=16)[:, :, 0:1],
                -1.0, 1.0, ALU.mult, ALU.add))
            step("vector", lambda v: v.tensor_mul(bas_t[:, :], bas_t[:, :], OH[:, :]))
            step("vector", lambda v: v.tensor_mul(slo_t[:, :], slo_t[:, :], OH[:, :]))
            step("vector", lambda v: v.reduce_sum(bcol[:, :],
                                                  bas_t[:, :].rearrange("p (a b) -> p a b", b=16),
                                                  axis=AX.X))
            step("vector", lambda v: v.reduce_sum(scol[:, :],
                                                  slo_t[:, :].rearrange("p (a b) -> p a b", b=16),
                                                  axis=AX.X))
            step("vector", lambda v: v.tensor_mul(dlt[:, :], scol[:, :], xl32[:, :]))
            step("vector", lambda v: v.tensor_add(dlt[:, :], dlt[:, :], bcol[:, :]))
            step("vector", lambda v: v.tensor_scalar_mul(dlt[:, :], dlt[:, :], c_os))
            sqv = p.val(ch("sq"))
            dx2 = p.dma("sync", xa[:, :], x_dram[r * 128:(r + 1) * 128, :], ch("x2d"),
                        waits=[(ch("sq"), sqv)])
            dy2 = p.dma("sync", ya[:, :], y_dram[r * 128:(r + 1) * 128, :], ch("y2d"),
                        waits=[(ch("sq"), sqv), (ch("yst"), p.val(ch("yst")))])
            step("vector", lambda v: v.tensor_add(ya[:, :], ya[:, :], xa[:, :]),
                 extra=[(ch("x2d"), dx2), (ch("y2d"), dy2)])
            for i in range(64):
                step("vector", lambda v, i=i: v.scalar_tensor_tensor(
                    ya[:, 32 * i:32 * i + 32], dlt[:, :], ieq[:, i:i + 1],
                    ya[:, 32 * i:32 * i + 32], ALU.mult, ALU.add))
            sqv = p.val(ch("sq"))
            p.dma("sync", y_dram[r * 128:(r + 1) * 128, :], ya[:, :], ch("yfw"),
                  waits=[(ch("sq"), sqv)])
        yfw = p.val(ch("yfw"))
        p.wait("sync", ch("yfw"), yfw)
        p.wait("tensor", ch("yfw"), yfw)
        p.wait("vector", ch("yfw"), yfw)
        p.wait("gpsimd", ch("yfw"), yfw)
    return {"idx": idx_all[:, :], "gmp": gmp[:, :], "xnd": xn_dram,
            "tmm": tmm[:, :], "ccout": cc_out, "m64a": m64a[:, :]}


def build_l1ffn_launch(debug=False):
    nc = bass.Bass("TRN2", target_bir_lowering=False)
    x_in = nc.dram_tensor("x_in", [NTOK, D], F32, kind="ExternalInput").ap()
    gamma128 = nc.dram_tensor("gamma128", [128, D], F32, kind="ExternalInput").ap()
    beta128 = nc.dram_tensor("beta128", [128, D], F32, kind="ExternalInput").ap()
    bases_pad = nc.dram_tensor("bases_pad", [128, 512], F32, kind="ExternalInput").ap()
    slopes_pad = nc.dram_tensor("slopes_pad", [128, 512], F32, kind="ExternalInput").ap()
    oscale = nc.dram_tensor("oscale", [64, 1], F32, kind="ExternalInput").ap()
    y = nc.dram_tensor("y", [NTOK, D], F32, kind="ExternalOutput").ap()
    cc_in = nc.dram_tensor("cc_in", [96, 2], F32, kind="Internal").ap()
    cc_out = nc.dram_tensor("cc_out", [96, 2], F32, kind="Internal").ap()
    if debug:
        didx = nc.dram_tensor("didx", [128, 8], F32, kind="ExternalOutput").ap()
        dgmp = nc.dram_tensor("dgmp", [64, 4], F32, kind="ExternalOutput").ap()
        dxn = nc.dram_tensor("dxn", [NTOK, D], F32, kind="ExternalOutput").ap()
        dtmm = nc.dram_tensor("dtmm", [64, 4], F32, kind="ExternalOutput").ap()
        dcc = nc.dram_tensor("dcc", [64, 2], F32, kind="ExternalOutput").ap()
        dm64 = nc.dram_tensor("dm64", [128, 64], F32, kind="ExternalOutput").ap()
    with ExitStack() as ctx:
        p = Prog(nc, ctx.enter_context)
        bufs = emit_l1ffn(p, ctx.enter_context, x_in, y, gamma128, beta128, bases_pad,
                   slopes_pad, oscale, cc_in, cc_out, "s")
        if debug:
            p.wait("gpsimd", "syfw", p.val("syfw"))
            p.dma("gpsimd", didx, bufs["idx"], "dbg")
            p.dma("gpsimd", dgmp, bufs["gmp"], "dbg")
            p.dma("gpsimd", dxn, bufs["xnd"], "dbg")
            p.dma("gpsimd", dtmm, bufs["tmm"], "dbg")
            p.dma("gpsimd", dcc, bufs["ccout"], "dbg")
            p.dma("gpsimd", dm64, bufs["m64a"], "dbg")
            p.wait("gpsimd", "dbg", p.val("dbg"))
        p.realize()
    return nc


def l1ffn_maps(ln_gamma, ln_beta, spline_bases, spline_slopes, output_scale):
    g = np.ascontiguousarray(np.broadcast_to(np.asarray(ln_gamma), (128, D))).astype(np.float32)
    b = np.ascontiguousarray(np.broadcast_to(np.asarray(ln_beta), (128, D))).astype(np.float32)
    bp = np.zeros((128, 512), np.float32)
    bp[0:64] = np.asarray(spline_bases).reshape(64, 512)
    sp = np.zeros((128, 512), np.float32)
    sp[0:64] = np.asarray(spline_slopes).reshape(64, 512)
    return {"gamma128": g, "beta128": b, "bases_pad": bp, "slopes_pad": sp,
            "oscale": np.asarray(output_scale).reshape(64, 1).astype(np.float32)}



def lhsT_slabs(W):
    out_f, in_f = W.shape
    a = W.reshape(out_f // 128, 128, in_f // 128, 128)
    a = a.transpose(0, 3, 2, 1)
    return np.ascontiguousarray(a.reshape(out_f // 128, 128, in_f))


def rhs_slabs(M, n_slice=512):
    K, N = M.shape
    a = M.reshape(K // 128, 128, N // n_slice, n_slice)
    a = a.transpose(2, 0, 1, 3)
    return np.ascontiguousarray(a)



def build_ffn_launch(debug=False):
    nc = bass.Bass("TRN2", target_bir_lowering=False)
    x_in = nc.dram_tensor("x_in", [NTOK, D], F32, kind="ExternalInput").ap()
    wg = nc.dram_tensor("wg", [HC, 128, D], F32, kind="ExternalInput").ap()
    wu = nc.dram_tensor("wu", [HC, 128, D], F32, kind="ExternalInput").ap()
    wd = nc.dram_tensor("wd", [4, HC, 128, 512], F32, kind="ExternalInput").ap()
    y = nc.dram_tensor("y", [NTOK, D], F32, kind="ExternalOutput").ap()
    if debug:
        dxnT = nc.dram_tensor("dxnT", [128, KC * NTOK], F32, kind="ExternalOutput").ap()
        dh = nc.dram_tensor("dh", [128, HC * NTOK], F32, kind="ExternalOutput").ap()
    with ExitStack() as ctx:
        p = Prog(nc, ctx.enter_context)
        bufs = emit_ffn(p, ctx.enter_context, x_in, y, wg, wu, wd, "f")
        if debug:
            p.wait("gpsimd", "fyw", p.val("fyw"))
            p.dma("gpsimd", dxnT, bufs["xnT"][:, :], "dbgd")
            p.dma("gpsimd", dh, bufs["h"][:, :], "dbgd")
            p.wait("gpsimd", "dbgd", p.val("dbgd"))
        p.realize()
    return nc


def ffn_weight_maps(w_gate, w_up, w_down):
    return {
        "wg": lhsT_slabs(np.asarray(w_gate)),
        "wu": lhsT_slabs(np.asarray(w_up)),
        "wd": rhs_slabs(np.ascontiguousarray(np.asarray(w_down).T)),
    }



_prog_cache = {}


def _get(name, builder):
    if name not in _prog_cache:
        _prog_cache[name] = builder()
    return _prog_cache[name]


def kernel_multilaunch(**inp):
    x = np.ascontiguousarray(np.asarray(inp["L0_attn_a_x"], np.float32))
    shards = [np.ascontiguousarray(x[c // 2, (c % 2) * NTOK:(c % 2 + 1) * NTOK])
              for c in range(8)]

    flags = [np.full((128, 1), float(c % 2), np.float32) for c in range(8)]
    lflags = [logflag_arr(c % 2) for c in range(8)]

    dn_m = deltanet_maps(inp["L0_attn_a_Wq"], inp["L0_attn_a_Wk"], inp["L0_attn_a_Wv"],
                         inp["L0_attn_a_Wo"], inp["L0_attn_a_Wb"], inp["L0_attn_a_bb"],
                         inp["L0_attn_a_Wd"], inp["L0_attn_a_bd"])
    ffn_m = ffn_weight_maps(inp["L0_ffn_w_gate"], inp["L0_ffn_w_up"],
                            inp["L0_ffn_w_down"])
    ema_m = ema_maps(inp["L0_attn_b_Wv"], inp["L0_attn_b_Wg"], inp["L0_attn_b_bg"],
                     inp["L0_attn_b_Wq"], inp["L0_attn_b_bq"], inp["L0_attn_b_Wo"])
    mha_a_m, cs_fn = mha_a_maps(inp["L1_attn_a_W_qkv"], inp["L1_attn_a_b_qkv"],
                                inp["L1_attn_a_W_dense"], inp["L1_attn_a_b_dense"])
    mha_b_m = mha_b_maps(inp["L1_attn_b_w_qkv"], inp["L1_attn_b_w_out"],
                         inp["L1_attn_b_b_out"])
    l1_m = l1ffn_maps(inp["L1_ffn_ln_gamma"], inp["L1_ffn_ln_beta"],
                      inp["L1_ffn_spline_bases"], inp["L1_ffn_spline_slopes"],
                      inp["L1_ffn_output_scale"])
    cos0, sin0 = cs_fn(0)
    cos1, sin1 = cs_fn(1)

    def run(nc, maps):
        res = run_bass_kernel_spmd(nc, maps, core_ids=list(range(8)))
        return [res.results[c]["y"] for c in range(8)]

    nc = _get("dn", build_deltanet_launch)
    shards = run(nc, [{"x_in": shards[c], "flag": flags[c], **dn_m}
                      for c in range(8)])
    nc = _get("ffn", build_ffn_launch)
    shards = run(nc, [{"x_in": shards[c], **ffn_m} for c in range(8)])
    nc = _get("ema", build_ema_launch)
    shards = run(nc, [{"x_in": shards[c], "flag": flags[c], **ema_m}
                      for c in range(8)])
    nc = _get("ffn", build_ffn_launch)
    shards = run(nc, [{"x_in": shards[c], **ffn_m} for c in range(8)])
    nc = _get("mha_a", lambda: build_mha_launch(True))
    maps = []
    for c in range(8):
        cos, sin = (cos0, sin0) if c % 2 == 0 else (cos1, sin1)
        maps.append({"x_in": shards[c], "logflag": lflags[c], "cos_in": cos,
                     "sin_in": sin, **mha_a_m})
    shards = run(nc, maps)
    nc = _get("l1", build_l1ffn_launch)
    shards = run(nc, [{"x_in": shards[c], **l1_m} for c in range(8)])
    nc = _get("mha_b", lambda: build_mha_launch(False))
    shards = run(nc, [{"x_in": shards[c], "logflag": lflags[c], **mha_b_m}
                      for c in range(8)])
    nc = _get("l1", build_l1ffn_launch)
    shards = run(nc, [{"x_in": shards[c], **l1_m} for c in range(8)])

    out = np.zeros((B, T, D), np.float32)
    for c in range(8):
        out[c // 2, (c % 2) * NTOK:(c % 2 + 1) * NTOK] = shards[c]
    return out



def build_fused(n_sub=8, dbg_h=False, dump_x=False):
    nc = bass.Bass("TRN2", target_bir_lowering=False)
    DT = lambda n, sh, dt=F32, kind="ExternalInput": nc.dram_tensor(n, sh, dt, kind=kind).ap()
    x_in = DT("x_in", [NTOK, D])
    y_out = DT("y_out", [NTOK, D], F32, "ExternalOutput")
    xb1 = DT("xb1", [NTOK, D], F32, "Internal")
    xb2 = DT("xb2", [NTOK, D], F32, "Internal")
    flag = DT("flag", [128, 1])
    logflag = DT("logflag", [128, 1])
    a_wqkbd = DT("a_wqkbd", [4, 128, KC * 64])
    a_wv = DT("a_wv", [128, KC * 128])
    a_wo = DT("a_wo", [4, 1, 128, 512])
    a_bbd = DT("a_bbd", [64, 2])
    a_ccin = DT("a_ccin", [68, 128], F32, "Internal")
    a_ccout = DT("a_ccout", [136, 128], F32, "Internal")
    f_wg = DT("f_wg", [HC, 128, D])
    f_wu = DT("f_wu", [HC, 128, D])
    f_wd = DT("f_wd", [4, HC, 128, 512])
    e_wv = DT("e_wv", [KC, 128, D])
    e_wg = DT("e_wg", [KC, 128, D])
    e_wq = DT("e_wq", [KC, 128, D])
    e_wo = DT("e_wo", [4, KC, 128, 512])
    e_bgq = DT("e_bgq", [128, 2 * KC])
    e_dv = DT("e_dv", [128, NTOK])
    e_ccin = DT("e_ccin", [132, KC], F32, "Internal")
    e_ccout = DT("e_ccout", [264, KC], F32, "Internal")
    cc_kin = DT("cc_kin", [D + 16, NTOK], BF16, "Internal")
    cc_kout = DT("cc_kout", [2 * D + 32, NTOK], BF16, "Internal")
    cc_vin = DT("cc_vin", [NTOK + 16, D], BF16, "Internal")
    cc_vout = DT("cc_vout", [2 * NTOK + 32, D], BF16, "Internal")
    ma_wq = DT("ma_wq", [16, 128, D]); ma_wk = DT("ma_wk", [16, 128, D])
    ma_wv = DT("ma_wv", [16, 128, D]); ma_wde = DT("ma_wde", [4, 16, 128, 512])
    ma_bqk = DT("ma_bqk", [128, 32]); ma_bv = DT("ma_bv", [128, 16])
    ma_bde = DT("ma_bde", [1, D])
    cos_in = DT("cos_in", [32, NTOK]); sin_in = DT("sin_in", [32, NTOK])
    rmat_in = DT("rmat_in", [32, 32])
    l_g = DT("l_g", [128, D]); l_b = DT("l_b", [128, D])
    l_bp = DT("l_bp", [128, 512]); l_sp = DT("l_sp", [128, 512])
    l_osc = DT("l_osc", [64, 1])
    l_ccin = DT("l_ccin", [96, 2], F32, "Internal")
    l_ccout = DT("l_ccout", [96, 2], F32, "Internal")
    l_ccin2 = DT("l_ccin2", [96, 2], F32, "Internal")
    l_ccout2 = DT("l_ccout2", [96, 2], F32, "Internal")
    mb_wq = DT("mb_wq", [16, 128, D]); mb_wk = DT("mb_wk", [16, 128, D])
    mb_wv = DT("mb_wv", [16, 128, D]); mb_wde = DT("mb_wde", [4, 16, 128, 512])
    mb_bout = DT("mb_bout", [1, D])

    def sub_k(k, p, E, xs, ys):
        if k == 0:
            emit_deltanet(p, E, xs, ys, a_wqkbd, a_wv, a_wo, a_bbd, flag,
                          a_ccin, a_ccout, "A")
            return "Ayw"
        if k == 1:
            emit_ffn(p, E, xs, ys, f_wg, f_wu, f_wd, "B")
            return "Byw"
        if k == 2:
            emit_ema(p, E, xs, ys, e_wv, e_wg, e_wq, e_wo, e_bgq, e_dv, flag,
                     e_ccin, e_ccout, "C")
            return "Cyw"
        if k == 3:
            emit_ffn(p, E, xs, ys, f_wg, f_wu, f_wd, "E")
            return "Eyw"
        if k == 4:
            emit_mha(p, E, xs, ys, ma_wq, ma_wk, ma_wv, ma_wde, "F",
                     cc_kin, cc_kout, cc_vin, cc_vout, logflag, rope=True,
                     cos_in=cos_in, sin_in=sin_in, rmat_in=rmat_in,
                     bqk=ma_bqk, bv=ma_bv, bdense=ma_bde)
            return "Fyw"
        if k == 5:
            emit_l1ffn(p, E, xs, ys, l_g, l_b, l_bp, l_sp, l_osc,
                       l_ccin, l_ccout, "G")
            return "Gyfw"
        if k == 6:
            emit_mha(p, E, xs, ys, mb_wq, mb_wk, mb_wv, mb_wde, "H",
                     cc_kin, cc_kout, cc_vin, cc_vout, logflag, rope=False,
                     bout=mb_bout, dbg=dbg_tensors)
            return "Hyw"
        emit_l1ffn(p, E, xs, ys, l_g, l_b, l_bp, l_sp, l_osc,
                   l_ccin2, l_ccout2, "I")
        return "Iyfw"

    dbg_tensors = None
    if dbg_h:
        dbg_tensors = (DT("dbgo", [128, RT * D], F32, "ExternalOutput"),
                       DT("dbgq", [128, 16 * NTOK], F32, "ExternalOutput"),
                       DT("dbgk", [128, 16 * NTOK], F32, "ExternalOutput"),
                       DT("dbgv", [128, RT * D], F32, "ExternalOutput"),
                       DT("dbgkp", [128, 16 * NTOK], F32, "ExternalOutput"),
                       DT("dbgvp", [128, RT * D], F32, "ExternalOutput"))
    with ExitStack() as ctx:
        p = Prog(nc, ctx.enter_context)
        cur = x_in
        pong = [xb1, xb2]
        for k in range(n_sub):
            ys = y_out if k == n_sub - 1 else pong[k % 2]
            with ExitStack() as s:
                term = sub_k(k, p, s.enter_context, cur, ys)
                p.seal(term)
            cur = ys
        if dump_x:
            dx1 = nc.dram_tensor("dump_x1", [NTOK, D], F32, kind="ExternalOutput").ap()
            dx2 = nc.dram_tensor("dump_x2", [NTOK, D], F32, kind="ExternalOutput").ap()
            p.dma("gpsimd", dx1, xb1, "dmpx")
            p.dma("gpsimd", dx2, xb2, "dmpx")
            p.wait("gpsimd", "dmpx", p.val("dmpx"))
        p.realize()
    return nc


def kernel_fused(**inp):
    x = np.ascontiguousarray(np.asarray(inp["L0_attn_a_x"], np.float32))
    shards = [np.ascontiguousarray(x[c // 2, (c % 2) * NTOK:(c % 2 + 1) * NTOK])
              for c in range(8)]
    dn_m = deltanet_maps(inp["L0_attn_a_Wq"], inp["L0_attn_a_Wk"], inp["L0_attn_a_Wv"],
                         inp["L0_attn_a_Wo"], inp["L0_attn_a_Wb"], inp["L0_attn_a_bb"],
                         inp["L0_attn_a_Wd"], inp["L0_attn_a_bd"])
    ffn_m = ffn_weight_maps(inp["L0_ffn_w_gate"], inp["L0_ffn_w_up"],
                            inp["L0_ffn_w_down"])
    ema_m = ema_maps(inp["L0_attn_b_Wv"], inp["L0_attn_b_Wg"], inp["L0_attn_b_bg"],
                     inp["L0_attn_b_Wq"], inp["L0_attn_b_bq"], inp["L0_attn_b_Wo"])
    mha_a_m, cs_fn = mha_a_maps(inp["L1_attn_a_W_qkv"], inp["L1_attn_a_b_qkv"],
                                inp["L1_attn_a_W_dense"], inp["L1_attn_a_b_dense"])
    mha_b_m = mha_b_maps(inp["L1_attn_b_w_qkv"], inp["L1_attn_b_w_out"],
                         inp["L1_attn_b_b_out"])
    l1_m = l1ffn_maps(inp["L1_ffn_ln_gamma"], inp["L1_ffn_ln_beta"],
                      inp["L1_ffn_spline_bases"], inp["L1_ffn_spline_slopes"],
                      inp["L1_ffn_output_scale"])
    cosl = [cs_fn(0), cs_fn(1)]
    nc = _get("fused", build_fused)
    maps = []
    for c in range(8):
        h = c % 2
        cos, sin = cosl[h]
        maps.append({
            "x_in": shards[c],
            "flag": np.full((128, 1), float(h), np.float32),
            "logflag": logflag_arr(h),
            "a_wqkbd": dn_m["wqkbd"], "a_wv": dn_m["wv"], "a_wo": dn_m["wo"],
            "a_bbd": dn_m["bbd"],
            "f_wg": ffn_m["wg"], "f_wu": ffn_m["wu"], "f_wd": ffn_m["wd"],
            "e_wv": ema_m["wv"], "e_wg": ema_m["wg"], "e_wq": ema_m["wq"],
            "e_wo": ema_m["wo"], "e_bgq": ema_m["bgq"], "e_dv": ema_m["dv128"],
            "ma_wq": mha_a_m["wq"], "ma_wk": mha_a_m["wk"], "ma_wv": mha_a_m["wv"],
            "ma_wde": mha_a_m["wde"], "ma_bqk": mha_a_m["bqk"],
            "ma_bv": mha_a_m["bv"], "ma_bde": mha_a_m["bdense"],
            "cos_in": cos, "sin_in": sin, "rmat_in": mha_a_m["rmat_in"],
            "l_g": l1_m["gamma128"], "l_b": l1_m["beta128"],
            "l_bp": l1_m["bases_pad"], "l_sp": l1_m["slopes_pad"],
            "l_osc": l1_m["oscale"],
            "mb_wq": mha_b_m["wq"], "mb_wk": mha_b_m["wk"], "mb_wv": mha_b_m["wv"],
            "mb_wde": mha_b_m["wde"], "mb_bout": mha_b_m["bout"],
        })
    res = run_bass_kernel_spmd(nc, maps, core_ids=list(range(8)))
    out = np.zeros((B, T, D), np.float32)
    for c in range(8):
        out[c // 2, (c % 2) * NTOK:(c % 2 + 1) * NTOK] = res.results[c]["y_out"]
    return out


def kernel(**inp):
    import os
    if os.environ.get("K_MULTILAUNCH"):
        return kernel_multilaunch(**inp)
    return kernel_fused(**inp)


# version 06
# speedup vs baseline: 1.0586x; 1.0006x over previous
import numpy as np
import ml_dtypes
from contextlib import ExitStack

import concourse.bass as bass
import concourse.mybir as mybir
from concourse.bass_utils import run_bass_kernel_spmd
from concourse.bass import IndirectOffsetOnAxis

F32 = mybir.dt.float32
BF16 = mybir.dt.bfloat16
I32 = mybir.dt.int32
U8 = mybir.dt.uint8
AF = mybir.ActivationFunctionType
ALU = mybir.AluOpType
AX = mybir.AxisListType

B, T, D = 4, 2048, 2048
HALF = T // 2
NTOK = HALF
RT = NTOK // 128
KC = D // 128
HD = 8192
HC = HD // 128
EPS = 1e-5
BIGNEG = -1e30
NONCE_BASE = 4096.0

PAIRS = [[0, 1], [2, 3], [4, 5], [6, 7]]
WORLD = [[0, 1, 2, 3, 4, 5, 6, 7]]

ENGINES = ("tensor", "vector", "scalar", "gpsimd", "sync")


class Prog:

    def __init__(self, nc, E, nphys=88):
        self.nc = nc
        self.E = E
        self.ops = {e: [] for e in ENGINES}
        self.phys = [E(nc.semaphore(f"ps{i}")) for i in range(nphys)]
        self.phys_cnt = [0] * nphys
        self.map = {}
        self.base = {}
        self.uid = 0
        self.free = list(range(nphys))
        self.guard_sems = [E(nc.semaphore(f"gsp{i}")) for i in range(9)]
        self.guard_i = 0

    def sem(self, name):
        if name not in self.map:
            if not self.free:
                raise RuntimeError("semaphore pool exhausted; call end_scope")
            idx = self.free.pop(0)
            self.map[name] = idx
            self.base[name] = self.phys_cnt[idx]
        return self.phys[self.map[name]]

    def val(self, name):
        self.sem(name)
        return self.phys_cnt[self.map[name]] - self.base[name]

    def _abs(self, name, v):
        self.sem(name)
        return self.base[name] + v

    def end_scope(self):
        for name in list(self.map):
            self.free.append(self.map.pop(name))
            del self.base[name]
        self.free.sort()

    def q(self, eng, fn, inc=None, inc_by=1, waits=(), incs=None):
        waits = [w for w in waits if w is not None]
        sem_objs = [(self.sem(n), self._abs(n, v)) for n, v in waits]
        all_incs = []
        if inc is not None:
            all_incs.append((inc, inc_by))
        if incs:
            all_incs.extend(incs)
        ret = None
        if all_incs:
            inc_objs = []
            for nm, by in all_incs:
                self.sem(nm)
                self.phys_cnt[self.map[nm]] += by
                inc_objs.append((self.phys[self.map[nm]], by))
            ret = self.val(all_incs[0][0])

            def wrapped(e, fn=fn, sem_objs=sem_objs, inc_objs=inc_objs):
                for s, v in sem_objs:
                    e.wait_ge(s, v)
                inst = fn(e)
                for s, by in inc_objs:
                    inst = inst.then_inc(s, by)
        else:
            def wrapped(e, fn=fn, sem_objs=sem_objs):
                for s, v in sem_objs:
                    e.wait_ge(s, v)
                fn(e)
        self.ops[eng].append(wrapped)
        return ret

    def wait(self, eng, name, value):
        s = self.sem(name)
        av = self._abs(name, value)
        self.ops[eng].append(lambda e, s=s, v=av: e.wait_ge(s, v))

    def dma(self, eng, out, in_, ch, waits=()):
        return self.q(eng, lambda e, o=out, i=in_: e.dma_start(o, i),
                      inc=ch, inc_by=16, waits=waits)

    def seal(self, chname):
        v = self.val(chname)
        for e in ENGINES:
            self.wait(e, chname, v)
        self.end_scope()

    def realize(self):
        block = self.E(self.nc.Block())
        for name in ENGINES:
            oplist = self.ops[name]
            getattr(block, name)(lambda eng, oplist=oplist: [f(eng) for f in oplist])



def emit_ident(p, ident_bf, tag):
    p.q("gpsimd", lambda g, ib=ident_bf: g.memset(ib[:, :], 0.0), inc=tag)
    p.q("gpsimd", lambda g, ib=ident_bf: g.affine_select(
        ib[:, :], ib[:, :], compare_op=ALU.not_equal, fill=1.0, base=0,
        pattern=[[-1, ib.shape[1]]], channel_multiplier=1), inc=tag)
    tgt = p.val(tag)
    p.wait("tensor", tag, tgt)




def emit_cc_nonce(p, E, tag, nonce_dram_ap, nonce):
    nc = p.nc
    dt = nonce_dram_ap.dtype
    t = E(nc.sbuf_tensor(tag + "nnc", [1, 64], dt))
    p.q("gpsimd", lambda g: g.memset(t[:, :], float(nonce)), inc=tag + "nncg")
    return p.dma("sync", nonce_dram_ap, t[:, :], tag + "nncd",
                 waits=[(tag + "nncg", p.val(tag + "nncg"))])


def emit_cc_guard(p, E, tag, probe_src_ap, nonce, wait_ch, wait_v, npts=1):
    nc = p.nc
    probe = E(nc.sbuf_tensor(tag + "prb", [1, npts * 4], I32))
    bits = int(nonce)
    gs = p.guard_sems[p.guard_i]
    p.guard_i += 1

    def spin(g):
        with g.register(tag + "rv") as rv, g.register(tag + "rc") as rc, \
             g.register(tag + "rd") as rd, g.register(tag + "rt") as rt:
            g.reg_mov(rc, 0)
            g.reg_mov(rd, 1)
            with g.While(rd):
                g.reg_add(rc, rc, 16)
                g.dma_start(probe[:, :], probe_src_ap).then_inc(gs, 16)
                g.wait_ge(gs, rc)
                g.reg_mov(rd, 0)
                for c in range(npts):
                    g.reg_load(rv, probe[0:1, c * 4 + 3:c * 4 + 4])
                    g.reg_alu(rt, rv, bits, ALU.not_equal)
                    g.reg_alu(rd, rd, rt, ALU.bitwise_or)
        return g.nop()
    p.q("gpsimd", spin, inc=tag + "ccok", waits=[(wait_ch, wait_v)])
    return p.val(tag + "ccok")



class LNBufs:
    def __init__(self, E, nc, tag, with_psum=True):
        self.x_rt = [E(nc.sbuf_tensor(f"{tag}x0", [128, D], F32))]
        self.scr = E(nc.sbuf_tensor(f"{tag}scr", [128, D], F32))
        self.xn_bf = E(nc.sbuf_tensor(f"{tag}xnbf", [128, D], BF16))
        self.cols = E(nc.sbuf_tensor(f"{tag}cols", [128, 8], F32))
        if with_psum:
            self.ps_tr = [E(nc.psum_tensor(f"{tag}pstr{i}", [128, 512], BF16))
                          for i in range(2)]


def emit_ln(p, lb, ident_bf, x_dram, xnT, tag, xn_tok_f32=None, xn_tok_bf=None,
            keep_x_sb=None):
    c_sum = lb.cols[:, 0:1]; c_sq = lb.cols[:, 1:2]; c_mean = lb.cols[:, 2:3]
    c_var = lb.cols[:, 3:4]; c_srt = lb.cols[:, 4:5]; c_msq = lb.cols[:, 5:6]
    c_rstd = lb.cols[:, 6:7]; c_eps = lb.cols[:, 7:8]
    p.q("gpsimd", lambda g, c=c_eps: g.memset(c, EPS), inc=f"{tag}epsg")
    p.wait("scalar", f"{tag}epsg", p.val(f"{tag}epsg"))
    for r in range(RT):
        xa = lb.x_rt[0]
        xw = [(f"{tag}xfree", p.val(f"{tag}xfree"))] if r >= 1 else []
        dv = p.dma("sync", xa[:, :], x_dram[r * 128:(r + 1) * 128, :], f"{tag}xd",
                   waits=xw)
        sw = [(f"{tag}xd", dv)]
        if r >= 1:
            sw.append((f"{tag}sxn", p.val(f"{tag}sxn")))
        p.q("vector", lambda v, xa=xa, c=c_sum: v.reduce_sum(c, xa[:, :], axis=AX.X),
            inc=f"{tag}vsum", waits=sw)
        p.q("scalar", lambda s, xa=xa, sc=lb.scr:
            s.activation(sc[:, :], xa[:, :], AF.Square),
            inc=f"{tag}ssq", waits=sw)
        p.q("vector", lambda v, sc=lb.scr, c=c_sq: v.reduce_sum(c, sc[:, :], axis=AX.X),
            inc=f"{tag}ssq", waits=[(f"{tag}ssq", p.val(f"{tag}ssq"))])

        p.q("vector", lambda v, c=c_sum, mean=c_mean: v.tensor_scalar_mul(mean, c, 1.0 / D),
            inc=f"{tag}vm", waits=[(f"{tag}vsum", p.val(f"{tag}vsum"))])
        p.q("scalar", lambda s, mean=c_mean, msq=c_msq: s.activation(msq, mean, AF.Square),
            inc=f"{tag}sms", waits=[(f"{tag}vm", p.val(f"{tag}vm"))])
        p.q("vector", lambda v, c_sq=c_sq, msq=c_msq, var=c_var:
            v.scalar_tensor_tensor(var, c_sq, 1.0 / D, msq, ALU.mult, ALU.subtract),
            inc=f"{tag}vv",
            waits=[(f"{tag}sms", p.val(f"{tag}sms")), (f"{tag}ssq", p.val(f"{tag}ssq"))])
        p.q("scalar", lambda s, var=c_var, srt=c_srt, ce=c_eps:
            s.activation(srt, var, AF.Sqrt, bias=ce),
            inc=f"{tag}srt", waits=[(f"{tag}vv", p.val(f"{tag}vv"))])
        p.q("vector", lambda v, srt=c_srt, rstd=c_rstd: v.reciprocal(rstd, srt),
            inc=f"{tag}vfs", waits=[(f"{tag}srt", p.val(f"{tag}srt"))])
        vfs = p.val(f"{tag}vfs")
        xnw = [(f"{tag}vfs", vfs)]
        if xnT is not None and r >= 1:
            xnw.append((f"{tag}ttr", p.val(f"{tag}ttr")))
        p.q("vector", lambda v, xa=xa, xb=lb.xn_bf, rstd=c_rstd, mn=c_mean:
            v.tensor_scalar(xb[:, :], xa[:, :], mn, rstd, ALU.subtract, ALU.mult),
            inc=f"{tag}sxn", waits=xnw)
        if xn_tok_f32 is not None:
            p.q("vector", lambda v, xa=xa, o=xn_tok_f32, r=r, rstd=c_rstd, mn=c_mean:
                v.tensor_scalar(o[:, r * D:(r + 1) * D], xa[:, :], mn, rstd,
                                ALU.subtract, ALU.mult), inc=f"{tag}sxn")
        if xn_tok_bf is not None:
            p.q("vector", lambda v, xa=xa, o=xn_tok_bf, r=r, rstd=c_rstd, mn=c_mean:
                v.tensor_scalar(o[:, r * D:(r + 1) * D], xa[:, :], mn, rstd,
                                ALU.subtract, ALU.mult), inc=f"{tag}sxn")
        if keep_x_sb is not None:
            p.q("vector", lambda v, xa=xa, o=keep_x_sb, r=r:
                v.tensor_copy(o[:, r * D:(r + 1) * D], xa[:, :]), inc=f"{tag}sxn")
        p.q("scalar", lambda s: s.nop(), inc=f"{tag}xfree",
            waits=[(f"{tag}sxn", p.val(f"{tag}sxn"))])
        if xnT is not None:
            for g4 in range(4):
                G = r * 4 + g4
                pst = lb.ps_tr[G % 2]
                for j in range(4):
                    kc = g4 * 4 + j
                    tw = []
                    if j == 0:
                        tw.append((f"{tag}sxn", p.val(f"{tag}sxn")))
                        if G >= 2:
                            tw.append((f"{tag}trev", G - 1))
                    p.q("tensor", lambda t, ps=pst, j=j, kc=kc, xb=lb.xn_bf, ib=ident_bf:
                        t.transpose(ps[:, j * 128:(j + 1) * 128],
                                    xb[:, kc * 128:(kc + 1) * 128], ib[:, :]),
                        inc=f"{tag}ttr" if j == 3 else None, waits=tw)

                def evac(v, ps=pst, g4=g4, r=r, xnT=xnT):
                    for j in range(4):
                        kc = g4 * 4 + j
                        inst = v.tensor_copy(
                            xnT[:, kc * NTOK + r * 128: kc * NTOK + (r + 1) * 128],
                            ps[:, j * 128:(j + 1) * 128])
                    return inst
                p.q("vector", evac, inc=f"{tag}trev",
                    waits=[(f"{tag}ttr", p.val(f"{tag}ttr"))])
    if xnT is not None:
        p.wait("tensor", f"{tag}trev", p.val(f"{tag}trev"))
    p.wait("tensor", f"{tag}sxn", p.val(f"{tag}sxn"))



def emit_ffn(p, E, x_dram, y_dram, wg_sl, wu_sl, wd_sl, tag):
    nc = p.nc
    ident_bf = E(nc.sbuf_tensor(tag + "id", [128, 128], BF16))
    emit_ident(p, ident_bf, tag + "mi")

    xnT = E(nc.sbuf_tensor(tag + "xnT", [128, KC * NTOK], BF16))
    h_buf = E(nc.sbuf_tensor(tag + "h", [128, HC * NTOK], BF16))
    wbuf = [E(nc.sbuf_tensor(f"{tag}w{i}", [128, D], BF16)) for i in range(4)]
    sil = [E(nc.sbuf_tensor(f"{tag}sil{i}", [128, 512], BF16)) for i in range(2)]
    y_sb = [E(nc.sbuf_tensor(f"{tag}y{i}", [128, 512], F32)) for i in range(2)]
    x2_sb = [E(nc.sbuf_tensor(f"{tag}x2{i}", [128, 512], F32)) for i in range(2)]

    with ExitStack() as ctxA:
        EA = ctxA.enter_context
        ln = LNBufs(EA, nc, tag + "L")
        ps_g = [EA(nc.psum_tensor(f"{tag}psg{i}", [128, 512], F32)) for i in range(2)]
        ps_u = [EA(nc.psum_tensor(f"{tag}psu{i}", [128, 512], F32)) for i in range(2)]

        emit_ln(p, ln, ident_bf, x_dram, xnT, tag + "n")

        NT = NTOK // 512
        for hc in range(HC):
            sg = wbuf[(hc % 2) * 2]
            su = wbuf[(hc % 2) * 2 + 1]
            war = [(tag + "mmu", NT * (hc - 1))] if hc >= 2 else []
            p.dma("gpsimd", sg[:, :], wg_sl[hc], tag + "wd", waits=war)
            du = p.dma("gpsimd", su[:, :], wu_sl[hc], tag + "wd")
            for nt in range(NT):
                G = hc * NT + nt
                w0 = []
                if nt == 0:
                    w0.append((tag + "wd", du))
                if hc >= 1:
                    w0 += [(tag + "sil", G - 1), (tag + "hready", G - 1)]
                for kcc in range(KC):
                    p.q("tensor", lambda t, pg=ps_g[nt], w=sg, kcc=kcc, nt=nt:
                        t.matmul(pg[:, :], w[:, kcc * 128:(kcc + 1) * 128],
                                 xnT[:, kcc * NTOK + nt * 512: kcc * NTOK + (nt + 1) * 512],
                                 start=(kcc == 0), stop=(kcc == KC - 1)),
                        inc=(tag + "mmg") if kcc == KC - 1 else None,
                        waits=w0 if kcc == 0 else ())
                for kcc in range(KC):
                    p.q("tensor", lambda t, pu=ps_u[nt], w=su, kcc=kcc, nt=nt:
                        t.matmul(pu[:, :], w[:, kcc * 128:(kcc + 1) * 128],
                                 xnT[:, kcc * NTOK + nt * 512: kcc * NTOK + (nt + 1) * 512],
                                 start=(kcc == 0), stop=(kcc == KC - 1)),
                        inc=(tag + "mmu") if kcc == KC - 1 else None)
                mg = p.val(tag + "mmg")
                mu = p.val(tag + "mmu")
                silw = [(tag + "mmg", mg)]
                if hc >= 1:
                    silw.append((tag + "hready", G - 1))
                p.q("scalar", lambda s, pg=ps_g[nt], si=sil[nt]:
                    s.activation(si[:, :], pg[:, :], AF.Silu),
                    inc=tag + "sil", waits=silw)
                p.q("vector", lambda v, pu=ps_u[nt], si=sil[nt], hc=hc, nt=nt:
                    v.tensor_tensor(h_buf[:, hc * NTOK + nt * 512: hc * NTOK + (nt + 1) * 512],
                                    si[:, :], pu[:, :], ALU.mult),
                    inc=tag + "hready",
                    waits=[(tag + "sil", p.val(tag + "sil")), (tag + "mmu", mu)])
            p.q("tensor", lambda t: t.nop(), inc=tag + "wused")
        hall = p.val(tag + "hready")
        p.wait("tensor", tag + "hready", hall)
        p.wait("gpsimd", tag + "hready", hall)
    emit_out_proj(p, E, h_buf, wd_sl, x_dram, y_dram, HC, wbuf, x2_sb, y_sb, tag)
    return {"xnT": xnT, "h": h_buf}




def emit_out_proj(p, E, act_buf, w_sl, x_dram, y_dram, MC, wbuf, x2_sb, y_sb,
                  tag, bias128=None, act_ix=None):
    if act_ix is None:
        act_ix = lambda hk, r: hk * NTOK + r * 128
    nc = p.nc
    ps_d = [E(nc.psum_tensor(f"{tag}psd{i}", [128, 512], F32)) for i in range(8)]
    junk = E(nc.sbuf_tensor(tag + "jnk", [128, 4], F32))
    nw = min(2, len(wbuf))
    for nd in range(4):
        for hk in range(MC):
            J = nd * MC + hk
            slot = wbuf[J % nw]
            war = [(tag + "dwu", J - 1)] if J >= 2 else []
            dw = p.dma("gpsimd", slot[:, 0:512], w_sl[nd, hk], tag + "dwd", waits=war)
            for r in range(RT):
                waits = [(tag + "dwd", dw)] if r == 0 else []
                if hk == 0 and nd >= 1:
                    waits.append((tag + "dev", (nd - 1) * RT + r + 1))
                if r == RT - 1:
                    inc = tag + "dwu"
                elif hk == MC - 1:
                    inc = tag + "dmm"
                else:
                    inc = None
                aix = act_ix(hk, r)
                p.q("tensor", lambda t, pd=ps_d[r], slot=slot, aix=aix, hk=hk:
                    t.matmul(pd[:, :], act_buf[:, aix:aix + 128],
                             slot[:, 0:512], start=(hk == 0), stop=(hk == MC - 1)),
                    inc=inc, waits=waits)
        for r in range(RT):
            Ga = nd * RT + r
            xs = x2_sb[r % 2]
            xw = [(tag + "dev", Ga - 1)] if Ga >= 2 else []
            dx = p.dma("sync", xs[:, :],
                       x_dram[r * 128:(r + 1) * 128, nd * 512:(nd + 1) * 512],
                       tag + "xr", waits=xw)
            if bias128 is not None:
                p.q("vector", lambda v, xs=xs, nd=nd:
                    v.tensor_add(xs[:, :], xs[:, :],
                                 bias128[:, nd * 512:(nd + 1) * 512]),
                    inc=tag + "xb", waits=[(tag + "xr", dx)])
            ys = y_sb[r % 2]
            if r < RT - 1:
                stopw = (tag + "dmm", nd * (RT - 1) + r + 1)
            else:
                stopw = (tag + "dwu", (nd + 1) * MC)
            vw = [stopw, (tag + "xr", dx)]
            if bias128 is not None:
                vw.append((tag + "xb", p.val(tag + "xb")))
            if Ga >= 2:
                vw.append((tag + "yw", 16 * (Ga - 1)))
            p.q("vector", lambda v, pd=ps_d[r]:
                v.tensor_copy(junk[:, :], pd[:, 0:4]), waits=vw)
            p.q("vector", lambda v, ys=ys, xs=xs, pd=ps_d[r]:
                v.tensor_add(ys[:, :], xs[:, :], pd[:, :]),
                inc=tag + "dev", waits=vw)
            p.dma("sync", y_dram[r * 128:(r + 1) * 128, nd * 512:(nd + 1) * 512],
                  ys[:, :], tag + "yw", waits=[(tag + "dev", p.val(tag + "dev"))])
    p.wait("sync", tag + "yw", p.val(tag + "yw"))
    p.wait("tensor", tag + "dev", p.val(tag + "dev"))



def emit_ema(p, E, x_dram, y_dram, wv_sl, wg_sl, wq_sl, wo_sl, bgq, dv128, flag,
             cc_in, cc_out, tag):
    nc = p.nc
    SC = float(np.sqrt(1.0 - 0.95))
    ident_bf = E(nc.sbuf_tensor(tag + "id", [128, 128], BF16))
    emit_ident(p, ident_bf, tag + "mi")
    xnT = E(nc.sbuf_tensor(tag + "xnT", [128, KC * NTOK], BF16))
    ylb = E(nc.sbuf_tensor(tag + "ylb", [128, KC * NTOK], BF16))
    wbuf = [E(nc.sbuf_tensor(f"{tag}w{i}", [128, D], BF16)) for i in range(4)]
    s_sb = E(nc.sbuf_tensor(tag + "s", [128, NTOK], F32))
    u1 = E(nc.sbuf_tensor(tag + "u1", [128, NTOK], F32))
    t1 = E(nc.sbuf_tensor(tag + "t1", [128, NTOK], F32))
    qs = E(nc.sbuf_tensor(tag + "qs", [128, NTOK], F32))
    gat = E(nc.sbuf_tensor(tag + "gat", [128, NTOK], F32))
    dec = E(nc.sbuf_tensor(tag + "dec", [128, NTOK], F32))
    dvs = E(nc.sbuf_tensor(tag + "dvs", [128, NTOK], F32))
    ylast = E(nc.sbuf_tensor(tag + "ylast", [128, KC], F32))
    carry = E(nc.sbuf_tensor(tag + "carry", [128, KC], F32))
    carry2 = E(nc.sbuf_tensor(tag + "carry2", [128, KC], F32))
    flg = E(nc.sbuf_tensor(tag + "flg", [128, 1], F32))
    bgq_sb = E(nc.sbuf_tensor(tag + "bgq", [128, 2 * KC], F32))
    y_sb = [E(nc.sbuf_tensor(f"{tag}y{i}", [128, 512], F32)) for i in range(2)]
    x2_sb = [E(nc.sbuf_tensor(f"{tag}x2{i}", [128, 512], F32)) for i in range(2)]

    p.dma("sync", flg[:, :], flag, tag + "cd")
    p.dma("sync", bgq_sb[:, :], bgq, tag + "cd")
    cd = p.dma("sync", dvs[:, :], dv128, tag + "cd")
    p.q("gpsimd", lambda g: g.memset(dec[:, :], 0.95), inc=tag + "cg")
    p.wait("vector", tag + "cg", p.val(tag + "cg"))
    p.wait("vector", tag + "cd", cd)
    p.wait("scalar", tag + "cd", cd)

    NT = NTOK // 512
    with ExitStack() as ctxA:
        EA = ctxA.enter_context
        ln = LNBufs(EA, nc, tag + "L")
        ps_v = [EA(nc.psum_tensor(f"{tag}psv{i}", [128, 512], F32)) for i in range(2)]
        ps_g = [EA(nc.psum_tensor(f"{tag}psg{i}", [128, 512], F32)) for i in range(2)]

        emit_ln(p, ln, ident_bf, x_dram, xnT, tag + "n")

        for mc in range(KC):
            sv = wbuf[(mc % 2) * 2]
            sg = wbuf[(mc % 2) * 2 + 1]
            dv_ = p.dma("gpsimd", sv[:, :], wv_sl[mc], tag + "vwd",
                        waits=[(tag + "mmv", NT * (mc - 1))] if mc >= 2 else [])
            dg_ = p.dma("gpsimd", sg[:, :], wg_sl[mc], tag + "gwd",
                        waits=[(tag + "mmg", NT * (mc - 1))] if mc >= 2 else [])
            for nt in range(NT):
                G = mc * NT + nt
                wv0 = [(tag + "vwd", dv_)] if nt == 0 else []
                if mc >= 1:
                    wv0.append((tag + "sready", G - 1))
                for kcc in range(KC):
                    p.q("tensor", lambda t, pv=ps_v[nt], w=sv, kcc=kcc, nt=nt:
                        t.matmul(pv[:, :], w[:, kcc * 128:(kcc + 1) * 128],
                                 xnT[:, kcc * NTOK + nt * 512: kcc * NTOK + (nt + 1) * 512],
                                 start=(kcc == 0), stop=(kcc == KC - 1)),
                        inc=(tag + "mmv") if kcc == KC - 1 else None,
                        waits=wv0 if kcc == 0 else ())
                wg0 = [(tag + "gwd", dg_)] if nt == 0 else []
                if mc >= 1:
                    wg0.append((tag + "gact", G - 1))
                for kcc in range(KC):
                    p.q("tensor", lambda t, pg=ps_g[nt], w=sg, kcc=kcc, nt=nt:
                        t.matmul(pg[:, :], w[:, kcc * 128:(kcc + 1) * 128],
                                 xnT[:, kcc * NTOK + nt * 512: kcc * NTOK + (nt + 1) * 512],
                                 start=(kcc == 0), stop=(kcc == KC - 1)),
                        inc=(tag + "mmg") if kcc == KC - 1 else None,
                        waits=wg0 if kcc == 0 else ())
                mv = p.val(tag + "mmv")
                mg = p.val(tag + "mmg")
                gaw = [(tag + "mmg", mg)]
                if mc >= 1:
                    gaw.append((tag + "sready", G - 1))
                p.q("scalar", lambda s, pg=ps_g[nt], nt=nt, mc=mc:
                    s.activation(gat[:, nt * 512:(nt + 1) * 512], pg[:, :], AF.Sigmoid,
                                 bias=bgq_sb[:, mc:mc + 1]),
                    inc=tag + "gact", waits=gaw)
                p.q("vector", lambda v, pv=ps_v[nt], nt=nt:
                    v.scalar_tensor_tensor(s_sb[:, nt * 512:(nt + 1) * 512], pv[:, :],
                                           SC, gat[:, nt * 512:(nt + 1) * 512],
                                           ALU.mult, ALU.mult),
                    inc=tag + "sready",
                    waits=[(tag + "gact", p.val(tag + "gact")), (tag + "mmv", mv)])
            p.q("vector", lambda v, mc=mc:
                v.tensor_tensor_scan(ylb[:, mc * NTOK:(mc + 1) * NTOK], dec[:, :],
                                     s_sb[:, :], 0.0, ALU.mult, ALU.add),
                inc=tag + "scan", waits=[(tag + "sready", p.val(tag + "sready"))])
            p.q("vector", lambda v, mc=mc:
                v.tensor_copy(ylast[:, mc:mc + 1],
                              ylb[:, mc * NTOK + NTOK - 1: mc * NTOK + NTOK]),
                inc=tag + "ylst", waits=[(tag + "scan", p.val(tag + "scan"))])
        p.wait("tensor", tag + "sready", p.val(tag + "sready"))
    p.dma("sync", cc_in[0:128, :], ylast[:, :], tag + "agd",
          waits=[(tag + "ylst", p.val(tag + "ylst"))])
    nn = emit_cc_nonce(p, E, tag,
                       cc_in[128:132, :].rearrange("a b -> (a b)").rearrange(
                           "(o f) -> o f", o=1), NONCE_BASE + 64)
    agd = p.val(tag + "agd")
    p.q("gpsimd", lambda g: g.collective_compute(
        "AllGather", ALU.bypass, replica_groups=PAIRS, ins=[cc_in.opt()],
        outs=[cc_out.opt()]), inc=tag + "agc",
        waits=[(tag + "agd", agd), (tag + "nncd", nn)])
    gok = emit_cc_guard(p, E, tag,
                        cc_out[131:132, 12:16], NONCE_BASE + 64,
                        tag + "agc", p.val(tag + "agc"))
    agr = p.dma("sync", carry[:, :], cc_out[0:128, :], tag + "agr",
                waits=[(tag + "ccok", gok)])
    p.q("vector", lambda v: v.tensor_scalar_mul(carry2[:, :], carry[:, :], flg[:, 0:1]),
        inc=tag + "cready", waits=[(tag + "agr", agr)])
    with ExitStack() as ctxB:
        EB = ctxB.enter_context
        ps_q = [EB(nc.psum_tensor(f"{tag}psq{i}", [128, 512], F32)) for i in range(2)]
        for mc in range(KC):
            sq = wbuf[mc % 2]
            dq_ = p.dma("gpsimd", sq[:, :], wq_sl[mc], tag + "qwd",
                        waits=[(tag + "mmq", NT * (mc - 1))] if mc >= 2 else
                              [(tag + "mmv", NT * KC), (tag + "mmg", NT * KC)])
            for nt in range(NT):
                G = mc * NT + nt
                wq0 = [(tag + "qwd", dq_)] if nt == 0 else []
                if mc >= 1:
                    wq0.append((tag + "qact", G - 1))
                for kcc in range(KC):
                    p.q("tensor", lambda t, pq=ps_q[nt], w=sq, kcc=kcc, nt=nt:
                        t.matmul(pq[:, :], w[:, kcc * 128:(kcc + 1) * 128],
                                 xnT[:, kcc * NTOK + nt * 512: kcc * NTOK + (nt + 1) * 512],
                                 start=(kcc == 0), stop=(kcc == KC - 1)),
                        inc=(tag + "mmq") if kcc == KC - 1 else None,
                        waits=wq0 if kcc == 0 else ())
                qaw = [(tag + "mmq", p.val(tag + "mmq"))]
                if mc >= 1:
                    qaw.append((tag + "ldv", mc))
                p.q("scalar", lambda s, pq=ps_q[nt], nt=nt, mc=mc:
                    s.activation(qs[:, nt * 512:(nt + 1) * 512], pq[:, :], AF.Sigmoid,
                                 bias=bgq_sb[:, KC + mc:KC + mc + 1]),
                    inc=tag + "qact", waits=qaw)
            p.q("vector", lambda v, mc=mc:
                v.tensor_scalar_mul(t1[:, :], dvs[:, :], carry2[:, mc:mc + 1]),
                inc=tag + "t1v", waits=[(tag + "cready", p.val(tag + "cready"))])
            p.q("vector", lambda v, mc=mc:
                v.tensor_add(u1[:, :], ylb[:, mc * NTOK:(mc + 1) * NTOK], t1[:, :]),
                inc=tag + "u1v", waits=[(tag + "t1v", p.val(tag + "t1v"))])
            p.q("vector", lambda v, mc=mc:
                v.scalar_tensor_tensor(ylb[:, mc * NTOK:(mc + 1) * NTOK], qs[:, :],
                                       SC, u1[:, :], ALU.mult, ALU.mult),
                inc=tag + "ldv",
                waits=[(tag + "u1v", p.val(tag + "u1v")),
                       (tag + "qact", p.val(tag + "qact"))])
        p.wait("tensor", tag + "ldv", p.val(tag + "ldv"))
        p.wait("gpsimd", tag + "mmq", p.val(tag + "mmq"))
    emit_out_proj(p, E, ylb, wo_sl, x_dram, y_dram, KC, wbuf, x2_sb, y_sb, tag)


def build_ema_launch():
    nc = bass.Bass("TRN2", target_bir_lowering=False)
    x_in = nc.dram_tensor("x_in", [NTOK, D], F32, kind="ExternalInput").ap()
    wv = nc.dram_tensor("wv", [KC, 128, D], F32, kind="ExternalInput").ap()
    wg = nc.dram_tensor("wg", [KC, 128, D], F32, kind="ExternalInput").ap()
    wq = nc.dram_tensor("wq", [KC, 128, D], F32, kind="ExternalInput").ap()
    wo = nc.dram_tensor("wo", [4, KC, 128, 512], F32, kind="ExternalInput").ap()
    bgq = nc.dram_tensor("bgq", [128, 2 * KC], F32, kind="ExternalInput").ap()
    dv128 = nc.dram_tensor("dv128", [128, NTOK], F32, kind="ExternalInput").ap()
    flag = nc.dram_tensor("flag", [128, 1], F32, kind="ExternalInput").ap()
    y = nc.dram_tensor("y", [NTOK, D], F32, kind="ExternalOutput").ap()
    cc_in = nc.dram_tensor("cc_in", [132, KC], F32, kind="Internal").ap()
    cc_out = nc.dram_tensor("cc_out", [264, KC], F32, kind="Internal").ap()
    with ExitStack() as ctx:
        p = Prog(nc, ctx.enter_context)
        emit_ema(p, ctx.enter_context, x_in, y, wv, wg, wq, wo, bgq, dv128, flag,
                 cc_in, cc_out, "e")
        p.realize()
    return nc


def ema_maps(Wv, Wg, bg, Wq, bq, Wo):
    bgq = np.zeros((128, 2 * KC), np.float32)
    bgq[:, 0:KC] = np.asarray(bg).reshape(KC, 128).T
    bgq[:, KC:2 * KC] = np.asarray(bq).reshape(KC, 128).T
    dv = (0.95 ** (np.arange(NTOK, dtype=np.float64) + 1)).astype(np.float32)
    dv128 = np.broadcast_to(dv, (128, NTOK)).copy()
    return {
        "wv": lhsT_slabs(np.ascontiguousarray(np.asarray(Wv).T)),
        "wg": lhsT_slabs(np.ascontiguousarray(np.asarray(Wg).T)),
        "wq": lhsT_slabs(np.ascontiguousarray(np.asarray(Wq).T)),
        "wo": rhs_slabs(np.asarray(Wo)),
        "bgq": bgq, "dv128": dv128,
    }





def emit_deltanet(p, E, x_dram, y_dram, wqkbd_sl, wv_sl, wo_sl, bbd, flag,
                  cc_in, cc_out, tag):
    nc = p.nc
    C = 64
    NCH = NTOK // C
    ch = lambda n: tag + n

    ident_bf = E(nc.sbuf_tensor(tag + "id", [128, 128], BF16))
    emit_ident(p, ident_bf, tag + "mi")
    ident_f = E(nc.sbuf_tensor(tag + "idf", [128, 128], F32))
    p.q("gpsimd", lambda g: g.memset(ident_f[:, :], 0.0), inc=ch("cg"))
    p.q("gpsimd", lambda g: g.affine_select(
        ident_f[:, :], ident_f[:, :], compare_op=ALU.not_equal, fill=1.0, base=0,
        pattern=[[-1, 128]], channel_multiplier=1), inc=ch("cg"))
    triU = E(nc.sbuf_tensor(tag + "triU", [64, 64], F32))
    p.q("gpsimd", lambda g: g.memset(triU[:, :], 1.0), inc=ch("cg"))
    p.q("gpsimd", lambda g: g.affine_select(
        triU[:, :], triU[:, :], compare_op=ALU.is_ge, fill=0.0, base=0,
        pattern=[[1, 64]], channel_multiplier=-1), inc=ch("cg"))
    mask_u8 = E(nc.sbuf_tensor(tag + "msk", [64, 64], mybir.dt.int8))
    p.q("gpsimd", lambda g: g.memset(mask_u8[:, :], 1), inc=ch("cg"))
    p.q("gpsimd", lambda g: g.affine_select(
        mask_u8[:, :], mask_u8[:, :], compare_op=ALU.is_ge, fill=0, base=0,
        pattern=[[1, 64]], channel_multiplier=-1), inc=ch("cg"))
    zer_f = E(nc.sbuf_tensor(tag + "zer", [64, 64], F32))
    p.q("gpsimd", lambda g: g.memset(zer_f[:, :], 0.0), inc=ch("cg"))
    onecol = E(nc.sbuf_tensor(tag + "onec", [64, 1], F32))
    p.q("gpsimd", lambda g: g.memset(onecol[:, :], 1.0), inc=ch("cg"))
    cgv = p.val(ch("cg"))
    for e in ("tensor", "vector", "scalar"):
        p.wait(e, ch("cg"), cgv)

    xnT = E(nc.sbuf_tensor(tag + "xnT", [128, KC * NTOK], BF16))
    pw = E(nc.sbuf_tensor(tag + "pw", [128, 4 * KC * 64], BF16))
    pwv = E(nc.sbuf_tensor(tag + "pwv", [128, KC * 128], BF16))
    q_f = E(nc.sbuf_tensor(tag + "qf", [64, NTOK], F32))
    k_f = E(nc.sbuf_tensor(tag + "kf", [64, NTOK], F32))
    zbd = E(nc.sbuf_tensor(tag + "zbd", [64, 2 * NTOK], F32))
    escr = E(nc.sbuf_tensor(tag + "escr", [64, 2 * NTOK], F32))
    kd = E(nc.sbuf_tensor(tag + "kd", [64, NTOK], F32))
    v_bf = E(nc.sbuf_tensor(tag + "vbf", [128, NTOK], BF16))
    vT = E(nc.sbuf_tensor(tag + "vT", [64, NCH * 128], BF16))
    qt_buf = E(nc.sbuf_tensor(tag + "qt", [64, NTOK], BF16))
    kt_sb = E(nc.sbuf_tensor(tag + "kt", [64, 64], BF16))
    ktp_sb = E(nc.sbuf_tensor(tag + "ktp", [64, 64], BF16))
    ktpT_sb = E(nc.sbuf_tensor(tag + "ktpT", [64, 64], BF16))
    P_sb = E(nc.sbuf_tensor(tag + "P", [64, 64], BF16))
    lbT_sb = E(nc.sbuf_tensor(tag + "lbT", [64, 64], F32))
    LT_sb = E(nc.sbuf_tensor(tag + "LT", [64, 64], F32))
    L_sb = E(nc.sbuf_tensor(tag + "Lsb", [64, 64], F32))
    eL = E(nc.sbuf_tensor(tag + "eL", [64, 64], F32))
    enL = E(nc.sbuf_tensor(tag + "enL", [64, 64], F32))
    elc = E(nc.sbuf_tensor(tag + "elc", [64, 64], F32))
    Lc_cols = E(nc.sbuf_tensor(tag + "Lc", [64, NCH], F32))
    Gpre = E(nc.sbuf_tensor(tag + "Gp", [64, NCH + 1], F32))
    eGc = E(nc.sbuf_tensor(tag + "eGc", [64, 2], F32))
    S_sb = E(nc.sbuf_tensor(tag + "S", [64, 128], F32))
    S_bf = E(nc.sbuf_tensor(tag + "Sbf", [64, 128], BF16))
    Ssd = E(nc.sbuf_tensor(tag + "Ssd", [64, 128], F32))
    Scor = E(nc.sbuf_tensor(tag + "Scor", [64, 128], BF16))
    oT = E(nc.sbuf_tensor(tag + "oT", [128, NTOK], BF16))
    flg = E(nc.sbuf_tensor(tag + "flg", [128, 1], F32))
    bbd_sb = E(nc.sbuf_tensor(tag + "bbd", [64, 2], F32))
    y_sb = [E(nc.sbuf_tensor(f"{tag}y{i}", [128, 512], F32)) for i in range(2)]
    x2_sb = [E(nc.sbuf_tensor(f"{tag}x2{i}", [128, 512], F32)) for i in range(2)]
    wbuf = [E(nc.sbuf_tensor(f"{tag}w{i}", [128, 512], BF16)) for i in range(2)]

    p.dma("sync", flg[:, :], flag, ch("cd"))
    p.dma("sync", bbd_sb[:, :], bbd, ch("cd"))
    for i in range(4):
        p.dma("gpsimd", pw[:, i * KC * 64:(i + 1) * KC * 64], wqkbd_sl[i], ch("pwd"))
    p.dma("gpsimd", pwv[:, :], wv_sl, ch("pwd"))
    cdv = p.val(ch("cd"))
    p.wait("scalar", ch("cd"), cdv)
    p.wait("vector", ch("cd"), cdv)

    def step(eng, fn, extra=()):
        k = p.val(ch("sq"))
        p.q(eng, fn, inc=ch("sq"), waits=[(ch("sq"), k)] + list(extra))

    with ExitStack() as ctxLN:
        ln = LNBufs(ctxLN.enter_context, nc, tag + "L")
        emit_ln(p, ln, ident_bf, x_dram, xnT, tag + "n")

    with ExitStack() as ctxA:
        EA = ctxA.enter_context
        ps_p = [EA(nc.psum_tensor(f"{tag}psp{i}", [128, 512], F32)) for i in range(2)]
        ps_fA = EA(nc.psum_tensor(tag + "psfA", [64, 64], F32))
        ps_fB = EA(nc.psum_tensor(tag + "psfB", [64, 64], F32))
        ps_sc = EA(nc.psum_tensor(tag + "pssc", [64, 64], F32))
        ps_o = EA(nc.psum_tensor(tag + "pso", [128, 64], F32))
        ps_ds = EA(nc.psum_tensor(tag + "psds", [64, 128], F32))
        ps_bt = EA(nc.psum_tensor(tag + "psbt", [64, 128], BF16))

        p.wait("tensor", ch("pwd"), p.val(ch("pwd")))
        NT = NTOK // 512
        for gi in range(4):
            for nt in range(NT):
                G = gi * NT + nt
                pp = ps_p[G % 2]
                wts = [(ch("pev"), G - 1)] if G >= 2 else []
                for kcc in range(KC):
                    p.q("tensor", lambda t, pp=pp, gi=gi, kcc=kcc, nt=nt:
                        t.matmul(pp[0:64, :],
                                 pw[:, gi * KC * 64 + kcc * 64: gi * KC * 64 + (kcc + 1) * 64],
                                 xnT[:, kcc * NTOK + nt * 512: kcc * NTOK + (nt + 1) * 512],
                                 start=(kcc == 0), stop=(kcc == KC - 1)),
                        inc=ch("pmm") if kcc == KC - 1 else None,
                        waits=wts if kcc == 0 else ())
                pv = p.val(ch("pmm"))
                if gi == 0:
                    p.q("vector", lambda v, pp=pp, nt=nt:
                        v.tensor_copy(q_f[:, nt * 512:(nt + 1) * 512], pp[0:64, :]),
                        inc=ch("pev"), waits=[(ch("pmm"), pv)])
                elif gi == 1:
                    p.q("vector", lambda v, pp=pp, nt=nt:
                        v.tensor_copy(k_f[:, nt * 512:(nt + 1) * 512], pp[0:64, :]),
                        inc=ch("pev"), waits=[(ch("pmm"), pv)])
                else:
                    off = 0 if gi == 2 else NTOK
                    bcol = gi - 2
                    p.q("vector", lambda v, pp=pp, nt=nt, off=off, bcol=bcol:
                        v.tensor_scalar_add(zbd[:, off + nt * 512: off + (nt + 1) * 512],
                                            pp[0:64, :], bbd_sb[:, bcol:bcol + 1]),
                        inc=ch("pev"), waits=[(ch("pmm"), pv)])
        for nt in range(NT):
            G = 4 * NT + nt
            pp = ps_p[G % 2]
            for kcc in range(KC):
                p.q("tensor", lambda t, pp=pp, kcc=kcc, nt=nt:
                    t.matmul(pp[:, :], pwv[:, kcc * 128:(kcc + 1) * 128],
                             xnT[:, kcc * NTOK + nt * 512: kcc * NTOK + (nt + 1) * 512],
                             start=(kcc == 0), stop=(kcc == KC - 1)),
                    inc=ch("pmm") if kcc == KC - 1 else None,
                    waits=[(ch("pev"), G - 1)] if kcc == 0 else ())
            p.q("vector", lambda v, pp=pp, nt=nt:
                v.tensor_copy(v_bf[:, nt * 512:(nt + 1) * 512], pp[:, :]),
                inc=ch("pev"), waits=[(ch("pmm"), p.val(ch("pmm")))])
        pe_all = p.val(ch("pev"))

        step("scalar", lambda s: s.activation(escr[:, :], zbd[:, :], AF.Abs),
             extra=[(ch("pev"), pe_all)])
        step("scalar", lambda s: s.activation(escr[:, :], escr[:, :], AF.Exp, scale=-1.0))
        step("scalar", lambda s: s.activation(escr[:, :], escr[:, :], AF.Ln,
                                              bias=onecol[:, 0:1]))
        step("vector", lambda v: v.tensor_scalar_min(zbd[:, 0:NTOK], zbd[:, 0:NTOK], 0.0))
        step("vector", lambda v: v.tensor_scalar_max(zbd[:, NTOK:], zbd[:, NTOK:], 0.0))
        step("vector", lambda v: v.tensor_sub(zbd[:, 0:NTOK], zbd[:, 0:NTOK],
                                              escr[:, 0:NTOK]))
        step("vector", lambda v: v.tensor_add(zbd[:, NTOK:], zbd[:, NTOK:],
                                              escr[:, NTOK:]))
        step("vector", lambda v: v.tensor_mul(kd[:, :], k_f[:, :], zbd[:, NTOK:]))
        step("gpsimd", lambda g: g.memset(Gpre[:, :], 0.0))
        step("gpsimd", lambda g: g.memset(S_sb[:, :], 0.0))
        step("gpsimd", lambda g: g.memset(S_bf[:, :], 0.0))

        for c in range(NCH):
            t0 = c * C
            step("tensor", lambda t, t0=t0: t.transpose(
                ps_bt[:, :], v_bf[:, t0:t0 + C], ident_bf[:, :]))
            step("vector", lambda v, c=c: v.tensor_copy(
                vT[:, c * 128:(c + 1) * 128], ps_bt[0:64, :]))
            step("tensor", lambda t, t0=t0: t.transpose(
                ps_fA[:, :], zbd[:, t0:t0 + C], ident_f[0:64, 0:64]))
            step("vector", lambda v: v.tensor_copy(lbT_sb[:, :], ps_fA[:, :]))
            step("tensor", lambda t: t.matmul(ps_fB[:, :], triU[:, :], lbT_sb[:, :],
                                              start=True, stop=True))
            step("vector", lambda v: v.tensor_copy(LT_sb[:, :], ps_fB[:, :]))
            step("tensor", lambda t: t.transpose(ps_fA[:, :], LT_sb[:, :],
                                                 ident_f[0:64, 0:64]))
            step("vector", lambda v: v.tensor_copy(L_sb[:, :], ps_fA[:, :]))
            step("vector", lambda v, c=c: v.tensor_copy(Lc_cols[:, c:c + 1],
                                                        L_sb[:, C - 1:C]))
            step("vector", lambda v, c=c: v.tensor_add(Gpre[:, c + 1:c + 2],
                                                       Gpre[:, c:c + 1],
                                                       Lc_cols[:, c:c + 1]))
            step("scalar", lambda s: s.activation(eL[:, :], L_sb[:, :], AF.Exp))
            step("scalar", lambda s: s.activation(enL[:, :], L_sb[:, :], AF.Exp,
                                                  scale=-1.0))
            step("scalar", lambda s, c=c: s.activation(elc[:, :], L_sb[:, :], AF.Exp,
                                                       scale=-1.0,
                                                       bias=Lc_cols[:, c:c + 1]))
            step("scalar", lambda s, c=c: s.activation(eGc[:, 0:1],
                                                       Lc_cols[:, c:c + 1], AF.Exp))
            step("vector", lambda v, t0=t0: v.tensor_mul(qt_buf[:, t0:t0 + C],
                                                         q_f[:, t0:t0 + C], eL[:, :]))
            step("vector", lambda v, t0=t0: v.tensor_mul(kt_sb[:, :],
                                                         kd[:, t0:t0 + C], enL[:, :]))
            step("vector", lambda v, t0=t0: v.tensor_mul(ktp_sb[:, :],
                                                         kd[:, t0:t0 + C], elc[:, :]))
            step("tensor", lambda t, t0=t0: t.matmul(ps_sc[:, :], kt_sb[:, :],
                                                     qt_buf[:, t0:t0 + C],
                                                     start=True, stop=True))
            step("vector", lambda v: v.select(P_sb[:, :], mask_u8[:, :], ps_sc[:, :],
                                              zer_f[:, :]))
            step("tensor", lambda t, t0=t0: t.matmul(ps_o[:, 0:C], S_bf[:, :],
                                                     qt_buf[:, t0:t0 + C],
                                                     start=True, stop=False))
            step("tensor", lambda t, c=c: t.matmul(ps_o[:, 0:C],
                                                   vT[:, c * 128:(c + 1) * 128],
                                                   P_sb[:, :], start=False, stop=True))
            step("vector", lambda v, t0=t0: v.tensor_copy(oT[:, t0:t0 + C],
                                                          ps_o[:, 0:C]))
            step("tensor", lambda t: t.transpose(ps_bt[:, 0:64], ktp_sb[:, :],
                                                 ident_bf[0:64, 0:64]))
            step("vector", lambda v: v.tensor_copy(ktpT_sb[:, :], ps_bt[0:64, 0:64]))
            step("tensor", lambda t, c=c: t.matmul(ps_ds[:, :], ktpT_sb[:, :],
                                                   vT[:, c * 128:(c + 1) * 128],
                                                   start=True, stop=True))
            step("vector", lambda v: v.scalar_tensor_tensor(S_sb[:, :], S_sb[:, :],
                                                            eGc[:, 0:1], ps_ds[:, :],
                                                            ALU.mult, ALU.add))
            step("vector", lambda v: v.tensor_copy(S_bf[:, :], S_sb[:, :]))
        sq0 = p.val(ch("sq"))
        p.dma("sync", cc_in[0:64, :], S_sb[:, :], ch("agd"), waits=[(ch("sq"), sq0)])
        nn = emit_cc_nonce(p, EA, tag, cc_in[64:68, 64:128].rearrange(
            "a b -> (a b)").rearrange("(o f) -> o f", o=1) if False else None, NONCE_BASE + 0) if False else None
        ncd4 = EA(nc.sbuf_tensor(tag + "nc4", [4, 64], F32))
        p.q("gpsimd", lambda g: g.memset(ncd4[:, :], float(NONCE_BASE + 0)),
            inc=ch("nng"))
        nn = p.dma("sync", cc_in[64:68, 64:128], ncd4[:, :], tag + "nncd",
                   waits=[(ch("nng"), p.val(ch("nng")))])
        agd = p.val(ch("agd"))
        p.q("gpsimd", lambda g: g.collective_compute(
            "AllGather", ALU.bypass, replica_groups=PAIRS, ins=[cc_in.opt()],
            outs=[cc_out.opt()]), inc=ch("agc"),
            waits=[(ch("agd"), agd), (tag + "nncd", nn)])
        gok = emit_cc_guard(p, EA, tag,
                            cc_out[64:68, 124:128].rearrange("a (o b) -> a o b", o=1),
                            NONCE_BASE + 0, ch("agc"), p.val(ch("agc")), npts=4)
        agr = p.dma("sync", Ssd[:, :], cc_out[0:64, :], ch("agr"),
                    waits=[(tag + "ccok", gok)])
        step("vector", lambda v: v.tensor_scalar_mul(Ssd[:, :], Ssd[:, :],
                                                     flg[0:64, 0:1]),
             extra=[(ch("agr"), agr)])
        for c in range(NCH):
            t0 = c * C
            step("scalar", lambda s, c=c: s.activation(eGc[:, 1:2], Gpre[:, c:c + 1],
                                                       AF.Exp))
            step("vector", lambda v: v.tensor_scalar_mul(Scor[:, :], Ssd[:, :],
                                                         eGc[:, 1:2]))
            step("tensor", lambda t, t0=t0: t.matmul(ps_o[:, 0:C], Scor[:, :],
                                                     qt_buf[:, t0:t0 + C],
                                                     start=True, stop=True))
            step("vector", lambda v, t0=t0: v.tensor_add(oT[:, t0:t0 + C],
                                                         oT[:, t0:t0 + C],
                                                         ps_o[:, 0:C]))
        sqf = p.val(ch("sq"))
        p.wait("tensor", ch("sq"), sqf)
        p.wait("gpsimd", ch("sq"), sqf)
    emit_out_proj(p, E, oT, wo_sl, x_dram, y_dram, 1, wbuf, x2_sb, y_sb, tag)
    return {"q_f": q_f[:, :], "zbd": zbd[:, :], "qt": qt_buf[:, :], "oT": oT[:, :],
            "Lc": Lc_cols[:, :], "Sf": S_sb[:, :], "vT": vT[:, :], "kd": kd[:, :]}


def build_deltanet_launch(debug=False):
    nc = bass.Bass("TRN2", target_bir_lowering=False)
    x_in = nc.dram_tensor("x_in", [NTOK, D], F32, kind="ExternalInput").ap()
    wqkbd = nc.dram_tensor("wqkbd", [4, 128, KC * 64], F32, kind="ExternalInput").ap()
    wv = nc.dram_tensor("wv", [128, KC * 128], F32, kind="ExternalInput").ap()
    wo = nc.dram_tensor("wo", [4, 1, 128, 512], F32, kind="ExternalInput").ap()
    bbd = nc.dram_tensor("bbd", [64, 2], F32, kind="ExternalInput").ap()
    flag = nc.dram_tensor("flag", [128, 1], F32, kind="ExternalInput").ap()
    y = nc.dram_tensor("y", [NTOK, D], F32, kind="ExternalOutput").ap()
    cc_in = nc.dram_tensor("cc_in", [68, 128], F32, kind="Internal").ap()
    cc_out = nc.dram_tensor("cc_out", [136, 128], F32, kind="Internal").ap()
    dbg = {}
    if debug:
        for nm, sh in [("q_f", [64, NTOK]), ("zbd", [64, 2*NTOK]), ("qt", [64, NTOK]),
                       ("oT", [128, NTOK]), ("Lc", [64, NTOK // 64]),
                       ("Sf", [64, 128]), ("vT", [64, (NTOK//64) * 128]),
                       ("kd", [64, NTOK])]:
            dbg[nm] = nc.dram_tensor("dbg_" + nm, sh, F32, kind="ExternalOutput").ap()
    with ExitStack() as ctx:
        p = Prog(nc, ctx.enter_context)
        bufs = emit_deltanet(p, ctx.enter_context, x_in, y, wqkbd, wv, wo, bbd, flag,
                      cc_in, cc_out, "d")
        if debug:
            p.wait("gpsimd", "dyw", p.val("dyw"))
            for nm in dbg:
                p.dma("gpsimd", dbg[nm], bufs[nm], "dbgd")
            p.wait("gpsimd", "dbgd", p.val("dbgd"))
        p.realize()
    return nc


def lhsT_small(W):
    in_f, out_f = W.shape
    a = W.reshape(in_f // 128, 128, out_f).transpose(1, 0, 2)
    return np.ascontiguousarray(a.reshape(128, (in_f // 128) * out_f))


def deltanet_maps(Wq, Wk, Wv, Wo, Wb, bb, Wd, bd):
    wqkbd = np.stack([lhsT_small(np.asarray(w)) for w in (Wq, Wk, Wb, Wd)])
    bbd = np.stack([np.asarray(bb), np.asarray(bd)], axis=1).astype(np.float32)
    return {
        "wqkbd": wqkbd,
        "wv": lhsT_small(np.asarray(Wv)),
        "wo": rhs_slabs(np.asarray(Wo)).reshape(4, 1, 128, 512),
        "bbd": bbd,
    }





def emit_mha(p, E, x_dram, y_dram, wq_sl, wk_sl, wv_sl, wdense_sl, tag,
             cc_kin, cc_kout, cc_vin, cc_vout, logflag,
             rope=False, cos_in=None, sin_in=None, rmat_in=None,
             bqk=None, bv=None, bdense=None, bout=None, dbg=None):
    nc = p.nc
    NH = 16
    SCL = 1.0 / float(np.sqrt(128.0))
    ch = lambda n: tag + n

    ident_bf = E(nc.sbuf_tensor(tag + "id", [128, 128], BF16))
    emit_ident(p, ident_bf, tag + "mi")
    tri_f = E(nc.sbuf_tensor(tag + "tri", [128, 128], BF16))
    p.q("gpsimd", lambda g: g.memset(tri_f[:, :], 1.0), inc=ch("cg"))
    p.q("gpsimd", lambda g: g.affine_select(
        tri_f[:, :], tri_f[:, :], compare_op=ALU.is_ge, fill=0.0, base=0,
        pattern=[[-1, 128]], channel_multiplier=1), inc=ch("cg"))
    ones1 = E(nc.sbuf_tensor(tag + "on1", [1, 128], BF16))
    p.q("gpsimd", lambda g: g.memset(ones1[:, :], 1.0), inc=ch("cg"))
    cgv = p.val(ch("cg"))
    p.wait("vector", ch("cg"), cgv)
    p.wait("tensor", ch("cg"), cgv)

    cols = E(nc.sbuf_tensor(tag + "cls", [128, 16], F32))
    lfl = E(nc.sbuf_tensor(tag + "lfl", [128, 1], F32))
    p.dma("sync", lfl[:, :], logflag, ch("cd"))
    if bqk is not None:
        bqk_sb = E(nc.sbuf_tensor(tag + "bqk", [128, 32], F32))
        p.dma("sync", bqk_sb[:, :], bqk, ch("cd"))
    if bv is not None:
        bv_sb = E(nc.sbuf_tensor(tag + "bv", [128, 16], F32))
        p.dma("sync", bv_sb[:, :], bv, ch("cd"))
    bias_f = E(nc.sbuf_tensor(tag + "bef", [1, D], F32))
    if rope:
        cos_sb = E(nc.sbuf_tensor(tag + "cos", [32, NTOK], F32))
        sin_sb = E(nc.sbuf_tensor(tag + "sin", [32, NTOK], F32))
        rmat = E(nc.sbuf_tensor(tag + "rm", [32, 32], BF16))
        p.dma("sync", cos_sb[:, :], cos_in, ch("cd"))
        p.dma("sync", sin_sb[:, :], sin_in, ch("cd"))
        p.dma("gpsimd", rmat[:, :], rmat_in, ch("cd"))
    cdv = p.val(ch("cd"))
    for e in ("vector", "scalar", "tensor"):
        p.wait(e, ch("cd"), cdv)

    wbuf = [E(nc.sbuf_tensor(f"{tag}w{i}", [128, D], BF16)) for i in range(2)]
    o_dram = nc.dram_tensor(tag + "odr", [128, RT * D], BF16, kind="Internal").ap()

    NT = NTOK // 512
    with ExitStack() as ctxQ:
        EQ = ctxQ.enter_context
        qb = EQ(nc.sbuf_tensor(tag + "qb", [128, NH * NTOK], BF16))
        k_own = EQ(nc.sbuf_tensor(tag + "ko", [128, NH * NTOK], BF16))
        v_own = EQ(nc.sbuf_tensor(tag + "vo", [128, RT * D], BF16))
        with ExitStack() as ctxA:
            EA = ctxA.enter_context
            ln = LNBufs(EA, nc, tag + "L")
            xnT = EA(nc.sbuf_tensor(tag + "xnT", [128, KC * NTOK], BF16))
            v_head = EA(nc.sbuf_tensor(tag + "vh", [128, NTOK], BF16))
            ps_p = [EA(nc.psum_tensor(f"{tag}psp{i}", [128, 512], F32)) for i in range(2)]
            emit_ln(p, ln, ident_bf, x_dram, xnT, tag + "n")

            for which, wsl in (("q", wq_sl), ("k", wk_sl), ("v", wv_sl)):
                mmch = ch("mm" + which)
                evch = ch(which + "ev")
                for hd in range(NH):
                    slot = wbuf[hd % 2]
                    if hd >= 2:
                        war = [(mmch, NT * (hd - 1))]
                    elif which == "k":
                        war = [(ch("mmq"), NT * NH)]
                    elif which == "v":
                        war = [(ch("mmk"), NT * NH)]
                    else:
                        war = []
                    dwv = p.dma("gpsimd", slot[:, :], wsl[hd], ch(which + "wd"),
                                waits=war)
                    for nt in range(NT):
                        G = hd * NT + nt
                        w0 = [(ch(which + "wd"), dwv)] if nt == 0 else []
                        if G >= 2:
                            w0.append((evch, G - 1))
                        for kcc in range(KC):
                            p.q("tensor", lambda t, pp=ps_p[nt], w=slot, kcc=kcc, nt=nt:
                                t.matmul(pp[:, :], w[:, kcc * 128:(kcc + 1) * 128],
                                         xnT[:, kcc * NTOK + nt * 512: kcc * NTOK + (nt + 1) * 512],
                                         start=(kcc == 0), stop=(kcc == KC - 1)),
                                inc=mmch if kcc == KC - 1 else None,
                                waits=w0 if kcc == 0 else ())
                        mval = p.val(mmch)
                        if which == "q":
                            if bqk is not None:
                                p.q("vector", lambda v, pp=ps_p[nt], hd=hd, nt=nt:
                                    v.tensor_scalar(qb[:, hd * NTOK + nt * 512: hd * NTOK + (nt + 1) * 512],
                                                    pp[:, :], bqk_sb[:, hd:hd + 1], SCL,
                                                    ALU.add, ALU.mult),
                                    inc=evch, waits=[(mmch, mval)])
                            else:
                                p.q("vector", lambda v, pp=ps_p[nt], hd=hd, nt=nt:
                                    v.tensor_scalar_mul(qb[:, hd * NTOK + nt * 512: hd * NTOK + (nt + 1) * 512],
                                                        pp[:, :], SCL),
                                    inc=evch, waits=[(mmch, mval)])
                        elif which == "k":
                            if bqk is not None:
                                p.q("vector", lambda v, pp=ps_p[nt], hd=hd, nt=nt:
                                    v.tensor_scalar_add(k_own[:, hd * NTOK + nt * 512: hd * NTOK + (nt + 1) * 512],
                                                        pp[:, :], bqk_sb[:, 16 + hd:17 + hd]),
                                    inc=evch, waits=[(mmch, mval)])
                            else:
                                p.q("vector", lambda v, pp=ps_p[nt], hd=hd, nt=nt:
                                    v.tensor_copy(k_own[:, hd * NTOK + nt * 512: hd * NTOK + (nt + 1) * 512],
                                                  pp[:, :]),
                                    inc=evch, waits=[(mmch, mval)])
                        else:
                            vw_ = [(mmch, mval)]
                            if hd >= 1 and nt == 0:
                                vw_.append((ch("vte"), 2 * hd))
                            p.q("vector", lambda v, pp=ps_p[nt], nt=nt:
                                v.tensor_copy(v_head[:, nt * 512:(nt + 1) * 512],
                                              pp[:, :]),
                                inc=evch, waits=vw_)
                    if which == "v":
                        ve = p.val(evch)
                        for g4 in range(2):
                            Gt = hd * 2 + g4
                            for jj in range(4):
                                rr = g4 * 4 + jj
                                tw = [(evch, ve)] if jj == 0 else []
                                if jj == 0 and Gt >= 2:
                                    tw.append((ch("vte"), Gt - 1))
                                p.q("tensor", lambda t, rr=rr, jj=jj, Gt=Gt:
                                    t.transpose(ln.ps_tr[Gt % 2][:, jj * 128:(jj + 1) * 128],
                                                v_head[:, rr * 128:(rr + 1) * 128],
                                                ident_bf[:, :]),
                                    inc=ch("vtr") if jj == 3 else None, waits=tw)

                            def vevac(v, g4=g4, hd=hd, Gt=Gt):
                                for jj in range(4):
                                    rr = g4 * 4 + jj
                                    inst = v.tensor_copy(
                                        v_own[:, rr * D + hd * 128: rr * D + (hd + 1) * 128],
                                        ln.ps_tr[Gt % 2][:, jj * 128:(jj + 1) * 128])
                                return inst
                            p.q("vector", vevac, inc=ch("vte"),
                                waits=[(ch("vtr"), p.val(ch("vtr")))])
            if rope:
                ps_r = EA(nc.psum_tensor(tag + "psr", [32, NTOK], F32))
                rt1 = EA(nc.sbuf_tensor(tag + "rt1", [32, NTOK], F32))
                rt2 = EA(nc.sbuf_tensor(tag + "rt2", [32, NTOK], F32))
                p.wait("tensor", ch("qev"), NT * NH)
                p.wait("tensor", ch("kev"), NT * NH)
                for which, buf in (("q", qb), ("k", k_own)):
                    for hd in range(NH):
                        i = (0 if which == "q" else NH) + hd
                        for jh in range(2):
                            p.q("tensor", lambda t, buf=buf, hd=hd, jh=jh:
                                t.matmul(ps_r[:, jh * 512:(jh + 1) * 512], rmat[:, :],
                                         buf[0:32, hd * NTOK + jh * 512: hd * NTOK + (jh + 1) * 512],
                                         start=True, stop=True),
                                inc=ch("rmm") if jh == 1 else None,
                                waits=[(ch("rdone"), i)] if (i >= 1 and jh == 0) else [])
                        p.q("vector", lambda v, buf=buf, hd=hd:
                            v.tensor_mul(rt1[:, :], buf[0:32, hd * NTOK:(hd + 1) * NTOK],
                                         cos_sb[:, :]),
                            inc=ch("rc1"), waits=[(ch("rmm"), p.val(ch("rmm")))])
                        p.q("vector", lambda v: v.tensor_mul(rt2[:, :], sin_sb[:, :],
                                                             ps_r[:, :]),
                            inc=ch("rc2"), waits=[(ch("rc1"), p.val(ch("rc1")))])
                        p.q("vector", lambda v, buf=buf, hd=hd:
                            v.tensor_add(buf[0:32, hd * NTOK:(hd + 1) * NTOK],
                                         rt1[:, :], rt2[:, :]),
                            inc=ch("rdone"), waits=[(ch("rc2"), p.val(ch("rc2")))])
                p.wait("tensor", ch("rdone"), p.val(ch("rdone")))
            p.wait("tensor", ch("vte"), p.val(ch("vte")))
            p.wait("sync", ch("vte"), p.val(ch("vte")))
            p.wait("sync", ch("qev"), NT * NH)
            p.wait("sync", ch("kev"), NT * NH)
            if rope:
                p.wait("sync", ch("rdone"), p.val(ch("rdone")))
            if dbg is not None:
                p.wait("gpsimd", ch("qev"), NT * NH)
                p.wait("gpsimd", ch("kev"), NT * NH)
                p.wait("gpsimd", ch("vte"), p.val(ch("vte")))
                p.dma("gpsimd", dbg[1], qb[:, :], ch("dbg"))
                p.dma("gpsimd", dbg[2], k_own[:, :], ch("dbg"))
                p.dma("gpsimd", dbg[3], v_own[:, :], ch("dbg"))
        with ExitStack() as ctxB:
            EB = ctxB.enter_context
            nonce = NONCE_BASE + (128 if rope else 256)
            KR = D + 16
            VR = NTOK + 16
            p.dma("sync", cc_kin[0:D, :].rearrange("(hc p) t -> p hc t", p=128),
                  k_own[:, :].rearrange("p (hc t) -> p hc t", hc=NH), ch("agd"))
            p.dma("sync", cc_vin[0:NTOK, :].rearrange("(r p) d -> p r d", p=128),
                  v_own[:, :].rearrange("p (r d) -> p r d", r=RT), ch("agd"))
            ncd8 = EB(nc.sbuf_tensor(tag + "nc8", [16, 64], BF16))
            p.q("gpsimd", lambda g: g.memset(ncd8[:, :], float(nonce)), inc=ch("nng"))
            p.dma("sync", cc_kin[D:D + 16, NTOK - 64:NTOK],
                  ncd8[:, :], ch("agd"), waits=[(ch("nng"), p.val(ch("nng")))])
            ncd8v = EB(nc.sbuf_tensor(tag + "nc8v", [16, 64], BF16))
            p.q("gpsimd", lambda g: g.memset(ncd8v[:, :], float(nonce + 64)),
                inc=ch("nng"))
            vag = p.dma("sync", cc_vin[NTOK:NTOK + 16, D - 64:D],
                        ncd8v[:, :], ch("agd"),
                        waits=[(ch("nng"), p.val(ch("nng")))])
            p.q("gpsimd", lambda g: g.collective_compute(
                "AllGather", ALU.bypass, replica_groups=PAIRS, ins=[cc_kin.opt()],
                outs=[cc_kout.opt()]), inc=ch("agc"), waits=[(ch("agd"), vag)])
            p.q("gpsimd", lambda g: g.collective_compute(
                "AllGather", ALU.bypass, replica_groups=PAIRS, ins=[cc_vin.opt()],
                outs=[cc_vout.opt()]), inc=ch("agc"))
            kprior = EB(nc.sbuf_tensor(tag + "kp", [128, NH * NTOK], BF16))
            vprior = EB(nc.sbuf_tensor(tag + "vp", [128, RT * D], BF16))
            agc = p.val(ch("agc"))
            emit_cc_guard(p, EB, tag + "K",
                          cc_kout[D:D + 16, NTOK - 4:NTOK].rearrange(
                              "a (o b) -> a o b", o=1), nonce,
                          ch("agc"), agc, npts=16)
            gok = emit_cc_guard(p, EB, tag + "V",
                                cc_vout[NTOK:NTOK + 16, D - 4:D].rearrange(
                                    "a (o b) -> a o b", o=1), nonce + 64,
                                ch("agc"), agc, npts=16)
            kok = p.val(tag + "K" + "ccok")
            p.dma("sync", kprior[:, :].rearrange("p (hc t) -> p hc t", hc=NH),
                  cc_kout[0:D, :].rearrange("(hc p) t -> p hc t", p=128),
                  ch("agr"), waits=[(tag + "K" + "ccok", kok)])
            p.dma("sync", vprior[:, :].rearrange("p (r d) -> p r d", r=RT),
                  cc_vout[0:NTOK, :].rearrange("(r p) d -> p r d", p=128),
                  ch("agr"), waits=[(tag + "V" + "ccok", gok)])
            agr = p.val(ch("agr"))
            p.wait("tensor", ch("agr"), agr)
            p.wait("vector", ch("agr"), agr)
            p.wait("scalar", ch("agr"), agr)
            if dbg is not None and len(dbg) > 4:
                p.wait("gpsimd", ch("agr"), agr)
                p.dma("gpsimd", dbg[4], kprior[:, :], ch("dbg"))
                p.dma("gpsimd", dbg[5], vprior[:, :], ch("dbg"))
                p.wait("gpsimd", ch("dbg"), p.val(ch("dbg")))

            Pp = EB(nc.sbuf_tensor(tag + "Pp", [128, NTOK], BF16))
            Po = EB(nc.sbuf_tensor(tag + "Po", [128, NTOK], BF16))
            PT = EB(nc.sbuf_tensor(tag + "PT", [128, 2 * NTOK], BF16))
            obf = EB(nc.sbuf_tensor(tag + "obf", [128, D], BF16))
            oT_r = EB(nc.sbuf_tensor(tag + "oTr", [128, D], BF16))
            ps_sp = EB(nc.psum_tensor(tag + "pssp", [128, 1024], F32))
            ps_so = EB(nc.psum_tensor(tag + "psso", [128, 1024], F32))
            ps_pt = EB(nc.psum_tensor(tag + "pspt", [128, 512], BF16))
            ps_pv = EB(nc.psum_tensor(tag + "pspv", [128, 128], F32))
            ps_sum = EB(nc.psum_tensor(tag + "pssm", [128, 1], F32))
            ones_bf = EB(nc.sbuf_tensor(tag + "ob1", [128, 1], BF16))
            p.q("gpsimd", lambda g: g.memset(ones_bf[:, :], 1.0), inc=ch("cg3"))
            p.wait("tensor", ch("cg3"), p.val(ch("cg3")))
            c_mp = cols[:, 0:1]; c_mo = cols[:, 1:2]; c_mf = cols[:, 2:3]
            c_bp = cols[:, 3:4]; c_bo = cols[:, 4:5]; c_sp = cols[:, 5:6]
            c_so = cols[:, 6:7]; c_sm = cols[:, 7:8]; c_ri = cols[:, 8:9]
            p.q("gpsimd", lambda g: g.memset(c_bo, 0.0), inc=ch("cg3"))
            cg3v = p.val(ch("cg3"))
            last_ptr = 0
            for r in range(RT):
                nko = r + 1
                for hd in range(NH):
                    i = r * NH + hd
                    qt = qb[:, hd * NTOK + r * 128: hd * NTOK + (r + 1) * 128]
                    sw = [(ch("expd"), 2 * i)] if i >= 1 else []
                    for j in range(2):
                        p.q("tensor", lambda t, qt=qt, hd=hd, j=j:
                            t.matmul(ps_sp[:, j * 512:(j + 1) * 512], qt,
                                     kprior[:, hd * NTOK + j * 512: hd * NTOK + (j + 1) * 512],
                                     start=True, stop=True),
                            waits=sw if j == 0 else ())
                    nso = (nko * 128 + 511) // 512
                    for j in range(nso):
                        hi = min(nko * 128, (j + 1) * 512)
                        p.q("tensor", lambda t, qt=qt, hd=hd, j=j, hi=hi:
                            t.matmul(ps_so[:, j * 512:hi], qt,
                                     k_own[:, hd * NTOK + j * 512: hd * NTOK + hi],
                                     start=True, stop=True),
                            inc=ch("smm") if j == nso - 1 else None)
                    smm = p.val(ch("smm"))
                    ew = [(ch("smm"), smm), (ch("ptr"), last_ptr)]
                    if i == 0:
                        ew.append((ch("cg3"), cg3v))
                    p.q("scalar", lambda s: s.activation(Pp[:, :], ps_sp[:, :], AF.Exp,
                                                         bias=lfl[:, 0:1]),
                        inc=ch("expd"), waits=ew)
                    p.q("scalar", lambda s, nko=nko:
                        s.activation(Po[:, 0:nko * 128], ps_so[:, 0:nko * 128], AF.Exp,
                                     bias=c_bo),
                        inc=ch("expd"), waits=ew)
                    expd = p.val(ch("expd"))
                    p.q("vector", lambda v, nko=nko:
                        v.tensor_tensor(Po[:, (nko - 1) * 128:nko * 128],
                                        Po[:, (nko - 1) * 128:nko * 128],
                                        tri_f[:, :], ALU.mult),
                        inc=ch("msk2"), waits=[(ch("expd"), expd)])
                    msk2 = p.val(ch("msk2"))
                    ntile = 8 + nko
                    for g4 in range((ntile + 3) // 4):
                        js = list(range(g4 * 4, min(ntile, g4 * 4 + 4)))
                        for jj, j in enumerate(js):
                            tw = []
                            if jj == 0:
                                tw.append((ch("expd"), expd))
                                tw.append((ch("pte"), p.val(ch("pte"))))
                                tw.append((ch("ote"), p.val(ch("ote"))))
                            if j == ntile - 1:
                                tw.append((ch("msk2"), msk2))
                            src_ = (Pp[:, j * 128:(j + 1) * 128] if j < 8 else
                                    Po[:, (j - 8) * 128:(j - 7) * 128])
                            p.q("tensor", lambda t, src_=src_, jj=jj:
                                t.transpose(ps_pt[:, jj * 128:(jj + 1) * 128], src_,
                                            ident_bf[:, :]),
                                inc=ch("ptr") if jj == len(js) - 1 else None, waits=tw)

                        def evac(v, js=js):
                            for jj, j in enumerate(js):
                                inst = v.tensor_copy(PT[:, j * 128:(j + 1) * 128],
                                                     ps_pt[:, jj * 128:(jj + 1) * 128])
                            return inst
                        evw = [(ch("ptr"), p.val(ch("ptr")))]
                        if i >= 1 or g4 >= 1:
                            evw.append((ch("pvmm"), p.val(ch("pvmm"))))
                        p.q("vector", evac, inc=ch("pte"), waits=evw)
                    last_ptr = p.val(ch("ptr"))
                    pte = p.val(ch("pte"))
                    for j in range(ntile):
                        vsrc = vprior if j < 8 else v_own
                        jr = j if j < 8 else j - 8
                        pw_ = [(ch("pte"), pte)] if j == 0 else []
                        if j == 0 and i >= 1:
                            pw_.append((ch("oev"), i))
                        p.q("tensor", lambda t, j=j, jr=jr, vsrc=vsrc, hd=hd, ntile=ntile:
                            t.matmul(ps_pv[:, :], PT[:, j * 128:(j + 1) * 128],
                                     vsrc[:, jr * D + hd * 128: jr * D + (hd + 1) * 128],
                                     start=(j == 0), stop=(j == ntile - 1)),
                            waits=pw_)
                        sw_ = [(ch("rsum"), i)] if (j == 0 and i >= 1) else []
                        p.q("tensor", lambda t, j=j, ntile=ntile:
                            t.matmul(ps_sum[:, :], PT[:, j * 128:(j + 1) * 128],
                                     ones_bf[:, :], start=(j == 0),
                                     stop=(j == ntile - 1)),
                            inc=ch("pvmm") if j == ntile - 1 else None, waits=sw_)
                    p.q("vector", lambda v: v.reciprocal(c_ri, ps_sum[:, :]),
                        inc=ch("rsum"), waits=[(ch("pvmm"), p.val(ch("pvmm")))])
                    ovw = [(ch("rsum"), p.val(ch("rsum")))]
                    if r >= 1 and hd == 0:
                        ovw.append((ch("odw"), 16 * r))
                    p.q("vector", lambda v, hd=hd:
                        v.tensor_scalar_mul(obf[:, hd * 128:(hd + 1) * 128], ps_pv[:, :],
                                            c_ri),
                        inc=ch("oev"), waits=ovw)
                for g4 in range(4):
                    for jj in range(4):
                        kc = g4 * 4 + jj
                        tw = []
                        if jj == 0:
                            tw.append((ch("oev"), p.val(ch("oev"))))
                            tw.append((ch("ote"), p.val(ch("ote"))))
                            tw.append((ch("pte"), p.val(ch("pte"))))
                            if r >= 1:
                                tw.append((ch("odw"), 16 * r))
                        p.q("tensor", lambda t, kc=kc, jj=jj:
                            t.transpose(ps_pt[:, jj * 128:(jj + 1) * 128],
                                        obf[:, kc * 128:(kc + 1) * 128], ident_bf[:, :]),
                            inc=ch("otr") if jj == 3 else None, waits=tw)

                    def oevac(v, g4=g4):
                        for jj in range(4):
                            kc = g4 * 4 + jj
                            inst = v.tensor_copy(oT_r[:, kc * 128:(kc + 1) * 128],
                                                 ps_pt[:, jj * 128:(jj + 1) * 128])
                        return inst
                    p.q("vector", oevac, inc=ch("ote"),
                        waits=[(ch("otr"), p.val(ch("otr")))])
                p.dma("sync", o_dram[:, r * D:(r + 1) * D], oT_r[:, :], ch("odw"),
                      waits=[(ch("ote"), p.val(ch("ote")))])
            p.wait("tensor", ch("odw"), p.val(ch("odw")))
            p.wait("gpsimd", ch("odw"), p.val(ch("odw")))
            p.wait("vector", ch("odw"), p.val(ch("odw")))
            p.wait("scalar", ch("odw"), p.val(ch("odw")))
            p.wait("sync", ch("odw"), p.val(ch("odw")))
    oT_sb = E(nc.sbuf_tensor(tag + "oT", [128, RT * D], BF16))
    y_sb = [E(nc.sbuf_tensor(f"{tag}y{i}", [128, 512], F32)) for i in range(2)]
    x2_sb = [E(nc.sbuf_tensor(f"{tag}x2{i}", [128, 512], F32)) for i in range(2)]
    bias128 = E(nc.sbuf_tensor(tag + "b128", [128, D], F32))
    ones_f = E(nc.sbuf_tensor(tag + "onf", [1, 128], F32))
    p.q("gpsimd", lambda g: g.memset(ones_f[:, :], 1.0), inc=ch("cg2"))
    orl = p.dma("sync", oT_sb[:, :], o_dram, ch("orl"))
    p.wait("tensor", ch("orl"), orl)
    with ExitStack() as ctxC:
        EC = ctxC.enter_context
        ps_b = EC(nc.psum_tensor(tag + "psb", [1, 512], F32))
        ps_bc = EC(nc.psum_tensor(tag + "psbc", [128, 512], F32))
        if bv is not None:
            bvbf = EC(nc.sbuf_tensor(tag + "bvb", [128, 16], BF16))
            bde_sb = EC(nc.sbuf_tensor(tag + "bde", [1, D], F32))
            bwbuf = [EC(nc.sbuf_tensor(f"{tag}bw{i}", [128, 512], BF16))
                     for i in range(2)]
            p.q("vector", lambda v: v.tensor_copy(bvbf[:, :], bv_sb[:, :]), inc=ch("bvb"))
            p.wait("tensor", ch("bvb"), p.val(ch("bvb")))
            for nd in range(4):
                for kcc in range(KC):
                    J = nd * KC + kcc
                    slot = bwbuf[J % 2]
                    dwb = p.dma("gpsimd", slot[:, 0:512], wdense_sl[nd, kcc], ch("bwd"),
                                waits=[(ch("bmm"), J - 1)] if J >= 2 else [])
                    p.q("tensor", lambda t, slot=slot, kcc=kcc:
                        t.matmul(ps_b[:, :], bvbf[:, kcc:kcc + 1], slot[:, 0:512],
                                 start=(kcc == 0), stop=(kcc == KC - 1)),
                        inc=ch("bmm"), waits=[(ch("bwd"), dwb)] +
                            ([(ch("bev"), nd)] if kcc == 0 and nd >= 1 else []))
                p.q("vector", lambda v, nd=nd:
                    v.tensor_copy(bias_f[:, nd * 512:(nd + 1) * 512], ps_b[:, :]),
                    inc=ch("bev"), waits=[(ch("bmm"), p.val(ch("bmm")))])
            bd_ = p.dma("sync", bde_sb[:, :], bdense, ch("bdd"))
            p.q("vector", lambda v: v.tensor_add(bias_f[:, :], bias_f[:, :], bde_sb[:, :]),
                inc=ch("bfin"), waits=[(ch("bdd"), bd_), (ch("bev"), p.val(ch("bev")))])
        else:
            bd_ = p.dma("sync", bias_f[:, :], bout, ch("bdd"))
            p.q("vector", lambda v: v.tensor_copy(bias_f[:, :], bias_f[:, :]),
                inc=ch("bfin"), waits=[(ch("bdd"), bd_)]) if False else None
            p.q("gpsimd", lambda g: g.nop(), inc=ch("bfin"), waits=[(ch("bdd"), bd_)])
        bfin = p.val(ch("bfin"))
        p.wait("tensor", ch("bfin"), bfin)
        p.wait("tensor", ch("cg2"), p.val(ch("cg2")))
        for nd in range(4):
            p.q("tensor", lambda t, nd=nd:
                t.matmul(ps_bc[:, :], ones_f[:, :], bias_f[:, nd * 512:(nd + 1) * 512],
                         start=True, stop=True),
                inc=ch("bcm"), waits=[(ch("bce"), nd)] if nd >= 1 else [])
            p.q("vector", lambda v, nd=nd:
                v.tensor_copy(bias128[:, nd * 512:(nd + 1) * 512], ps_bc[:, :]),
                inc=ch("bce"), waits=[(ch("bcm"), p.val(ch("bcm")))])
        p.wait("tensor", ch("bce"), p.val(ch("bce")))
        p.wait("gpsimd", ch("bce"), p.val(ch("bce")))
        p.wait("vector", ch("bce"), p.val(ch("bce")))
    emit_out_proj(p, E, oT_sb, wdense_sl, x_dram, y_dram, NH, wbuf, x2_sb, y_sb, tag,
                  bias128=bias128, act_ix=lambda hk, r: r * D + hk * 128)


def build_mha_launch(rope, debug=False):
    nc = bass.Bass("TRN2", target_bir_lowering=False)
    x_in = nc.dram_tensor("x_in", [NTOK, D], F32, kind="ExternalInput").ap()
    wq = nc.dram_tensor("wq", [16, 128, D], F32, kind="ExternalInput").ap()
    wk = nc.dram_tensor("wk", [16, 128, D], F32, kind="ExternalInput").ap()
    wv = nc.dram_tensor("wv", [16, 128, D], F32, kind="ExternalInput").ap()
    wde = nc.dram_tensor("wde", [4, 16, 128, 512], F32, kind="ExternalInput").ap()
    logflag = nc.dram_tensor("logflag", [128, 1], F32, kind="ExternalInput").ap()
    y = nc.dram_tensor("y", [NTOK, D], F32, kind="ExternalOutput").ap()
    cc_kin = nc.dram_tensor("cc_kin", [D + 16, NTOK], BF16, kind="Internal").ap()
    cc_kout = nc.dram_tensor("cc_kout", [2 * D + 32, NTOK], BF16, kind="Internal").ap()
    cc_vin = nc.dram_tensor("cc_vin", [NTOK + 16, D], BF16, kind="Internal").ap()
    cc_vout = nc.dram_tensor("cc_vout", [2 * NTOK + 32, D], BF16, kind="Internal").ap()
    kw = {}
    if rope:
        kw["cos_in"] = nc.dram_tensor("cos_in", [32, NTOK], F32, kind="ExternalInput").ap()
        kw["sin_in"] = nc.dram_tensor("sin_in", [32, NTOK], F32, kind="ExternalInput").ap()
        kw["rmat_in"] = nc.dram_tensor("rmat_in", [32, 32], F32, kind="ExternalInput").ap()
        kw["bqk"] = nc.dram_tensor("bqk", [128, 32], F32, kind="ExternalInput").ap()
        kw["bv"] = nc.dram_tensor("bv", [128, 16], F32, kind="ExternalInput").ap()
        kw["bdense"] = nc.dram_tensor("bdense", [1, D], F32, kind="ExternalInput").ap()
    else:
        kw["bout"] = nc.dram_tensor("bout", [1, D], F32, kind="ExternalInput").ap()
    if debug:
        dbgo = nc.dram_tensor("dbgo", [128, RT * D], F32, kind="ExternalOutput").ap()
        dbgq = nc.dram_tensor("dbgq", [128, 16 * NTOK], F32, kind="ExternalOutput").ap()
        dbgk = nc.dram_tensor("dbgk", [128, 16 * NTOK], F32, kind="ExternalOutput").ap()
        dbgv = nc.dram_tensor("dbgv", [128, RT * D], F32, kind="ExternalOutput").ap()
    with ExitStack() as ctx:
        p = Prog(nc, ctx.enter_context)
        bufs = emit_mha(p, ctx.enter_context, x_in, y, wq, wk, wv, wde, "m",
                 cc_kin, cc_kout, cc_vin, cc_vout, logflag, rope=rope, **kw,
                 dbg=(dbgo, dbgq, dbgk, dbgv) if debug else None)
        p.realize()
    return nc


def mha_a_maps(W_qkv, b_qkv, W_dense, b_dense):
    W = np.asarray(W_qkv).reshape(D, 16, 3, 128)
    Wq_all = np.ascontiguousarray(W[:, :, 0, :].reshape(D, D))
    Wk_all = np.ascontiguousarray(W[:, :, 1, :].reshape(D, D))
    Wv_all = np.ascontiguousarray(W[:, :, 2, :].reshape(D, D))
    b = np.asarray(b_qkv).reshape(16, 3, 128)
    bqk = np.zeros((128, 32), np.float32)
    bqk[:, 0:16] = b[:, 0, :].T
    bqk[:, 16:32] = b[:, 1, :].T
    bv = np.ascontiguousarray(b[:, 2, :].T)
    ROT = 32
    inv_freq = 1.0 / (10000.0 ** (np.arange(0, ROT, 2, dtype=np.float64) / ROT))
    rmat = np.zeros((32, 32), np.float32)
    for m in range(16):
        rmat[16 + m, m] = -1.0
        rmat[m, 16 + m] = 1.0
    def cs(parity):
        t = parity * NTOK + np.arange(NTOK, dtype=np.float64)
        fr = np.outer(t, inv_freq)
        emb = np.concatenate([fr, fr], axis=1)
        return (np.cos(emb).T.astype(np.float32),
                np.sin(emb).T.astype(np.float32))
    return {
        "wq": lhsT_slabs(np.ascontiguousarray(Wq_all.T)),
        "wk": lhsT_slabs(np.ascontiguousarray(Wk_all.T)),
        "wv": lhsT_slabs(np.ascontiguousarray(Wv_all.T)),
        "wde": rhs_slabs(np.asarray(W_dense)),
        "bqk": bqk, "bv": bv,
        "bdense": np.asarray(b_dense).reshape(1, D).astype(np.float32),
        "rmat_in": rmat,
    }, cs


def mha_b_maps(w_qkv, w_out, b_out):
    W = np.asarray(w_qkv)
    return {
        "wq": lhsT_slabs(np.ascontiguousarray(W[:, 0:D].T)),
        "wk": lhsT_slabs(np.ascontiguousarray(W[:, D:2 * D].T)),
        "wv": lhsT_slabs(np.ascontiguousarray(W[:, 2 * D:].T)),
        "wde": rhs_slabs(np.asarray(w_out)),
        "bout": np.asarray(b_out).reshape(1, D).astype(np.float32),
    }


def logflag_arr(parity):
    return np.full((128, 1), 0.0 if parity == 1 else -1e30, np.float32)





def emit_l1ffn(p, E, x_dram, y_dram, gamma128, beta128, bases_pad, slopes_pad,
               oscale, cc_in, cc_out, tag):
    nc = p.nc
    ch = lambda n: tag + n
    BIG = 1e30

    def step(eng, fn, extra=()):
        k = p.val(ch("sq"))
        p.q(eng, fn, inc=ch("sq"), waits=[(ch("sq"), k)] + list(extra))

    ident_f = E(nc.sbuf_tensor(tag + "idf", [128, 128], F32))
    p.q("gpsimd", lambda g: g.memset(ident_f[:, :], 0.0), inc=ch("cg"))
    p.q("gpsimd", lambda g: g.affine_select(
        ident_f[:, :], ident_f[:, :], compare_op=ALU.not_equal, fill=1.0, base=0,
        pattern=[[-1, 128]], channel_multiplier=1), inc=ch("cg"))
    tileid = E(nc.sbuf_tensor(tag + "tid", [128, 64], F32))
    p.q("gpsimd", lambda g: g.iota(tileid[:, :], [[1, 64]], base=0,
                                   channel_multiplier=0,
                                   allow_small_or_imprecise_dtypes=True), inc=ch("cg"))
    jrev64 = E(nc.sbuf_tensor(tag + "jr64", [128, 64], F32))
    p.q("gpsimd", lambda g: g.iota(jrev64[:, :], [[-1, 64]], base=63,
                                   channel_multiplier=0,
                                   allow_small_or_imprecise_dtypes=True), inc=ch("cg"))
    jrev8 = E(nc.sbuf_tensor(tag + "jr8", [128, 8], F32))
    p.q("gpsimd", lambda g: g.iota(jrev8[:, :], [[-1, 8]], base=7,
                                   channel_multiplier=0,
                                   allow_small_or_imprecise_dtypes=True), inc=ch("cg"))
    clc = E(nc.sbuf_tensor(tag + "clc", [128, 64], F32))
    p.q("gpsimd", lambda g: g.iota(clc[:, :].rearrange("p (a b) -> p a b", b=8),
                                   [[1, 8], [0, 8]], base=0, channel_multiplier=0,
                                   allow_small_or_imprecise_dtypes=True), inc=ch("cg"))
    gv116 = E(nc.sbuf_tensor(tag + "gv", [128, 512], F32))
    p.q("gpsimd", lambda g: g.iota(gv116[:, :].rearrange("p (a b) -> p a b", b=16),
                                   [[0, 32], [1, 16]], base=1, channel_multiplier=0,
                                   allow_small_or_imprecise_dtypes=True), inc=ch("cg"))
    negbig = E(nc.sbuf_tensor(tag + "nb", [128, 64], F32))
    p.q("gpsimd", lambda g: g.memset(negbig[:, :], -BIG), inc=ch("cg"))
    iotap = E(nc.sbuf_tensor(tag + "iop", [128, 1], F32))
    p.q("gpsimd", lambda g: g.iota(iotap[:, :], [[0, 1]], base=0, channel_multiplier=1,
                                   allow_small_or_imprecise_dtypes=True), inc=ch("cg"))
    ones_f = E(nc.sbuf_tensor(tag + "onf", [1, 128], F32))
    p.q("gpsimd", lambda g: g.memset(ones_f[:, :], 1.0), inc=ch("cg"))
    cgv = p.val(ch("cg"))
    for e in ("tensor", "vector", "scalar", "sync"):
        p.wait(e, ch("cg"), cgv)

    g128 = E(nc.sbuf_tensor(tag + "g128", [128, D], F32))
    b128 = E(nc.sbuf_tensor(tag + "b128", [128, D], F32))
    slo_sb = E(nc.sbuf_tensor(tag + "slo", [64, 512], F32))
    osc_sb = E(nc.sbuf_tensor(tag + "osc", [64, 1], F32))
    p.dma("sync", g128[:, :], gamma128, ch("cd"))
    p.dma("sync", b128[:, :], beta128, ch("cd"))
    p.dma("sync", slo_sb[:, :], slopes_pad[0:64, :], ch("cd"))
    cdv = p.dma("sync", osc_sb[:, :], oscale, ch("cd"))
    p.wait("vector", ch("cd"), cdv)
    p.wait("scalar", ch("cd"), cdv)

    xa = E(nc.sbuf_tensor(tag + "xa", [128, D], F32))
    ya = E(nc.sbuf_tensor(tag + "ya", [128, D], F32))
    xna = E(nc.sbuf_tensor(tag + "xna", [128, D], F32))
    ta = E(nc.sbuf_tensor(tag + "ta", [128, D], F32))
    scr = E(nc.sbuf_tensor(tag + "scr", [128, D], F32))
    cols = E(nc.sbuf_tensor(tag + "cols", [128, 24], F32))
    ss32 = E(nc.sbuf_tensor(tag + "ss32", [64, 32], F32))
    ssb = E(nc.sbuf_tensor(tag + "ssb", [128, D], F32))
    ssrow = E(nc.sbuf_tensor(tag + "ssrow", [1, D], F32))
    ieq = E(nc.sbuf_tensor(tag + "ieq", [128, 64], F32))
    ieq8 = E(nc.sbuf_tensor(tag + "ieq8", [128, 64], mybir.dt.int8))
    msel = E(nc.sbuf_tensor(tag + "msel", [128, 64], F32))
    m64a = E(nc.sbuf_tensor(tag + "m64a", [128, 64], F32))
    m64b = E(nc.sbuf_tensor(tag + "m64b", [128, 64], F32))
    idx_all = E(nc.sbuf_tensor(tag + "idxa", [128, 8], F32))
    widx_i = E(nc.sbuf_tensor(tag + "widx", [128, 8], I32))
    bidx_i = E(nc.sbuf_tensor(tag + "bidx", [128, 8], I32))
    xsw = E(nc.sbuf_tensor(tag + "xsw", [128, 32], F32))
    bas_t = E(nc.sbuf_tensor(tag + "bast", [128, 512], F32))
    slo_t = E(nc.sbuf_tensor(tag + "slot", [128, 512], F32))
    GE = E(nc.sbuf_tensor(tag + "GE", [128, 512], F32))
    OH = E(nc.sbuf_tensor(tag + "OH", [128, 512], F32))
    u32 = E(nc.sbuf_tensor(tag + "u32", [128, 32], F32))
    gif = E(nc.sbuf_tensor(tag + "gif", [128, 32], F32))
    xl32 = E(nc.sbuf_tensor(tag + "xl32", [128, 32], F32))
    bcol = E(nc.sbuf_tensor(tag + "bcol", [128, 32], F32))
    scol = E(nc.sbuf_tensor(tag + "scol", [128, 32], F32))
    dlt = E(nc.sbuf_tensor(tag + "dlt", [128, 32], F32))
    tmm = E(nc.sbuf_tensor(tag + "tmm", [64, 4], F32))
    ohT = E(nc.sbuf_tensor(tag + "ohT", [64, 128], F32))
    gmp = E(nc.sbuf_tensor(tag + "gmp", [64, 4], F32))
    mnx = E(nc.sbuf_tensor(tag + "mnx", [128, 4], F32))
    xn_dram = nc.dram_tensor(tag + "xnd", [NTOK, D], F32, kind="Internal").ap()
    ssd = nc.dram_tensor(tag + "ssd", [64, 32], F32, kind="Internal").ap()

    with ExitStack() as ctxA:
        EA = ctxA.enter_context
        ps_a = EA(nc.psum_tensor(tag + "psa", [128, 512], F32))
        ps_b = EA(nc.psum_tensor(tag + "psb", [128, 128], F32))

        c_s1 = cols[:, 0:1]; c_q1 = cols[:, 1:2]; c_m1 = cols[:, 2:3]
        c_v1 = cols[:, 3:4]; c_r1 = cols[:, 4:5]; c_sq2 = cols[:, 5:6]
        c_m2 = cols[:, 6:7]; c_v2 = cols[:, 7:8]; c_r2 = cols[:, 8:9]
        c_a = cols[:, 9:10]; c_b = cols[:, 10:11]; c_c = cols[:, 11:12]
        c_d = cols[:, 12:13]; c_e = cols[:, 13:14]; c_f = cols[:, 14:15]
        c_g = cols[:, 15:16]; c_h = cols[:, 16:17]; c_i = cols[:, 17:18]
        c_mn = mnx[:, 0:1]; c_mx = mnx[:, 1:2]; c_os = mnx[:, 2:3]
        c_eps = cols[:, 18:19]
        p.q("gpsimd", lambda g: g.memset(c_eps, EPS), inc=ch("cg2"))
        p.wait("scalar", ch("cg2"), p.val(ch("cg2")))

        step("vector", lambda v: v.reduce_sum(ss32[:, :],
                                              slo_sb[:, :].rearrange("p (a b) -> p a b", b=16),
                                              axis=AX.X),
             extra=[(ch("cd"), cdv)])
        step("vector", lambda v: v.tensor_scalar_mul(ss32[:, :], ss32[:, :], 1.0 / 16))
        step("scalar", lambda s: s.activation(ss32[:, :], ss32[:, :], AF.Sign))
        sq0 = p.val(ch("sq"))
        d1 = p.dma("sync", ssd, ss32[:, :], ch("ssd"), waits=[(ch("sq"), sq0)])
        d2 = p.dma("sync", ssrow[:, :],
                   ssd.rearrange("a b -> (a b)").rearrange("(o f) -> o f", o=1),
                   ch("ssd"), waits=[(ch("ssd"), d1)])
        step("tensor", lambda t: t.matmul(ps_a[:, :], ones_f[:, :], ssrow[:, 0:512],
                                          start=True, stop=True),
             extra=[(ch("ssd"), d2)])
        step("vector", lambda v: v.tensor_copy(ssb[:, 0:512], ps_a[:, :]))
        for nd in range(1, 4):
            step("tensor", lambda t, nd=nd: t.matmul(ps_a[:, :], ones_f[:, :],
                                                     ssrow[:, nd * 512:(nd + 1) * 512],
                                                     start=True, stop=True))
            step("vector", lambda v, nd=nd: v.tensor_copy(ssb[:, nd * 512:(nd + 1) * 512],
                                                          ps_a[:, :]))
        step("gpsimd", lambda g: g.memset(tmm[:, :], BIG))
        step("gpsimd", lambda g: g.memset(tmm[:, 1:2], -BIG))

        for r in range(RT):
            sqv = p.val(ch("sq"))
            dx = p.dma("sync", xa[:, :], x_dram[r * 128:(r + 1) * 128, :], ch("xd"),
                       waits=[(ch("sq"), sqv)])
            step("vector", lambda v: v.reduce_sum(c_s1, xa[:, :], axis=AX.X),
                 extra=[(ch("xd"), dx)])
            step("scalar", lambda s: s.activation(scr[:, :], xa[:, :], AF.Square))
            step("vector", lambda v: v.reduce_sum(c_q1, scr[:, :], axis=AX.X))
            step("vector", lambda v: v.tensor_scalar_mul(c_m1, c_s1, 1.0 / D))
            step("scalar", lambda s: s.activation(c_v1, c_m1, AF.Square))
            step("vector", lambda v: v.scalar_tensor_tensor(c_v1, c_q1, 1.0 / D, c_v1,
                                                            ALU.mult, ALU.subtract))
            step("scalar", lambda s: s.activation(c_r1, c_v1, AF.Sqrt, bias=c_eps))
            step("vector", lambda v: v.reciprocal(c_r1, c_r1)) if False else None
            step("vector", lambda v: v.reciprocal(c_a, c_r1))
            step("vector", lambda v: v.tensor_scalar(ya[:, :], xa[:, :], c_m1, c_a,
                                                     ALU.subtract, ALU.mult))
            step("vector", lambda v: v.reduce_sum(c_b, ya[:, :], axis=AX.X))
            step("scalar", lambda s: s.activation(scr[:, :], ya[:, :], AF.Square))
            step("vector", lambda v: v.reduce_sum(c_sq2, scr[:, :], axis=AX.X))
            step("vector", lambda v: v.tensor_scalar_mul(c_m2, c_b, 1.0 / D))
            step("scalar", lambda s: s.activation(c_v2, c_m2, AF.Square))
            step("vector", lambda v: v.scalar_tensor_tensor(c_v2, c_sq2, 1.0 / D, c_v2,
                                                            ALU.mult, ALU.subtract))
            step("scalar", lambda s: s.activation(c_r2, c_v2, AF.Sqrt, bias=c_eps))
            step("vector", lambda v: v.reciprocal(c_c, c_r2))
            step("vector", lambda v: v.tensor_scalar(xna[:, :], ya[:, :], c_m2, c_c,
                                                     ALU.subtract, ALU.mult))
            step("vector", lambda v: v.tensor_mul(xna[:, :], xna[:, :], g128[:, :]))
            step("vector", lambda v: v.tensor_add(xna[:, :], xna[:, :], b128[:, :]))
            sqv = p.val(ch("sq"))
            dxn = p.dma("sync", xn_dram[r * 128:(r + 1) * 128, :], xna[:, :], ch("xnw"),
                        waits=[(ch("sq"), sqv)])
            dyw = p.dma("sync", y_dram[r * 128:(r + 1) * 128, :], ya[:, :], ch("yst"),
                        waits=[(ch("sq"), sqv)])
            step("scalar", lambda s: s.activation(ta[:, :], xna[:, :], AF.Sign))
            step("vector", lambda v: v.tensor_mul(ta[:, :], ta[:, :], ssb[:, :]))
            step("vector", lambda v: v.reduce_sum(m64a[:, 0:8],
                                                  ta[:, :].rearrange("p (a b) -> p a b", b=256),
                                                  axis=AX.X))
            step("vector", lambda v: v.reduce_sum(m64b[:, :],
                                                  ta[:, :].rearrange("p (a b) -> p a b", b=32),
                                                  axis=AX.X))
            step("vector", lambda v: v.scalar_tensor_tensor(msel[:, 0:8], m64a[:, 0:8],
                                                            8.0, jrev8[:, :],
                                                            ALU.mult, ALU.add))
            step("vector", lambda v: v.reduce_max(c_d, msel[:, 0:8], axis=AX.X))
            step("vector", lambda v: v.reduce_max(c_e, m64a[:, 0:8], axis=AX.X))
            step("vector", lambda v: v.tensor_scalar(c_f, c_e, 8.0, 7.0, ALU.mult,
                                                     ALU.add))
            step("vector", lambda v: v.tensor_sub(c_f, c_f, c_d))
            step("vector", lambda v: v.tensor_single_scalar(ieq[:, :], clc[:, :], c_f,
                                                            ALU.is_equal))
            step("vector", lambda v: v.tensor_copy(ieq8[:, :], ieq[:, :]))
            step("vector", lambda v: v.select(msel[:, :], ieq8[:, :], m64b[:, :],
                                              negbig[:, :]))
            step("vector", lambda v: v.scalar_tensor_tensor(m64a[:, :], msel[:, :],
                                                            64.0, jrev64[:, :],
                                                            ALU.mult, ALU.add))
            step("vector", lambda v: v.reduce_max(c_g, m64a[:, :], axis=AX.X))
            step("vector", lambda v: v.reduce_max(c_h, msel[:, :], axis=AX.X))
            step("vector", lambda v: v.tensor_scalar(c_i, c_h, 64.0, 63.0, ALU.mult,
                                                     ALU.add))
            step("vector", lambda v, r=r: v.tensor_sub(idx_all[:, r:r + 1], c_i, c_g))
            step("vector", lambda v, r=r: v.tensor_scalar(c_a, iotap[:, :], float(r * 128),
                                                          64.0, ALU.add, ALU.mult))
            step("vector", lambda v, r=r: v.tensor_add(c_b, c_a, idx_all[:, r:r + 1]))
            step("vector", lambda v, r=r: v.tensor_copy(widx_i[:, r:r + 1], c_b))
            step("vector", lambda v, r=r: v.tensor_copy(bidx_i[:, r:r + 1],
                                                        idx_all[:, r:r + 1]))
            sqv = p.val(ch("sq"))
            dg = p.dma("gpsimd", xsw[:, :], None, ch("gd"),
                       waits=[]) if False else None
            p.q("gpsimd", lambda g, r=r: g.indirect_dma_start(
                xsw[:, :], None, xn_dram[0:128, 0:32],
                IndirectOffsetOnAxis(widx_i[:, r:r + 1], 0)),
                inc=ch("gd"), inc_by=16,
                waits=[(ch("sq"), sqv), (ch("xnw"), dxn)])
            gdv = p.val(ch("gd"))
            step("vector", lambda v: v.tensor_reduce(c_a, xsw[:, :], axis=AX.X,
                                                     op=ALU.min),
                 extra=[(ch("gd"), gdv)])
            step("vector", lambda v: v.tensor_reduce(c_b, xsw[:, :], axis=AX.X,
                                                     op=ALU.max))
            step("vector", lambda v, r=r: v.tensor_single_scalar(
                ieq[:, :], tileid[:, :], idx_all[:, r:r + 1], ALU.is_equal))
            step("vector", lambda v: v.tensor_scalar(msel[:, :], ieq[:, :], -BIG, BIG,
                                                     ALU.mult, ALU.add))
            step("vector", lambda v: v.tensor_scalar_mul(m64a[:, :], ieq[:, :], c_a))
            step("vector", lambda v: v.tensor_add(m64a[:, :], m64a[:, :], msel[:, :]))
            step("vector", lambda v: v.tensor_scalar(msel[:, :], ieq[:, :], BIG, -BIG,
                                                     ALU.mult, ALU.add))
            step("vector", lambda v: v.tensor_scalar_mul(m64b[:, :], ieq[:, :], c_b))
            step("vector", lambda v: v.tensor_add(m64b[:, :], m64b[:, :], msel[:, :]))
            step("tensor", lambda t: t.transpose(ps_b[:, :], m64a[:, 0:64][0:128, :],
                                                 ident_f[:, :])) if False else None
            step("tensor", lambda t: t.transpose(ps_a[0:64, 0:128], m64a[:, :],
                                                 ident_f[:, :]))
            step("vector", lambda v: v.tensor_reduce(tmm[:, 2:3], ps_a[0:64, 0:128],
                                                     axis=AX.X, op=ALU.min))
            step("tensor", lambda t: t.transpose(ps_a[0:64, 0:128], m64b[:, :],
                                                 ident_f[:, :]))
            step("vector", lambda v: v.tensor_reduce(tmm[:, 3:4], ps_a[0:64, 0:128],
                                                     axis=AX.X, op=ALU.max))
            step("vector", lambda v: v.tensor_tensor(tmm[:, 0:1], tmm[:, 0:1],
                                                     tmm[:, 2:3], ALU.min))
            step("vector", lambda v: v.tensor_tensor(tmm[:, 1:2], tmm[:, 1:2],
                                                     tmm[:, 3:4], ALU.max))
        step("vector", lambda v: v.tensor_scalar_mul(tmm[:, 2:3], tmm[:, 1:2], -1.0))
        step("vector", lambda v: v.tensor_copy(tmm[:, 3:4], tmm[:, 0:1]))
        sqv = p.val(ch("sq"))
        p.dma("sync", cc_in[0:64, :], tmm[:, 2:4], ch("ard"), waits=[(ch("sq"), sqv)])
        nonce = NONCE_BASE + (384 if tag == "G" else 448)
        nn = emit_cc_nonce(p, EA, tag,
                           cc_in[64:96, :].rearrange("a b -> (a b)").rearrange(
                               "(o f) -> o f", o=1), nonce)
        da = p.val(ch("ard"))
        p.q("gpsimd", lambda g: g.collective_compute(
            "AllReduce", ALU.min, replica_groups=WORLD, ins=[cc_in.opt()],
            outs=[cc_out.opt()]), inc=ch("arc"),
            waits=[(ch("ard"), da), (tag + "nncd", nn)])
        gok = emit_cc_guard(p, EA, tag,
                            cc_out[94:96, :].rearrange("a b -> (a b)").rearrange(
                                "(o f) -> o f", o=1), nonce,
                            ch("arc"), p.val(ch("arc")))
        db = p.dma("sync", gmp[:, 0:2], cc_out[0:64, :], ch("arr"),
                   waits=[(tag + "ccok", gok)])
        step("vector", lambda v: v.tensor_scalar_mul(gmp[:, 2:3], gmp[:, 0:1], -1.0),
             extra=[(ch("arr"), db)])
        step("vector", lambda v: v.tensor_copy(gmp[:, 0:1], gmp[:, 1:2]))
        step("vector", lambda v: v.tensor_copy(gmp[:, 1:2], gmp[:, 2:3]))
        step("vector", lambda v: v.tensor_copy(gmp[:, 2:3], osc_sb[:, :]))

        for r in range(RT):
            sqv = p.val(ch("sq"))
            p.q("gpsimd", lambda g, r=r: g.indirect_dma_start(
                xsw[:, :], None, xn_dram[0:128, 0:32],
                IndirectOffsetOnAxis(widx_i[:, r:r + 1], 0)),
                inc=ch("gd"), inc_by=16, waits=[(ch("sq"), sqv)])
            p.q("gpsimd", lambda g, r=r: g.indirect_dma_start(
                bas_t[:, :], None, bases_pad[0:128, 0:512],
                IndirectOffsetOnAxis(bidx_i[:, r:r + 1], 0)),
                inc=ch("gd"), inc_by=16)
            p.q("gpsimd", lambda g, r=r: g.indirect_dma_start(
                slo_t[:, :], None, slopes_pad[0:128, 0:512],
                IndirectOffsetOnAxis(bidx_i[:, r:r + 1], 0)),
                inc=ch("gd"), inc_by=16)
            gdv = p.val(ch("gd"))
            step("vector", lambda v, r=r: v.tensor_single_scalar(
                ieq[:, :], tileid[:, :], idx_all[:, r:r + 1], ALU.is_equal),
                 extra=[(ch("gd"), gdv)])
            step("tensor", lambda t: t.transpose(ps_a[0:64, 0:128], ieq[:, :],
                                                 ident_f[:, :]))
            step("vector", lambda v: v.tensor_copy(ohT[:, :], ps_a[0:64, 0:128]))
            step("tensor", lambda t: t.matmul(ps_b[:, 0:3], ohT[:, :], gmp[:, 0:3],
                                              start=True, stop=True))
            step("vector", lambda v: v.tensor_copy(mnx[:, 0:3], ps_b[:, 0:3]))
            step("vector", lambda v: v.tensor_sub(c_a, c_mx, c_mn))
            step("vector", lambda v: v.tensor_scalar_add(c_a, c_a, 1e-8))
            step("vector", lambda v: v.reciprocal(c_b, c_a))
            step("vector", lambda v: v.tensor_scalar(u32[:, :], xsw[:, :], c_mn, c_b,
                                                     ALU.subtract, ALU.mult))
            step("vector", lambda v: v.tensor_scalar(u32[:, :], u32[:, :], 0.0,
                                                     1.0 - 1e-6, ALU.max, ALU.min))
            step("vector", lambda v: v.tensor_scalar_mul(u32[:, :], u32[:, :], 16.0))
            for g_ in range(16):
                step("vector", lambda v, g_=g_: v.tensor_single_scalar(
                    GE[:, :].rearrange("p (a b) -> p a b", b=16)[:, :, g_:g_ + 1],
                    u32[:, :].rearrange("p (a o) -> p a o", o=1), float(g_ + 1), ALU.is_ge))
            step("vector", lambda v: v.reduce_sum(gif[:, :],
                                                  GE[:, :].rearrange("p (a b) -> p a b", b=16),
                                                  axis=AX.X))
            step("vector", lambda v: v.tensor_sub(xl32[:, :], u32[:, :], gif[:, :]))
            step("vector", lambda v: v.tensor_sub(
                OH[:, :].rearrange("p (a b) -> p a b", b=16)[:, :, 1:16],
                GE[:, :].rearrange("p (a b) -> p a b", b=16)[:, :, 0:15],
                GE[:, :].rearrange("p (a b) -> p a b", b=16)[:, :, 1:16]))
            step("vector", lambda v: v.tensor_scalar(
                OH[:, :].rearrange("p (a b) -> p a b", b=16)[:, :, 0:1],
                GE[:, :].rearrange("p (a b) -> p a b", b=16)[:, :, 0:1],
                -1.0, -1.0, ALU.mult, ALU.subtract)) if False else None
            step("vector", lambda v: v.tensor_scalar(
                OH[:, :].rearrange("p (a b) -> p a b", b=16)[:, :, 0:1],
                GE[:, :].rearrange("p (a b) -> p a b", b=16)[:, :, 0:1],
                -1.0, 1.0, ALU.mult, ALU.add))
            step("vector", lambda v: v.tensor_mul(bas_t[:, :], bas_t[:, :], OH[:, :]))
            step("vector", lambda v: v.tensor_mul(slo_t[:, :], slo_t[:, :], OH[:, :]))
            step("vector", lambda v: v.reduce_sum(bcol[:, :],
                                                  bas_t[:, :].rearrange("p (a b) -> p a b", b=16),
                                                  axis=AX.X))
            step("vector", lambda v: v.reduce_sum(scol[:, :],
                                                  slo_t[:, :].rearrange("p (a b) -> p a b", b=16),
                                                  axis=AX.X))
            step("vector", lambda v: v.tensor_mul(dlt[:, :], scol[:, :], xl32[:, :]))
            step("vector", lambda v: v.tensor_add(dlt[:, :], dlt[:, :], bcol[:, :]))
            step("vector", lambda v: v.tensor_scalar_mul(dlt[:, :], dlt[:, :], c_os))
            sqv = p.val(ch("sq"))
            dx2 = p.dma("sync", xa[:, :], x_dram[r * 128:(r + 1) * 128, :], ch("x2d"),
                        waits=[(ch("sq"), sqv)])
            dy2 = p.dma("sync", ya[:, :], y_dram[r * 128:(r + 1) * 128, :], ch("y2d"),
                        waits=[(ch("sq"), sqv), (ch("yst"), p.val(ch("yst")))])
            step("vector", lambda v: v.tensor_add(ya[:, :], ya[:, :], xa[:, :]),
                 extra=[(ch("x2d"), dx2), (ch("y2d"), dy2)])
            for i in range(64):
                step("vector", lambda v, i=i: v.scalar_tensor_tensor(
                    ya[:, 32 * i:32 * i + 32], dlt[:, :], ieq[:, i:i + 1],
                    ya[:, 32 * i:32 * i + 32], ALU.mult, ALU.add))
            sqv = p.val(ch("sq"))
            p.dma("sync", y_dram[r * 128:(r + 1) * 128, :], ya[:, :], ch("yfw"),
                  waits=[(ch("sq"), sqv)])
        yfw = p.val(ch("yfw"))
        p.wait("sync", ch("yfw"), yfw)
        p.wait("tensor", ch("yfw"), yfw)
        p.wait("vector", ch("yfw"), yfw)
        p.wait("gpsimd", ch("yfw"), yfw)
    return {"idx": idx_all[:, :], "gmp": gmp[:, :], "xnd": xn_dram,
            "tmm": tmm[:, :], "ccout": cc_out, "m64a": m64a[:, :]}


def build_l1ffn_launch(debug=False):
    nc = bass.Bass("TRN2", target_bir_lowering=False)
    x_in = nc.dram_tensor("x_in", [NTOK, D], F32, kind="ExternalInput").ap()
    gamma128 = nc.dram_tensor("gamma128", [128, D], F32, kind="ExternalInput").ap()
    beta128 = nc.dram_tensor("beta128", [128, D], F32, kind="ExternalInput").ap()
    bases_pad = nc.dram_tensor("bases_pad", [128, 512], F32, kind="ExternalInput").ap()
    slopes_pad = nc.dram_tensor("slopes_pad", [128, 512], F32, kind="ExternalInput").ap()
    oscale = nc.dram_tensor("oscale", [64, 1], F32, kind="ExternalInput").ap()
    y = nc.dram_tensor("y", [NTOK, D], F32, kind="ExternalOutput").ap()
    cc_in = nc.dram_tensor("cc_in", [96, 2], F32, kind="Internal").ap()
    cc_out = nc.dram_tensor("cc_out", [96, 2], F32, kind="Internal").ap()
    if debug:
        didx = nc.dram_tensor("didx", [128, 8], F32, kind="ExternalOutput").ap()
        dgmp = nc.dram_tensor("dgmp", [64, 4], F32, kind="ExternalOutput").ap()
        dxn = nc.dram_tensor("dxn", [NTOK, D], F32, kind="ExternalOutput").ap()
        dtmm = nc.dram_tensor("dtmm", [64, 4], F32, kind="ExternalOutput").ap()
        dcc = nc.dram_tensor("dcc", [64, 2], F32, kind="ExternalOutput").ap()
        dm64 = nc.dram_tensor("dm64", [128, 64], F32, kind="ExternalOutput").ap()
    with ExitStack() as ctx:
        p = Prog(nc, ctx.enter_context)
        bufs = emit_l1ffn(p, ctx.enter_context, x_in, y, gamma128, beta128, bases_pad,
                   slopes_pad, oscale, cc_in, cc_out, "s")
        if debug:
            p.wait("gpsimd", "syfw", p.val("syfw"))
            p.dma("gpsimd", didx, bufs["idx"], "dbg")
            p.dma("gpsimd", dgmp, bufs["gmp"], "dbg")
            p.dma("gpsimd", dxn, bufs["xnd"], "dbg")
            p.dma("gpsimd", dtmm, bufs["tmm"], "dbg")
            p.dma("gpsimd", dcc, bufs["ccout"], "dbg")
            p.dma("gpsimd", dm64, bufs["m64a"], "dbg")
            p.wait("gpsimd", "dbg", p.val("dbg"))
        p.realize()
    return nc


def l1ffn_maps(ln_gamma, ln_beta, spline_bases, spline_slopes, output_scale):
    g = np.ascontiguousarray(np.broadcast_to(np.asarray(ln_gamma), (128, D))).astype(np.float32)
    b = np.ascontiguousarray(np.broadcast_to(np.asarray(ln_beta), (128, D))).astype(np.float32)
    bp = np.zeros((128, 512), np.float32)
    bp[0:64] = np.asarray(spline_bases).reshape(64, 512)
    sp = np.zeros((128, 512), np.float32)
    sp[0:64] = np.asarray(spline_slopes).reshape(64, 512)
    return {"gamma128": g, "beta128": b, "bases_pad": bp, "slopes_pad": sp,
            "oscale": np.asarray(output_scale).reshape(64, 1).astype(np.float32)}



def lhsT_slabs(W):
    out_f, in_f = W.shape
    a = W.reshape(out_f // 128, 128, in_f // 128, 128)
    a = a.transpose(0, 3, 2, 1)
    return np.ascontiguousarray(a.reshape(out_f // 128, 128, in_f))


def rhs_slabs(M, n_slice=512):
    K, N = M.shape
    a = M.reshape(K // 128, 128, N // n_slice, n_slice)
    a = a.transpose(2, 0, 1, 3)
    return np.ascontiguousarray(a)



def build_ffn_launch(debug=False):
    nc = bass.Bass("TRN2", target_bir_lowering=False)
    x_in = nc.dram_tensor("x_in", [NTOK, D], F32, kind="ExternalInput").ap()
    wg = nc.dram_tensor("wg", [HC, 128, D], F32, kind="ExternalInput").ap()
    wu = nc.dram_tensor("wu", [HC, 128, D], F32, kind="ExternalInput").ap()
    wd = nc.dram_tensor("wd", [4, HC, 128, 512], F32, kind="ExternalInput").ap()
    y = nc.dram_tensor("y", [NTOK, D], F32, kind="ExternalOutput").ap()
    if debug:
        dxnT = nc.dram_tensor("dxnT", [128, KC * NTOK], F32, kind="ExternalOutput").ap()
        dh = nc.dram_tensor("dh", [128, HC * NTOK], F32, kind="ExternalOutput").ap()
    with ExitStack() as ctx:
        p = Prog(nc, ctx.enter_context)
        bufs = emit_ffn(p, ctx.enter_context, x_in, y, wg, wu, wd, "f")
        if debug:
            p.wait("gpsimd", "fyw", p.val("fyw"))
            p.dma("gpsimd", dxnT, bufs["xnT"][:, :], "dbgd")
            p.dma("gpsimd", dh, bufs["h"][:, :], "dbgd")
            p.wait("gpsimd", "dbgd", p.val("dbgd"))
        p.realize()
    return nc


def ffn_weight_maps(w_gate, w_up, w_down):
    return {
        "wg": lhsT_slabs(np.asarray(w_gate)),
        "wu": lhsT_slabs(np.asarray(w_up)),
        "wd": rhs_slabs(np.ascontiguousarray(np.asarray(w_down).T)),
    }



_prog_cache = {}


def _get(name, builder):
    if name not in _prog_cache:
        _prog_cache[name] = builder()
    return _prog_cache[name]


def kernel_multilaunch(**inp):
    x = np.ascontiguousarray(np.asarray(inp["L0_attn_a_x"], np.float32))
    shards = [np.ascontiguousarray(x[c // 2, (c % 2) * NTOK:(c % 2 + 1) * NTOK])
              for c in range(8)]

    flags = [np.full((128, 1), float(c % 2), np.float32) for c in range(8)]
    lflags = [logflag_arr(c % 2) for c in range(8)]

    dn_m = deltanet_maps(inp["L0_attn_a_Wq"], inp["L0_attn_a_Wk"], inp["L0_attn_a_Wv"],
                         inp["L0_attn_a_Wo"], inp["L0_attn_a_Wb"], inp["L0_attn_a_bb"],
                         inp["L0_attn_a_Wd"], inp["L0_attn_a_bd"])
    ffn_m = ffn_weight_maps(inp["L0_ffn_w_gate"], inp["L0_ffn_w_up"],
                            inp["L0_ffn_w_down"])
    ema_m = ema_maps(inp["L0_attn_b_Wv"], inp["L0_attn_b_Wg"], inp["L0_attn_b_bg"],
                     inp["L0_attn_b_Wq"], inp["L0_attn_b_bq"], inp["L0_attn_b_Wo"])
    mha_a_m, cs_fn = mha_a_maps(inp["L1_attn_a_W_qkv"], inp["L1_attn_a_b_qkv"],
                                inp["L1_attn_a_W_dense"], inp["L1_attn_a_b_dense"])
    mha_b_m = mha_b_maps(inp["L1_attn_b_w_qkv"], inp["L1_attn_b_w_out"],
                         inp["L1_attn_b_b_out"])
    l1_m = l1ffn_maps(inp["L1_ffn_ln_gamma"], inp["L1_ffn_ln_beta"],
                      inp["L1_ffn_spline_bases"], inp["L1_ffn_spline_slopes"],
                      inp["L1_ffn_output_scale"])
    cos0, sin0 = cs_fn(0)
    cos1, sin1 = cs_fn(1)

    def run(nc, maps):
        res = run_bass_kernel_spmd(nc, maps, core_ids=list(range(8)))
        return [res.results[c]["y"] for c in range(8)]

    nc = _get("dn", build_deltanet_launch)
    shards = run(nc, [{"x_in": shards[c], "flag": flags[c], **dn_m}
                      for c in range(8)])
    nc = _get("ffn", build_ffn_launch)
    shards = run(nc, [{"x_in": shards[c], **ffn_m} for c in range(8)])
    nc = _get("ema", build_ema_launch)
    shards = run(nc, [{"x_in": shards[c], "flag": flags[c], **ema_m}
                      for c in range(8)])
    nc = _get("ffn", build_ffn_launch)
    shards = run(nc, [{"x_in": shards[c], **ffn_m} for c in range(8)])
    nc = _get("mha_a", lambda: build_mha_launch(True))
    maps = []
    for c in range(8):
        cos, sin = (cos0, sin0) if c % 2 == 0 else (cos1, sin1)
        maps.append({"x_in": shards[c], "logflag": lflags[c], "cos_in": cos,
                     "sin_in": sin, **mha_a_m})
    shards = run(nc, maps)
    nc = _get("l1", build_l1ffn_launch)
    shards = run(nc, [{"x_in": shards[c], **l1_m} for c in range(8)])
    nc = _get("mha_b", lambda: build_mha_launch(False))
    shards = run(nc, [{"x_in": shards[c], "logflag": lflags[c], **mha_b_m}
                      for c in range(8)])
    nc = _get("l1", build_l1ffn_launch)
    shards = run(nc, [{"x_in": shards[c], **l1_m} for c in range(8)])

    out = np.zeros((B, T, D), np.float32)
    for c in range(8):
        out[c // 2, (c % 2) * NTOK:(c % 2 + 1) * NTOK] = shards[c]
    return out



def build_fused(n_sub=8, dbg_h=False, dump_x=False):
    nc = bass.Bass("TRN2", target_bir_lowering=False)
    DT = lambda n, sh, dt=F32, kind="ExternalInput": nc.dram_tensor(n, sh, dt, kind=kind).ap()
    x_in = DT("x_in", [NTOK, D])
    y_out = DT("y_out", [NTOK, D], F32, "ExternalOutput")
    xb1 = DT("xb1", [NTOK, D], F32, "Internal")
    xb2 = DT("xb2", [NTOK, D], F32, "Internal")
    flag = DT("flag", [128, 1])
    logflag = DT("logflag", [128, 1])
    a_wqkbd = DT("a_wqkbd", [4, 128, KC * 64])
    a_wv = DT("a_wv", [128, KC * 128])
    a_wo = DT("a_wo", [4, 1, 128, 512])
    a_bbd = DT("a_bbd", [64, 2])
    a_ccin = DT("a_ccin", [68, 128], F32, "Internal")
    a_ccout = DT("a_ccout", [136, 128], F32, "Internal")
    f_wg = DT("f_wg", [HC, 128, D])
    f_wu = DT("f_wu", [HC, 128, D])
    f_wd = DT("f_wd", [4, HC, 128, 512])
    e_wv = DT("e_wv", [KC, 128, D])
    e_wg = DT("e_wg", [KC, 128, D])
    e_wq = DT("e_wq", [KC, 128, D])
    e_wo = DT("e_wo", [4, KC, 128, 512])
    e_bgq = DT("e_bgq", [128, 2 * KC])
    e_dv = DT("e_dv", [128, NTOK])
    e_ccin = DT("e_ccin", [132, KC], F32, "Internal")
    e_ccout = DT("e_ccout", [264, KC], F32, "Internal")
    cc_kin = DT("cc_kin", [D + 16, NTOK], BF16, "Internal")
    cc_kout = DT("cc_kout", [2 * D + 32, NTOK], BF16, "Internal")
    cc_vin = DT("cc_vin", [NTOK + 16, D], BF16, "Internal")
    cc_vout = DT("cc_vout", [2 * NTOK + 32, D], BF16, "Internal")
    ma_wq = DT("ma_wq", [16, 128, D]); ma_wk = DT("ma_wk", [16, 128, D])
    ma_wv = DT("ma_wv", [16, 128, D]); ma_wde = DT("ma_wde", [4, 16, 128, 512])
    ma_bqk = DT("ma_bqk", [128, 32]); ma_bv = DT("ma_bv", [128, 16])
    ma_bde = DT("ma_bde", [1, D])
    cos_in = DT("cos_in", [32, NTOK]); sin_in = DT("sin_in", [32, NTOK])
    rmat_in = DT("rmat_in", [32, 32])
    l_g = DT("l_g", [128, D]); l_b = DT("l_b", [128, D])
    l_bp = DT("l_bp", [128, 512]); l_sp = DT("l_sp", [128, 512])
    l_osc = DT("l_osc", [64, 1])
    l_ccin = DT("l_ccin", [96, 2], F32, "Internal")
    l_ccout = DT("l_ccout", [96, 2], F32, "Internal")
    l_ccin2 = DT("l_ccin2", [96, 2], F32, "Internal")
    l_ccout2 = DT("l_ccout2", [96, 2], F32, "Internal")
    mb_wq = DT("mb_wq", [16, 128, D]); mb_wk = DT("mb_wk", [16, 128, D])
    mb_wv = DT("mb_wv", [16, 128, D]); mb_wde = DT("mb_wde", [4, 16, 128, 512])
    mb_bout = DT("mb_bout", [1, D])

    def sub_k(k, p, E, xs, ys):
        if k == 0:
            emit_deltanet(p, E, xs, ys, a_wqkbd, a_wv, a_wo, a_bbd, flag,
                          a_ccin, a_ccout, "A")
            return "Ayw"
        if k == 1:
            emit_ffn(p, E, xs, ys, f_wg, f_wu, f_wd, "B")
            return "Byw"
        if k == 2:
            emit_ema(p, E, xs, ys, e_wv, e_wg, e_wq, e_wo, e_bgq, e_dv, flag,
                     e_ccin, e_ccout, "C")
            return "Cyw"
        if k == 3:
            emit_ffn(p, E, xs, ys, f_wg, f_wu, f_wd, "E")
            return "Eyw"
        if k == 4:
            emit_mha(p, E, xs, ys, ma_wq, ma_wk, ma_wv, ma_wde, "F",
                     cc_kin, cc_kout, cc_vin, cc_vout, logflag, rope=True,
                     cos_in=cos_in, sin_in=sin_in, rmat_in=rmat_in,
                     bqk=ma_bqk, bv=ma_bv, bdense=ma_bde)
            return "Fyw"
        if k == 5:
            emit_l1ffn(p, E, xs, ys, l_g, l_b, l_bp, l_sp, l_osc,
                       l_ccin, l_ccout, "G")
            return "Gyfw"
        if k == 6:
            emit_mha(p, E, xs, ys, mb_wq, mb_wk, mb_wv, mb_wde, "H",
                     cc_kin, cc_kout, cc_vin, cc_vout, logflag, rope=False,
                     bout=mb_bout, dbg=dbg_tensors)
            return "Hyw"
        emit_l1ffn(p, E, xs, ys, l_g, l_b, l_bp, l_sp, l_osc,
                   l_ccin2, l_ccout2, "I")
        return "Iyfw"

    dbg_tensors = None
    if dbg_h:
        dbg_tensors = (DT("dbgo", [128, RT * D], F32, "ExternalOutput"),
                       DT("dbgq", [128, 16 * NTOK], F32, "ExternalOutput"),
                       DT("dbgk", [128, 16 * NTOK], F32, "ExternalOutput"),
                       DT("dbgv", [128, RT * D], F32, "ExternalOutput"),
                       DT("dbgkp", [128, 16 * NTOK], F32, "ExternalOutput"),
                       DT("dbgvp", [128, RT * D], F32, "ExternalOutput"))
    with ExitStack() as ctx:
        p = Prog(nc, ctx.enter_context)
        cur = x_in
        pong = [xb1, xb2]
        for k in range(n_sub):
            ys = y_out if k == n_sub - 1 else pong[k % 2]
            with ExitStack() as s:
                term = sub_k(k, p, s.enter_context, cur, ys)
                p.seal(term)
            cur = ys
        if dump_x:
            dx1 = nc.dram_tensor("dump_x1", [NTOK, D], F32, kind="ExternalOutput").ap()
            dx2 = nc.dram_tensor("dump_x2", [NTOK, D], F32, kind="ExternalOutput").ap()
            p.dma("gpsimd", dx1, xb1, "dmpx")
            p.dma("gpsimd", dx2, xb2, "dmpx")
            p.wait("gpsimd", "dmpx", p.val("dmpx"))
        p.realize()
    return nc


def kernel_fused(**inp):
    x = np.ascontiguousarray(np.asarray(inp["L0_attn_a_x"], np.float32))
    shards = [np.ascontiguousarray(x[c // 2, (c % 2) * NTOK:(c % 2 + 1) * NTOK])
              for c in range(8)]
    dn_m = deltanet_maps(inp["L0_attn_a_Wq"], inp["L0_attn_a_Wk"], inp["L0_attn_a_Wv"],
                         inp["L0_attn_a_Wo"], inp["L0_attn_a_Wb"], inp["L0_attn_a_bb"],
                         inp["L0_attn_a_Wd"], inp["L0_attn_a_bd"])
    ffn_m = ffn_weight_maps(inp["L0_ffn_w_gate"], inp["L0_ffn_w_up"],
                            inp["L0_ffn_w_down"])
    ema_m = ema_maps(inp["L0_attn_b_Wv"], inp["L0_attn_b_Wg"], inp["L0_attn_b_bg"],
                     inp["L0_attn_b_Wq"], inp["L0_attn_b_bq"], inp["L0_attn_b_Wo"])
    mha_a_m, cs_fn = mha_a_maps(inp["L1_attn_a_W_qkv"], inp["L1_attn_a_b_qkv"],
                                inp["L1_attn_a_W_dense"], inp["L1_attn_a_b_dense"])
    mha_b_m = mha_b_maps(inp["L1_attn_b_w_qkv"], inp["L1_attn_b_w_out"],
                         inp["L1_attn_b_b_out"])
    l1_m = l1ffn_maps(inp["L1_ffn_ln_gamma"], inp["L1_ffn_ln_beta"],
                      inp["L1_ffn_spline_bases"], inp["L1_ffn_spline_slopes"],
                      inp["L1_ffn_output_scale"])
    cosl = [cs_fn(0), cs_fn(1)]
    nc = _get("fused", build_fused)
    maps = []
    for c in range(8):
        h = c % 2
        cos, sin = cosl[h]
        maps.append({
            "x_in": shards[c],
            "flag": np.full((128, 1), float(h), np.float32),
            "logflag": logflag_arr(h),
            "a_wqkbd": dn_m["wqkbd"], "a_wv": dn_m["wv"], "a_wo": dn_m["wo"],
            "a_bbd": dn_m["bbd"],
            "f_wg": ffn_m["wg"], "f_wu": ffn_m["wu"], "f_wd": ffn_m["wd"],
            "e_wv": ema_m["wv"], "e_wg": ema_m["wg"], "e_wq": ema_m["wq"],
            "e_wo": ema_m["wo"], "e_bgq": ema_m["bgq"], "e_dv": ema_m["dv128"],
            "ma_wq": mha_a_m["wq"], "ma_wk": mha_a_m["wk"], "ma_wv": mha_a_m["wv"],
            "ma_wde": mha_a_m["wde"], "ma_bqk": mha_a_m["bqk"],
            "ma_bv": mha_a_m["bv"], "ma_bde": mha_a_m["bdense"],
            "cos_in": cos, "sin_in": sin, "rmat_in": mha_a_m["rmat_in"],
            "l_g": l1_m["gamma128"], "l_b": l1_m["beta128"],
            "l_bp": l1_m["bases_pad"], "l_sp": l1_m["slopes_pad"],
            "l_osc": l1_m["oscale"],
            "mb_wq": mha_b_m["wq"], "mb_wk": mha_b_m["wk"], "mb_wv": mha_b_m["wv"],
            "mb_wde": mha_b_m["wde"], "mb_bout": mha_b_m["bout"],
        })
    res = run_bass_kernel_spmd(nc, maps, core_ids=list(range(8)))
    out = np.zeros((B, T, D), np.float32)
    for c in range(8):
        out[c // 2, (c % 2) * NTOK:(c % 2 + 1) * NTOK] = res.results[c]["y_out"]
    return out


def kernel(**inp):
    import os
    if os.environ.get("K_MULTILAUNCH"):
        return kernel_multilaunch(**inp)
    return kernel_fused(**inp)
